# version 03
# baseline (speedup 1.0000x reference)
import contextlib
import numpy as np
import ml_dtypes
import concourse.bass as bass
import concourse.mybir as mybir
from concourse.bass_utils import run_bass_kernel_spmd

F32 = mybir.dt.float32
BF16 = mybir.dt.bfloat16
AF = mybir.ActivationFunctionType
ALU = mybir.AluOpType
AX = mybir.AxisListType
nbf = ml_dtypes.bfloat16

NCORE = 8
T = 1024
TT = 512
NTT = T // TT
D = 2048
DC = D // 128
INTER = 8192
EPS = 1e-5

ENGINES = ("sync", "tensor", "vector", "scalar", "gpsimd")



class Sem:
    __slots__ = ("h", "n")

    def __init__(self, handle):
        self.h = handle
        self.n = 0


class Buf:
    __slots__ = ("prod", "cons", "dma_owner", "dsem")

    def __init__(self, dma_owner=False):
        self.prod = {}
        self.cons = {}
        self.dma_owner = dma_owner
        self.dsem = None

    def read_waits(self, w):
        w.extend(self.prod.values())

    def note_read(self, h, v):
        k = id(h)
        if k not in self.cons or self.cons[k][1] < v:
            self.cons[k] = (h, v)

    def write_waits(self, w):
        w.extend(self.prod.values())
        w.extend(self.cons.values())

    def note_write(self, h, v):
        k = id(h)
        if k not in self.prod or self.prod[k][1] < v:
            self.prod[k] = (h, v)


class Em:
    def __init__(self, nc):
        self.nc = nc
        self.q = {e: [] for e in ENGINES}
        self.hw = {e: {} for e in ENGINES}
        self.pool = []
        self.used = []

    def take_sem(self):
        s = self.pool.pop()
        self.used.append(s)
        return s

    def mark(self):
        return len(self.used)

    def release(self, mark):
        self.pool.extend(self.used[mark:])
        del self.used[mark:]

    def _dedupe(self, eng, waits):
        out = []
        hw = self.hw[eng]
        for h, v in waits:
            k = id(h)
            if hw.get(k, -1) >= v:
                continue
            hw[k] = v
            out.append((h, v))
        return tuple(out)

    def emit(self, eng, method, args, kwargs, sem, inc, waits, bufs_r=(), bufs_w=()):
        w = []
        for b in bufs_r:
            b.read_waits(w)
        for b in bufs_w:
            b.write_waits(w)
        w.extend(waits)
        w = self._dedupe(eng, w)
        if sem is not None:
            sem.n += inc
            tgt = (sem.h, sem.n)
        else:
            tgt = None

        def f(e, method=method, args=args, kwargs=kwargs, w=w, tgt=tgt, inc=inc):
            for hh, vv in w:
                e.wait_ge(hh, vv)
            ins = getattr(e, method)(*args, **kwargs)
            if tgt is not None:
                ins.then_inc(tgt[0], inc)
        self.q[eng].append(f)
        if tgt is not None:
            for b in bufs_r:
                b.note_read(*tgt)
            for b in bufs_w:
                b.note_write(*tgt)
        return sem.n if sem is not None else None

    def wait(self, eng, waits):
        w = self._dedupe(eng, tuple(waits))
        if not w:
            return

        def f(e, w=w):
            for hh, vv in w:
                e.wait_ge(hh, vv)
        self.q[eng].append(f)

    def finalize(self):
        with self.nc.Block() as block:
            for eng in ENGINES:
                thunks = self.q[eng]
                if not thunks:
                    continue
                getattr(block, eng)(lambda e, th=tuple(thunks): [f(e) for f in th])


def dma(em, eng, out, in_, waits=(), bufs_r=(), bufs_w=(), transpose=False,
        sem=None):
    if sem is None:
        for b in tuple(bufs_w) + tuple(bufs_r):
            if b.dma_owner:
                if b.dsem is None:
                    b.dsem = em.take_sem()
                sem = b.dsem
                break
    assert sem is not None, "dma needs an owner buffer or explicit sem"
    kw = {"out": out, "in_": in_}
    if transpose:
        kw["transpose"] = True
    em.emit(eng, "dma_start", (), kw, sem, 16, waits, bufs_r, bufs_w)
    return (sem.h, sem.n)


def mm(em, sem, out, lhsT, rhs, start, stop, waits=(), bufs_r=(), bufs_w=()):
    return em.emit("tensor", "matmul", (out, lhsT, rhs),
                   {"start": start, "stop": stop}, sem, 1, waits, bufs_r, bufs_w)


def op(em, eng, sem, method, *args, waits=(), bufs_r=(), bufs_w=(), **kwargs):
    return em.emit(eng, method, args, kwargs, sem, 1, waits, bufs_r, bufs_w)


def full_barrier(em, sems):
    pts = [(s.h, s.n) for s in em.used if s.n > 0]
    pts += [(s.h, s.n) for s in em.pool if s.n > 0]
    pts += [(s.h, s.n) for s in sems.values() if s.n > 0]
    for eng in ENGINES:
        em.wait(eng, list(pts))



def emit_weight_prep(em, w_ap, rows, cols, bf_ap, bf_buf, t_ap, t_buf,
                     cbuf, tslab):
    CB = max(1, 8192 // cols)
    block_rows = 128 * CB
    nblk = rows // block_rows
    for g in range(nblk):
        b = cbuf[g % 2]
        view = b["h"][:].rearrange("p (b h) -> p b h", h=cols)[:, :CB, :]
        src = w_ap[g * block_rows:(g + 1) * block_rows, :].rearrange(
            "(b p) h -> p b h", p=128)
        dma(em, "gpsimd", view, src, bufs_w=[b["buf"]])
        dst = bf_ap[g * block_rows:(g + 1) * block_rows, :].rearrange(
            "(b p) h -> p b h", p=128)
        dma(em, "gpsimd", dst, view, bufs_r=[b["buf"]], bufs_w=[bf_buf])
    nch = cols // 128
    for ch in range(nch):
        b = tslab[ch % 2]
        dma(em, "sync", b["h"][:, :rows], bf_ap[:, ch * 128:(ch + 1) * 128],
            bufs_r=[bf_buf], bufs_w=[b["buf"]], transpose=True)
        dma(em, "sync", t_ap[ch * 128:(ch + 1) * 128, :], b["h"][:, :rows],
            bufs_r=[b["buf"]], bufs_w=[t_buf])



def emit_ln(em, sems, X, Xbuf, Y, Ybuf, tt, C, scr):
    st_, sv, sc, sp = sems["st"], sems["sv"], sems["sc"], sems["sp"]
    ts = slice(tt * TT, (tt + 1) * TT)
    ps = scr["ps"]
    psb = scr["psb"]
    sq, sqb = scr["sq"], scr["sqb"]
    rowsb = scr["rowsb"]
    m_row, t_row, mr_row = scr["rowaps"]
    msq_row = v_row = sd_row = rstd_row = t_row

    for hc in range(DC):
        mm(em, st_ if hc == DC - 1 else None, ps[0][:1, :], C["ones_col"],
           X[:, hc, ts], start=(hc == 0), stop=(hc == DC - 1),
           bufs_r=[Xbuf[tt]], bufs_w=[psb[0]] if hc == 0 else ())
    s1 = (st_.h, st_.n)
    for hc in range(DC):
        op(em, "scalar", sc, "activation", sq[hc % 2]["h"][:], X[:, hc, ts],
           AF.Square, bufs_r=[Xbuf[tt]], bufs_w=[sqb[hc % 2]])
        mm(em, st_, ps[1][:1, :], C["ones_col"], sq[hc % 2]["h"][:],
           start=(hc == 0), stop=(hc == DC - 1),
           bufs_r=[sqb[hc % 2]], bufs_w=[psb[1]] if hc == 0 else ())
    s2 = (st_.h, st_.n)

    op(em, "vector", sv, "tensor_scalar_mul", m_row, ps[0][:1, :], 1.0 / D,
       waits=[s1], bufs_r=[psb[0]], bufs_w=[rowsb])
    op(em, "scalar", sc, "activation", msq_row, m_row, AF.Square,
       bufs_r=[rowsb], bufs_w=[rowsb])
    op(em, "vector", sv, "scalar_tensor_tensor", v_row, ps[1][:1, :], 1.0 / D,
       msq_row, ALU.mult, ALU.subtract, waits=[s2], bufs_r=[psb[1], rowsb],
       bufs_w=[rowsb])
    op(em, "vector", sv, "tensor_scalar_add", v_row, v_row, EPS,
       bufs_r=[rowsb], bufs_w=[rowsb])
    op(em, "scalar", sc, "activation", sd_row, v_row, AF.Sqrt,
       bufs_r=[rowsb], bufs_w=[rowsb])
    op(em, "vector", sv, "reciprocal", rstd_row, sd_row,
       bufs_r=[rowsb], bufs_w=[rowsb])
    op(em, "vector", sv, "tensor_mul", mr_row, m_row, rstd_row,
       bufs_r=[rowsb], bufs_w=[rowsb])

    mm(em, st_, ps[2][:, :], C["ones_row"], rstd_row, start=True, stop=True,
       bufs_r=[rowsb], bufs_w=[psb[2]])
    mm(em, st_, ps[3][:, :], C["ones_row"], mr_row, start=True, stop=True,
       bufs_r=[rowsb], bufs_w=[psb[3]])
    op(em, "vector", sv, "tensor_copy", scr["R"]["h"][:], ps[2][:, :],
       bufs_r=[psb[2]], bufs_w=[scr["R"]["buf"]])
    op(em, "vector", sv, "tensor_copy", scr["MR"]["h"][:], ps[3][:, :],
       bufs_r=[psb[3]], bufs_w=[scr["MR"]["buf"]])

    tmp, tmpb = scr["tmp"], scr["tmpb"]
    for hc in range(DC):
        op(em, "gpsimd", sp, "tensor_mul", tmp[hc % 2]["h"][:], X[:, hc, ts],
           scr["R"]["h"][:], bufs_r=[Xbuf[tt], scr["R"]["buf"]],
           bufs_w=[tmpb[hc % 2]])
        op(em, "vector", sv, "tensor_sub", Y[:, hc, :], tmp[hc % 2]["h"][:],
           scr["MR"]["h"][:], bufs_r=[tmpb[hc % 2], scr["MR"]["buf"]],
           bufs_w=[Ybuf] if hc == 0 else (),)
        Ybuf.note_write(sv.h, sv.n)



def declare_ffn_l0(nc):
    P = {}
    P["w_gu"] = nc.dram_tensor("w_gu", [2 * INTER, D], F32, kind="ExternalInput").ap()
    P["w_dn"] = nc.dram_tensor("w_dn", [D, INTER], F32, kind="ExternalInput").ap()
    P["gu_bf"] = nc.dram_tensor("gu_bf", [2 * INTER, D], BF16).ap()
    P["dn_bf"] = nc.dram_tensor("dn_bf", [D, INTER], BF16).ap()
    P["guT"] = nc.dram_tensor("guT", [D, 2 * INTER], BF16).ap()
    P["dnT"] = nc.dram_tensor("dnT", [INTER, D], BF16).ap()
    P["gu_bf_b"], P["dn_bf_b"], P["guT_b"], P["dnT_b"] = \
        Buf(), Buf(), Buf(), Buf()
    return P


def emit_ffn_l0(nc, em, sems, sh, P, pre, do_prep):
    st_, sv, sc, sp = (sems[k] for k in ("st", "sv", "sc", "sp"))
    X, Xbuf, C = sh["X"], sh["Xbuf"], sh["C"]
    w_gu, w_dn = P["w_gu"], P["w_dn"]
    gu_bf, dn_bf, guT, dnT = P["gu_bf"], P["dn_bf"], P["guT"], P["dnT"]
    gu_bf_b, dn_bf_b, guT_b, dnT_b = (P["gu_bf_b"], P["dn_bf_b"],
                                      P["guT_b"], P["dnT_b"])
    smark = em.mark()
    with contextlib.ExitStack() as stack:
        ent = stack.enter_context

        def sb(name, shape, dtype):
            return {"h": ent(nc.sbuf_tensor(pre + name, shape, dtype)),
                    "buf": Buf(True)}

        if do_prep:
            with contextlib.ExitStack() as prep:
                def psb_(name, shape, dtype):
                    return {"h": prep.enter_context(
                        nc.sbuf_tensor(pre + name, shape, dtype)),
                        "buf": Buf(True)}
                cbuf = [psb_(f"cbuf{i}", [128, 8192], BF16) for i in range(2)]
                tslab = [psb_(f"tsl{i}", [128, 2 * INTER], BF16) for i in range(2)]
                emit_weight_prep(em, w_gu, 2 * INTER, D, gu_bf, gu_bf_b,
                                 guT, guT_b, cbuf, tslab)
                emit_weight_prep(em, w_dn, D, INTER, dn_bf, dn_bf_b,
                                 dnT, dnT_b, cbuf, tslab)
            full_barrier(em, sems)

        Yh = ent(nc.sbuf_tensor(pre + "Y", [128, DC, TT], BF16))
        Y = Yh[:]
        Ybuf = Buf()
        hbuf = sb("hbuf", [128, 16, TT], BF16)
        slab_g = [sb(f"slg{i}", [128, DC, 512], BF16) for i in range(2)]
        slab_u = [sb(f"slu{i}", [128, DC, 512], BF16) for i in range(2)]
        slab_d = [sb(f"sld{i}", [128, 16, 256], BF16) for i in range(2)]
        sq = [sb(f"sq{i}", [128, TT], F32) for i in range(2)]
        s_scr = [sb(f"ss{i}", [128, TT], F32) for i in range(2)]
        tmp = [sb(f"tmp{i}", [128, TT], F32) for i in range(2)]
        rows = sb("rows", [1, 3, TT], F32)
        Rb = sb("Rb", [128, TT], F32)
        MRb = sb("MRb", [128, TT], F32)
        psh = [ent(nc.psum_tensor(pre + f"ps{i}", [128, 512], F32))
               for i in range(8)]
        psb = [Buf() for _ in range(8)]

        rh = rows["h"]
        rowaps = [rh[:1, 0, :], rh[:1, 1, :], rh[:1, 2, :]]
        scr = {"ps": psh, "psb": psb, "sq": sq, "sqb": [s["buf"] for s in sq],
               "rowaps": rowaps, "rowsb": rows["buf"],
               "R": Rb, "MR": MRb, "tmp": tmp, "tmpb": [t["buf"] for t in tmp]}

        guT_r = guT.rearrange("(hc p) o -> p hc o", p=128)
        dnT_r = dnT.rearrange("(ic p) h -> p ic h", p=128)

        for tt in range(NTT):
            emit_ln(em, sems, X, Xbuf, Y, Ybuf, tt, C, scr)
            for q in range(4):
                for i16 in range(16):
                    ic = q * 16 + i16
                    j = ic // 4
                    rel = ic % 4
                    if rel == 0:
                        dma(em, "sync", slab_g[j % 2]["h"][:],
                            guT_r[:, :, 512 * j:512 * (j + 1)],
                            bufs_r=[guT_b], bufs_w=[slab_g[j % 2]["buf"]])
                        dma(em, "sync", slab_u[j % 2]["h"][:],
                            guT_r[:, :, INTER + 512 * j:INTER + 512 * (j + 1)],
                            bufs_r=[guT_b], bufs_w=[slab_u[j % 2]["buf"]])
                    b = ic % 2
                    for hc in range(DC):
                        mm(em, st_ if hc == DC - 1 else None, psh[b][:, :],
                           slab_g[j % 2]["h"][:, hc, rel * 128:(rel + 1) * 128],
                           Y[:, hc, :], start=(hc == 0), stop=(hc == DC - 1),
                           bufs_r=[slab_g[j % 2]["buf"], Ybuf],
                           bufs_w=[psb[b]] if hc == 0 else ())
                    psb[b].note_write(st_.h, st_.n)
                    for hc in range(DC):
                        mm(em, st_ if hc == DC - 1 else None, psh[2 + b][:, :],
                           slab_u[j % 2]["h"][:, hc, rel * 128:(rel + 1) * 128],
                           Y[:, hc, :], start=(hc == 0), stop=(hc == DC - 1),
                           bufs_r=[slab_u[j % 2]["buf"], Ybuf],
                           bufs_w=[psb[2 + b]] if hc == 0 else ())
                    psb[2 + b].note_write(st_.h, st_.n)
                    op(em, "scalar", sc, "activation", s_scr[ic % 2]["h"][:],
                       psh[b][:, :], AF.Silu,
                       bufs_r=[psb[b]], bufs_w=[s_scr[ic % 2]["buf"]])
                    op(em, "vector", sv, "scalar_tensor_tensor",
                       hbuf["h"][:, i16, :], psh[2 + b][:, :], 1.0,
                       s_scr[ic % 2]["h"][:], ALU.mult, ALU.mult,
                       bufs_r=[psb[2 + b], s_scr[ic % 2]["buf"]],
                       bufs_w=[hbuf["buf"]] if i16 == 0 else ())
                    hbuf["buf"].note_write(sv.h, sv.n)
                for ocp in range(8):
                    dma(em, "sync", slab_d[ocp % 2]["h"][:],
                        dnT_r[:, q * 16:(q + 1) * 16, ocp * 256:(ocp + 1) * 256],
                        bufs_r=[dnT_b], bufs_w=[slab_d[ocp % 2]["buf"]])
                    for oc_rel in range(2):
                        oc = 2 * ocp + oc_rel
                        db = 4 + (oc % 2)
                        for i16 in range(16):
                            mm(em, st_ if i16 == 15 else None, psh[db][:, :],
                               slab_d[ocp % 2]["h"][:, i16,
                                                    oc_rel * 128:(oc_rel + 1) * 128],
                               hbuf["h"][:, i16, :], start=(i16 == 0),
                               stop=(i16 == 15),
                               bufs_r=[slab_d[ocp % 2]["buf"], hbuf["buf"]],
                               bufs_w=[psb[db]] if i16 == 0 else ())
                        psb[db].note_write(st_.h, st_.n)
                        op(em, "vector", sv, "scalar_tensor_tensor",
                           X[:, oc, tt * TT:(tt + 1) * TT], psh[db][:, :], 0.5,
                           X[:, oc, tt * TT:(tt + 1) * TT], ALU.mult, ALU.add,
                           bufs_r=[psb[db], Xbuf[tt]], bufs_w=[Xbuf[tt]])

    full_barrier(em, sems)
    em.release(smark)



def _had(n):
    h = np.array([[1.0]], np.float32)
    while h.shape[0] < n:
        h = np.block([[h, h], [h, -h]])
    return h


def declare_ffn_l1(nc):
    P = {}
    P["w_up"] = nc.dram_tensor("w_up", [INTER, D], F32, kind="ExternalInput").ap()
    P["w_gt"] = nc.dram_tensor("w_gt", [INTER, D], F32, kind="ExternalInput").ap()
    P["w_dn1"] = nc.dram_tensor("w_dn1", [D, INTER], F32, kind="ExternalInput").ap()
    P["su_ug"] = nc.dram_tensor("su_ug", [16, 128], F32, kind="ExternalInput").ap()
    P["su_dp"] = nc.dram_tensor("su_dp", [64, 128], F32, kind="ExternalInput").ap()
    P["wsc_ug"] = nc.dram_tensor("wsc_ug", [128, 128], F32, kind="ExternalInput").ap()
    P["wsc_dp"] = nc.dram_tensor("wsc_dp", [16, 128], F32, kind="ExternalInput").ap()
    P["up_bf"] = nc.dram_tensor("up_bf", [INTER, D], BF16).ap()
    P["gt_bf"] = nc.dram_tensor("gt_bf", [INTER, D], BF16).ap()
    P["dn1_bf"] = nc.dram_tensor("dn1_bf", [D, INTER], BF16).ap()
    P["upT"] = nc.dram_tensor("upT", [D, INTER], BF16).ap()
    P["gtT"] = nc.dram_tensor("gtT", [D, INTER], BF16).ap()
    P["dn1T"] = nc.dram_tensor("dn1T", [INTER, D], BF16).ap()
    P["u_dram"] = nc.dram_tensor("u_dram", [D, T], BF16).ap()
    P["u2_dram"] = nc.dram_tensor("u2_dram", [INTER, T], BF16).ap()
    for k in ("up_bf_b", "gt_bf_b", "dn1_bf_b", "upT_b", "gtT_b", "dn1T_b",
              "u_dram_b", "u2_dram_b"):
        P[k] = Buf()
    H128 = _had(128)
    H16 = _had(16)
    H64 = _had(64)
    P["h128a_d"] = nc.inline_tensor((H128 / 32.0).astype(nbf), name="h128a_d").ap()
    P["h128b_d"] = nc.inline_tensor((H128 / 64.0).astype(nbf), name="h128b_d").ap()
    bd16 = np.zeros((128, 128), np.float32)
    for pp in range(8):
        bd16[16 * pp:16 * (pp + 1), 16 * pp:16 * (pp + 1)] = H16 / np.sqrt(2.0)
    P["bd16_d"] = nc.inline_tensor(bd16.astype(nbf), name="bd16_d").ap()
    bd64 = np.zeros((128, 128), np.float32)
    for pp in range(2):
        bd64[64 * pp:64 * (pp + 1), 64 * pp:64 * (pp + 1)] = H64 / np.sqrt(2.0)
    P["bd64_d"] = nc.inline_tensor(bd64.astype(nbf), name="bd64_d").ap()
    P["identf_d"] = nc.inline_tensor(np.eye(128, dtype=np.float32),
                                     name="identf_d").ap()
    return P


def emit_ffn_l1(nc, em, sems, sh, P, pre, do_prep):
    st_, sv, sc, sp = (sems[k] for k in ("st", "sv", "sc", "sp"))
    X, Xbuf, C = sh["X"], sh["Xbuf"], sh["C"]
    w_up, w_gt, w_dn = P["w_up"], P["w_gt"], P["w_dn1"]
    up_bf, gt_bf, dn_bf = P["up_bf"], P["gt_bf"], P["dn1_bf"]
    upT, gtT, dnT = P["upT"], P["gtT"], P["dn1T"]
    up_bf_b, gt_bf_b, dn_bf_b = P["up_bf_b"], P["gt_bf_b"], P["dn1_bf_b"]
    upT_b, gtT_b, dnT_b = P["upT_b"], P["gtT_b"], P["dn1T_b"]
    u_dram, u2_dram = P["u_dram"], P["u2_dram"]
    u_dram_b, u2_dram_b = P["u_dram_b"], P["u2_dram_b"]
    su_ug_d, su_dp_d = P["su_ug"], P["su_dp"]
    wsc_ug_d, wsc_dp_d = P["wsc_ug"], P["wsc_dp"]
    smark = em.mark()
    with contextlib.ExitStack() as stack:
        ent = stack.enter_context

        def sb(name, shape, dtype):
            return {"h": ent(nc.sbuf_tensor(pre + name, shape, dtype)),
                    "buf": Buf(True)}

        onc, onr = sh["onc"], sh["onr"]
        identf = sb("identf", [128, 128], F32)
        h128a = sb("h128a", [128, 128], BF16)
        h128b = sb("h128b", [128, 128], BF16)
        bd16_s = sb("bd16_s", [128, 128], BF16)
        bd64_s = sb("bd64_s", [128, 128], BF16)
        vrow = sb("vrow", [128, 128], F32)
        su_ug_c = sb("su_ug_c", [128, 16], F32)
        wsuS_c = sb("wsuS_c", [128, 64], F32)
        wsg_c = sb("wsg_c", [128, 64], F32)
        wdp_c = sb("wdp_c", [128, 16], F32)
        su_dp_c = sb("su_dp_c", [128, 64], F32)
        h128su = sb("h128su", [128, DC, 128], BF16)

        cr = []
        for tile, srcd in ((identf, P["identf_d"]), (h128a, P["h128a_d"]),
                           (h128b, P["h128b_d"]), (bd16_s, P["bd16_d"]),
                           (bd64_s, P["bd64_d"])):
            cr.append(dma(em, "sync", tile["h"][:], srcd, bufs_w=[tile["buf"]]))
        consts_ready = cr
        em.wait("tensor", cr)

        if do_prep:
            with contextlib.ExitStack() as prep:
                def psb_(name, shape, dtype):
                    return {"h": prep.enter_context(
                        nc.sbuf_tensor(pre + name, shape, dtype)),
                        "buf": Buf(True)}
                cbuf = [psb_(f"cbuf{i}", [128, 8192], BF16) for i in range(2)]
                tslab = [psb_(f"tsl{i}", [128, INTER], BF16) for i in range(2)]
                emit_weight_prep(em, w_up, INTER, D, up_bf, up_bf_b, upT, upT_b,
                                 cbuf, tslab)
                emit_weight_prep(em, w_gt, INTER, D, gt_bf, gt_bf_b, gtT, gtT_b,
                                 cbuf, tslab)
                emit_weight_prep(em, w_dn, D, INTER, dn_bf, dn_bf_b, dnT, dnT_b,
                                 cbuf, tslab)
            full_barrier(em, sems)

        with contextlib.ExitStack() as vsc:
            vps = vsc.enter_context(nc.psum_tensor(pre + "vps", [128, 512], F32))
            vpsb = Buf()
            for rows_n, src, dstc in ((16, su_ug_d, su_ug_c), (64, su_dp_d, su_dp_c),
                                      (16, wsc_dp_d, wdp_c)):
                dma(em, "sync", vrow["h"][:rows_n, :], src,
                    bufs_w=[vrow["buf"]])
                op(em, "tensor", st_, "transpose", vps[:, :rows_n],
                   vrow["h"][:rows_n, :], identf["h"][:rows_n, :rows_n],
                   bufs_r=[vrow["buf"], identf["buf"]], bufs_w=[vpsb])
                op(em, "vector", sv, "tensor_copy", dstc["h"][:], vps[:, :rows_n],
                   bufs_r=[vpsb], bufs_w=[dstc["buf"]])
            dma(em, "sync", vrow["h"][:, :], wsc_ug_d, bufs_w=[vrow["buf"]])
            op(em, "tensor", st_, "transpose", vps[:, :128], vrow["h"][:, :],
               identf["h"][:], bufs_r=[vrow["buf"], identf["buf"]], bufs_w=[vpsb])
            op(em, "vector", sv, "tensor_copy", wsuS_c["h"][:], vps[:, :64],
               bufs_r=[vpsb], bufs_w=[wsuS_c["buf"]])
            op(em, "vector", sv, "tensor_copy", wsg_c["h"][:], vps[:, 64:128],
               bufs_r=[vpsb], bufs_w=[wsg_c["buf"]])
            op(em, "vector", sv, "tensor_mul", wsuS_c["h"][:], wsuS_c["h"][:],
               su_dp_c["h"][:], bufs_r=[wsuS_c["buf"], su_dp_c["buf"]],
               bufs_w=[wsuS_c["buf"]])
            op(em, "vector", sv, "tensor_scalar_mul", wdp_c["h"][:],
               wdp_c["h"][:], 0.5, bufs_r=[wdp_c["buf"]], bufs_w=[wdp_c["buf"]])
            for c in range(DC):
                op(em, "vector", sv, "tensor_scalar_mul",
                   h128su["h"][:, c, :], h128a["h"][:], su_ug_c["h"][:, c:c + 1],
                   bufs_r=[h128a["buf"], su_ug_c["buf"]], bufs_w=[h128su["buf"]])
        full_barrier(em, sems)

        Yh = ent(nc.sbuf_tensor(pre + "Y", [128, DC, TT], BF16))
        Y = Yh[:]
        Ybuf = Buf()
        sq = [sb(f"sq{i}", [128, TT], F32) for i in range(2)]
        tmp = [sb(f"tmp{i}", [128, TT], F32) for i in range(2)]
        rows = sb("rows", [1, 3, TT], F32)
        Rb = sb("Rb", [128, TT], F32)
        MRb = sb("MRb", [128, TT], F32)
        u_sb = sb("u_sb", [128, DC, TT], BF16)
        z1 = {"h": Yh, "buf": None}
        slab_u = sb("slab_u", [128, DC, 512], BF16)
        slab_g = sb("slab_g", [128, DC, 512], BF16)
        slab_d = [sb("sld0", [128, 16, 256], BF16)]
        h_t = [sb(f"h_t{i}", [128, TT], BF16) for i in range(2)]
        s_scr = [sb(f"ss{i}", [128, TT], F32) for i in range(2)]
        u2g = sb("u2g", [128, 8, TT], BF16)
        zin2 = sb("zin2", [128, 16, TT], BF16)
        z2 = sb("z2", [128, 16, TT], BF16)
        psh = [ent(nc.psum_tensor(pre + f"ps{i}", [128, 512], F32))
               for i in range(8)]
        psb = [Buf() for _ in range(8)]
        z1["buf"] = Ybuf

        rh = rows["h"]
        rowaps = [rh[:1, 0, :], rh[:1, 1, :], rh[:1, 2, :]]
        scr = {"ps": psh, "psb": psb, "sq": sq, "sqb": [s["buf"] for s in sq],
               "rowaps": rowaps, "rowsb": rows["buf"],
               "R": Rb, "MR": MRb, "tmp": tmp, "tmpb": [t["buf"] for t in tmp]}

        upT_r = upT.rearrange("(c2 pj pp) i -> pp c2 pj i", c2=16, pj=16, pp=8)
        gtT_r = gtT.rearrange("(c2 pj pp) i -> pp c2 pj i", c2=16, pj=16, pp=8)
        dnT_r = dnT.rearrange("(a2 pj pp) h -> pp a2 pj h", a2=64, pj=64, pp=2)
        ud_r = u_dram.rearrange("(c pj pp) t -> pp c pj t", c=16, pj=16, pp=8)
        u2d_r = u2_dram.rearrange("(a pj pp) t -> pp a pj t", a=64, pj=64, pp=2)

        for tt in range(NTT):
            ts = slice(tt * TT, (tt + 1) * TT)
            emit_ln(em, sems, X, Xbuf, Y, Ybuf, tt, {"ones_col": onc["h"][:],
                    "ones_row": onr["h"][:]}, scr)
            for c in range(DC):
                b = c % 2
                mm(em, st_, psh[b][:, :], h128su["h"][:, c, :], Y[:, c, :],
                   start=True, stop=True,
                   bufs_r=[h128su["buf"], Ybuf], bufs_w=[psb[b]])
                op(em, "vector", sv, "tensor_copy", u_sb["h"][:, c, :],
                   psh[b][:, :], bufs_r=[psb[b]], bufs_w=[u_sb["buf"]])
            dma(em, "sync", u_dram.rearrange("(c p) t -> p c t", p=128)[:, :, ts],
                u_sb["h"][:], bufs_r=[u_sb["buf"]], bufs_w=[u_dram_b])
            dma(em, "sync", u_sb["h"][:], ud_r[:, :, :, ts],
                bufs_r=[u_dram_b], bufs_w=[u_sb["buf"]])
            for pj in range(DC):
                b = pj % 2
                mm(em, st_, psh[b][:, :], bd16_s["h"][:], u_sb["h"][:, pj, :],
                   start=True, stop=True,
                   bufs_r=[bd16_s["buf"], u_sb["buf"]], bufs_w=[psb[b]])
                op(em, "vector", sv, "tensor_copy", z1["h"][:, pj, :],
                   psh[b][:, :], bufs_r=[psb[b]], bufs_w=[z1["buf"]])
            for a in range(64):
                jb = a % 2
                if a % 4 == 0:
                    dma(em, "sync", slab_u["h"][:],
                        upT_r[:, :, :, 128 * a:128 * (a + 4)],
                        bufs_r=[upT_b], bufs_w=[slab_u["buf"]])
                    dma(em, "sync", slab_g["h"][:],
                        gtT_r[:, :, :, 128 * a:128 * (a + 4)],
                        bufs_r=[gtT_b], bufs_w=[slab_g["buf"]])
                rel = a % 4
                for pj in range(DC):
                    mm(em, st_ if pj == DC - 1 else None, psh[jb][:, :],
                       slab_g["h"][:, pj, rel * 128:(rel + 1) * 128],
                       z1["h"][:, pj, :], start=(pj == 0), stop=(pj == DC - 1),
                       bufs_r=[slab_g["buf"], z1["buf"]],
                       bufs_w=[psb[jb]] if pj == 0 else ())
                psb[jb].note_write(st_.h, st_.n)
                for pj in range(DC):
                    mm(em, st_ if pj == DC - 1 else None, psh[2 + jb][:, :],
                       slab_u["h"][:, pj, rel * 128:(rel + 1) * 128],
                       z1["h"][:, pj, :], start=(pj == 0), stop=(pj == DC - 1),
                       bufs_r=[slab_u["buf"], z1["buf"]],
                       bufs_w=[psb[2 + jb]] if pj == 0 else ())
                psb[2 + jb].note_write(st_.h, st_.n)
                op(em, "scalar", sc, "activation", s_scr[a % 2]["h"][:],
                   psh[jb][:, :], AF.Silu, scale=wsg_c["h"][:, a:a + 1],
                   bufs_r=[psb[jb], wsg_c["buf"]], bufs_w=[s_scr[a % 2]["buf"]])
                op(em, "vector", sv, "scalar_tensor_tensor", h_t[a % 2]["h"][:],
                   psh[2 + jb][:, :], wsuS_c["h"][:, a:a + 1], s_scr[a % 2]["h"][:],
                   ALU.mult, ALU.mult,
                   bufs_r=[psb[2 + jb], wsuS_c["buf"], s_scr[a % 2]["buf"]],
                   bufs_w=[h_t[a % 2]["buf"]])
                mm(em, st_, psh[4 + jb][:, :], h128b["h"][:], h_t[a % 2]["h"][:],
                   start=True, stop=True,
                   bufs_r=[h128b["buf"], h_t[a % 2]["buf"]], bufs_w=[psb[4 + jb]])
                op(em, "vector", sv, "tensor_copy", u2g["h"][:, a % 8, :],
                   psh[4 + jb][:, :], bufs_r=[psb[4 + jb]], bufs_w=[u2g["buf"]])
                if a % 8 == 7:
                    dma(em, "sync",
                        u2_dram.rearrange("(g p) t -> p g t", p=128)[
                            :, a - 7:a + 1, ts],
                        u2g["h"][:], bufs_r=[u2g["buf"]], bufs_w=[u2_dram_b])
            for grp in range(4):
                for pp_ in range(2):
                    dma(em, "sync", zin2["h"][64 * pp_:64 * (pp_ + 1), :, :],
                        u2d_r[pp_, :, 16 * grp:16 * (grp + 1), ts],
                        bufs_r=[u2_dram_b], bufs_w=[zin2["buf"]])
                for pjr in range(16):
                    b = pjr % 2
                    mm(em, st_, psh[b][:, :], bd64_s["h"][:],
                       zin2["h"][:, pjr, :], start=True, stop=True,
                       bufs_r=[bd64_s["buf"], zin2["buf"]], bufs_w=[psb[b]])
                    op(em, "vector", sv, "tensor_copy", z2["h"][:, pjr, :],
                       psh[b][:, :], bufs_r=[psb[b]], bufs_w=[z2["buf"]])
                for ocp in range(8):
                    for pp_ in range(2):
                        dma(em, "sync",
                            slab_d[0]["h"][64 * pp_:64 * (pp_ + 1), :, :],
                            dnT_r[pp_, :, 16 * grp:16 * (grp + 1),
                                  ocp * 256:(ocp + 1) * 256],
                            bufs_r=[dnT_b], bufs_w=[slab_d[0]["buf"]])
                    for oc_rel in range(2):
                        oc = 2 * ocp + oc_rel
                        db = 4 + (oc % 2)
                        for pjr in range(16):
                            mm(em, st_ if pjr == 15 else None, psh[db][:, :],
                               slab_d[0]["h"][:, pjr,
                                              oc_rel * 128:(oc_rel + 1) * 128],
                               z2["h"][:, pjr, :], start=(pjr == 0),
                               stop=(pjr == 15),
                               bufs_r=[slab_d[0]["buf"], z2["buf"]],
                               bufs_w=[psb[db]] if pjr == 0 else ())
                        psb[db].note_write(st_.h, st_.n)
                        op(em, "vector", sv, "scalar_tensor_tensor",
                           X[:, oc, ts], psh[db][:, :], wdp_c["h"][:, oc:oc + 1],
                           X[:, oc, ts], ALU.mult, ALU.add,
                           bufs_r=[psb[db], wdp_c["buf"], Xbuf[tt]],
                           bufs_w=[Xbuf[tt]])

    full_barrier(em, sems)
    em.release(smark)



def declare_attn_l0(nc):
    H = 16
    P = {}
    P["w_qkv"] = nc.dram_tensor("w_qkv", [D, 3 * D], F32, kind="ExternalInput").ap()
    P["w_out"] = nc.dram_tensor("w_out", [D, D], F32, kind="ExternalInput").ap()
    P["decay"] = nc.dram_tensor("decay_i", [1, H], F32, kind="ExternalInput").ap()
    P["csel"] = nc.dram_tensor("csel_i", [128, 1], F32, kind="ExternalInput").ap()
    P["att_dram"] = nc.dram_tensor("att_dram", [D, T], BF16).ap()
    P["contrib"] = nc.dram_tensor("contrib0", [128, 4], F32)
    P["gathered"] = nc.dram_tensor("gathered0", [256, 4], F32)
    jj, tt_ = np.meshgrid(np.arange(128), np.arange(128), indexing="ij")
    E_np = np.maximum(tt_ - jj, 0).astype(np.float32)
    LT_np = (jj <= tt_).astype(np.float32)
    P["E_d"] = nc.inline_tensor(E_np, name="E_d").ap()
    P["LT_d"] = nc.inline_tensor(LT_np, name="LT_d").ap()
    P["iota_d"] = nc.inline_tensor(
        np.arange(1, T + 1, dtype=np.float32).reshape(1, T), name="iota_d").ap()
    P["zcol_d"] = nc.inline_tensor(np.zeros((128, 1), np.float32),
                                   name="zcol_d").ap()
    return P


def emit_attn_l0(nc, em, sems, sh, P, pre):
    H = 16
    st_, sv, sc, sp, scc = (sems[k] for k in ("st", "sv", "sc", "sp", "cc"))
    X, Xbuf, C = sh["X"], sh["Xbuf"], sh["C"]
    w_qkv, w_out = P["w_qkv"], P["w_out"]
    decay_d, csel_d = P["decay"], P["csel"]
    att_dram, contrib, gathered = P["att_dram"], P["contrib"], P["gathered"]
    E_d, LT_d, iota_d, zcol_d = P["E_d"], P["LT_d"], P["iota_d"], P["zcol_d"]
    ident_d = sh["ident_d"]
    contrib_buf = Buf()
    gathered_buf = Buf()
    smark = em.mark()
    with contextlib.ExitStack() as stack:
        ent = stack.enter_context

        def sb(name, shape, dtype):
            return {"h": ent(nc.sbuf_tensor(pre + name, shape, dtype)),
                    "buf": Buf(True)}

        att_dram_b = Buf()
        onc, onr = sh["onc"], sh["onr"]
        E_sb = sb("E_sb", [128, 128], F32)
        LT_sb = sb("LT_sb", [128, 128], F32)
        iota_sb = sb("iota", [1, T], F32)
        ident = sb("ident", [128, 128], BF16)
        zcol = sb("zcol", [128, 1], F32)
        csel = sb("csel", [128, 1], F32)
        dec = sb("dec", [1, 2, H], F32)
        Yh = ent(nc.sbuf_tensor(pre + "Y", [128, DC, T], BF16))
        Y = Yh[:]
        Ybuf = Buf()
        sq = [sb(f"sq{i}", [128, TT], F32) for i in range(2)]
        tmp = [sb(f"tmp{i}", [128, TT], F32) for i in range(2)]
        rows = sb("rows", [1, 3, TT], F32)
        Rb = sb("Rb", [128, TT], F32)
        MRb = sb("MRb", [128, TT], F32)

        cr = []
        for tile, srcd in ((E_sb, E_d),
                           (LT_sb, LT_d), (iota_sb, iota_d), (ident, ident_d),
                           (zcol, zcol_d), (csel, csel_d)):
            cr.append(dma(em, "sync", tile["h"][:], srcd, bufs_w=[tile["buf"]]))
        cr.append(dma(em, "sync", dec["h"][:1, 0, :], decay_d,
                      bufs_w=[dec["buf"]]))
        consts_ready = cr
        em.wait("tensor", cr)
        em.wait("scalar", cr)
        em.wait("vector", cr)
        C = {"ones_col": onc["h"][:], "ones_row": onr["h"][:]}

        op(em, "scalar", sc, "activation", dec["h"][:1, 0, :], dec["h"][:1, 0, :],
           AF.Sigmoid, bufs_r=[dec["buf"]], bufs_w=[dec["buf"]])
        op(em, "scalar", sc, "activation", dec["h"][:1, 1, :], dec["h"][:1, 0, :],
           AF.Ln, bufs_r=[dec["buf"]], bufs_w=[dec["buf"]])
        lnd = dec["h"]

        with contextlib.ExitStack() as lnsc:
            lps = [lnsc.enter_context(
                nc.psum_tensor(pre + f"lps{i}", [128, 512], F32))
                   for i in range(4)]
            lpsb = [Buf() for _ in range(4)]
            rh = rows["h"]
            rowaps = [rh[:1, 0, :], rh[:1, 1, :], rh[:1, 2, :]]
            scr = {"ps": lps, "psb": lpsb, "sq": sq, "sqb": [s["buf"] for s in sq],
                   "rowaps": rowaps, "rowsb": rows["buf"],
                   "R": Rb, "MR": MRb, "tmp": tmp, "tmpb": [t["buf"] for t in tmp]}
            for tt in range(NTT):
                emit_ln(em, sems, X, Xbuf, Y[:, :, tt * TT:(tt + 1) * TT], Ybuf,
                        tt, C, scr)
        full_barrier(em, sems)

        wq_r = w_qkv.rearrange("(kc p) o -> p kc o", p=128)

        with contextlib.ExitStack() as body:
            def bsb(name, shape, dtype):
                return {"h": body.enter_context(
                    nc.sbuf_tensor(pre + name, shape, dtype)),
                        "buf": Buf(True)}
            q_sb = bsb("q_sb", [128, 2, T], BF16)
            k_sb = bsb("k_sb", [128, 2, T], BF16)
            kv_sb = bsb("kv_sb", [128, 2, T], BF16)
            v_sb = bsb("v_sb", [128, 2, T], BF16)
            S_scr = bsb("S_scr", [128, 4, T], F32)
            att_q = bsb("att_q", [128, 2, T], BF16)
            slab = bsb("slab", [128, DC, 256], BF16)
            phi_mn = sq
            phi_ex = tmp
            phi_rl = [bsb(f"phrl{i}", [128, TT], F32) for i in range(2)]
            xT = [bsb(f"xT{i}", [128, 128], BF16) for i in range(2)]
            A_bf = bsb("A_bf", [128, 2, 128], BF16)
            aexp = bsb("aexp", [128, 128], F32)
            powf = bsb("powf", [128, 2, T], F32)
            prow = bsb("prow", [1, T], F32)
            u_scr = bsb("u_scr", [128, T], F32)
            t2_scr = bsb("t2_scr", [128, T], F32)
            carry = bsb("carry", [128, 4], F32)
            block0 = bsb("block0", [128, 4], F32)
            Ksel = bsb("Ksel", [128, 4], F32)
            den_sb = bsb("den_sb", [1, T], F32)
            rden_sb = bsb("rden_sb", [1, T], F32)

            pj = [body.enter_context(
                nc.psum_tensor(pre + f"pj{i}", [128, 512], F32))
                  for i in range(2)]
            pjb = [Buf(), Buf()]
            tbf = [body.enter_context(
                nc.psum_tensor(pre + f"tb{i}", [128, 1024], BF16))
                   for i in range(2)]
            tbfb = [Buf(), Buf()]
            msc = [body.enter_context(
                nc.psum_tensor(pre + f"ms{i}", [128, 512], F32))
                   for i in range(2)]
            mscb = [Buf(), Buf()]

            lnd_b = bsb("lnd_b", [128, H], F32)
            mm(em, st_, msc[0][:, :H], C["ones_row"], lnd[:1, 1, :],
               start=True, stop=True, bufs_r=[dec["buf"], onr["buf"]],
               bufs_w=[mscb[0]])
            op(em, "vector", sv, "tensor_copy", lnd_b["h"][:], msc[0][:, :H],
               bufs_r=[mscb[0]], bufs_w=[lnd_b["buf"]])

            for hq in range(8):
                heads = (2 * hq, 2 * hq + 1)
                for kind, base in (("v", 2 * D), ("k", D), ("q", 0)):
                    span = base + 256 * hq
                    dma(em, "gpsimd", slab["h"][:],
                        wq_r[:, :, span:span + 256],
                        bufs_w=[slab["buf"]])
                    for tt in range(NTT):
                        ts = slice(tt * TT, (tt + 1) * TT)
                        for hr in range(2):
                            pb = (tt * 2 + hr) % 2
                            for kc in range(DC):
                                mm(em, st_ if kc == DC - 1 else None, pj[pb][:, :],
                                   slab["h"][:, kc, hr * 128:(hr + 1) * 128],
                                   Y[:, kc, ts], start=(kc == 0), stop=(kc == DC - 1),
                                   bufs_r=[slab["buf"], Ybuf],
                                   bufs_w=[pjb[pb]] if kc == 0 else ())
                            pjb[pb].note_write(st_.h, st_.n)
                            if kind == "v":
                                op(em, "vector", sv, "tensor_copy",
                                   v_sb["h"][:, hr, ts], pj[pb][:, :],
                                   bufs_r=[pjb[pb]], bufs_w=[v_sb["buf"]])
                            else:
                                dst = q_sb if kind == "q" else k_sb
                                op(em, "vector", sv, "tensor_scalar_min",
                                   phi_mn[hr]["h"][:], pj[pb][:, :], 0.0,
                                   bufs_r=[pjb[pb]], bufs_w=[phi_mn[hr]["buf"]])
                                op(em, "scalar", sc, "activation",
                                   phi_ex[hr]["h"][:], phi_mn[hr]["h"][:], AF.Exp,
                                   bufs_r=[phi_mn[hr]["buf"]],
                                   bufs_w=[phi_ex[hr]["buf"]])
                                op(em, "scalar", sc, "activation",
                                   phi_rl[hr]["h"][:], pj[pb][:, :], AF.Relu,
                                   bufs_r=[pjb[pb]], bufs_w=[phi_rl[hr]["buf"]])
                                op(em, "vector", sv, "tensor_add",
                                   dst["h"][:, hr, ts], phi_rl[hr]["h"][:],
                                   phi_ex[hr]["h"][:],
                                   bufs_r=[phi_rl[hr]["buf"], phi_ex[hr]["buf"]],
                                   bufs_w=[dst["buf"]])
                                if kind == "k":
                                    op(em, "gpsimd", sp, "tensor_mul",
                                       kv_sb["h"][:, hr, ts], k_sb["h"][:, hr, ts],
                                       v_sb["h"][:, hr, ts],
                                       bufs_r=[k_sb["buf"], v_sb["buf"]],
                                       bufs_w=[kv_sb["buf"]])

                for hr, h in enumerate(heads):
                    lnd_ap = lnd_b["h"][:, h:h + 1]
                    op(em, "scalar", sc, "activation", aexp["h"][:], E_sb["h"][:],
                       AF.Exp, scale=lnd_ap,
                       bufs_r=[E_sb["buf"], lnd_b["buf"]], bufs_w=[aexp["buf"]])
                    op(em, "vector", sv, "tensor_mul", A_bf["h"][:, hr, :],
                       aexp["h"][:], LT_sb["h"][:],
                       bufs_r=[aexp["buf"], LT_sb["buf"]], bufs_w=[A_bf["buf"]])
                    op(em, "scalar", sc, "activation", prow["h"][:], iota_sb["h"][:],
                       AF.Exp, scale=lnd[:1, 1, h:h + 1],
                       bufs_r=[iota_sb["buf"], dec["buf"]], bufs_w=[prow["buf"]])
                    for half in range(2):
                        mm(em, st_, msc[half][:, :], C["ones_row"],
                           prow["h"][:1, half * 512:(half + 1) * 512],
                           start=True, stop=True,
                           bufs_r=[prow["buf"], onr["buf"]], bufs_w=[mscb[half]])
                        op(em, "vector", sv, "tensor_copy",
                           powf["h"][:, hr, half * 512:(half + 1) * 512],
                           msc[half][:, :], bufs_r=[mscb[half]],
                           bufs_w=[powf["buf"]])

                    for kind_i, src in ((0, kv_sb), (1, k_sb)):
                        sidx = 2 * hr + kind_i
                        for tc in range(8):
                            tcs = slice(tc * 128, (tc + 1) * 128)
                            op(em, "tensor", st_, "transpose",
                               tbf[tc % 2][:, :128], src["h"][:, hr, tcs],
                               ident["h"][:],
                               bufs_r=[src["buf"], ident["buf"]],
                               bufs_w=[tbfb[tc % 2]])
                            op(em, "vector", sv, "tensor_copy", xT[tc % 2]["h"][:],
                               tbf[tc % 2][:, :128],
                               bufs_r=[tbfb[tc % 2]], bufs_w=[xT[tc % 2]["buf"]])
                            mm(em, st_, msc[tc % 2][:, :128], xT[tc % 2]["h"][:],
                               A_bf["h"][:, hr, :], start=True, stop=True,
                               bufs_r=[xT[tc % 2]["buf"], A_bf["buf"]],
                               bufs_w=[mscb[tc % 2]])
                            cprev = (S_scr["h"][:, sidx, tc * 128 - 1:tc * 128]
                                     if tc > 0 else zcol["h"][:])
                            op(em, "vector", sv, "scalar_tensor_tensor",
                               S_scr["h"][:, sidx, tcs],
                               powf["h"][:, hr, 0:128], cprev,
                               msc[tc % 2][:, :128], ALU.mult, ALU.add,
                               bufs_r=[mscb[tc % 2], powf["buf"], S_scr["buf"],
                                       zcol["buf"]],
                               bufs_w=[S_scr["buf"]])

                for hr in range(2):
                    for kind_i in range(2):
                        sidx = 2 * hr + kind_i
                        op(em, "vector", sv, "tensor_copy",
                           carry["h"][:, sidx:sidx + 1],
                           S_scr["h"][:, sidx, T - 1:T],
                           bufs_r=[S_scr["buf"]], bufs_w=[carry["buf"]])
                dma(em, "sync", contrib[:, :], carry["h"][:],
                    bufs_r=[carry["buf"]], bufs_w=[contrib_buf])
                em.emit("gpsimd", "collective_compute",
                        ("AllGather", ALU.bypass,
                         [[0, 1], [2, 3], [4, 5], [6, 7]],
                         [contrib.ap().opt()], [gathered.ap().opt()]),
                        {}, scc, 1, (), bufs_r=[contrib_buf],
                        bufs_w=[gathered_buf])
                dma(em, "sync", block0["h"][:], gathered[0:128, :],
                    bufs_r=[gathered_buf], bufs_w=[block0["buf"]])
                op(em, "vector", sv, "tensor_scalar_mul", Ksel["h"][:],
                   block0["h"][:], csel["h"][:],
                   bufs_r=[block0["buf"], csel["buf"]], bufs_w=[Ksel["buf"]])

                for hr, h in enumerate(heads):
                    for kind_i in range(2):
                        sidx = 2 * hr + kind_i
                        op(em, "vector", sv, "scalar_tensor_tensor",
                           S_scr["h"][:, sidx, :], powf["h"][:, hr, :],
                           Ksel["h"][:, sidx:sidx + 1], S_scr["h"][:, sidx, :],
                           ALU.mult, ALU.add,
                           bufs_r=[powf["buf"], Ksel["buf"], S_scr["buf"]],
                           bufs_w=[S_scr["buf"]])
                    op(em, "gpsimd", sp, "tensor_mul", u_scr["h"][:],
                       q_sb["h"][:, hr, :], S_scr["h"][:, 2 * hr + 1, :],
                       bufs_r=[q_sb["buf"], S_scr["buf"]], bufs_w=[u_scr["buf"]])
                    op(em, "vector", sv, "tensor_mul", t2_scr["h"][:],
                       q_sb["h"][:, hr, :], S_scr["h"][:, 2 * hr, :],
                       bufs_r=[q_sb["buf"], S_scr["buf"]], bufs_w=[t2_scr["buf"]])
                    for half in range(2):
                        hs = slice(half * 512, (half + 1) * 512)
                        mm(em, st_, msc[0][:1, :], C["ones_col"],
                           u_scr["h"][:, hs], start=True, stop=True,
                           bufs_r=[u_scr["buf"], onc["buf"]], bufs_w=[mscb[0]])
                        op(em, "vector", sv, "tensor_scalar_max",
                           den_sb["h"][:, hs], msc[0][:1, :], 1e-6,
                           bufs_r=[mscb[0]], bufs_w=[den_sb["buf"]])
                        op(em, "vector", sv, "reciprocal", rden_sb["h"][:, hs],
                           den_sb["h"][:, hs],
                           bufs_r=[den_sb["buf"]], bufs_w=[rden_sb["buf"]])
                        mm(em, st_, msc[1][:, :], C["ones_row"],
                           rden_sb["h"][:1, hs], start=True, stop=True,
                           bufs_r=[rden_sb["buf"], onr["buf"]], bufs_w=[mscb[1]])
                        op(em, "vector", sv, "tensor_mul",
                           att_q["h"][:, hr, hs], t2_scr["h"][:, hs],
                           msc[1][:, :],
                           bufs_r=[t2_scr["buf"], mscb[1]], bufs_w=[att_q["buf"]])
                dma(em, "sync",
                    att_dram[256 * hq:256 * (hq + 1), :].rearrange(
                        "(r p) t -> p r t", p=128),
                    att_q["h"][:], bufs_r=[att_q["buf"]], bufs_w=[att_dram_b])

        full_barrier(em, sems)
        with contextlib.ExitStack() as osc:
            def osb(name, shape, dtype):
                return {"h": osc.enter_context(
                    nc.sbuf_tensor(pre + name, shape, dtype)),
                        "buf": Buf(True)}
            wslab = [osb(f"wo{i}", [128, DC, 512], BF16) for i in range(2)]
            arhs = [osb(f"ar{i}", [128, DC, TT], BF16) for i in range(2)]
            po = [osc.enter_context(
                nc.psum_tensor(pre + f"po{i}", [128, 512], F32))
                  for i in range(2)]
            pob = [Buf(), Buf()]
            wo_r = w_out.rearrange("(kc p) o -> p kc o", p=128)
            att_r = att_dram.rearrange("(kc p) t -> p kc t", p=128)
            for tt in range(NTT):
                ts = slice(tt * TT, (tt + 1) * TT)
                dma(em, "sync", arhs[tt % 2]["h"][:], att_r[:, :, ts],
                    bufs_r=[att_dram_b], bufs_w=[arhs[tt % 2]["buf"]])
                for sp_i in range(4):
                    dma(em, "gpsimd", wslab[sp_i % 2]["h"][:],
                        wo_r[:, :, 512 * sp_i:512 * (sp_i + 1)],
                        bufs_w=[wslab[sp_i % 2]["buf"]])
                    for oc_rel in range(4):
                        oc = 4 * sp_i + oc_rel
                        pb = oc % 2
                        for kc in range(DC):
                            mm(em, st_ if kc == DC - 1 else None, po[pb][:, :],
                               wslab[sp_i % 2]["h"][:, kc,
                                                    oc_rel * 128:(oc_rel + 1) * 128],
                               arhs[tt % 2]["h"][:, kc, :],
                               start=(kc == 0), stop=(kc == DC - 1),
                               bufs_r=[wslab[sp_i % 2]["buf"], arhs[tt % 2]["buf"]],
                               bufs_w=[pob[pb]] if kc == 0 else ())
                        pob[pb].note_write(st_.h, st_.n)
                        op(em, "vector", sv, "tensor_add",
                           X[:, oc, ts], po[pb][:, :], X[:, oc, ts],
                           bufs_r=[pob[pb], Xbuf[tt]], bufs_w=[Xbuf[tt]])

    full_barrier(em, sems)
    em.release(smark)



def declare_attn_l1(nc):
    H = 16
    BIG = 1e9
    P = {}
    P["w_dkv"] = nc.dram_tensor("w_dkv", [D, 128], F32, kind="ExternalInput").ap()
    P["w_dq"] = nc.dram_tensor("w_dq", [D, 128], F32, kind="ExternalInput").ap()
    P["w_kr"] = nc.dram_tensor("w_kr", [D, 64], F32, kind="ExternalInput").ap()
    P["w_qc"] = nc.dram_tensor("w_qc", [H, 128, 64], F32, kind="ExternalInput").ap()
    P["w_qr"] = nc.dram_tensor("w_qr", [H, 128, 64], F32, kind="ExternalInput").ap()
    P["w_uk"] = nc.dram_tensor("w_uk", [H, 128, 64], F32, kind="ExternalInput").ap()
    P["w_uv"] = nc.dram_tensor("w_uv", [H, 128, 128], F32, kind="ExternalInput").ap()
    P["w_o"] = nc.dram_tensor("w_o", [D, D], F32, kind="ExternalInput").ap()
    P["cosf"] = nc.dram_tensor("cosf_i", [64, T], F32, kind="ExternalInput").ap()
    P["sinf"] = nc.dram_tensor("sinf_i", [64, T], F32, kind="ExternalInput").ap()
    P["pc"] = nc.dram_tensor("pc_i", [128, 1], F32, kind="ExternalInput").ap()
    P["msel"] = nc.dram_tensor("msel_i", [128, 2], F32, kind="ExternalInput").ap()
    P["contrib"] = nc.dram_tensor("contrib1", [192, T], BF16)
    P["gathered"] = nc.dram_tensor("gathered1", [384, T], BF16)
    psw = np.zeros((64, 64), np.float32)
    for m in range(64):
        psw[m ^ 1, m] = 1.0
    P["pswap_d"] = nc.inline_tensor(psw.astype(nbf), name="pswap_d").ap()
    qq = np.arange(128)[:, None]
    kk = np.arange(512)[None, :]
    md = np.stack([np.where(kk <= 128 * qm + qq, 0.0, -BIG)
                   for qm in range(4)], axis=1).astype(np.float32)
    P["mdiag_d"] = nc.inline_tensor(md, name="mdiag_d").ap()
    return P


def emit_attn_l1(nc, em, sems, sh, P, pre):
    H = 16
    SC = 0.08838834764831845
    st_, sv, sc, sp, scc = (sems[k] for k in ("st", "sv", "sc", "sp", "cc"))
    X, Xbuf, C = sh["X"], sh["Xbuf"], sh["C"]
    w_dkv, w_dq, w_kr = P["w_dkv"], P["w_dq"], P["w_kr"]
    w_qc, w_qr, w_uk, w_uv, w_o = (P["w_qc"], P["w_qr"], P["w_uk"],
                                   P["w_uv"], P["w_o"])
    cosf_d, sinf_d, pc_d, msel_d = P["cosf"], P["sinf"], P["pc"], P["msel"]
    contrib, gathered = P["contrib"], P["gathered"]
    pswap_d, mdiag_d = P["pswap_d"], P["mdiag_d"]
    ident_d = sh["ident_d"]
    ones_row_d = sh["onr_d"]
    contrib_b, gathered_b = Buf(), Buf()
    smark = em.mark()
    with contextlib.ExitStack() as stack:
        ent = stack.enter_context

        def sb(name, shape, dtype):
            return {"h": ent(nc.sbuf_tensor(pre + name, shape, dtype)),
                    "buf": Buf(True)}

        onc = sh["onc"]
        ident = sb("ident", [128, 128], BF16)
        pswap = sb("pswap", [64, 64], BF16)
        cosf = sb("cosf", [64, T], F32)
        sinf = sb("sinf", [64, T], F32)
        pc = sb("pc", [128, 1], F32)
        msel = sb("msel", [128, 2], F32)
        wdkv_s = sb("wdkv_s", [128, DC, 128], BF16)
        wdq_s = sb("wdq_s", [128, DC, 128], BF16)
        wkr_s = sb("wkr_s", [128, DC, 64], BF16)
        wqc_s = sb("wqc_s", [128, H, 64], BF16)
        wqr_s = sb("wqr_s", [128, H, 64], BF16)
        qnr = sb("qnr", [128, H, T], BF16)
        ckv_of = sb("ckv_of", [128, 2 * T], BF16)
        krf_of = sb("krf_of", [64, 2 * T], BF16)
        att_sb = sb("att_sb", [128, DC, T], BF16)

        cr = []
        for tile, srcd in ((ident, ident_d),
                           (pswap, pswap_d),
                           (cosf, cosf_d), (sinf, sinf_d), (pc, pc_d),
                           (msel, msel_d)):
            cr.append(dma(em, "sync", tile["h"][:], srcd, bufs_w=[tile["buf"]]))
        for tile, srcd in (
                (wdkv_s, w_dkv.rearrange("(kc p) o -> p kc o", p=128)),
                (wdq_s, w_dq.rearrange("(kc p) o -> p kc o", p=128)),
                (wkr_s, w_kr.rearrange("(kc p) o -> p kc o", p=128)),
                (wqc_s, w_qc.rearrange("h p c -> p h c")),
                (wqr_s, w_qr.rearrange("h p c -> p h c"))):
            cr.append(dma(em, "gpsimd", tile["h"][:], srcd,
                          bufs_w=[tile["buf"]]))
        consts_ready = cr
        C = {"ones_col": onc["h"][:], "ones_row": None}

        em.wait("tensor", consts_ready)
        em.wait("scalar", consts_ready)
        em.wait("vector", consts_ready)

        with contextlib.ExitStack() as s1:
            def s1b(name, shape, dtype):
                return {"h": s1.enter_context(
                    nc.sbuf_tensor(pre + name, shape, dtype)),
                        "buf": Buf(True)}
            Yh = s1.enter_context(nc.sbuf_tensor(pre + "Y", [128, DC, TT], BF16))
            Y = Yh[:]
            Ybuf = Buf()
            sq = [s1b(f"sq{i}", [128, TT], F32) for i in range(2)]
            tmp = [s1b(f"tmp{i}", [128, TT], F32) for i in range(2)]
            rows = s1b("rows", [1, 3, TT], F32)
            Rb = s1b("Rb", [128, TT], F32)
            MRb = s1b("MRb", [128, TT], F32)
            onr = s1b("onr", [1, 128], F32)
            cr2 = [dma(em, "sync", onr["h"][:], ones_row_d,
                       bufs_w=[onr["buf"]])]
            em.wait("tensor", cr2)
            dq_bf = s1b("dq_bf", [128, T], BF16)
            ckv_bf = s1b("ckv_bf", [128, T], BF16)
            kr_bf = s1b("kr_bf", [64, T], BF16)
            krop_bf = s1b("krop_bf", [64, T], BF16)
            qr_bf = s1b("qr_bf", [64, TT], BF16)
            rsc1 = s1b("rsc1", [64, TT], F32)
            rsc2 = s1b("rsc2", [64, TT], F32)
            b0c, b1c = dq_bf, ckv_bf
            b0k, b1k = kr_bf, krop_bf

            lps = [s1.enter_context(
                nc.psum_tensor(pre + f"lps{i}", [128, 512], F32))
                   for i in range(8)]
            lpsb = [Buf() for _ in range(8)]
            rh = rows["h"]
            rowaps = [rh[:1, 0, :], rh[:1, 1, :], rh[:1, 2, :]]
            scr = {"ps": lps, "psb": lpsb, "sq": sq,
                   "sqb": [s["buf"] for s in sq],
                   "rowaps": rowaps, "rowsb": rows["buf"],
                   "R": Rb, "MR": MRb, "tmp": tmp,
                   "tmpb": [t["buf"] for t in tmp]}

            for tt in range(NTT):
                ts = slice(tt * TT, (tt + 1) * TT)
                emit_ln(em, sems, X, Xbuf, Y, Ybuf, tt,
                        {"ones_col": onc["h"][:], "ones_row": onr["h"][:]}, scr)
                for dst, wsl, pb in ((ckv_bf, wdkv_s, 4), (dq_bf, wdq_s, 5)):
                    for kc in range(DC):
                        mm(em, st_ if kc == DC - 1 else None, lps[pb][:, :],
                           wsl["h"][:, kc, :], Y[:, kc, :],
                           start=(kc == 0), stop=(kc == DC - 1),
                           bufs_r=[wsl["buf"], Ybuf],
                           bufs_w=[lpsb[pb]] if kc == 0 else ())
                    lpsb[pb].note_write(st_.h, st_.n)
                    op(em, "vector", sv, "tensor_copy", dst["h"][:, ts],
                       lps[pb][:, :], bufs_r=[lpsb[pb]], bufs_w=[dst["buf"]])
                for kc in range(DC):
                    mm(em, st_ if kc == DC - 1 else None, lps[6][:64, :],
                       wkr_s["h"][:, kc, :], Y[:, kc, :],
                       start=(kc == 0), stop=(kc == DC - 1),
                       bufs_r=[wkr_s["buf"], Ybuf],
                       bufs_w=[lpsb[6]] if kc == 0 else ())
                lpsb[6].note_write(st_.h, st_.n)
                op(em, "vector", sv, "tensor_copy", kr_bf["h"][:, ts],
                   lps[6][:64, :], bufs_r=[lpsb[6]], bufs_w=[kr_bf["buf"]])
                mm(em, st_, lps[7][:64, :], pswap["h"][:], kr_bf["h"][:, ts],
                   start=True, stop=True,
                   bufs_r=[pswap["buf"], kr_bf["buf"]], bufs_w=[lpsb[7]])
                op(em, "vector", sv, "tensor_mul", rsc1["h"][:],
                   lps[6][:64, :], cosf["h"][:, ts],
                   bufs_r=[lpsb[6], cosf["buf"]], bufs_w=[rsc1["buf"]])
                op(em, "vector", sv, "tensor_mul", rsc2["h"][:],
                   lps[7][:64, :], sinf["h"][:, ts],
                   bufs_r=[lpsb[7], sinf["buf"]], bufs_w=[rsc2["buf"]])
                op(em, "vector", sv, "tensor_add", krop_bf["h"][:, ts],
                   rsc1["h"][:], rsc2["h"][:],
                   bufs_r=[rsc1["buf"], rsc2["buf"]], bufs_w=[krop_bf["buf"]])
                for h in range(H):
                    pb = h % 2
                    mm(em, st_, lps[pb][:64, :], wqc_s["h"][:, h, :],
                       dq_bf["h"][:, ts], start=True, stop=True,
                       bufs_r=[wqc_s["buf"], dq_bf["buf"]], bufs_w=[lpsb[pb]])
                    op(em, "vector", sv, "tensor_copy", qnr["h"][:64, h, ts],
                       lps[pb][:64, :], bufs_r=[lpsb[pb]], bufs_w=[qnr["buf"]])
                    mm(em, st_, lps[2 + pb][:64, :], wqr_s["h"][:, h, :],
                       dq_bf["h"][:, ts], start=True, stop=True,
                       bufs_r=[wqr_s["buf"], dq_bf["buf"]], bufs_w=[lpsb[2 + pb]])
                    op(em, "vector", sv, "tensor_copy", qr_bf["h"][:],
                       lps[2 + pb][:64, :], bufs_r=[lpsb[2 + pb]],
                       bufs_w=[qr_bf["buf"]])
                    mm(em, st_, lps[6][:64, :], pswap["h"][:], qr_bf["h"][:],
                       start=True, stop=True,
                       bufs_r=[pswap["buf"], qr_bf["buf"]], bufs_w=[lpsb[6]])
                    op(em, "vector", sv, "tensor_mul", rsc1["h"][:],
                       lps[2 + pb][:64, :], cosf["h"][:, ts],
                       bufs_r=[lpsb[2 + pb], cosf["buf"]], bufs_w=[rsc1["buf"]])
                    op(em, "vector", sv, "tensor_mul", rsc2["h"][:],
                       lps[6][:64, :], sinf["h"][:, ts],
                       bufs_r=[lpsb[6], sinf["buf"]], bufs_w=[rsc2["buf"]])
                    op(em, "vector", sv, "tensor_add", qnr["h"][64:128, h, ts],
                       rsc1["h"][:], rsc2["h"][:],
                       bufs_r=[rsc1["buf"], rsc2["buf"]], bufs_w=[qnr["buf"]])

            dma(em, "sync", contrib[0:128, :], ckv_bf["h"][:],
                bufs_r=[ckv_bf["buf"]], bufs_w=[contrib_b])
            dma(em, "sync", contrib[128:192, :], krop_bf["h"][:],
                bufs_r=[krop_bf["buf"]], bufs_w=[contrib_b])
            em.emit("gpsimd", "collective_compute",
                    ("AllGather", ALU.bypass,
                     [[0, 1], [2, 3], [4, 5], [6, 7]],
                     [contrib.ap().opt()], [gathered.ap().opt()]),
                    {}, scc, 1, (), bufs_r=[contrib_b], bufs_w=[gathered_b])
            dma(em, "sync", b0c["h"][:], gathered[0:128, :],
                bufs_r=[gathered_b], bufs_w=[b0c["buf"]])
            dma(em, "sync", b1c["h"][:], gathered[192:320, :],
                bufs_r=[gathered_b], bufs_w=[b1c["buf"]])
            dma(em, "sync", b0k["h"][:], gathered[128:192, :],
                bufs_r=[gathered_b], bufs_w=[b0k["buf"]])
            dma(em, "sync", b1k["h"][:], gathered[320:384, :],
                bufs_r=[gathered_b], bufs_w=[b1k["buf"]])
            for dst, b0, b1, np_ in ((ckv_of, b0c, b1c, 128),
                                     (krf_of, b0k, b1k, 64)):
                op(em, "vector", sv, "tensor_scalar_mul", dst["h"][:np_, 0:T],
                   b0["h"][:], msel["h"][:np_, 0:1],
                   bufs_r=[b0["buf"], msel["buf"]], bufs_w=[dst["buf"]])
                op(em, "vector", sv, "scalar_tensor_tensor", dst["h"][:np_, 0:T],
                   b1["h"][:], msel["h"][:np_, 1:2], dst["h"][:np_, 0:T],
                   ALU.mult, ALU.add,
                   bufs_r=[b1["buf"], msel["buf"], dst["buf"]],
                   bufs_w=[dst["buf"]])
                op(em, "vector", sv, "tensor_scalar_mul", dst["h"][:np_, T:2 * T],
                   b0["h"][:], msel["h"][:np_, 1:2],
                   bufs_r=[b0["buf"], msel["buf"]], bufs_w=[dst["buf"]])
                op(em, "vector", sv, "scalar_tensor_tensor", dst["h"][:np_, T:2 * T],
                   b1["h"][:], msel["h"][:np_, 0:1], dst["h"][:np_, T:2 * T],
                   ALU.mult, ALU.add,
                   bufs_r=[b1["buf"], msel["buf"], dst["buf"]],
                   bufs_w=[dst["buf"]])
        full_barrier(em, sems)

        with contextlib.ExitStack() as s2:
            def s2b(name, shape, dtype):
                return {"h": s2.enter_context(
                    nc.sbuf_tensor(pre + name, shape, dtype)),
                        "buf": Buf(True)}
            knr = s2b("knr", [128, 2 * T], BF16)
            v_h = s2b("v_h", [128, 16, 128], BF16)
            mdiag = s2b("mdiag", [128, 4, 512], F32)
            wuk_s = s2b("wuk_s", [128, H, 64], BF16)
            wuv_s = s2b("wuv_s", [128, H, 128], BF16)
            dma(em, "sync", mdiag["h"][:], mdiag_d, bufs_w=[mdiag["buf"]])
            dma(em, "gpsimd", wuk_s["h"][:], w_uk.rearrange("h p c -> p h c"),
                bufs_w=[wuk_s["buf"]])
            dma(em, "gpsimd", wuv_s["h"][:], w_uv.rearrange("h p c -> p h c"),
                bufs_w=[wuv_s["buf"]])
            P_sb = s2b("P_sb", [128, 4, 512], BF16)
            P_T = s2b("P_T", [128, 16, 128], BF16)
            av_bf = s2b("av_bf", [128, 128], BF16)
            cols = s2b("cols", [128, 16], F32)
            sps = [s2.enter_context(
                nc.psum_tensor(pre + f"sps{i}", [128, 512], F32))
                   for i in range(4)]
            spsb = [Buf() for _ in range(4)]
            tpb = [s2.enter_context(
                nc.psum_tensor(pre + f"tpb{i}", [128, 1024], BF16))
                   for i in range(2)]
            tpbb = [Buf(), Buf()]
            avp = [s2.enter_context(
                nc.psum_tensor(pre + f"avp{i}", [128, 512], F32))
                   for i in range(2)]
            avpb = [Buf(), Buf()]
            ch = cols["h"]
            mx_c = [ch[:, i:i + 1] for i in range(4)]
            own_c, prt_c, mxt_c = ch[:, 4:5], ch[:, 5:6], ch[:, 6:7]
            bias_c, bias2_c = ch[:, 7:8], ch[:, 8:9]
            den_c, rden_c = ch[:, 9:10], ch[:, 10:11]

            op(em, "vector", sv, "tensor_copy", knr["h"][64:128, :],
               krf_of["h"][:, :], bufs_r=[krf_of["buf"]], bufs_w=[knr["buf"]])
            for h in range(H):
                for kt in range(4):
                    ks = slice(kt * 512, (kt + 1) * 512)
                    pb = kt % 2
                    mm(em, st_, sps[pb][:64, :], wuk_s["h"][:, h, :],
                       ckv_of["h"][:, ks], start=True, stop=True,
                       bufs_r=[wuk_s["buf"], ckv_of["buf"]], bufs_w=[spsb[pb]])
                    op(em, "vector", sv, "tensor_copy", knr["h"][0:64, ks],
                       sps[pb][:64, :], bufs_r=[spsb[pb]], bufs_w=[knr["buf"]])
                for kc in range(16):
                    pb = kc % 2
                    mm(em, st_, sps[pb][:, :128],
                       ckv_of["h"][:, kc * 128:(kc + 1) * 128],
                       wuv_s["h"][:, h, :], start=True, stop=True,
                       bufs_r=[ckv_of["buf"], wuv_s["buf"]], bufs_w=[spsb[pb]])
                    op(em, "vector", sv, "tensor_copy", v_h["h"][:, kc, :],
                       sps[pb][:, :128], bufs_r=[spsb[pb]], bufs_w=[v_h["buf"]])
                for qc in range(8):
                    nown = qc // 4 + 1
                    kts = list(range(nown)) + [2, 3]
                    qsl = qnr["h"][:, h, qc * 128:(qc + 1) * 128]
                    for i, kt in enumerate(kts):
                        koff = kt * 512
                        mm(em, st_, sps[i][:, :], qsl,
                           knr["h"][:, koff:koff + 512], start=True, stop=True,
                           bufs_r=[qnr["buf"], knr["buf"]], bufs_w=[spsb[i]])
                    op(em, "vector", sv, "tensor_add", sps[nown - 1][:, :],
                       sps[nown - 1][:, :], mdiag["h"][:, qc % 4, :],
                       bufs_r=[spsb[nown - 1], mdiag["buf"]],
                       bufs_w=[spsb[nown - 1]])
                    for i, kt in enumerate(kts):
                        op(em, "vector", sv, "reduce_max", mx_c[i],
                           sps[i][:, :], AX.X, bufs_r=[spsb[i]],
                           bufs_w=[cols["buf"]])
                    if nown == 1:
                        op(em, "vector", sv, "tensor_copy", own_c, mx_c[0],
                           bufs_r=[cols["buf"]], bufs_w=[cols["buf"]])
                    else:
                        op(em, "vector", sv, "tensor_max", own_c, mx_c[0],
                           mx_c[1], bufs_r=[cols["buf"]], bufs_w=[cols["buf"]])
                    op(em, "vector", sv, "tensor_max", prt_c, mx_c[nown],
                       mx_c[nown + 1], bufs_r=[cols["buf"]], bufs_w=[cols["buf"]])
                    op(em, "vector", sv, "tensor_sub", prt_c, prt_c,
                       pc["h"][:], bufs_r=[cols["buf"], pc["buf"]],
                       bufs_w=[cols["buf"]])
                    op(em, "vector", sv, "tensor_max", mxt_c, own_c, prt_c,
                       bufs_r=[cols["buf"]], bufs_w=[cols["buf"]])
                    op(em, "vector", sv, "tensor_scalar_mul", bias_c, mxt_c,
                       -SC, bufs_r=[cols["buf"]], bufs_w=[cols["buf"]])
                    op(em, "vector", sv, "tensor_sub", bias2_c, bias_c,
                       pc["h"][:], bufs_r=[cols["buf"], pc["buf"]],
                       bufs_w=[cols["buf"]])
                    for i, kt in enumerate(kts):
                        bias_ap = bias_c if kt < 2 else bias2_c
                        acc = ch[:, 11 + i:12 + i]
                        op(em, "scalar", sc, "activation", P_sb["h"][:, kt, :],
                           sps[i][:, :], AF.Exp, scale=SC, bias=bias_ap,
                           accum_out=acc,
                           bufs_r=[spsb[i], cols["buf"]],
                           bufs_w=[P_sb["buf"], cols["buf"]])
                    op(em, "vector", sv, "tensor_add", den_c, ch[:, 11:12],
                       ch[:, 12:13], bufs_r=[cols["buf"]], bufs_w=[cols["buf"]])
                    for i in range(2, len(kts)):
                        op(em, "vector", sv, "tensor_add", den_c, den_c,
                           ch[:, 11 + i:12 + i], bufs_r=[cols["buf"]],
                           bufs_w=[cols["buf"]])
                    op(em, "vector", sv, "reciprocal", rden_c, den_c,
                       bufs_r=[cols["buf"]], bufs_w=[cols["buf"]])
                    ocks = list(range(qc + 1))
                    for j, ck in enumerate(ocks):
                        op(em, "tensor", st_, "transpose",
                           tpb[0][:, j * 128:(j + 1) * 128],
                           P_sb["h"][:, ck // 4, (ck % 4) * 128:(ck % 4 + 1) * 128],
                           ident["h"][:],
                           bufs_r=[P_sb["buf"], ident["buf"]], bufs_w=[tpbb[0]])
                    op(em, "vector", sv, "tensor_copy",
                       P_T["h"][:, 0:qc + 1, :], tpb[0][:, 0:(qc + 1) * 128],
                       bufs_r=[tpbb[0]], bufs_w=[P_T["buf"]])
                    for j in range(8):
                        sl = 2 + j // 4
                        op(em, "tensor", st_, "transpose",
                           tpb[1][:, j * 128:(j + 1) * 128],
                           P_sb["h"][:, sl, (j % 4) * 128:(j % 4 + 1) * 128],
                           ident["h"][:],
                           bufs_r=[P_sb["buf"], ident["buf"]], bufs_w=[tpbb[1]])
                    op(em, "vector", sv, "tensor_copy",
                       P_T["h"][:, 8:16, :], tpb[1][:, :],
                       bufs_r=[tpbb[1]], bufs_w=[P_T["buf"]])
                    ab = qc % 2
                    cks = ocks + list(range(8, 16))
                    for j, ck in enumerate(cks):
                        mm(em, st_ if j == len(cks) - 1 else None,
                           avp[ab][:, :128], P_T["h"][:, ck % 16, :],
                           v_h["h"][:, ck % 16, :],
                           start=(j == 0), stop=(j == len(cks) - 1),
                           bufs_r=[P_T["buf"], v_h["buf"]],
                           bufs_w=[avpb[ab]] if j == 0 else ())
                    avpb[ab].note_write(st_.h, st_.n)
                    op(em, "vector", sv, "tensor_scalar_mul", av_bf["h"][:],
                       avp[ab][:, :128], rden_c,
                       bufs_r=[avpb[ab], cols["buf"]], bufs_w=[av_bf["buf"]])
                    op(em, "tensor", st_, "transpose", tpb[0][:, :128],
                       av_bf["h"][:], ident["h"][:],
                       bufs_r=[av_bf["buf"], ident["buf"]], bufs_w=[tpbb[0]])
                    op(em, "vector", sv, "tensor_copy",
                       att_sb["h"][:, h, qc * 128:(qc + 1) * 128],
                       tpb[0][:, :128], bufs_r=[tpbb[0]], bufs_w=[att_sb["buf"]])
        full_barrier(em, sems)

        with contextlib.ExitStack() as s3:
            def s3b(name, shape, dtype):
                return {"h": s3.enter_context(
                    nc.sbuf_tensor(pre + name, shape, dtype)),
                        "buf": Buf(True)}
            wslab = [s3b(f"wo{i}", [128, DC, 512], BF16) for i in range(2)]
            po = [s3.enter_context(
                nc.psum_tensor(pre + f"po{i}", [128, 512], F32))
                  for i in range(2)]
            pob = [Buf(), Buf()]
            wo_r = w_o.rearrange("(kc p) o -> p kc o", p=128)
            for tt in range(NTT):
                ts = slice(tt * TT, (tt + 1) * TT)
                for sp_i in range(4):
                    if tt == 0:
                        dma(em, "gpsimd", wslab[sp_i % 2]["h"][:],
                            wo_r[:, :, 512 * sp_i:512 * (sp_i + 1)],
                            bufs_w=[wslab[sp_i % 2]["buf"]])
                    for oc_rel in range(4):
                        oc = 4 * sp_i + oc_rel
                        pb = oc % 2
                        for kc in range(DC):
                            mm(em, st_ if kc == DC - 1 else None, po[pb][:, :],
                               wslab[sp_i % 2]["h"][:, kc,
                                                    oc_rel * 128:(oc_rel + 1) * 128],
                               att_sb["h"][:, kc, ts],
                               start=(kc == 0), stop=(kc == DC - 1),
                               bufs_r=[wslab[sp_i % 2]["buf"], att_sb["buf"]],
                               bufs_w=[pob[pb]] if kc == 0 else ())
                        pob[pb].note_write(st_.h, st_.n)
                        op(em, "vector", sv, "tensor_add",
                           X[:, oc, ts], po[pb][:, :], X[:, oc, ts],
                           bufs_r=[pob[pb], Xbuf[tt]], bufs_w=[Xbuf[tt]])

    full_barrier(em, sems)
    em.release(smark)


def build_fused():
    nc = bass.Bass("TRN2", target_bir_lowering=False)
    x_i = nc.dram_tensor("x_i", [D, T], F32, kind="ExternalInput").ap()
    x_o = nc.dram_tensor("x_o", [D, T], F32, kind="ExternalOutput").ap()
    P0f = declare_ffn_l0(nc)
    P0a = declare_attn_l0(nc)
    P1f = declare_ffn_l1(nc)
    P1a = declare_attn_l1(nc)
    ones_col_d = nc.inline_tensor(np.ones((128, 1), np.float32), name="onc_d").ap()
    ones_row_d = nc.inline_tensor(np.ones((1, 128), np.float32), name="onr_d").ap()
    ident_d = nc.inline_tensor(np.eye(128, dtype=nbf), name="ident_d").ap()

    em = Em(nc)
    with contextlib.ExitStack() as stack:
        ent = stack.enter_context
        em.pool = [Sem(ent(nc.semaphore(f"dp{i}"))) for i in range(88)]
        sems = {k: Sem(ent(nc.semaphore(k))) for k in
                ("st", "sv", "sc", "sp", "cc")}
        Xh = ent(nc.sbuf_tensor("X", [128, DC, T], F32))
        Xbuf = [Buf(True), Buf(True)]
        Xbuf[1].dsem = Xbuf[0].dsem = em.take_sem()
        onc = {"h": ent(nc.sbuf_tensor("onc", [128, 1], F32)), "buf": Buf(True)}
        onr = {"h": ent(nc.sbuf_tensor("onr", [1, 128], F32)), "buf": Buf(True)}
        dma(em, "sync", Xh[:], x_i.rearrange("(c p) t -> p c t", p=128),
            bufs_w=Xbuf)
        cr = [dma(em, "sync", onc["h"][:], ones_col_d, bufs_w=[onc["buf"]]),
              dma(em, "sync", onr["h"][:], ones_row_d, bufs_w=[onr["buf"]])]
        for e in ENGINES:
            em.wait(e, cr)
        sh = {"X": Xh[:], "Xbuf": Xbuf, "onc": onc, "onr": onr,
              "C": {"ones_col": onc["h"][:], "ones_row": onr["h"][:]},
              "ident_d": ident_d, "onr_d": ones_row_d}

        emit_ffn_l0(nc, em, sems, sh, P0f, "fa_", do_prep=True)
        emit_attn_l0(nc, em, sems, sh, P0a, "aa_")
        emit_ffn_l0(nc, em, sems, sh, P0f, "fb_", do_prep=False)
        emit_ffn_l1(nc, em, sems, sh, P1f, "ga_", do_prep=True)
        emit_attn_l1(nc, em, sems, sh, P1a, "ha_")
        emit_ffn_l1(nc, em, sems, sh, P1f, "gb_", do_prep=False)

        fin = dma(em, "sync", x_o.rearrange("(c p) t -> p c t", p=128), Xh[:],
                  bufs_r=[Xbuf[0], Xbuf[1]])
        em.wait("sync", [fin])
        em.finalize()
    return nc



def shard_x(x):
    out = []
    for c in range(NCORE):
        b, h = c // 2, c % 2
        out.append(np.ascontiguousarray(
            np.asarray(x)[b, h * T:(h + 1) * T, :].T.astype(np.float32)))
    return out


def unshard_x(shards):
    x = np.empty((4, 2048, D), np.float32)
    for c in range(NCORE):
        b, h = c // 2, c % 2
        x[b, h * T:(h + 1) * T, :] = shards[c].T
    return x


_CACHE = {}


def _rope_tables(off):
    import math
    half = 32
    freqs = np.exp(np.arange(half, dtype=np.float32) * (-math.log(10000.0) / half))
    tg = np.arange(off, off + T, dtype=np.float32)
    ang = tg[:, None] * freqs[None, :]
    cos = np.cos(ang).astype(np.float32)
    sin = np.sin(ang).astype(np.float32)
    cosf = np.empty((64, T), np.float32)
    sinf = np.empty((64, T), np.float32)
    cosf[0::2] = cos.T
    cosf[1::2] = cos.T
    sinf[0::2] = -sin.T
    sinf[1::2] = sin.T
    return cosf, sinf


def kernel(**inputs):
    if "fused" not in _CACHE:
        _CACHE["fused"] = build_fused()
    xs = shard_x(inputs["L0_attn_x"])
    base = {
        "w_gu": np.asarray(inputs["L0_ffn_w_gate_up"]),
        "w_dn": np.asarray(inputs["L0_ffn_w_down"]),
        "w_qkv": np.asarray(inputs["L0_attn_w_qkv"]),
        "w_out": np.asarray(inputs["L0_attn_w_out"]),
        "decay_i": np.asarray(inputs["L0_attn_decay_param"]).reshape(1, 16)
                     .astype(np.float32),
        "w_up": np.asarray(inputs["L1_ffn_W_up"]),
        "w_gt": np.asarray(inputs["L1_ffn_W_gate"]),
        "w_dn1": np.asarray(inputs["L1_ffn_W_down"]),
        "su_ug": np.asarray(inputs["L1_ffn_SU_ug"]).reshape(16, 128),
        "su_dp": np.asarray(inputs["L1_ffn_SU_dp"]).reshape(64, 128),
        "wsc_ug": np.asarray(inputs["L1_ffn_Wscale_ug"]).reshape(128, 128),
        "wsc_dp": np.asarray(inputs["L1_ffn_Wscale_dp"]).reshape(16, 128),
        "w_dkv": np.asarray(inputs["L1_attn_W_dkv"]),
        "w_dq": np.asarray(inputs["L1_attn_W_dq"]),
        "w_kr": np.asarray(inputs["L1_attn_W_kr"]),
        "w_qc": np.asarray(inputs["L1_attn_W_qc"]),
        "w_qr": np.asarray(inputs["L1_attn_W_qr"]),
        "w_uk": np.asarray(inputs["L1_attn_W_uk"]),
        "w_uv": np.asarray(inputs["L1_attn_W_uv"]),
        "w_o": np.asarray(inputs["L1_attn_W_o"]),
    }
    in_maps = []
    for c in range(NCORE):
        p = c % 2
        cosf, sinf = _rope_tables(p * T)
        in_maps.append(dict(
            base, x_i=xs[c],
            csel_i=np.full((128, 1), float(p), np.float32),
            cosf_i=cosf, sinf_i=sinf,
            pc_i=np.full((128, 1), 1e9 if p == 0 else 0.0, np.float32),
            msel_i=np.tile(np.array([[1.0 - p, p]], np.float32), (128, 1)),
        ))
    res = run_bass_kernel_spmd(_CACHE["fused"], in_maps,
                               core_ids=list(range(NCORE)))
    return unshard_x([res.results[c]["x_o"] for c in range(NCORE)])


# version 04
# speedup vs baseline: 1.0302x; 1.0302x over previous
import contextlib
import numpy as np
import ml_dtypes
import concourse.bass as bass
import concourse.mybir as mybir
from concourse.bass_utils import run_bass_kernel_spmd

F32 = mybir.dt.float32
BF16 = mybir.dt.bfloat16
AF = mybir.ActivationFunctionType
ALU = mybir.AluOpType
AX = mybir.AxisListType
nbf = ml_dtypes.bfloat16

NCORE = 8
T = 1024
TT = 512
NTT = T // TT
D = 2048
DC = D // 128
INTER = 8192
EPS = 1e-5

ENGINES = ("sync", "tensor", "vector", "scalar", "gpsimd")



class Sem:
    __slots__ = ("h", "n")

    def __init__(self, handle):
        self.h = handle
        self.n = 0


class Buf:
    __slots__ = ("prod", "cons", "dma_owner", "dsem")

    def __init__(self, dma_owner=False):
        self.prod = {}
        self.cons = {}
        self.dma_owner = dma_owner
        self.dsem = None

    def read_waits(self, w):
        w.extend(self.prod.values())

    def note_read(self, h, v):
        k = id(h)
        if k not in self.cons or self.cons[k][1] < v:
            self.cons[k] = (h, v)

    def write_waits(self, w):
        w.extend(self.prod.values())
        w.extend(self.cons.values())

    def note_write(self, h, v):
        k = id(h)
        if k not in self.prod or self.prod[k][1] < v:
            self.prod[k] = (h, v)


class Em:
    def __init__(self, nc):
        self.nc = nc
        self.q = {e: [] for e in ENGINES}
        self.hw = {e: {} for e in ENGINES}
        self.pool = []
        self.used = []

    def take_sem(self):
        s = self.pool.pop()
        self.used.append(s)
        return s

    def mark(self):
        return len(self.used)

    def release(self, mark):
        self.pool.extend(self.used[mark:])
        del self.used[mark:]

    def _dedupe(self, eng, waits):
        out = []
        hw = self.hw[eng]
        for h, v in waits:
            k = id(h)
            if hw.get(k, -1) >= v:
                continue
            hw[k] = v
            out.append((h, v))
        return tuple(out)

    def emit(self, eng, method, args, kwargs, sem, inc, waits, bufs_r=(), bufs_w=()):
        w = []
        for b in bufs_r:
            b.read_waits(w)
        for b in bufs_w:
            b.write_waits(w)
        w.extend(waits)
        w = self._dedupe(eng, w)
        if sem is not None:
            sem.n += inc
            tgt = (sem.h, sem.n)
        else:
            tgt = None

        def f(e, method=method, args=args, kwargs=kwargs, w=w, tgt=tgt, inc=inc):
            for hh, vv in w:
                e.wait_ge(hh, vv)
            ins = getattr(e, method)(*args, **kwargs)
            if tgt is not None:
                ins.then_inc(tgt[0], inc)
        self.q[eng].append(f)
        if tgt is not None:
            for b in bufs_r:
                b.note_read(*tgt)
            for b in bufs_w:
                b.note_write(*tgt)
        return sem.n if sem is not None else None

    def wait(self, eng, waits):
        w = self._dedupe(eng, tuple(waits))
        if not w:
            return

        def f(e, w=w):
            for hh, vv in w:
                e.wait_ge(hh, vv)
        self.q[eng].append(f)

    def finalize(self):
        with self.nc.Block() as block:
            for eng in ENGINES:
                thunks = self.q[eng]
                if not thunks:
                    continue
                getattr(block, eng)(lambda e, th=tuple(thunks): [f(e) for f in th])


def dma(em, eng, out, in_, waits=(), bufs_r=(), bufs_w=(), transpose=False,
        sem=None):
    if sem is None:
        for b in tuple(bufs_w) + tuple(bufs_r):
            if b.dma_owner:
                if b.dsem is None:
                    b.dsem = em.take_sem()
                sem = b.dsem
                break
    assert sem is not None, "dma needs an owner buffer or explicit sem"
    kw = {"out": out, "in_": in_}
    if transpose:
        kw["transpose"] = True
    em.emit(eng, "dma_start", (), kw, sem, 16, waits, bufs_r, bufs_w)
    return (sem.h, sem.n)


def mm(em, sem, out, lhsT, rhs, start, stop, waits=(), bufs_r=(), bufs_w=()):
    return em.emit("tensor", "matmul", (out, lhsT, rhs),
                   {"start": start, "stop": stop}, sem, 1, waits, bufs_r, bufs_w)


def op(em, eng, sem, method, *args, waits=(), bufs_r=(), bufs_w=(), **kwargs):
    return em.emit(eng, method, args, kwargs, sem, 1, waits, bufs_r, bufs_w)


def full_barrier(em, sems):
    pts = [(s.h, s.n) for s in em.used if s.n > 0]
    pts += [(s.h, s.n) for s in em.pool if s.n > 0]
    pts += [(s.h, s.n) for s in sems.values() if s.n > 0]
    for eng in ENGINES:
        em.wait(eng, list(pts))



def emit_weight_prep(em, w_ap, rows, cols, bf_ap, bf_buf, t_ap, t_buf,
                     cbuf, tslab):
    CB = max(1, 8192 // cols)
    block_rows = 128 * CB
    nblk = rows // block_rows
    for g in range(nblk):
        b = cbuf[g % 2]
        view = b["h"][:].rearrange("p (b h) -> p b h", h=cols)[:, :CB, :]
        src = w_ap[g * block_rows:(g + 1) * block_rows, :].rearrange(
            "(b p) h -> p b h", p=128)
        dma(em, "gpsimd", view, src, bufs_w=[b["buf"]])
        dst = bf_ap[g * block_rows:(g + 1) * block_rows, :].rearrange(
            "(b p) h -> p b h", p=128)
        dma(em, "gpsimd", dst, view, bufs_r=[b["buf"]], bufs_w=[bf_buf])
    nch = cols // 128
    for ch in range(nch):
        b = tslab[ch % 2]
        dma(em, "sync", b["h"][:, :rows], bf_ap[:, ch * 128:(ch + 1) * 128],
            bufs_r=[bf_buf], bufs_w=[b["buf"]], transpose=True)
        dma(em, "sync", t_ap[ch * 128:(ch + 1) * 128, :], b["h"][:, :rows],
            bufs_r=[b["buf"]], bufs_w=[t_buf])



def emit_ln(em, sems, X, Xbuf, Y, Ybuf, tt, C, scr):
    st_, sv, sc, sp = sems["st"], sems["sv"], sems["sc"], sems["sp"]
    ts = slice(tt * TT, (tt + 1) * TT)
    ps = scr["ps"]
    psb = scr["psb"]
    sq, sqb = scr["sq"], scr["sqb"]
    rowsb = scr["rowsb"]
    m_row, t_row, mr_row = scr["rowaps"]
    msq_row = v_row = sd_row = rstd_row = t_row

    for hc in range(DC):
        mm(em, st_ if hc == DC - 1 else None, ps[0][:1, :], C["ones_col"],
           X[:, hc, ts], start=(hc == 0), stop=(hc == DC - 1),
           bufs_r=[Xbuf[tt]], bufs_w=[psb[0]] if hc == 0 else ())
    s1 = (st_.h, st_.n)
    for hc in range(DC):
        op(em, "scalar", sc, "activation", sq[hc % 2]["h"][:], X[:, hc, ts],
           AF.Square, bufs_r=[Xbuf[tt]], bufs_w=[sqb[hc % 2]])
        mm(em, st_, ps[1][:1, :], C["ones_col"], sq[hc % 2]["h"][:],
           start=(hc == 0), stop=(hc == DC - 1),
           bufs_r=[sqb[hc % 2]], bufs_w=[psb[1]] if hc == 0 else ())
    s2 = (st_.h, st_.n)

    op(em, "vector", sv, "tensor_scalar_mul", m_row, ps[0][:1, :], 1.0 / D,
       waits=[s1], bufs_r=[psb[0]], bufs_w=[rowsb])
    op(em, "scalar", sc, "activation", msq_row, m_row, AF.Square,
       bufs_r=[rowsb], bufs_w=[rowsb])
    op(em, "vector", sv, "scalar_tensor_tensor", v_row, ps[1][:1, :], 1.0 / D,
       msq_row, ALU.mult, ALU.subtract, waits=[s2], bufs_r=[psb[1], rowsb],
       bufs_w=[rowsb])
    op(em, "vector", sv, "tensor_scalar_add", v_row, v_row, EPS,
       bufs_r=[rowsb], bufs_w=[rowsb])
    op(em, "scalar", sc, "activation", sd_row, v_row, AF.Sqrt,
       bufs_r=[rowsb], bufs_w=[rowsb])
    op(em, "vector", sv, "reciprocal", rstd_row, sd_row,
       bufs_r=[rowsb], bufs_w=[rowsb])
    op(em, "vector", sv, "tensor_mul", mr_row, m_row, rstd_row,
       bufs_r=[rowsb], bufs_w=[rowsb])

    mm(em, st_, ps[2][:, :], C["ones_row"], rstd_row, start=True, stop=True,
       bufs_r=[rowsb], bufs_w=[psb[2]])
    mm(em, st_, ps[3][:, :], C["ones_row"], mr_row, start=True, stop=True,
       bufs_r=[rowsb], bufs_w=[psb[3]])
    op(em, "vector", sv, "tensor_copy", scr["R"]["h"][:], ps[2][:, :],
       bufs_r=[psb[2]], bufs_w=[scr["R"]["buf"]])
    op(em, "vector", sv, "tensor_copy", scr["MR"]["h"][:], ps[3][:, :],
       bufs_r=[psb[3]], bufs_w=[scr["MR"]["buf"]])

    tmp, tmpb = scr["tmp"], scr["tmpb"]
    for hc in range(DC):
        op(em, "gpsimd", sp, "tensor_mul", tmp[hc % 2]["h"][:], X[:, hc, ts],
           scr["R"]["h"][:], bufs_r=[Xbuf[tt], scr["R"]["buf"]],
           bufs_w=[tmpb[hc % 2]])
        op(em, "vector", sv, "tensor_sub", Y[:, hc, :], tmp[hc % 2]["h"][:],
           scr["MR"]["h"][:], bufs_r=[tmpb[hc % 2], scr["MR"]["buf"]],
           bufs_w=[Ybuf] if hc == 0 else (),)
        Ybuf.note_write(sv.h, sv.n)



def declare_ffn_l0(nc):
    P = {}
    P["w_gu"] = nc.dram_tensor("w_gu", [2 * INTER, D], F32, kind="ExternalInput").ap()
    P["w_dn"] = nc.dram_tensor("w_dn", [D, INTER], F32, kind="ExternalInput").ap()
    P["gu_bf"] = nc.dram_tensor("gu_bf", [2 * INTER, D], BF16).ap()
    P["dn_bf"] = nc.dram_tensor("dn_bf", [D, INTER], BF16).ap()
    P["guT"] = nc.dram_tensor("guT", [D, 2 * INTER], BF16).ap()
    P["dnT"] = nc.dram_tensor("dnT", [INTER, D], BF16).ap()
    P["gu_bf_b"], P["dn_bf_b"], P["guT_b"], P["dnT_b"] = \
        Buf(), Buf(), Buf(), Buf()
    return P


def emit_ffn_l0(nc, em, sems, sh, P, pre, do_prep):
    st_, sv, sc, sp = (sems[k] for k in ("st", "sv", "sc", "sp"))
    X, Xbuf, C = sh["X"], sh["Xbuf"], sh["C"]
    w_gu, w_dn = P["w_gu"], P["w_dn"]
    gu_bf, dn_bf, guT, dnT = P["gu_bf"], P["dn_bf"], P["guT"], P["dnT"]
    gu_bf_b, dn_bf_b, guT_b, dnT_b = (P["gu_bf_b"], P["dn_bf_b"],
                                      P["guT_b"], P["dnT_b"])
    smark = em.mark()
    with contextlib.ExitStack() as stack:
        ent = stack.enter_context

        def sb(name, shape, dtype):
            return {"h": ent(nc.sbuf_tensor(pre + name, shape, dtype)),
                    "buf": Buf(True)}

        if do_prep:
            with contextlib.ExitStack() as prep:
                def psb_(name, shape, dtype):
                    return {"h": prep.enter_context(
                        nc.sbuf_tensor(pre + name, shape, dtype)),
                        "buf": Buf(True)}
                cbuf = [psb_(f"cbuf{i}", [128, 8192], BF16) for i in range(2)]
                tslab = [psb_(f"tsl{i}", [128, 2 * INTER], BF16) for i in range(2)]
                emit_weight_prep(em, w_gu, 2 * INTER, D, gu_bf, gu_bf_b,
                                 guT, guT_b, cbuf, tslab)
                emit_weight_prep(em, w_dn, D, INTER, dn_bf, dn_bf_b,
                                 dnT, dnT_b, cbuf, tslab)
            full_barrier(em, sems)

        Yh = ent(nc.sbuf_tensor(pre + "Y", [128, DC, TT], BF16))
        Y = Yh[:]
        Ybuf = Buf()
        hbuf = sb("hbuf", [128, 16, TT], BF16)
        slab_g = [sb(f"slg{i}", [128, DC, 512], BF16) for i in range(2)]
        slab_u = [sb(f"slu{i}", [128, DC, 512], BF16) for i in range(2)]
        slab_d = [sb(f"sld{i}", [128, 16, 256], BF16) for i in range(2)]
        sq = [sb(f"sq{i}", [128, TT], F32) for i in range(2)]
        s_scr = [sb(f"ss{i}", [128, TT], F32) for i in range(2)]
        tmp = [sb(f"tmp{i}", [128, TT], F32) for i in range(2)]
        rows = sb("rows", [1, 3, TT], F32)
        Rb = sb("Rb", [128, TT], F32)
        MRb = sb("MRb", [128, TT], F32)
        psh = [ent(nc.psum_tensor(pre + f"ps{i}", [128, 512], F32))
               for i in range(8)]
        psb = [Buf() for _ in range(8)]

        rh = rows["h"]
        rowaps = [rh[:1, 0, :], rh[:1, 1, :], rh[:1, 2, :]]
        scr = {"ps": psh, "psb": psb, "sq": sq, "sqb": [s["buf"] for s in sq],
               "rowaps": rowaps, "rowsb": rows["buf"],
               "R": Rb, "MR": MRb, "tmp": tmp, "tmpb": [t["buf"] for t in tmp]}

        guT_r = guT.rearrange("(hc p) o -> p hc o", p=128)
        dnT_r = dnT.rearrange("(ic p) h -> p ic h", p=128)

        for tt in range(NTT):
            emit_ln(em, sems, X, Xbuf, Y, Ybuf, tt, C, scr)
            for q in range(4):
                for i16 in range(16):
                    ic = q * 16 + i16
                    j = ic // 4
                    rel = ic % 4
                    if rel == 0:
                        dma(em, "sync", slab_g[j % 2]["h"][:],
                            guT_r[:, :, 512 * j:512 * (j + 1)],
                            bufs_r=[guT_b], bufs_w=[slab_g[j % 2]["buf"]])
                        dma(em, "sync", slab_u[j % 2]["h"][:],
                            guT_r[:, :, INTER + 512 * j:INTER + 512 * (j + 1)],
                            bufs_r=[guT_b], bufs_w=[slab_u[j % 2]["buf"]])
                    b = ic % 2
                    for hc in range(DC):
                        mm(em, st_ if hc == DC - 1 else None, psh[b][:, :],
                           slab_g[j % 2]["h"][:, hc, rel * 128:(rel + 1) * 128],
                           Y[:, hc, :], start=(hc == 0), stop=(hc == DC - 1),
                           bufs_r=[slab_g[j % 2]["buf"], Ybuf],
                           bufs_w=[psb[b]] if hc == 0 else ())
                    psb[b].note_write(st_.h, st_.n)
                    for hc in range(DC):
                        mm(em, st_ if hc == DC - 1 else None, psh[2 + b][:, :],
                           slab_u[j % 2]["h"][:, hc, rel * 128:(rel + 1) * 128],
                           Y[:, hc, :], start=(hc == 0), stop=(hc == DC - 1),
                           bufs_r=[slab_u[j % 2]["buf"], Ybuf],
                           bufs_w=[psb[2 + b]] if hc == 0 else ())
                    psb[2 + b].note_write(st_.h, st_.n)
                    op(em, "scalar", sc, "activation", s_scr[ic % 2]["h"][:],
                       psh[b][:, :], AF.Silu,
                       bufs_r=[psb[b]], bufs_w=[s_scr[ic % 2]["buf"]])
                    op(em, "vector", sv, "scalar_tensor_tensor",
                       hbuf["h"][:, i16, :], psh[2 + b][:, :], 1.0,
                       s_scr[ic % 2]["h"][:], ALU.mult, ALU.mult,
                       bufs_r=[psb[2 + b], s_scr[ic % 2]["buf"]],
                       bufs_w=[hbuf["buf"]] if i16 == 0 else ())
                    hbuf["buf"].note_write(sv.h, sv.n)
                for ocp in range(8):
                    dma(em, "sync", slab_d[ocp % 2]["h"][:],
                        dnT_r[:, q * 16:(q + 1) * 16, ocp * 256:(ocp + 1) * 256],
                        bufs_r=[dnT_b], bufs_w=[slab_d[ocp % 2]["buf"]])
                    for oc_rel in range(2):
                        oc = 2 * ocp + oc_rel
                        db = 4 + (oc % 2)
                        for i16 in range(16):
                            mm(em, st_ if i16 == 15 else None, psh[db][:, :],
                               slab_d[ocp % 2]["h"][:, i16,
                                                    oc_rel * 128:(oc_rel + 1) * 128],
                               hbuf["h"][:, i16, :], start=(i16 == 0),
                               stop=(i16 == 15),
                               bufs_r=[slab_d[ocp % 2]["buf"], hbuf["buf"]],
                               bufs_w=[psb[db]] if i16 == 0 else ())
                        psb[db].note_write(st_.h, st_.n)
                        op(em, "vector", sv, "scalar_tensor_tensor",
                           X[:, oc, tt * TT:(tt + 1) * TT], psh[db][:, :], 0.5,
                           X[:, oc, tt * TT:(tt + 1) * TT], ALU.mult, ALU.add,
                           bufs_r=[psb[db], Xbuf[tt]], bufs_w=[Xbuf[tt]])

    full_barrier(em, sems)
    em.release(smark)



def _had(n):
    h = np.array([[1.0]], np.float32)
    while h.shape[0] < n:
        h = np.block([[h, h], [h, -h]])
    return h


def declare_ffn_l1(nc):
    P = {}
    P["w_up"] = nc.dram_tensor("w_up", [INTER, D], F32, kind="ExternalInput").ap()
    P["w_gt"] = nc.dram_tensor("w_gt", [INTER, D], F32, kind="ExternalInput").ap()
    P["w_dn1"] = nc.dram_tensor("w_dn1", [D, INTER], F32, kind="ExternalInput").ap()
    P["su_ug"] = nc.dram_tensor("su_ug", [16, 128], F32, kind="ExternalInput").ap()
    P["su_dp"] = nc.dram_tensor("su_dp", [64, 128], F32, kind="ExternalInput").ap()
    P["wsc_ug"] = nc.dram_tensor("wsc_ug", [128, 128], F32, kind="ExternalInput").ap()
    P["wsc_dp"] = nc.dram_tensor("wsc_dp", [16, 128], F32, kind="ExternalInput").ap()
    P["up_bf"] = nc.dram_tensor("up_bf", [INTER, D], BF16).ap()
    P["gt_bf"] = nc.dram_tensor("gt_bf", [INTER, D], BF16).ap()
    P["dn1_bf"] = nc.dram_tensor("dn1_bf", [D, INTER], BF16).ap()
    P["upT"] = nc.dram_tensor("upT", [D, INTER], BF16).ap()
    P["gtT"] = nc.dram_tensor("gtT", [D, INTER], BF16).ap()
    P["dn1T"] = nc.dram_tensor("dn1T", [INTER, D], BF16).ap()
    P["u_dram"] = nc.dram_tensor("u_dram", [D, T], BF16).ap()
    P["u2_dram"] = nc.dram_tensor("u2_dram", [INTER, T], BF16).ap()
    for k in ("up_bf_b", "gt_bf_b", "dn1_bf_b", "upT_b", "gtT_b", "dn1T_b",
              "u_dram_b", "u2_dram_b"):
        P[k] = Buf()
    H128 = _had(128)
    H16 = _had(16)
    H64 = _had(64)
    P["h128a_d"] = nc.inline_tensor((H128 / 32.0).astype(nbf), name="h128a_d").ap()
    P["h128b_d"] = nc.inline_tensor((H128 / 64.0).astype(nbf), name="h128b_d").ap()
    bd16 = np.zeros((128, 128), np.float32)
    for pp in range(8):
        bd16[16 * pp:16 * (pp + 1), 16 * pp:16 * (pp + 1)] = H16 / np.sqrt(2.0)
    P["bd16_d"] = nc.inline_tensor(bd16.astype(nbf), name="bd16_d").ap()
    bd64 = np.zeros((128, 128), np.float32)
    for pp in range(2):
        bd64[64 * pp:64 * (pp + 1), 64 * pp:64 * (pp + 1)] = H64 / np.sqrt(2.0)
    P["bd64_d"] = nc.inline_tensor(bd64.astype(nbf), name="bd64_d").ap()
    P["identf_d"] = nc.inline_tensor(np.eye(128, dtype=np.float32),
                                     name="identf_d").ap()
    return P


def emit_ffn_l1(nc, em, sems, sh, P, pre, do_prep):
    st_, sv, sc, sp = (sems[k] for k in ("st", "sv", "sc", "sp"))
    X, Xbuf, C = sh["X"], sh["Xbuf"], sh["C"]
    w_up, w_gt, w_dn = P["w_up"], P["w_gt"], P["w_dn1"]
    up_bf, gt_bf, dn_bf = P["up_bf"], P["gt_bf"], P["dn1_bf"]
    upT, gtT, dnT = P["upT"], P["gtT"], P["dn1T"]
    up_bf_b, gt_bf_b, dn_bf_b = P["up_bf_b"], P["gt_bf_b"], P["dn1_bf_b"]
    upT_b, gtT_b, dnT_b = P["upT_b"], P["gtT_b"], P["dn1T_b"]
    u_dram, u2_dram = P["u_dram"], P["u2_dram"]
    u_dram_b, u2_dram_b = P["u_dram_b"], P["u2_dram_b"]
    su_ug_d, su_dp_d = P["su_ug"], P["su_dp"]
    wsc_ug_d, wsc_dp_d = P["wsc_ug"], P["wsc_dp"]
    smark = em.mark()
    with contextlib.ExitStack() as stack:
        ent = stack.enter_context

        def sb(name, shape, dtype):
            return {"h": ent(nc.sbuf_tensor(pre + name, shape, dtype)),
                    "buf": Buf(True)}

        onc, onr = sh["onc"], sh["onr"]
        identf = sb("identf", [128, 128], F32)
        h128a = sb("h128a", [128, 128], BF16)
        h128b = sb("h128b", [128, 128], BF16)
        bd16_s = sb("bd16_s", [128, 128], BF16)
        bd64_s = sb("bd64_s", [128, 128], BF16)
        vrow = sb("vrow", [128, 128], F32)
        su_ug_c = sb("su_ug_c", [128, 16], F32)
        wsuS_c = sb("wsuS_c", [128, 64], F32)
        wsg_c = sb("wsg_c", [128, 64], F32)
        wdp_c = sb("wdp_c", [128, 16], F32)
        su_dp_c = sb("su_dp_c", [128, 64], F32)
        h128su = sb("h128su", [128, DC, 128], BF16)

        cr = []
        for tile, srcd in ((identf, P["identf_d"]), (h128a, P["h128a_d"]),
                           (h128b, P["h128b_d"]), (bd16_s, P["bd16_d"]),
                           (bd64_s, P["bd64_d"])):
            cr.append(dma(em, "sync", tile["h"][:], srcd, bufs_w=[tile["buf"]]))
        consts_ready = cr
        em.wait("tensor", cr)

        if do_prep:
            with contextlib.ExitStack() as prep:
                def psb_(name, shape, dtype):
                    return {"h": prep.enter_context(
                        nc.sbuf_tensor(pre + name, shape, dtype)),
                        "buf": Buf(True)}
                cbuf = [psb_(f"cbuf{i}", [128, 8192], BF16) for i in range(2)]
                tslab = [psb_(f"tsl{i}", [128, INTER], BF16) for i in range(2)]
                emit_weight_prep(em, w_up, INTER, D, up_bf, up_bf_b, upT, upT_b,
                                 cbuf, tslab)
                emit_weight_prep(em, w_gt, INTER, D, gt_bf, gt_bf_b, gtT, gtT_b,
                                 cbuf, tslab)
                emit_weight_prep(em, w_dn, D, INTER, dn_bf, dn_bf_b, dnT, dnT_b,
                                 cbuf, tslab)
            full_barrier(em, sems)

        with contextlib.ExitStack() as vsc:
            vps = vsc.enter_context(nc.psum_tensor(pre + "vps", [128, 512], F32))
            vpsb = Buf()
            for rows_n, src, dstc in ((16, su_ug_d, su_ug_c), (64, su_dp_d, su_dp_c),
                                      (16, wsc_dp_d, wdp_c)):
                dma(em, "sync", vrow["h"][:rows_n, :], src,
                    bufs_w=[vrow["buf"]])
                op(em, "tensor", st_, "transpose", vps[:, :rows_n],
                   vrow["h"][:rows_n, :], identf["h"][:rows_n, :rows_n],
                   bufs_r=[vrow["buf"], identf["buf"]], bufs_w=[vpsb])
                op(em, "vector", sv, "tensor_copy", dstc["h"][:], vps[:, :rows_n],
                   bufs_r=[vpsb], bufs_w=[dstc["buf"]])
            dma(em, "sync", vrow["h"][:, :], wsc_ug_d, bufs_w=[vrow["buf"]])
            op(em, "tensor", st_, "transpose", vps[:, :128], vrow["h"][:, :],
               identf["h"][:], bufs_r=[vrow["buf"], identf["buf"]], bufs_w=[vpsb])
            op(em, "vector", sv, "tensor_copy", wsuS_c["h"][:], vps[:, :64],
               bufs_r=[vpsb], bufs_w=[wsuS_c["buf"]])
            op(em, "vector", sv, "tensor_copy", wsg_c["h"][:], vps[:, 64:128],
               bufs_r=[vpsb], bufs_w=[wsg_c["buf"]])
            op(em, "vector", sv, "tensor_mul", wsuS_c["h"][:], wsuS_c["h"][:],
               su_dp_c["h"][:], bufs_r=[wsuS_c["buf"], su_dp_c["buf"]],
               bufs_w=[wsuS_c["buf"]])
            op(em, "vector", sv, "tensor_scalar_mul", wdp_c["h"][:],
               wdp_c["h"][:], 0.5, bufs_r=[wdp_c["buf"]], bufs_w=[wdp_c["buf"]])
            for c in range(DC):
                op(em, "vector", sv, "tensor_scalar_mul",
                   h128su["h"][:, c, :], h128a["h"][:], su_ug_c["h"][:, c:c + 1],
                   bufs_r=[h128a["buf"], su_ug_c["buf"]], bufs_w=[h128su["buf"]])
        full_barrier(em, sems)

        Yh = ent(nc.sbuf_tensor(pre + "Y", [128, DC, TT], BF16))
        Y = Yh[:]
        Ybuf = Buf()
        sq = [sb(f"sq{i}", [128, TT], F32) for i in range(2)]
        tmp = [sb(f"tmp{i}", [128, TT], F32) for i in range(2)]
        rows = sb("rows", [1, 3, TT], F32)
        Rb = sb("Rb", [128, TT], F32)
        MRb = sb("MRb", [128, TT], F32)
        u_sb = sb("u_sb", [128, DC, TT], BF16)
        z1 = {"h": Yh, "buf": None}
        slab_u = [sb(f"slab_u{i}", [128, DC, 256], BF16) for i in range(2)]
        slab_g = [sb(f"slab_g{i}", [128, DC, 256], BF16) for i in range(2)]
        slab_d = [sb(f"sld{i}", [128, 16, 256], BF16) for i in range(2)]
        h_t = [sb(f"h_t{i}", [128, TT], BF16) for i in range(2)]
        s_scr = [sb(f"ss{i}", [128, TT], F32) for i in range(2)]
        u2g = sb("u2g", [128, 8, TT], BF16)
        zin2 = u_sb
        z2 = sb("z2", [128, 16, TT], BF16)
        psh = [ent(nc.psum_tensor(pre + f"ps{i}", [128, 512], F32))
               for i in range(8)]
        psb = [Buf() for _ in range(8)]
        z1["buf"] = Ybuf

        rh = rows["h"]
        rowaps = [rh[:1, 0, :], rh[:1, 1, :], rh[:1, 2, :]]
        scr = {"ps": psh, "psb": psb, "sq": sq, "sqb": [s["buf"] for s in sq],
               "rowaps": rowaps, "rowsb": rows["buf"],
               "R": Rb, "MR": MRb, "tmp": tmp, "tmpb": [t["buf"] for t in tmp]}

        upT_r = upT.rearrange("(c2 pj pp) i -> pp c2 pj i", c2=16, pj=16, pp=8)
        gtT_r = gtT.rearrange("(c2 pj pp) i -> pp c2 pj i", c2=16, pj=16, pp=8)
        dnT_r = dnT.rearrange("(a2 pj pp) h -> pp a2 pj h", a2=64, pj=64, pp=2)
        ud_r = u_dram.rearrange("(c pj pp) t -> pp c pj t", c=16, pj=16, pp=8)
        u2d_r = u2_dram.rearrange("(a pj pp) t -> pp a pj t", a=64, pj=64, pp=2)

        for tt in range(NTT):
            ts = slice(tt * TT, (tt + 1) * TT)
            emit_ln(em, sems, X, Xbuf, Y, Ybuf, tt, {"ones_col": onc["h"][:],
                    "ones_row": onr["h"][:]}, scr)
            for c in range(DC):
                b = c % 2
                mm(em, st_, psh[b][:, :], h128su["h"][:, c, :], Y[:, c, :],
                   start=True, stop=True,
                   bufs_r=[h128su["buf"], Ybuf], bufs_w=[psb[b]])
                op(em, "vector", sv, "tensor_copy", u_sb["h"][:, c, :],
                   psh[b][:, :], bufs_r=[psb[b]], bufs_w=[u_sb["buf"]])
            dma(em, "sync", u_dram.rearrange("(c p) t -> p c t", p=128)[:, :, ts],
                u_sb["h"][:], bufs_r=[u_sb["buf"]], bufs_w=[u_dram_b])
            dma(em, "sync", u_sb["h"][:], ud_r[:, :, :, ts],
                bufs_r=[u_dram_b], bufs_w=[u_sb["buf"]])
            for pj in range(DC):
                b = pj % 2
                mm(em, st_, psh[b][:, :], bd16_s["h"][:], u_sb["h"][:, pj, :],
                   start=True, stop=True,
                   bufs_r=[bd16_s["buf"], u_sb["buf"]], bufs_w=[psb[b]])
                op(em, "vector", sv, "tensor_copy", z1["h"][:, pj, :],
                   psh[b][:, :], bufs_r=[psb[b]], bufs_w=[z1["buf"]])
            for a in range(64):
                jb = a % 2
                if a % 2 == 0:
                    sbi = (a // 2) % 2
                    dma(em, "sync", slab_u[sbi]["h"][:],
                        upT_r[:, :, :, 128 * a:128 * (a + 2)],
                        bufs_r=[upT_b], bufs_w=[slab_u[sbi]["buf"]])
                    dma(em, "sync", slab_g[sbi]["h"][:],
                        gtT_r[:, :, :, 128 * a:128 * (a + 2)],
                        bufs_r=[gtT_b], bufs_w=[slab_g[sbi]["buf"]])
                sbi = (a // 2) % 2
                rel = a % 2
                for pj in range(DC):
                    mm(em, st_ if pj == DC - 1 else None, psh[jb][:, :],
                       slab_g[sbi]["h"][:, pj, rel * 128:(rel + 1) * 128],
                       z1["h"][:, pj, :], start=(pj == 0), stop=(pj == DC - 1),
                       bufs_r=[slab_g[sbi]["buf"], z1["buf"]],
                       bufs_w=[psb[jb]] if pj == 0 else ())
                psb[jb].note_write(st_.h, st_.n)
                for pj in range(DC):
                    mm(em, st_ if pj == DC - 1 else None, psh[2 + jb][:, :],
                       slab_u[sbi]["h"][:, pj, rel * 128:(rel + 1) * 128],
                       z1["h"][:, pj, :], start=(pj == 0), stop=(pj == DC - 1),
                       bufs_r=[slab_u[sbi]["buf"], z1["buf"]],
                       bufs_w=[psb[2 + jb]] if pj == 0 else ())
                psb[2 + jb].note_write(st_.h, st_.n)
                op(em, "scalar", sc, "activation", s_scr[a % 2]["h"][:],
                   psh[jb][:, :], AF.Silu, scale=wsg_c["h"][:, a:a + 1],
                   bufs_r=[psb[jb], wsg_c["buf"]], bufs_w=[s_scr[a % 2]["buf"]])
                op(em, "vector", sv, "scalar_tensor_tensor", h_t[a % 2]["h"][:],
                   psh[2 + jb][:, :], wsuS_c["h"][:, a:a + 1], s_scr[a % 2]["h"][:],
                   ALU.mult, ALU.mult,
                   bufs_r=[psb[2 + jb], wsuS_c["buf"], s_scr[a % 2]["buf"]],
                   bufs_w=[h_t[a % 2]["buf"]])
                mm(em, st_, psh[4 + jb][:, :], h128b["h"][:], h_t[a % 2]["h"][:],
                   start=True, stop=True,
                   bufs_r=[h128b["buf"], h_t[a % 2]["buf"]], bufs_w=[psb[4 + jb]])
                op(em, "vector", sv, "tensor_copy", u2g["h"][:, a % 8, :],
                   psh[4 + jb][:, :], bufs_r=[psb[4 + jb]], bufs_w=[u2g["buf"]])
                if a % 8 == 7:
                    dma(em, "sync",
                        u2_dram.rearrange("(g p) t -> p g t", p=128)[
                            :, a - 7:a + 1, ts],
                        u2g["h"][:], bufs_r=[u2g["buf"]], bufs_w=[u2_dram_b])
            for grp in range(4):
                for pp_ in range(2):
                    dma(em, "sync", zin2["h"][64 * pp_:64 * (pp_ + 1), :, :],
                        u2d_r[pp_, :, 16 * grp:16 * (grp + 1), ts],
                        bufs_r=[u2_dram_b], bufs_w=[zin2["buf"]])
                for pjr in range(16):
                    b = pjr % 2
                    mm(em, st_, psh[b][:, :], bd64_s["h"][:],
                       zin2["h"][:, pjr, :], start=True, stop=True,
                       bufs_r=[bd64_s["buf"], zin2["buf"]], bufs_w=[psb[b]])
                    op(em, "vector", sv, "tensor_copy", z2["h"][:, pjr, :],
                       psh[b][:, :], bufs_r=[psb[b]], bufs_w=[z2["buf"]])
                for ocp in range(8):
                    sdb = ocp % 2
                    for pp_ in range(2):
                        dma(em, "sync",
                            slab_d[sdb]["h"][64 * pp_:64 * (pp_ + 1), :, :],
                            dnT_r[pp_, :, 16 * grp:16 * (grp + 1),
                                  ocp * 256:(ocp + 1) * 256],
                            bufs_r=[dnT_b], bufs_w=[slab_d[sdb]["buf"]])
                    for oc_rel in range(2):
                        oc = 2 * ocp + oc_rel
                        db = 4 + (oc % 2)
                        for pjr in range(16):
                            mm(em, st_ if pjr == 15 else None, psh[db][:, :],
                               slab_d[sdb]["h"][:, pjr,
                                                oc_rel * 128:(oc_rel + 1) * 128],
                               z2["h"][:, pjr, :], start=(pjr == 0),
                               stop=(pjr == 15),
                               bufs_r=[slab_d[sdb]["buf"], z2["buf"]],
                               bufs_w=[psb[db]] if pjr == 0 else ())
                        psb[db].note_write(st_.h, st_.n)
                        op(em, "vector", sv, "scalar_tensor_tensor",
                           X[:, oc, ts], psh[db][:, :], wdp_c["h"][:, oc:oc + 1],
                           X[:, oc, ts], ALU.mult, ALU.add,
                           bufs_r=[psb[db], wdp_c["buf"], Xbuf[tt]],
                           bufs_w=[Xbuf[tt]])

    full_barrier(em, sems)
    em.release(smark)



def declare_attn_l0(nc):
    H = 16
    P = {}
    P["w_qkv"] = nc.dram_tensor("w_qkv", [D, 3 * D], F32, kind="ExternalInput").ap()
    P["w_out"] = nc.dram_tensor("w_out", [D, D], F32, kind="ExternalInput").ap()
    P["decay"] = nc.dram_tensor("decay_i", [1, H], F32, kind="ExternalInput").ap()
    P["csel"] = nc.dram_tensor("csel_i", [128, 1], F32, kind="ExternalInput").ap()
    P["att_dram"] = nc.dram_tensor("att_dram", [D, T], BF16).ap()
    P["contrib"] = nc.dram_tensor("contrib0", [128, 4], F32)
    P["gathered"] = nc.dram_tensor("gathered0", [256, 4], F32)
    jj, tt_ = np.meshgrid(np.arange(128), np.arange(128), indexing="ij")
    E_np = np.maximum(tt_ - jj, 0).astype(np.float32)
    LT_np = (jj <= tt_).astype(np.float32)
    P["E_d"] = nc.inline_tensor(E_np, name="E_d").ap()
    P["LT_d"] = nc.inline_tensor(LT_np, name="LT_d").ap()
    P["iota_d"] = nc.inline_tensor(
        np.arange(1, T + 1, dtype=np.float32).reshape(1, T), name="iota_d").ap()
    P["zcol_d"] = nc.inline_tensor(np.zeros((128, 1), np.float32),
                                   name="zcol_d").ap()
    return P


def emit_attn_l0(nc, em, sems, sh, P, pre):
    H = 16
    st_, sv, sc, sp, scc = (sems[k] for k in ("st", "sv", "sc", "sp", "cc"))
    X, Xbuf, C = sh["X"], sh["Xbuf"], sh["C"]
    w_qkv, w_out = P["w_qkv"], P["w_out"]
    decay_d, csel_d = P["decay"], P["csel"]
    att_dram, contrib, gathered = P["att_dram"], P["contrib"], P["gathered"]
    E_d, LT_d, iota_d, zcol_d = P["E_d"], P["LT_d"], P["iota_d"], P["zcol_d"]
    ident_d = sh["ident_d"]
    contrib_buf = Buf()
    gathered_buf = Buf()
    smark = em.mark()
    with contextlib.ExitStack() as stack:
        ent = stack.enter_context

        def sb(name, shape, dtype):
            return {"h": ent(nc.sbuf_tensor(pre + name, shape, dtype)),
                    "buf": Buf(True)}

        att_dram_b = Buf()
        onc, onr = sh["onc"], sh["onr"]
        E_sb = sb("E_sb", [128, 128], F32)
        LT_sb = sb("LT_sb", [128, 128], F32)
        iota_sb = sb("iota", [1, T], F32)
        ident = sb("ident", [128, 128], BF16)
        zcol = sb("zcol", [128, 1], F32)
        csel = sb("csel", [128, 1], F32)
        dec = sb("dec", [1, 2, H], F32)
        Yh = ent(nc.sbuf_tensor(pre + "Y", [128, DC, T], BF16))
        Y = Yh[:]
        Ybuf = Buf()
        sq = [sb(f"sq{i}", [128, TT], F32) for i in range(2)]
        tmp = [sb(f"tmp{i}", [128, TT], F32) for i in range(2)]
        rows = sb("rows", [1, 3, TT], F32)
        Rb = sb("Rb", [128, TT], F32)
        MRb = sb("MRb", [128, TT], F32)

        cr = []
        for tile, srcd in ((E_sb, E_d),
                           (LT_sb, LT_d), (iota_sb, iota_d), (ident, ident_d),
                           (zcol, zcol_d), (csel, csel_d)):
            cr.append(dma(em, "sync", tile["h"][:], srcd, bufs_w=[tile["buf"]]))
        cr.append(dma(em, "sync", dec["h"][:1, 0, :], decay_d,
                      bufs_w=[dec["buf"]]))
        consts_ready = cr
        em.wait("tensor", cr)
        em.wait("scalar", cr)
        em.wait("vector", cr)
        C = {"ones_col": onc["h"][:], "ones_row": onr["h"][:]}

        op(em, "scalar", sc, "activation", dec["h"][:1, 0, :], dec["h"][:1, 0, :],
           AF.Sigmoid, bufs_r=[dec["buf"]], bufs_w=[dec["buf"]])
        op(em, "scalar", sc, "activation", dec["h"][:1, 1, :], dec["h"][:1, 0, :],
           AF.Ln, bufs_r=[dec["buf"]], bufs_w=[dec["buf"]])
        lnd = dec["h"]

        with contextlib.ExitStack() as lnsc:
            lps = [lnsc.enter_context(
                nc.psum_tensor(pre + f"lps{i}", [128, 512], F32))
                   for i in range(4)]
            lpsb = [Buf() for _ in range(4)]
            rh = rows["h"]
            rowaps = [rh[:1, 0, :], rh[:1, 1, :], rh[:1, 2, :]]
            scr = {"ps": lps, "psb": lpsb, "sq": sq, "sqb": [s["buf"] for s in sq],
                   "rowaps": rowaps, "rowsb": rows["buf"],
                   "R": Rb, "MR": MRb, "tmp": tmp, "tmpb": [t["buf"] for t in tmp]}
            for tt in range(NTT):
                emit_ln(em, sems, X, Xbuf, Y[:, :, tt * TT:(tt + 1) * TT], Ybuf,
                        tt, C, scr)
        full_barrier(em, sems)

        wq_r = w_qkv.rearrange("(kc p) o -> p kc o", p=128)

        with contextlib.ExitStack() as body:
            def bsb(name, shape, dtype):
                return {"h": body.enter_context(
                    nc.sbuf_tensor(pre + name, shape, dtype)),
                        "buf": Buf(True)}
            q_sb = bsb("q_sb", [128, 2, T], BF16)
            k_sb = bsb("k_sb", [128, 2, T], BF16)
            kv_sb = bsb("kv_sb", [128, 2, T], BF16)
            v_sb = bsb("v_sb", [128, 2, T], BF16)
            S_scr = bsb("S_scr", [128, 4, T], F32)
            att_q = bsb("att_q", [128, 2, T], BF16)
            slab = bsb("slab", [128, DC, 256], BF16)
            phi_mn = sq
            phi_ex = tmp
            phi_rl = [bsb(f"phrl{i}", [128, TT], F32) for i in range(2)]
            xT = [bsb(f"xT{i}", [128, 128], BF16) for i in range(2)]
            A_bf = bsb("A_bf", [128, 2, 128], BF16)
            aexp = bsb("aexp", [128, 128], F32)
            powf = bsb("powf", [128, 2, T], F32)
            prow = bsb("prow", [1, T], F32)
            u_scr = bsb("u_scr", [128, T], F32)
            t2_scr = bsb("t2_scr", [128, T], F32)
            carry = bsb("carry", [128, 4], F32)
            block0 = bsb("block0", [128, 4], F32)
            Ksel = bsb("Ksel", [128, 4], F32)
            den_sb = bsb("den_sb", [1, T], F32)
            rden_sb = bsb("rden_sb", [1, T], F32)

            pj = [body.enter_context(
                nc.psum_tensor(pre + f"pj{i}", [128, 512], F32))
                  for i in range(2)]
            pjb = [Buf(), Buf()]
            tbf = [body.enter_context(
                nc.psum_tensor(pre + f"tb{i}", [128, 1024], BF16))
                   for i in range(2)]
            tbfb = [Buf(), Buf()]
            msc = [body.enter_context(
                nc.psum_tensor(pre + f"ms{i}", [128, 512], F32))
                   for i in range(2)]
            mscb = [Buf(), Buf()]

            lnd_b = bsb("lnd_b", [128, H], F32)
            mm(em, st_, msc[0][:, :H], C["ones_row"], lnd[:1, 1, :],
               start=True, stop=True, bufs_r=[dec["buf"], onr["buf"]],
               bufs_w=[mscb[0]])
            op(em, "vector", sv, "tensor_copy", lnd_b["h"][:], msc[0][:, :H],
               bufs_r=[mscb[0]], bufs_w=[lnd_b["buf"]])

            for hq in range(8):
                heads = (2 * hq, 2 * hq + 1)
                for kind, base in (("v", 2 * D), ("k", D), ("q", 0)):
                    span = base + 256 * hq
                    dma(em, "gpsimd", slab["h"][:],
                        wq_r[:, :, span:span + 256],
                        bufs_w=[slab["buf"]])
                    for tt in range(NTT):
                        ts = slice(tt * TT, (tt + 1) * TT)
                        for hr in range(2):
                            pb = (tt * 2 + hr) % 2
                            for kc in range(DC):
                                mm(em, st_ if kc == DC - 1 else None, pj[pb][:, :],
                                   slab["h"][:, kc, hr * 128:(hr + 1) * 128],
                                   Y[:, kc, ts], start=(kc == 0), stop=(kc == DC - 1),
                                   bufs_r=[slab["buf"], Ybuf],
                                   bufs_w=[pjb[pb]] if kc == 0 else ())
                            pjb[pb].note_write(st_.h, st_.n)
                            if kind == "v":
                                op(em, "vector", sv, "tensor_copy",
                                   v_sb["h"][:, hr, ts], pj[pb][:, :],
                                   bufs_r=[pjb[pb]], bufs_w=[v_sb["buf"]])
                            else:
                                dst = q_sb if kind == "q" else k_sb
                                op(em, "vector", sv, "tensor_scalar_min",
                                   phi_mn[hr]["h"][:], pj[pb][:, :], 0.0,
                                   bufs_r=[pjb[pb]], bufs_w=[phi_mn[hr]["buf"]])
                                op(em, "scalar", sc, "activation",
                                   phi_ex[hr]["h"][:], phi_mn[hr]["h"][:], AF.Exp,
                                   bufs_r=[phi_mn[hr]["buf"]],
                                   bufs_w=[phi_ex[hr]["buf"]])
                                op(em, "scalar", sc, "activation",
                                   phi_rl[hr]["h"][:], pj[pb][:, :], AF.Relu,
                                   bufs_r=[pjb[pb]], bufs_w=[phi_rl[hr]["buf"]])
                                op(em, "vector", sv, "tensor_add",
                                   dst["h"][:, hr, ts], phi_rl[hr]["h"][:],
                                   phi_ex[hr]["h"][:],
                                   bufs_r=[phi_rl[hr]["buf"], phi_ex[hr]["buf"]],
                                   bufs_w=[dst["buf"]])
                                if kind == "k":
                                    op(em, "gpsimd", sp, "tensor_mul",
                                       kv_sb["h"][:, hr, ts], k_sb["h"][:, hr, ts],
                                       v_sb["h"][:, hr, ts],
                                       bufs_r=[k_sb["buf"], v_sb["buf"]],
                                       bufs_w=[kv_sb["buf"]])

                for hr, h in enumerate(heads):
                    lnd_ap = lnd_b["h"][:, h:h + 1]
                    op(em, "scalar", sc, "activation", aexp["h"][:], E_sb["h"][:],
                       AF.Exp, scale=lnd_ap,
                       bufs_r=[E_sb["buf"], lnd_b["buf"]], bufs_w=[aexp["buf"]])
                    op(em, "vector", sv, "tensor_mul", A_bf["h"][:, hr, :],
                       aexp["h"][:], LT_sb["h"][:],
                       bufs_r=[aexp["buf"], LT_sb["buf"]], bufs_w=[A_bf["buf"]])
                    op(em, "scalar", sc, "activation", prow["h"][:], iota_sb["h"][:],
                       AF.Exp, scale=lnd[:1, 1, h:h + 1],
                       bufs_r=[iota_sb["buf"], dec["buf"]], bufs_w=[prow["buf"]])
                    for half in range(2):
                        mm(em, st_, msc[half][:, :], C["ones_row"],
                           prow["h"][:1, half * 512:(half + 1) * 512],
                           start=True, stop=True,
                           bufs_r=[prow["buf"], onr["buf"]], bufs_w=[mscb[half]])
                        op(em, "vector", sv, "tensor_copy",
                           powf["h"][:, hr, half * 512:(half + 1) * 512],
                           msc[half][:, :], bufs_r=[mscb[half]],
                           bufs_w=[powf["buf"]])

                    for kind_i, src in ((0, kv_sb), (1, k_sb)):
                        sidx = 2 * hr + kind_i
                        for tc in range(8):
                            tcs = slice(tc * 128, (tc + 1) * 128)
                            op(em, "tensor", st_, "transpose",
                               tbf[tc % 2][:, :128], src["h"][:, hr, tcs],
                               ident["h"][:],
                               bufs_r=[src["buf"], ident["buf"]],
                               bufs_w=[tbfb[tc % 2]])
                            op(em, "vector", sv, "tensor_copy", xT[tc % 2]["h"][:],
                               tbf[tc % 2][:, :128],
                               bufs_r=[tbfb[tc % 2]], bufs_w=[xT[tc % 2]["buf"]])
                            mm(em, st_, msc[tc % 2][:, :128], xT[tc % 2]["h"][:],
                               A_bf["h"][:, hr, :], start=True, stop=True,
                               bufs_r=[xT[tc % 2]["buf"], A_bf["buf"]],
                               bufs_w=[mscb[tc % 2]])
                            cprev = (S_scr["h"][:, sidx, tc * 128 - 1:tc * 128]
                                     if tc > 0 else zcol["h"][:])
                            op(em, "vector", sv, "scalar_tensor_tensor",
                               S_scr["h"][:, sidx, tcs],
                               powf["h"][:, hr, 0:128], cprev,
                               msc[tc % 2][:, :128], ALU.mult, ALU.add,
                               bufs_r=[mscb[tc % 2], powf["buf"], S_scr["buf"],
                                       zcol["buf"]],
                               bufs_w=[S_scr["buf"]])

                for hr in range(2):
                    for kind_i in range(2):
                        sidx = 2 * hr + kind_i
                        op(em, "vector", sv, "tensor_copy",
                           carry["h"][:, sidx:sidx + 1],
                           S_scr["h"][:, sidx, T - 1:T],
                           bufs_r=[S_scr["buf"]], bufs_w=[carry["buf"]])
                dma(em, "sync", contrib[:, :], carry["h"][:],
                    bufs_r=[carry["buf"]], bufs_w=[contrib_buf])
                em.emit("gpsimd", "collective_compute",
                        ("AllGather", ALU.bypass,
                         [[0, 1], [2, 3], [4, 5], [6, 7]],
                         [contrib.ap().opt()], [gathered.ap().opt()]),
                        {}, scc, 1, (), bufs_r=[contrib_buf],
                        bufs_w=[gathered_buf])
                dma(em, "sync", block0["h"][:], gathered[0:128, :],
                    bufs_r=[gathered_buf], bufs_w=[block0["buf"]])
                op(em, "vector", sv, "tensor_scalar_mul", Ksel["h"][:],
                   block0["h"][:], csel["h"][:],
                   bufs_r=[block0["buf"], csel["buf"]], bufs_w=[Ksel["buf"]])

                for hr, h in enumerate(heads):
                    for kind_i in range(2):
                        sidx = 2 * hr + kind_i
                        op(em, "vector", sv, "scalar_tensor_tensor",
                           S_scr["h"][:, sidx, :], powf["h"][:, hr, :],
                           Ksel["h"][:, sidx:sidx + 1], S_scr["h"][:, sidx, :],
                           ALU.mult, ALU.add,
                           bufs_r=[powf["buf"], Ksel["buf"], S_scr["buf"]],
                           bufs_w=[S_scr["buf"]])
                    op(em, "gpsimd", sp, "tensor_mul", u_scr["h"][:],
                       q_sb["h"][:, hr, :], S_scr["h"][:, 2 * hr + 1, :],
                       bufs_r=[q_sb["buf"], S_scr["buf"]], bufs_w=[u_scr["buf"]])
                    op(em, "vector", sv, "tensor_mul", t2_scr["h"][:],
                       q_sb["h"][:, hr, :], S_scr["h"][:, 2 * hr, :],
                       bufs_r=[q_sb["buf"], S_scr["buf"]], bufs_w=[t2_scr["buf"]])
                    for half in range(2):
                        hs = slice(half * 512, (half + 1) * 512)
                        mm(em, st_, msc[0][:1, :], C["ones_col"],
                           u_scr["h"][:, hs], start=True, stop=True,
                           bufs_r=[u_scr["buf"], onc["buf"]], bufs_w=[mscb[0]])
                        op(em, "vector", sv, "tensor_scalar_max",
                           den_sb["h"][:, hs], msc[0][:1, :], 1e-6,
                           bufs_r=[mscb[0]], bufs_w=[den_sb["buf"]])
                        op(em, "vector", sv, "reciprocal", rden_sb["h"][:, hs],
                           den_sb["h"][:, hs],
                           bufs_r=[den_sb["buf"]], bufs_w=[rden_sb["buf"]])
                        mm(em, st_, msc[1][:, :], C["ones_row"],
                           rden_sb["h"][:1, hs], start=True, stop=True,
                           bufs_r=[rden_sb["buf"], onr["buf"]], bufs_w=[mscb[1]])
                        op(em, "vector", sv, "tensor_mul",
                           att_q["h"][:, hr, hs], t2_scr["h"][:, hs],
                           msc[1][:, :],
                           bufs_r=[t2_scr["buf"], mscb[1]], bufs_w=[att_q["buf"]])
                dma(em, "sync",
                    att_dram[256 * hq:256 * (hq + 1), :].rearrange(
                        "(r p) t -> p r t", p=128),
                    att_q["h"][:], bufs_r=[att_q["buf"]], bufs_w=[att_dram_b])

        full_barrier(em, sems)
        with contextlib.ExitStack() as osc:
            def osb(name, shape, dtype):
                return {"h": osc.enter_context(
                    nc.sbuf_tensor(pre + name, shape, dtype)),
                        "buf": Buf(True)}
            wslab = [osb(f"wo{i}", [128, DC, 512], BF16) for i in range(2)]
            arhs = [osb(f"ar{i}", [128, DC, TT], BF16) for i in range(2)]
            po = [osc.enter_context(
                nc.psum_tensor(pre + f"po{i}", [128, 512], F32))
                  for i in range(2)]
            pob = [Buf(), Buf()]
            wo_r = w_out.rearrange("(kc p) o -> p kc o", p=128)
            att_r = att_dram.rearrange("(kc p) t -> p kc t", p=128)
            for tt in range(NTT):
                ts = slice(tt * TT, (tt + 1) * TT)
                dma(em, "sync", arhs[tt % 2]["h"][:], att_r[:, :, ts],
                    bufs_r=[att_dram_b], bufs_w=[arhs[tt % 2]["buf"]])
                for sp_i in range(4):
                    dma(em, "gpsimd", wslab[sp_i % 2]["h"][:],
                        wo_r[:, :, 512 * sp_i:512 * (sp_i + 1)],
                        bufs_w=[wslab[sp_i % 2]["buf"]])
                    for oc_rel in range(4):
                        oc = 4 * sp_i + oc_rel
                        pb = oc % 2
                        for kc in range(DC):
                            mm(em, st_ if kc == DC - 1 else None, po[pb][:, :],
                               wslab[sp_i % 2]["h"][:, kc,
                                                    oc_rel * 128:(oc_rel + 1) * 128],
                               arhs[tt % 2]["h"][:, kc, :],
                               start=(kc == 0), stop=(kc == DC - 1),
                               bufs_r=[wslab[sp_i % 2]["buf"], arhs[tt % 2]["buf"]],
                               bufs_w=[pob[pb]] if kc == 0 else ())
                        pob[pb].note_write(st_.h, st_.n)
                        op(em, "vector", sv, "tensor_add",
                           X[:, oc, ts], po[pb][:, :], X[:, oc, ts],
                           bufs_r=[pob[pb], Xbuf[tt]], bufs_w=[Xbuf[tt]])

    full_barrier(em, sems)
    em.release(smark)



def declare_attn_l1(nc):
    H = 16
    BIG = 1e9
    P = {}
    P["w_dkv"] = nc.dram_tensor("w_dkv", [D, 128], F32, kind="ExternalInput").ap()
    P["w_dq"] = nc.dram_tensor("w_dq", [D, 128], F32, kind="ExternalInput").ap()
    P["w_kr"] = nc.dram_tensor("w_kr", [D, 64], F32, kind="ExternalInput").ap()
    P["w_qc"] = nc.dram_tensor("w_qc", [H, 128, 64], F32, kind="ExternalInput").ap()
    P["w_qr"] = nc.dram_tensor("w_qr", [H, 128, 64], F32, kind="ExternalInput").ap()
    P["w_uk"] = nc.dram_tensor("w_uk", [H, 128, 64], F32, kind="ExternalInput").ap()
    P["w_uv"] = nc.dram_tensor("w_uv", [H, 128, 128], F32, kind="ExternalInput").ap()
    P["w_o"] = nc.dram_tensor("w_o", [D, D], F32, kind="ExternalInput").ap()
    P["cosf"] = nc.dram_tensor("cosf_i", [64, T], F32, kind="ExternalInput").ap()
    P["sinf"] = nc.dram_tensor("sinf_i", [64, T], F32, kind="ExternalInput").ap()
    P["pc"] = nc.dram_tensor("pc_i", [128, 1], F32, kind="ExternalInput").ap()
    P["msel"] = nc.dram_tensor("msel_i", [128, 2], F32, kind="ExternalInput").ap()
    P["contrib"] = nc.dram_tensor("contrib1", [192, T], BF16)
    P["gathered"] = nc.dram_tensor("gathered1", [384, T], BF16)
    psw = np.zeros((64, 64), np.float32)
    for m in range(64):
        psw[m ^ 1, m] = 1.0
    P["pswap_d"] = nc.inline_tensor(psw.astype(nbf), name="pswap_d").ap()
    qq = np.arange(128)[:, None]
    kk = np.arange(512)[None, :]
    md = np.stack([np.where(kk <= 128 * qm + qq, 0.0, -BIG)
                   for qm in range(4)], axis=1).astype(np.float32)
    P["mdiag_d"] = nc.inline_tensor(md, name="mdiag_d").ap()
    return P


def emit_attn_l1(nc, em, sems, sh, P, pre):
    H = 16
    SC = 0.08838834764831845
    st_, sv, sc, sp, scc = (sems[k] for k in ("st", "sv", "sc", "sp", "cc"))
    X, Xbuf, C = sh["X"], sh["Xbuf"], sh["C"]
    w_dkv, w_dq, w_kr = P["w_dkv"], P["w_dq"], P["w_kr"]
    w_qc, w_qr, w_uk, w_uv, w_o = (P["w_qc"], P["w_qr"], P["w_uk"],
                                   P["w_uv"], P["w_o"])
    cosf_d, sinf_d, pc_d, msel_d = P["cosf"], P["sinf"], P["pc"], P["msel"]
    contrib, gathered = P["contrib"], P["gathered"]
    pswap_d, mdiag_d = P["pswap_d"], P["mdiag_d"]
    ident_d = sh["ident_d"]
    ones_row_d = sh["onr_d"]
    contrib_b, gathered_b = Buf(), Buf()
    smark = em.mark()
    with contextlib.ExitStack() as stack:
        ent = stack.enter_context

        def sb(name, shape, dtype):
            return {"h": ent(nc.sbuf_tensor(pre + name, shape, dtype)),
                    "buf": Buf(True)}

        onc = sh["onc"]
        ident = sb("ident", [128, 128], BF16)
        pswap = sb("pswap", [64, 64], BF16)
        cosf = sb("cosf", [64, T], F32)
        sinf = sb("sinf", [64, T], F32)
        pc = sb("pc", [128, 1], F32)
        msel = sb("msel", [128, 2], F32)
        wdkv_s = sb("wdkv_s", [128, DC, 128], BF16)
        wdq_s = sb("wdq_s", [128, DC, 128], BF16)
        wkr_s = sb("wkr_s", [128, DC, 64], BF16)
        wqc_s = sb("wqc_s", [128, H, 64], BF16)
        wqr_s = sb("wqr_s", [128, H, 64], BF16)
        qnr = sb("qnr", [128, H, T], BF16)
        ckv_of = sb("ckv_of", [128, 2 * T], BF16)
        krf_of = sb("krf_of", [64, 2 * T], BF16)
        att_sb = sb("att_sb", [128, DC, T], BF16)

        cr = []
        for tile, srcd in ((ident, ident_d),
                           (pswap, pswap_d),
                           (cosf, cosf_d), (sinf, sinf_d), (pc, pc_d),
                           (msel, msel_d)):
            cr.append(dma(em, "sync", tile["h"][:], srcd, bufs_w=[tile["buf"]]))
        for tile, srcd in (
                (wdkv_s, w_dkv.rearrange("(kc p) o -> p kc o", p=128)),
                (wdq_s, w_dq.rearrange("(kc p) o -> p kc o", p=128)),
                (wkr_s, w_kr.rearrange("(kc p) o -> p kc o", p=128)),
                (wqc_s, w_qc.rearrange("h p c -> p h c")),
                (wqr_s, w_qr.rearrange("h p c -> p h c"))):
            cr.append(dma(em, "gpsimd", tile["h"][:], srcd,
                          bufs_w=[tile["buf"]]))
        consts_ready = cr
        C = {"ones_col": onc["h"][:], "ones_row": None}

        em.wait("tensor", consts_ready)
        em.wait("scalar", consts_ready)
        em.wait("vector", consts_ready)

        with contextlib.ExitStack() as s1:
            def s1b(name, shape, dtype):
                return {"h": s1.enter_context(
                    nc.sbuf_tensor(pre + name, shape, dtype)),
                        "buf": Buf(True)}
            Yh = s1.enter_context(nc.sbuf_tensor(pre + "Y", [128, DC, TT], BF16))
            Y = Yh[:]
            Ybuf = Buf()
            sq = [s1b(f"sq{i}", [128, TT], F32) for i in range(2)]
            tmp = [s1b(f"tmp{i}", [128, TT], F32) for i in range(2)]
            rows = s1b("rows", [1, 3, TT], F32)
            Rb = s1b("Rb", [128, TT], F32)
            MRb = s1b("MRb", [128, TT], F32)
            onr = s1b("onr", [1, 128], F32)
            cr2 = [dma(em, "sync", onr["h"][:], ones_row_d,
                       bufs_w=[onr["buf"]])]
            em.wait("tensor", cr2)
            dq_bf = s1b("dq_bf", [128, T], BF16)
            ckv_bf = s1b("ckv_bf", [128, T], BF16)
            kr_bf = s1b("kr_bf", [64, T], BF16)
            krop_bf = s1b("krop_bf", [64, T], BF16)
            qr_bf = s1b("qr_bf", [64, TT], BF16)
            rsc1 = s1b("rsc1", [64, TT], F32)
            rsc2 = s1b("rsc2", [64, TT], F32)
            b0c, b1c = dq_bf, ckv_bf
            b0k, b1k = kr_bf, krop_bf

            lps = [s1.enter_context(
                nc.psum_tensor(pre + f"lps{i}", [128, 512], F32))
                   for i in range(8)]
            lpsb = [Buf() for _ in range(8)]
            rh = rows["h"]
            rowaps = [rh[:1, 0, :], rh[:1, 1, :], rh[:1, 2, :]]
            scr = {"ps": lps, "psb": lpsb, "sq": sq,
                   "sqb": [s["buf"] for s in sq],
                   "rowaps": rowaps, "rowsb": rows["buf"],
                   "R": Rb, "MR": MRb, "tmp": tmp,
                   "tmpb": [t["buf"] for t in tmp]}

            for tt in range(NTT):
                ts = slice(tt * TT, (tt + 1) * TT)
                emit_ln(em, sems, X, Xbuf, Y, Ybuf, tt,
                        {"ones_col": onc["h"][:], "ones_row": onr["h"][:]}, scr)
                for dst, wsl, pb in ((ckv_bf, wdkv_s, 4), (dq_bf, wdq_s, 5)):
                    for kc in range(DC):
                        mm(em, st_ if kc == DC - 1 else None, lps[pb][:, :],
                           wsl["h"][:, kc, :], Y[:, kc, :],
                           start=(kc == 0), stop=(kc == DC - 1),
                           bufs_r=[wsl["buf"], Ybuf],
                           bufs_w=[lpsb[pb]] if kc == 0 else ())
                    lpsb[pb].note_write(st_.h, st_.n)
                    op(em, "vector", sv, "tensor_copy", dst["h"][:, ts],
                       lps[pb][:, :], bufs_r=[lpsb[pb]], bufs_w=[dst["buf"]])
                for kc in range(DC):
                    mm(em, st_ if kc == DC - 1 else None, lps[6][:64, :],
                       wkr_s["h"][:, kc, :], Y[:, kc, :],
                       start=(kc == 0), stop=(kc == DC - 1),
                       bufs_r=[wkr_s["buf"], Ybuf],
                       bufs_w=[lpsb[6]] if kc == 0 else ())
                lpsb[6].note_write(st_.h, st_.n)
                op(em, "vector", sv, "tensor_copy", kr_bf["h"][:, ts],
                   lps[6][:64, :], bufs_r=[lpsb[6]], bufs_w=[kr_bf["buf"]])
                mm(em, st_, lps[7][:64, :], pswap["h"][:], kr_bf["h"][:, ts],
                   start=True, stop=True,
                   bufs_r=[pswap["buf"], kr_bf["buf"]], bufs_w=[lpsb[7]])
                op(em, "vector", sv, "tensor_mul", rsc1["h"][:],
                   lps[6][:64, :], cosf["h"][:, ts],
                   bufs_r=[lpsb[6], cosf["buf"]], bufs_w=[rsc1["buf"]])
                op(em, "vector", sv, "tensor_mul", rsc2["h"][:],
                   lps[7][:64, :], sinf["h"][:, ts],
                   bufs_r=[lpsb[7], sinf["buf"]], bufs_w=[rsc2["buf"]])
                op(em, "vector", sv, "tensor_add", krop_bf["h"][:, ts],
                   rsc1["h"][:], rsc2["h"][:],
                   bufs_r=[rsc1["buf"], rsc2["buf"]], bufs_w=[krop_bf["buf"]])
                for h in range(H):
                    pb = h % 2
                    mm(em, st_, lps[pb][:64, :], wqc_s["h"][:, h, :],
                       dq_bf["h"][:, ts], start=True, stop=True,
                       bufs_r=[wqc_s["buf"], dq_bf["buf"]], bufs_w=[lpsb[pb]])
                    op(em, "vector", sv, "tensor_copy", qnr["h"][:64, h, ts],
                       lps[pb][:64, :], bufs_r=[lpsb[pb]], bufs_w=[qnr["buf"]])
                    mm(em, st_, lps[2 + pb][:64, :], wqr_s["h"][:, h, :],
                       dq_bf["h"][:, ts], start=True, stop=True,
                       bufs_r=[wqr_s["buf"], dq_bf["buf"]], bufs_w=[lpsb[2 + pb]])
                    op(em, "vector", sv, "tensor_copy", qr_bf["h"][:],
                       lps[2 + pb][:64, :], bufs_r=[lpsb[2 + pb]],
                       bufs_w=[qr_bf["buf"]])
                    mm(em, st_, lps[6][:64, :], pswap["h"][:], qr_bf["h"][:],
                       start=True, stop=True,
                       bufs_r=[pswap["buf"], qr_bf["buf"]], bufs_w=[lpsb[6]])
                    op(em, "vector", sv, "tensor_mul", rsc1["h"][:],
                       lps[2 + pb][:64, :], cosf["h"][:, ts],
                       bufs_r=[lpsb[2 + pb], cosf["buf"]], bufs_w=[rsc1["buf"]])
                    op(em, "vector", sv, "tensor_mul", rsc2["h"][:],
                       lps[6][:64, :], sinf["h"][:, ts],
                       bufs_r=[lpsb[6], sinf["buf"]], bufs_w=[rsc2["buf"]])
                    op(em, "vector", sv, "tensor_add", qnr["h"][64:128, h, ts],
                       rsc1["h"][:], rsc2["h"][:],
                       bufs_r=[rsc1["buf"], rsc2["buf"]], bufs_w=[qnr["buf"]])

            dma(em, "sync", contrib[0:128, :], ckv_bf["h"][:],
                bufs_r=[ckv_bf["buf"]], bufs_w=[contrib_b])
            dma(em, "sync", contrib[128:192, :], krop_bf["h"][:],
                bufs_r=[krop_bf["buf"]], bufs_w=[contrib_b])
            em.emit("gpsimd", "collective_compute",
                    ("AllGather", ALU.bypass,
                     [[0, 1], [2, 3], [4, 5], [6, 7]],
                     [contrib.ap().opt()], [gathered.ap().opt()]),
                    {}, scc, 1, (), bufs_r=[contrib_b], bufs_w=[gathered_b])
            dma(em, "sync", b0c["h"][:], gathered[0:128, :],
                bufs_r=[gathered_b], bufs_w=[b0c["buf"]])
            dma(em, "sync", b1c["h"][:], gathered[192:320, :],
                bufs_r=[gathered_b], bufs_w=[b1c["buf"]])
            dma(em, "sync", b0k["h"][:], gathered[128:192, :],
                bufs_r=[gathered_b], bufs_w=[b0k["buf"]])
            dma(em, "sync", b1k["h"][:], gathered[320:384, :],
                bufs_r=[gathered_b], bufs_w=[b1k["buf"]])
            for dst, b0, b1, np_ in ((ckv_of, b0c, b1c, 128),
                                     (krf_of, b0k, b1k, 64)):
                op(em, "vector", sv, "tensor_scalar_mul", dst["h"][:np_, 0:T],
                   b0["h"][:], msel["h"][:np_, 0:1],
                   bufs_r=[b0["buf"], msel["buf"]], bufs_w=[dst["buf"]])
                op(em, "vector", sv, "scalar_tensor_tensor", dst["h"][:np_, 0:T],
                   b1["h"][:], msel["h"][:np_, 1:2], dst["h"][:np_, 0:T],
                   ALU.mult, ALU.add,
                   bufs_r=[b1["buf"], msel["buf"], dst["buf"]],
                   bufs_w=[dst["buf"]])
                op(em, "vector", sv, "tensor_scalar_mul", dst["h"][:np_, T:2 * T],
                   b0["h"][:], msel["h"][:np_, 1:2],
                   bufs_r=[b0["buf"], msel["buf"]], bufs_w=[dst["buf"]])
                op(em, "vector", sv, "scalar_tensor_tensor", dst["h"][:np_, T:2 * T],
                   b1["h"][:], msel["h"][:np_, 0:1], dst["h"][:np_, T:2 * T],
                   ALU.mult, ALU.add,
                   bufs_r=[b1["buf"], msel["buf"], dst["buf"]],
                   bufs_w=[dst["buf"]])
        full_barrier(em, sems)

        with contextlib.ExitStack() as s2:
            def s2b(name, shape, dtype):
                return {"h": s2.enter_context(
                    nc.sbuf_tensor(pre + name, shape, dtype)),
                        "buf": Buf(True)}
            knr = s2b("knr", [128, 2 * T], BF16)
            v_h = s2b("v_h", [128, 16, 128], BF16)
            mdiag = s2b("mdiag", [128, 4, 512], F32)
            wuk_s = s2b("wuk_s", [128, H, 64], BF16)
            wuv_s = s2b("wuv_s", [128, H, 128], BF16)
            dma(em, "sync", mdiag["h"][:], mdiag_d, bufs_w=[mdiag["buf"]])
            dma(em, "gpsimd", wuk_s["h"][:], w_uk.rearrange("h p c -> p h c"),
                bufs_w=[wuk_s["buf"]])
            dma(em, "gpsimd", wuv_s["h"][:], w_uv.rearrange("h p c -> p h c"),
                bufs_w=[wuv_s["buf"]])
            P_sb = s2b("P_sb", [128, 4, 512], BF16)
            P_T = s2b("P_T", [128, 16, 128], BF16)
            av_bf = s2b("av_bf", [128, 128], BF16)
            cols = s2b("cols", [128, 16], F32)
            sps = [s2.enter_context(
                nc.psum_tensor(pre + f"sps{i}", [128, 512], F32))
                   for i in range(4)]
            spsb = [Buf() for _ in range(4)]
            tpb = [s2.enter_context(
                nc.psum_tensor(pre + f"tpb{i}", [128, 1024], BF16))
                   for i in range(2)]
            tpbb = [Buf(), Buf()]
            avp = [s2.enter_context(
                nc.psum_tensor(pre + f"avp{i}", [128, 512], F32))
                   for i in range(2)]
            avpb = [Buf(), Buf()]
            ch = cols["h"]
            mx_c = [ch[:, i:i + 1] for i in range(4)]
            own_c, prt_c, mxt_c = ch[:, 4:5], ch[:, 5:6], ch[:, 6:7]
            bias_c, bias2_c = ch[:, 7:8], ch[:, 8:9]
            den_c, rden_c = ch[:, 9:10], ch[:, 10:11]

            op(em, "vector", sv, "tensor_copy", knr["h"][64:128, :],
               krf_of["h"][:, :], bufs_r=[krf_of["buf"]], bufs_w=[knr["buf"]])
            for h in range(H):
                for kt in range(4):
                    ks = slice(kt * 512, (kt + 1) * 512)
                    pb = kt % 2
                    mm(em, st_, sps[pb][:64, :], wuk_s["h"][:, h, :],
                       ckv_of["h"][:, ks], start=True, stop=True,
                       bufs_r=[wuk_s["buf"], ckv_of["buf"]], bufs_w=[spsb[pb]])
                    op(em, "vector", sv, "tensor_copy", knr["h"][0:64, ks],
                       sps[pb][:64, :], bufs_r=[spsb[pb]], bufs_w=[knr["buf"]])
                for kc in range(16):
                    pb = kc % 2
                    mm(em, st_, sps[pb][:, :128],
                       ckv_of["h"][:, kc * 128:(kc + 1) * 128],
                       wuv_s["h"][:, h, :], start=True, stop=True,
                       bufs_r=[ckv_of["buf"], wuv_s["buf"]], bufs_w=[spsb[pb]])
                    op(em, "vector", sv, "tensor_copy", v_h["h"][:, kc, :],
                       sps[pb][:, :128], bufs_r=[spsb[pb]], bufs_w=[v_h["buf"]])
                for qc in range(8):
                    nown = qc // 4 + 1
                    kts = list(range(nown)) + [2, 3]
                    qsl = qnr["h"][:, h, qc * 128:(qc + 1) * 128]
                    for i, kt in enumerate(kts):
                        koff = kt * 512
                        mm(em, st_, sps[i][:, :], qsl,
                           knr["h"][:, koff:koff + 512], start=True, stop=True,
                           bufs_r=[qnr["buf"], knr["buf"]], bufs_w=[spsb[i]])
                    op(em, "vector", sv, "tensor_add", sps[nown - 1][:, :],
                       sps[nown - 1][:, :], mdiag["h"][:, qc % 4, :],
                       bufs_r=[spsb[nown - 1], mdiag["buf"]],
                       bufs_w=[spsb[nown - 1]])
                    for i, kt in enumerate(kts):
                        op(em, "vector", sv, "reduce_max", mx_c[i],
                           sps[i][:, :], AX.X, bufs_r=[spsb[i]],
                           bufs_w=[cols["buf"]])
                    if nown == 1:
                        op(em, "vector", sv, "tensor_copy", own_c, mx_c[0],
                           bufs_r=[cols["buf"]], bufs_w=[cols["buf"]])
                    else:
                        op(em, "vector", sv, "tensor_max", own_c, mx_c[0],
                           mx_c[1], bufs_r=[cols["buf"]], bufs_w=[cols["buf"]])
                    op(em, "vector", sv, "tensor_max", prt_c, mx_c[nown],
                       mx_c[nown + 1], bufs_r=[cols["buf"]], bufs_w=[cols["buf"]])
                    op(em, "vector", sv, "tensor_sub", prt_c, prt_c,
                       pc["h"][:], bufs_r=[cols["buf"], pc["buf"]],
                       bufs_w=[cols["buf"]])
                    op(em, "vector", sv, "tensor_max", mxt_c, own_c, prt_c,
                       bufs_r=[cols["buf"]], bufs_w=[cols["buf"]])
                    op(em, "vector", sv, "tensor_scalar_mul", bias_c, mxt_c,
                       -SC, bufs_r=[cols["buf"]], bufs_w=[cols["buf"]])
                    op(em, "vector", sv, "tensor_sub", bias2_c, bias_c,
                       pc["h"][:], bufs_r=[cols["buf"], pc["buf"]],
                       bufs_w=[cols["buf"]])
                    for i, kt in enumerate(kts):
                        bias_ap = bias_c if kt < 2 else bias2_c
                        acc = ch[:, 11 + i:12 + i]
                        op(em, "scalar", sc, "activation", P_sb["h"][:, kt, :],
                           sps[i][:, :], AF.Exp, scale=SC, bias=bias_ap,
                           accum_out=acc,
                           bufs_r=[spsb[i], cols["buf"]],
                           bufs_w=[P_sb["buf"], cols["buf"]])
                    op(em, "vector", sv, "tensor_add", den_c, ch[:, 11:12],
                       ch[:, 12:13], bufs_r=[cols["buf"]], bufs_w=[cols["buf"]])
                    for i in range(2, len(kts)):
                        op(em, "vector", sv, "tensor_add", den_c, den_c,
                           ch[:, 11 + i:12 + i], bufs_r=[cols["buf"]],
                           bufs_w=[cols["buf"]])
                    op(em, "vector", sv, "reciprocal", rden_c, den_c,
                       bufs_r=[cols["buf"]], bufs_w=[cols["buf"]])
                    ocks = list(range(qc + 1))
                    for j, ck in enumerate(ocks):
                        op(em, "tensor", st_, "transpose",
                           tpb[0][:, j * 128:(j + 1) * 128],
                           P_sb["h"][:, ck // 4, (ck % 4) * 128:(ck % 4 + 1) * 128],
                           ident["h"][:],
                           bufs_r=[P_sb["buf"], ident["buf"]], bufs_w=[tpbb[0]])
                    op(em, "vector", sv, "tensor_copy",
                       P_T["h"][:, 0:qc + 1, :], tpb[0][:, 0:(qc + 1) * 128],
                       bufs_r=[tpbb[0]], bufs_w=[P_T["buf"]])
                    for j in range(8):
                        sl = 2 + j // 4
                        op(em, "tensor", st_, "transpose",
                           tpb[1][:, j * 128:(j + 1) * 128],
                           P_sb["h"][:, sl, (j % 4) * 128:(j % 4 + 1) * 128],
                           ident["h"][:],
                           bufs_r=[P_sb["buf"], ident["buf"]], bufs_w=[tpbb[1]])
                    op(em, "vector", sv, "tensor_copy",
                       P_T["h"][:, 8:16, :], tpb[1][:, :],
                       bufs_r=[tpbb[1]], bufs_w=[P_T["buf"]])
                    ab = qc % 2
                    cks = ocks + list(range(8, 16))
                    for j, ck in enumerate(cks):
                        mm(em, st_ if j == len(cks) - 1 else None,
                           avp[ab][:, :128], P_T["h"][:, ck % 16, :],
                           v_h["h"][:, ck % 16, :],
                           start=(j == 0), stop=(j == len(cks) - 1),
                           bufs_r=[P_T["buf"], v_h["buf"]],
                           bufs_w=[avpb[ab]] if j == 0 else ())
                    avpb[ab].note_write(st_.h, st_.n)
                    op(em, "vector", sv, "tensor_scalar_mul", av_bf["h"][:],
                       avp[ab][:, :128], rden_c,
                       bufs_r=[avpb[ab], cols["buf"]], bufs_w=[av_bf["buf"]])
                    op(em, "tensor", st_, "transpose", tpb[0][:, :128],
                       av_bf["h"][:], ident["h"][:],
                       bufs_r=[av_bf["buf"], ident["buf"]], bufs_w=[tpbb[0]])
                    op(em, "vector", sv, "tensor_copy",
                       att_sb["h"][:, h, qc * 128:(qc + 1) * 128],
                       tpb[0][:, :128], bufs_r=[tpbb[0]], bufs_w=[att_sb["buf"]])
        full_barrier(em, sems)

        with contextlib.ExitStack() as s3:
            def s3b(name, shape, dtype):
                return {"h": s3.enter_context(
                    nc.sbuf_tensor(pre + name, shape, dtype)),
                        "buf": Buf(True)}
            wslab = [s3b(f"wo{i}", [128, DC, 512], BF16) for i in range(2)]
            po = [s3.enter_context(
                nc.psum_tensor(pre + f"po{i}", [128, 512], F32))
                  for i in range(2)]
            pob = [Buf(), Buf()]
            wo_r = w_o.rearrange("(kc p) o -> p kc o", p=128)
            for tt in range(NTT):
                ts = slice(tt * TT, (tt + 1) * TT)
                for sp_i in range(4):
                    if tt == 0:
                        dma(em, "gpsimd", wslab[sp_i % 2]["h"][:],
                            wo_r[:, :, 512 * sp_i:512 * (sp_i + 1)],
                            bufs_w=[wslab[sp_i % 2]["buf"]])
                    for oc_rel in range(4):
                        oc = 4 * sp_i + oc_rel
                        pb = oc % 2
                        for kc in range(DC):
                            mm(em, st_ if kc == DC - 1 else None, po[pb][:, :],
                               wslab[sp_i % 2]["h"][:, kc,
                                                    oc_rel * 128:(oc_rel + 1) * 128],
                               att_sb["h"][:, kc, ts],
                               start=(kc == 0), stop=(kc == DC - 1),
                               bufs_r=[wslab[sp_i % 2]["buf"], att_sb["buf"]],
                               bufs_w=[pob[pb]] if kc == 0 else ())
                        pob[pb].note_write(st_.h, st_.n)
                        op(em, "vector", sv, "tensor_add",
                           X[:, oc, ts], po[pb][:, :], X[:, oc, ts],
                           bufs_r=[pob[pb], Xbuf[tt]], bufs_w=[Xbuf[tt]])

    full_barrier(em, sems)
    em.release(smark)


def build_fused():
    nc = bass.Bass("TRN2", target_bir_lowering=False)
    x_i = nc.dram_tensor("x_i", [D, T], F32, kind="ExternalInput").ap()
    x_o = nc.dram_tensor("x_o", [D, T], F32, kind="ExternalOutput").ap()
    P0f = declare_ffn_l0(nc)
    P0a = declare_attn_l0(nc)
    P1f = declare_ffn_l1(nc)
    P1a = declare_attn_l1(nc)
    ones_col_d = nc.inline_tensor(np.ones((128, 1), np.float32), name="onc_d").ap()
    ones_row_d = nc.inline_tensor(np.ones((1, 128), np.float32), name="onr_d").ap()
    ident_d = nc.inline_tensor(np.eye(128, dtype=nbf), name="ident_d").ap()

    em = Em(nc)
    with contextlib.ExitStack() as stack:
        ent = stack.enter_context
        em.pool = [Sem(ent(nc.semaphore(f"dp{i}"))) for i in range(88)]
        sems = {k: Sem(ent(nc.semaphore(k))) for k in
                ("st", "sv", "sc", "sp", "cc")}
        Xh = ent(nc.sbuf_tensor("X", [128, DC, T], F32))
        Xbuf = [Buf(True), Buf(True)]
        Xbuf[1].dsem = Xbuf[0].dsem = em.take_sem()
        onc = {"h": ent(nc.sbuf_tensor("onc", [128, 1], F32)), "buf": Buf(True)}
        onr = {"h": ent(nc.sbuf_tensor("onr", [1, 128], F32)), "buf": Buf(True)}
        dma(em, "sync", Xh[:], x_i.rearrange("(c p) t -> p c t", p=128),
            bufs_w=Xbuf)
        cr = [dma(em, "sync", onc["h"][:], ones_col_d, bufs_w=[onc["buf"]]),
              dma(em, "sync", onr["h"][:], ones_row_d, bufs_w=[onr["buf"]])]
        for e in ENGINES:
            em.wait(e, cr)
        sh = {"X": Xh[:], "Xbuf": Xbuf, "onc": onc, "onr": onr,
              "C": {"ones_col": onc["h"][:], "ones_row": onr["h"][:]},
              "ident_d": ident_d, "onr_d": ones_row_d}

        emit_ffn_l0(nc, em, sems, sh, P0f, "fa_", do_prep=True)
        emit_attn_l0(nc, em, sems, sh, P0a, "aa_")
        emit_ffn_l0(nc, em, sems, sh, P0f, "fb_", do_prep=False)
        emit_ffn_l1(nc, em, sems, sh, P1f, "ga_", do_prep=True)
        emit_attn_l1(nc, em, sems, sh, P1a, "ha_")
        emit_ffn_l1(nc, em, sems, sh, P1f, "gb_", do_prep=False)

        fin = dma(em, "sync", x_o.rearrange("(c p) t -> p c t", p=128), Xh[:],
                  bufs_r=[Xbuf[0], Xbuf[1]])
        em.wait("sync", [fin])
        em.finalize()
    return nc



def shard_x(x):
    out = []
    for c in range(NCORE):
        b, h = c // 2, c % 2
        out.append(np.ascontiguousarray(
            np.asarray(x)[b, h * T:(h + 1) * T, :].T.astype(np.float32)))
    return out


def unshard_x(shards):
    x = np.empty((4, 2048, D), np.float32)
    for c in range(NCORE):
        b, h = c // 2, c % 2
        x[b, h * T:(h + 1) * T, :] = shards[c].T
    return x


_CACHE = {}


def _rope_tables(off):
    import math
    half = 32
    freqs = np.exp(np.arange(half, dtype=np.float32) * (-math.log(10000.0) / half))
    tg = np.arange(off, off + T, dtype=np.float32)
    ang = tg[:, None] * freqs[None, :]
    cos = np.cos(ang).astype(np.float32)
    sin = np.sin(ang).astype(np.float32)
    cosf = np.empty((64, T), np.float32)
    sinf = np.empty((64, T), np.float32)
    cosf[0::2] = cos.T
    cosf[1::2] = cos.T
    sinf[0::2] = -sin.T
    sinf[1::2] = sin.T
    return cosf, sinf


def kernel(**inputs):
    if "fused" not in _CACHE:
        _CACHE["fused"] = build_fused()
    xs = shard_x(inputs["L0_attn_x"])
    base = {
        "w_gu": np.asarray(inputs["L0_ffn_w_gate_up"]),
        "w_dn": np.asarray(inputs["L0_ffn_w_down"]),
        "w_qkv": np.asarray(inputs["L0_attn_w_qkv"]),
        "w_out": np.asarray(inputs["L0_attn_w_out"]),
        "decay_i": np.asarray(inputs["L0_attn_decay_param"]).reshape(1, 16)
                     .astype(np.float32),
        "w_up": np.asarray(inputs["L1_ffn_W_up"]),
        "w_gt": np.asarray(inputs["L1_ffn_W_gate"]),
        "w_dn1": np.asarray(inputs["L1_ffn_W_down"]),
        "su_ug": np.asarray(inputs["L1_ffn_SU_ug"]).reshape(16, 128),
        "su_dp": np.asarray(inputs["L1_ffn_SU_dp"]).reshape(64, 128),
        "wsc_ug": np.asarray(inputs["L1_ffn_Wscale_ug"]).reshape(128, 128),
        "wsc_dp": np.asarray(inputs["L1_ffn_Wscale_dp"]).reshape(16, 128),
        "w_dkv": np.asarray(inputs["L1_attn_W_dkv"]),
        "w_dq": np.asarray(inputs["L1_attn_W_dq"]),
        "w_kr": np.asarray(inputs["L1_attn_W_kr"]),
        "w_qc": np.asarray(inputs["L1_attn_W_qc"]),
        "w_qr": np.asarray(inputs["L1_attn_W_qr"]),
        "w_uk": np.asarray(inputs["L1_attn_W_uk"]),
        "w_uv": np.asarray(inputs["L1_attn_W_uv"]),
        "w_o": np.asarray(inputs["L1_attn_W_o"]),
    }
    in_maps = []
    for c in range(NCORE):
        p = c % 2
        cosf, sinf = _rope_tables(p * T)
        in_maps.append(dict(
            base, x_i=xs[c],
            csel_i=np.full((128, 1), float(p), np.float32),
            cosf_i=cosf, sinf_i=sinf,
            pc_i=np.full((128, 1), 1e9 if p == 0 else 0.0, np.float32),
            msel_i=np.tile(np.array([[1.0 - p, p]], np.float32), (128, 1)),
        ))
    res = run_bass_kernel_spmd(_CACHE["fused"], in_maps,
                               core_ids=list(range(NCORE)))
    return unshard_x([res.results[c]["x_o"] for c in range(NCORE)])


# version 06
# speedup vs baseline: 1.1110x; 1.0784x over previous
import contextlib
import numpy as np
import ml_dtypes
import concourse.bass as bass
import concourse.mybir as mybir
from concourse.bass_utils import run_bass_kernel_spmd

F32 = mybir.dt.float32
BF16 = mybir.dt.bfloat16
AF = mybir.ActivationFunctionType
ALU = mybir.AluOpType
AX = mybir.AxisListType
nbf = ml_dtypes.bfloat16

NCORE = 8
T = 1024
TT = 512
NTT = T // TT
D = 2048
DC = D // 128
INTER = 8192
EPS = 1e-5

ENGINES = ("sync", "tensor", "vector", "scalar", "gpsimd")



class Sem:
    __slots__ = ("h", "n")

    def __init__(self, handle):
        self.h = handle
        self.n = 0


class Buf:
    __slots__ = ("prod", "cons", "dma_owner", "dsem")

    def __init__(self, dma_owner=False):
        self.prod = {}
        self.cons = {}
        self.dma_owner = dma_owner
        self.dsem = None

    def read_waits(self, w):
        w.extend(self.prod.values())

    def note_read(self, h, v):
        k = id(h)
        if k not in self.cons or self.cons[k][1] < v:
            self.cons[k] = (h, v)

    def write_waits(self, w):
        w.extend(self.prod.values())
        w.extend(self.cons.values())

    def note_write(self, h, v):
        k = id(h)
        if k not in self.prod or self.prod[k][1] < v:
            self.prod[k] = (h, v)


class Em:
    def __init__(self, nc):
        self.nc = nc
        self.q = {e: [] for e in ENGINES}
        self.hw = {e: {} for e in ENGINES}
        self.pool = []
        self.used = []

    def take_sem(self):
        s = self.pool.pop()
        self.used.append(s)
        return s

    def mark(self):
        return len(self.used)

    def release(self, mark):
        self.pool.extend(self.used[mark:])
        del self.used[mark:]

    def _dedupe(self, eng, waits):
        out = []
        hw = self.hw[eng]
        for h, v in waits:
            k = id(h)
            if hw.get(k, -1) >= v:
                continue
            hw[k] = v
            out.append((h, v))
        return tuple(out)

    def emit(self, eng, method, args, kwargs, sem, inc, waits, bufs_r=(), bufs_w=()):
        w = []
        for b in bufs_r:
            b.read_waits(w)
        for b in bufs_w:
            b.write_waits(w)
        w.extend(waits)
        w = self._dedupe(eng, w)
        if sem is not None:
            sem.n += inc
            tgt = (sem.h, sem.n)
        else:
            tgt = None

        def f(e, method=method, args=args, kwargs=kwargs, w=w, tgt=tgt, inc=inc):
            for hh, vv in w:
                e.wait_ge(hh, vv)
            ins = getattr(e, method)(*args, **kwargs)
            if tgt is not None:
                ins.then_inc(tgt[0], inc)
        self.q[eng].append(f)
        if tgt is not None:
            for b in bufs_r:
                b.note_read(*tgt)
            for b in bufs_w:
                b.note_write(*tgt)
        return sem.n if sem is not None else None

    def wait(self, eng, waits):
        w = self._dedupe(eng, tuple(waits))
        if not w:
            return

        def f(e, w=w):
            for hh, vv in w:
                e.wait_ge(hh, vv)
        self.q[eng].append(f)

    def finalize(self):
        with self.nc.Block() as block:
            for eng in ENGINES:
                thunks = self.q[eng]
                if not thunks:
                    continue
                getattr(block, eng)(lambda e, th=tuple(thunks): [f(e) for f in th])


def dma(em, eng, out, in_, waits=(), bufs_r=(), bufs_w=(), transpose=False,
        sem=None):
    if sem is None:
        for b in tuple(bufs_w) + tuple(bufs_r):
            if b.dma_owner:
                if b.dsem is None:
                    b.dsem = em.take_sem()
                sem = b.dsem
                break
    assert sem is not None, "dma needs an owner buffer or explicit sem"
    kw = {"out": out, "in_": in_}
    if transpose:
        kw["transpose"] = True
    em.emit(eng, "dma_start", (), kw, sem, 16, waits, bufs_r, bufs_w)
    return (sem.h, sem.n)


def mm(em, sem, out, lhsT, rhs, start, stop, waits=(), bufs_r=(), bufs_w=()):
    return em.emit("tensor", "matmul", (out, lhsT, rhs),
                   {"start": start, "stop": stop}, sem, 1, waits, bufs_r, bufs_w)


def op(em, eng, sem, method, *args, waits=(), bufs_r=(), bufs_w=(), **kwargs):
    return em.emit(eng, method, args, kwargs, sem, 1, waits, bufs_r, bufs_w)


def full_barrier(em, sems, exclude=()):
    ex = set(id(s) for s in exclude)
    pts = [(s.h, s.n) for s in em.used if s.n > 0 and id(s) not in ex]
    pts += [(s.h, s.n) for s in em.pool if s.n > 0 and id(s) not in ex]
    pts += [(s.h, s.n) for s in sems.values() if s.n > 0]
    for eng in ENGINES:
        em.wait(eng, list(pts))



def emit_weight_prep(em, w_ap, rows, cols, bf_ap, bf_buf, t_ap, t_buf,
                     cbuf, tslab):
    CW = min(2048, cols)
    i = 0
    for g in range(rows // 128):
        for cb in range(cols // CW):
            b = cbuf[i % 2]
            i += 1
            rs = slice(g * 128, (g + 1) * 128)
            cs = slice(cb * CW, (cb + 1) * CW)
            dma(em, "gpsimd", b["h"][:, :CW], w_ap[rs, cs], bufs_w=[b["buf"]])
            dma(em, "gpsimd", bf_ap[rs, cs], b["h"][:, :CW],
                bufs_r=[b["buf"]], bufs_w=[bf_buf])
    nch = cols // 128
    RH = 2048
    nrh = (rows + RH - 1) // RH
    i = 0
    for ch in range(nch):
        for rh in range(nrh):
            r0, r1 = rh * RH, min((rh + 1) * RH, rows)
            b = tslab[i % 2]
            i += 1
            dma(em, "sync", b["h"][:, :r1 - r0],
                bf_ap[r0:r1, ch * 128:(ch + 1) * 128],
                bufs_r=[bf_buf], bufs_w=[b["buf"]], transpose=True)
            dma(em, "sync", t_ap[ch * 128:(ch + 1) * 128, r0:r1],
                b["h"][:, :r1 - r0], bufs_r=[b["buf"]], bufs_w=[t_buf])



def emit_ln(em, sems, X, Xbuf, Y, Ybuf, tt, C, scr):
    st_, sv, sc, sp = sems["st"], sems["sv"], sems["sc"], sems["sp"]
    ts = slice(tt * TT, (tt + 1) * TT)
    ps = scr["ps"]
    psb = scr["psb"]
    sq, sqb = scr["sq"], scr["sqb"]
    rowsb = scr["rowsb"]
    m_row, t_row = scr["rowaps"][0], scr["rowaps"][1]
    mr_row = m_row
    msq_row = v_row = sd_row = rstd_row = t_row

    for hc in range(DC):
        mm(em, st_ if hc == DC - 1 else None, ps[0][:1, :], C["ones_col"],
           X[:, hc, ts], start=(hc == 0), stop=(hc == DC - 1),
           bufs_r=[Xbuf[tt]], bufs_w=[psb[0]] if hc == 0 else ())
    s1 = (st_.h, st_.n)
    for hc in range(DC):
        op(em, "scalar", sc, "activation", sq[hc % 2]["h"][:], X[:, hc, ts],
           AF.Square, bufs_r=[Xbuf[tt]], bufs_w=[sqb[hc % 2]])
        mm(em, st_, ps[1][:1, :], C["ones_col"], sq[hc % 2]["h"][:],
           start=(hc == 0), stop=(hc == DC - 1),
           bufs_r=[sqb[hc % 2]], bufs_w=[psb[1]] if hc == 0 else ())
    s2 = (st_.h, st_.n)

    op(em, "vector", sv, "tensor_scalar_mul", m_row, ps[0][:1, :], 1.0 / D,
       waits=[s1], bufs_r=[psb[0]], bufs_w=[rowsb])
    op(em, "scalar", sc, "activation", msq_row, m_row, AF.Square,
       bufs_r=[rowsb], bufs_w=[rowsb])
    op(em, "vector", sv, "scalar_tensor_tensor", v_row, ps[1][:1, :], 1.0 / D,
       msq_row, ALU.mult, ALU.subtract, waits=[s2], bufs_r=[psb[1], rowsb],
       bufs_w=[rowsb])
    op(em, "vector", sv, "tensor_scalar_add", v_row, v_row, EPS,
       bufs_r=[rowsb], bufs_w=[rowsb])
    op(em, "scalar", sc, "activation", sd_row, v_row, AF.Sqrt,
       bufs_r=[rowsb], bufs_w=[rowsb])
    op(em, "vector", sv, "reciprocal", rstd_row, sd_row,
       bufs_r=[rowsb], bufs_w=[rowsb])
    op(em, "vector", sv, "tensor_mul", mr_row, m_row, rstd_row,
       bufs_r=[rowsb], bufs_w=[rowsb])

    mm(em, st_, ps[2][:, :], C["ones_row"], rstd_row, start=True, stop=True,
       bufs_r=[rowsb], bufs_w=[psb[2]])
    mm(em, st_, ps[3][:, :], C["ones_row"], mr_row, start=True, stop=True,
       bufs_r=[rowsb], bufs_w=[psb[3]])
    op(em, "vector", sv, "tensor_copy", scr["R"]["h"][:], ps[2][:, :],
       bufs_r=[psb[2]], bufs_w=[scr["R"]["buf"]])
    op(em, "vector", sv, "tensor_copy", scr["MR"]["h"][:], ps[3][:, :],
       bufs_r=[psb[3]], bufs_w=[scr["MR"]["buf"]])

    tmp, tmpb = scr["tmp"], scr["tmpb"]
    for hc in range(DC):
        op(em, "vector", sv, "tensor_mul", tmp[hc % 2]["h"][:], X[:, hc, ts],
           scr["R"]["h"][:], bufs_r=[Xbuf[tt], scr["R"]["buf"]],
           bufs_w=[tmpb[hc % 2]])
        op(em, "vector", sv, "tensor_sub", Y[:, hc, :], tmp[hc % 2]["h"][:],
           scr["MR"]["h"][:], bufs_r=[tmpb[hc % 2], scr["MR"]["buf"]],
           bufs_w=[Ybuf] if hc == 0 else (),)
        Ybuf.note_write(sv.h, sv.n)



def declare_ffn_l0(nc):
    P = {}
    P["w_gu"] = nc.dram_tensor("w_gu", [2 * INTER, D], F32, kind="ExternalInput").ap()
    P["w_dn"] = nc.dram_tensor("w_dn", [D, INTER], F32, kind="ExternalInput").ap()
    P["gu_bf"] = nc.dram_tensor("gu_bf", [2 * INTER, D], BF16).ap()
    P["dn_bf"] = nc.dram_tensor("dn_bf", [D, INTER], BF16).ap()
    P["guT"] = nc.dram_tensor("guT", [D, 2 * INTER], BF16).ap()
    P["dnT"] = nc.dram_tensor("dnT", [INTER, D], BF16).ap()
    P["gu_bf_b"], P["dn_bf_b"], P["guT_b"], P["dnT_b"] = \
        Buf(), Buf(), Buf(), Buf()
    return P


def emit_ffn_l0(nc, em, sems, sh, P, pre):
    st_, sv, sc, sp = (sems[k] for k in ("st", "sv", "sc", "sp"))
    X, Xbuf, C = sh["X"], sh["Xbuf"], sh["C"]
    w_gu, w_dn = P["w_gu"], P["w_dn"]
    gu_bf, dn_bf, guT, dnT = P["gu_bf"], P["dn_bf"], P["guT"], P["dnT"]
    gu_bf_b, dn_bf_b, guT_b, dnT_b = (P["gu_bf_b"], P["dn_bf_b"],
                                      P["guT_b"], P["dnT_b"])
    smark = em.mark()
    with contextlib.ExitStack() as stack:
        ent = stack.enter_context

        def sb(name, shape, dtype):
            return {"h": ent(nc.sbuf_tensor(pre + name, shape, dtype)),
                    "buf": Buf(True)}

        Yh = ent(nc.sbuf_tensor(pre + "Y", [128, DC, TT], BF16))
        Y = Yh[:]
        Ybuf = Buf()
        hbuf = sb("hbuf", [128, 16, TT], BF16)
        slab_g = [sb(f"slg{i}", [128, DC, 256], BF16) for i in range(2)]
        slab_u = [sb(f"slu{i}", [128, DC, 256], BF16) for i in range(2)]
        slab_d = [sb(f"sld{i}", [128, 16, 256], BF16) for i in range(2)]
        sq = [sb(f"sq{i}", [128, TT], F32) for i in range(2)]
        s_scr = [sb(f"ss{i}", [128, TT], F32) for i in range(2)]
        tmp = [sb(f"tmp{i}", [128, TT], F32) for i in range(2)]
        rows = sb("rows", [1, 2, TT], F32)
        Rb = sb("Rb", [128, TT], F32)
        MRb = sb("MRb", [128, TT], F32)
        psh = [ent(nc.psum_tensor(pre + f"ps{i}", [128, 512], F32))
               for i in range(8)]
        psb = [Buf() for _ in range(8)]

        rh = rows["h"]
        rowaps = [rh[:1, 0, :], rh[:1, 1, :]]
        scr = {"ps": psh, "psb": psb, "sq": sq, "sqb": [s["buf"] for s in sq],
               "rowaps": rowaps, "rowsb": rows["buf"],
               "R": Rb, "MR": MRb, "tmp": tmp, "tmpb": [t["buf"] for t in tmp]}

        guT_r = guT.rearrange("(hc p) o -> p hc o", p=128)
        dnT_r = dnT.rearrange("(ic p) h -> p ic h", p=128)

        for tt in range(NTT):
            emit_ln(em, sems, X, Xbuf, Y, Ybuf, tt, C, scr)
            for q in range(4):
                for i16 in range(16):
                    ic = q * 16 + i16
                    j = ic // 2
                    rel = ic % 2
                    if rel == 0:
                        dma(em, "sync", slab_g[j % 2]["h"][:],
                            guT_r[:, :, 256 * j:256 * (j + 1)],
                            bufs_r=[guT_b], bufs_w=[slab_g[j % 2]["buf"]])
                        dma(em, "sync", slab_u[j % 2]["h"][:],
                            guT_r[:, :, INTER + 256 * j:INTER + 256 * (j + 1)],
                            bufs_r=[guT_b], bufs_w=[slab_u[j % 2]["buf"]])
                    b = ic % 2
                    for hc in range(DC):
                        mm(em, st_ if hc == DC - 1 else None, psh[b][:, :],
                           slab_g[j % 2]["h"][:, hc, rel * 128:(rel + 1) * 128],
                           Y[:, hc, :], start=(hc == 0), stop=(hc == DC - 1),
                           bufs_r=[slab_g[j % 2]["buf"], Ybuf],
                           bufs_w=[psb[b]] if hc == 0 else ())
                    psb[b].note_write(st_.h, st_.n)
                    for hc in range(DC):
                        mm(em, st_ if hc == DC - 1 else None, psh[2 + b][:, :],
                           slab_u[j % 2]["h"][:, hc, rel * 128:(rel + 1) * 128],
                           Y[:, hc, :], start=(hc == 0), stop=(hc == DC - 1),
                           bufs_r=[slab_u[j % 2]["buf"], Ybuf],
                           bufs_w=[psb[2 + b]] if hc == 0 else ())
                    psb[2 + b].note_write(st_.h, st_.n)
                    op(em, "scalar", sc, "activation", s_scr[ic % 2]["h"][:],
                       psh[b][:, :], AF.Silu,
                       bufs_r=[psb[b]], bufs_w=[s_scr[ic % 2]["buf"]])
                    op(em, "vector", sv, "scalar_tensor_tensor",
                       hbuf["h"][:, i16, :], psh[2 + b][:, :], 1.0,
                       s_scr[ic % 2]["h"][:], ALU.mult, ALU.mult,
                       bufs_r=[psb[2 + b], s_scr[ic % 2]["buf"]],
                       bufs_w=[hbuf["buf"]] if i16 == 0 else ())
                    hbuf["buf"].note_write(sv.h, sv.n)
                for ocp in range(8):
                    dma(em, "sync", slab_d[ocp % 2]["h"][:],
                        dnT_r[:, q * 16:(q + 1) * 16, ocp * 256:(ocp + 1) * 256],
                        bufs_r=[dnT_b], bufs_w=[slab_d[ocp % 2]["buf"]])
                    for oc_rel in range(2):
                        oc = 2 * ocp + oc_rel
                        db = 4 + (oc % 2)
                        for i16 in range(16):
                            mm(em, st_ if i16 == 15 else None, psh[db][:, :],
                               slab_d[ocp % 2]["h"][:, i16,
                                                    oc_rel * 128:(oc_rel + 1) * 128],
                               hbuf["h"][:, i16, :], start=(i16 == 0),
                               stop=(i16 == 15),
                               bufs_r=[slab_d[ocp % 2]["buf"], hbuf["buf"]],
                               bufs_w=[psb[db]] if i16 == 0 else ())
                        psb[db].note_write(st_.h, st_.n)
                        op(em, "vector", sv, "scalar_tensor_tensor",
                           X[:, oc, tt * TT:(tt + 1) * TT], psh[db][:, :], 0.5,
                           X[:, oc, tt * TT:(tt + 1) * TT], ALU.mult, ALU.add,
                           bufs_r=[psb[db], Xbuf[tt]], bufs_w=[Xbuf[tt]])

    full_barrier(em, sems, exclude=sh.get("prep_sems", ()))
    em.release(smark)



def _had(n):
    h = np.array([[1.0]], np.float32)
    while h.shape[0] < n:
        h = np.block([[h, h], [h, -h]])
    return h


def declare_ffn_l1(nc):
    P = {}
    P["w_up"] = nc.dram_tensor("w_up", [INTER, D], F32, kind="ExternalInput").ap()
    P["w_gt"] = nc.dram_tensor("w_gt", [INTER, D], F32, kind="ExternalInput").ap()
    P["w_dn1"] = nc.dram_tensor("w_dn1", [D, INTER], F32, kind="ExternalInput").ap()
    P["su_ug"] = nc.dram_tensor("su_ug", [16, 128], F32, kind="ExternalInput").ap()
    P["su_dp"] = nc.dram_tensor("su_dp", [64, 128], F32, kind="ExternalInput").ap()
    P["wsc_ug"] = nc.dram_tensor("wsc_ug", [128, 128], F32, kind="ExternalInput").ap()
    P["wsc_dp"] = nc.dram_tensor("wsc_dp", [16, 128], F32, kind="ExternalInput").ap()
    P["up_bf"] = nc.dram_tensor("up_bf", [INTER, D], BF16).ap()
    P["gt_bf"] = nc.dram_tensor("gt_bf", [INTER, D], BF16).ap()
    P["dn1_bf"] = nc.dram_tensor("dn1_bf", [D, INTER], BF16).ap()
    P["upT"] = nc.dram_tensor("upT", [D, INTER], BF16).ap()
    P["gtT"] = nc.dram_tensor("gtT", [D, INTER], BF16).ap()
    P["dn1T"] = nc.dram_tensor("dn1T", [INTER, D], BF16).ap()
    P["u_dram"] = nc.dram_tensor("u_dram", [D, T], BF16).ap()
    P["u2_dram"] = nc.dram_tensor("u2_dram", [INTER, T], BF16).ap()
    for k in ("up_bf_b", "gt_bf_b", "dn1_bf_b", "upT_b", "gtT_b", "dn1T_b",
              "u_dram_b", "u2_dram_b"):
        P[k] = Buf()
    H128 = _had(128)
    H16 = _had(16)
    H64 = _had(64)
    P["h128a_d"] = nc.inline_tensor((H128 / 32.0).astype(nbf), name="h128a_d").ap()
    P["h128b_d"] = nc.inline_tensor((H128 / 64.0).astype(nbf), name="h128b_d").ap()
    bd16 = np.zeros((128, 128), np.float32)
    for pp in range(8):
        bd16[16 * pp:16 * (pp + 1), 16 * pp:16 * (pp + 1)] = H16 / np.sqrt(2.0)
    P["bd16_d"] = nc.inline_tensor(bd16.astype(nbf), name="bd16_d").ap()
    bd64 = np.zeros((128, 128), np.float32)
    for pp in range(2):
        bd64[64 * pp:64 * (pp + 1), 64 * pp:64 * (pp + 1)] = H64 / np.sqrt(2.0)
    P["bd64_d"] = nc.inline_tensor(bd64.astype(nbf), name="bd64_d").ap()
    P["identf_d"] = nc.inline_tensor(np.eye(128, dtype=np.float32),
                                     name="identf_d").ap()
    return P


def emit_ffn_l1(nc, em, sems, sh, P, pre):
    st_, sv, sc, sp = (sems[k] for k in ("st", "sv", "sc", "sp"))
    X, Xbuf, C = sh["X"], sh["Xbuf"], sh["C"]
    w_up, w_gt, w_dn = P["w_up"], P["w_gt"], P["w_dn1"]
    up_bf, gt_bf, dn_bf = P["up_bf"], P["gt_bf"], P["dn1_bf"]
    upT, gtT, dnT = P["upT"], P["gtT"], P["dn1T"]
    up_bf_b, gt_bf_b, dn_bf_b = P["up_bf_b"], P["gt_bf_b"], P["dn1_bf_b"]
    upT_b, gtT_b, dnT_b = P["upT_b"], P["gtT_b"], P["dn1T_b"]
    u_dram, u2_dram = P["u_dram"], P["u2_dram"]
    u_dram_b, u2_dram_b = P["u_dram_b"], P["u2_dram_b"]
    su_ug_d, su_dp_d = P["su_ug"], P["su_dp"]
    wsc_ug_d, wsc_dp_d = P["wsc_ug"], P["wsc_dp"]
    smark = em.mark()
    with contextlib.ExitStack() as stack:
        ent = stack.enter_context

        def sb(name, shape, dtype):
            return {"h": ent(nc.sbuf_tensor(pre + name, shape, dtype)),
                    "buf": Buf(True)}

        onc, onr = sh["onc"], sh["onr"]
        identf = sb("identf", [128, 128], F32)
        h128a = sb("h128a", [128, 128], BF16)
        h128b = sb("h128b", [128, 128], BF16)
        bd16_s = sb("bd16_s", [128, 128], BF16)
        bd64_s = sb("bd64_s", [128, 128], BF16)
        vrow = sb("vrow", [128, 128], F32)
        su_ug_c = sb("su_ug_c", [128, 16], F32)
        wsuS_c = sb("wsuS_c", [128, 64], F32)
        wsg_c = sb("wsg_c", [128, 64], F32)
        wdp_c = sb("wdp_c", [128, 16], F32)
        su_dp_c = sb("su_dp_c", [128, 64], F32)
        h128su = sb("h128su", [128, DC, 128], BF16)

        cr = []
        for tile, srcd in ((identf, P["identf_d"]), (h128a, P["h128a_d"]),
                           (h128b, P["h128b_d"]), (bd16_s, P["bd16_d"]),
                           (bd64_s, P["bd64_d"])):
            cr.append(dma(em, "sync", tile["h"][:], srcd, bufs_w=[tile["buf"]]))
        consts_ready = cr
        em.wait("tensor", cr)

        with contextlib.ExitStack() as vsc:
            vps = vsc.enter_context(nc.psum_tensor(pre + "vps", [128, 512], F32))
            vpsb = Buf()
            for rows_n, src, dstc in ((16, su_ug_d, su_ug_c), (64, su_dp_d, su_dp_c),
                                      (16, wsc_dp_d, wdp_c)):
                dma(em, "sync", vrow["h"][:rows_n, :], src,
                    bufs_w=[vrow["buf"]])
                op(em, "tensor", st_, "transpose", vps[:, :rows_n],
                   vrow["h"][:rows_n, :], identf["h"][:rows_n, :rows_n],
                   bufs_r=[vrow["buf"], identf["buf"]], bufs_w=[vpsb])
                op(em, "vector", sv, "tensor_copy", dstc["h"][:], vps[:, :rows_n],
                   bufs_r=[vpsb], bufs_w=[dstc["buf"]])
            dma(em, "sync", vrow["h"][:, :], wsc_ug_d, bufs_w=[vrow["buf"]])
            op(em, "tensor", st_, "transpose", vps[:, :128], vrow["h"][:, :],
               identf["h"][:], bufs_r=[vrow["buf"], identf["buf"]], bufs_w=[vpsb])
            op(em, "vector", sv, "tensor_copy", wsuS_c["h"][:], vps[:, :64],
               bufs_r=[vpsb], bufs_w=[wsuS_c["buf"]])
            op(em, "vector", sv, "tensor_copy", wsg_c["h"][:], vps[:, 64:128],
               bufs_r=[vpsb], bufs_w=[wsg_c["buf"]])
            op(em, "vector", sv, "tensor_mul", wsuS_c["h"][:], wsuS_c["h"][:],
               su_dp_c["h"][:], bufs_r=[wsuS_c["buf"], su_dp_c["buf"]],
               bufs_w=[wsuS_c["buf"]])
            op(em, "vector", sv, "tensor_scalar_mul", wdp_c["h"][:],
               wdp_c["h"][:], 0.5, bufs_r=[wdp_c["buf"]], bufs_w=[wdp_c["buf"]])
            for c in range(DC):
                op(em, "vector", sv, "tensor_scalar_mul",
                   h128su["h"][:, c, :], h128a["h"][:], su_ug_c["h"][:, c:c + 1],
                   bufs_r=[h128a["buf"], su_ug_c["buf"]], bufs_w=[h128su["buf"]])
        full_barrier(em, sems, exclude=sh.get("prep_sems", ()))

        Yh = ent(nc.sbuf_tensor(pre + "Y", [128, DC, TT], BF16))
        Y = Yh[:]
        Ybuf = Buf()
        sq = [sb(f"sq{i}", [128, TT], F32) for i in range(2)]
        tmp = [sb(f"tmp{i}", [128, TT], F32) for i in range(2)]
        rows = sb("rows", [1, 2, TT], F32)
        Rb = sb("Rb", [128, TT], F32)
        MRb = sb("MRb", [128, TT], F32)
        u_sb = sb("u_sb", [128, DC, TT], BF16)
        z1 = {"h": Yh, "buf": None}
        slab_u = [sb(f"slab_u{i}", [128, DC, 256], BF16) for i in range(2)]
        slab_g = [sb(f"slab_g{i}", [128, DC, 256], BF16) for i in range(2)]
        slab_d = [sb(f"sld{i}", [128, 16, 256], BF16) for i in range(2)]
        h_t = [sb(f"h_t{i}", [128, TT], BF16) for i in range(2)]
        s_scr = [sb(f"ss{i}", [128, TT], F32) for i in range(2)]
        u2g = sb("u2g", [128, 8, TT], BF16)
        zin2 = u_sb
        z2 = z1
        psh = [ent(nc.psum_tensor(pre + f"ps{i}", [128, 512], F32))
               for i in range(8)]
        psb = [Buf() for _ in range(8)]
        z1["buf"] = Ybuf

        rh = rows["h"]
        rowaps = [rh[:1, 0, :], rh[:1, 1, :]]
        scr = {"ps": psh, "psb": psb, "sq": sq, "sqb": [s["buf"] for s in sq],
               "rowaps": rowaps, "rowsb": rows["buf"],
               "R": Rb, "MR": MRb, "tmp": tmp, "tmpb": [t["buf"] for t in tmp]}

        upT_r = upT.rearrange("(c2 pj pp) i -> pp c2 pj i", c2=16, pj=16, pp=8)
        gtT_r = gtT.rearrange("(c2 pj pp) i -> pp c2 pj i", c2=16, pj=16, pp=8)
        dnT_r = dnT.rearrange("(a2 pj pp) h -> pp a2 pj h", a2=64, pj=64, pp=2)
        ud_r = u_dram.rearrange("(c pj pp) t -> pp c pj t", c=16, pj=16, pp=8)
        u2d_r = u2_dram.rearrange("(a pj pp) t -> pp a pj t", a=64, pj=64, pp=2)

        for tt in range(NTT):
            ts = slice(tt * TT, (tt + 1) * TT)
            emit_ln(em, sems, X, Xbuf, Y, Ybuf, tt, {"ones_col": onc["h"][:],
                    "ones_row": onr["h"][:]}, scr)
            for c in range(DC):
                b = c % 2
                mm(em, st_, psh[b][:, :], h128su["h"][:, c, :], Y[:, c, :],
                   start=True, stop=True,
                   bufs_r=[h128su["buf"], Ybuf], bufs_w=[psb[b]])
                op(em, "vector", sv, "tensor_copy", u_sb["h"][:, c, :],
                   psh[b][:, :], bufs_r=[psb[b]], bufs_w=[u_sb["buf"]])
            dma(em, "sync", u_dram.rearrange("(c p) t -> p c t", p=128)[:, :, ts],
                u_sb["h"][:], bufs_r=[u_sb["buf"]], bufs_w=[u_dram_b])
            dma(em, "sync", u_sb["h"][:], ud_r[:, :, :, ts],
                bufs_r=[u_dram_b], bufs_w=[u_sb["buf"]])
            for pj in range(DC):
                b = pj % 2
                mm(em, st_, psh[b][:, :], bd16_s["h"][:], u_sb["h"][:, pj, :],
                   start=True, stop=True,
                   bufs_r=[bd16_s["buf"], u_sb["buf"]], bufs_w=[psb[b]])
                op(em, "vector", sv, "tensor_copy", z1["h"][:, pj, :],
                   psh[b][:, :], bufs_r=[psb[b]], bufs_w=[z1["buf"]])
            for a in range(64):
                jb = a % 2
                if a % 2 == 0:
                    sbi = (a // 2) % 2
                    dma(em, "sync", slab_u[sbi]["h"][:],
                        upT_r[:, :, :, 128 * a:128 * (a + 2)],
                        bufs_r=[upT_b], bufs_w=[slab_u[sbi]["buf"]])
                    dma(em, "sync", slab_g[sbi]["h"][:],
                        gtT_r[:, :, :, 128 * a:128 * (a + 2)],
                        bufs_r=[gtT_b], bufs_w=[slab_g[sbi]["buf"]])
                sbi = (a // 2) % 2
                rel = a % 2
                for pj in range(DC):
                    mm(em, st_ if pj == DC - 1 else None, psh[jb][:, :],
                       slab_g[sbi]["h"][:, pj, rel * 128:(rel + 1) * 128],
                       z1["h"][:, pj, :], start=(pj == 0), stop=(pj == DC - 1),
                       bufs_r=[slab_g[sbi]["buf"], z1["buf"]],
                       bufs_w=[psb[jb]] if pj == 0 else ())
                psb[jb].note_write(st_.h, st_.n)
                for pj in range(DC):
                    mm(em, st_ if pj == DC - 1 else None, psh[2 + jb][:, :],
                       slab_u[sbi]["h"][:, pj, rel * 128:(rel + 1) * 128],
                       z1["h"][:, pj, :], start=(pj == 0), stop=(pj == DC - 1),
                       bufs_r=[slab_u[sbi]["buf"], z1["buf"]],
                       bufs_w=[psb[2 + jb]] if pj == 0 else ())
                psb[2 + jb].note_write(st_.h, st_.n)
                op(em, "scalar", sc, "activation", s_scr[a % 2]["h"][:],
                   psh[jb][:, :], AF.Silu, scale=wsg_c["h"][:, a:a + 1],
                   bufs_r=[psb[jb], wsg_c["buf"]], bufs_w=[s_scr[a % 2]["buf"]])
                op(em, "vector", sv, "scalar_tensor_tensor", h_t[a % 2]["h"][:],
                   psh[2 + jb][:, :], wsuS_c["h"][:, a:a + 1], s_scr[a % 2]["h"][:],
                   ALU.mult, ALU.mult,
                   bufs_r=[psb[2 + jb], wsuS_c["buf"], s_scr[a % 2]["buf"]],
                   bufs_w=[h_t[a % 2]["buf"]])
                mm(em, st_, psh[4 + jb][:, :], h128b["h"][:], h_t[a % 2]["h"][:],
                   start=True, stop=True,
                   bufs_r=[h128b["buf"], h_t[a % 2]["buf"]], bufs_w=[psb[4 + jb]])
                op(em, "vector", sv, "tensor_copy", u2g["h"][:, a % 8, :],
                   psh[4 + jb][:, :], bufs_r=[psb[4 + jb]], bufs_w=[u2g["buf"]])
                if a % 8 == 7:
                    dma(em, "sync",
                        u2_dram.rearrange("(g p) t -> p g t", p=128)[
                            :, a - 7:a + 1, ts],
                        u2g["h"][:], bufs_r=[u2g["buf"]], bufs_w=[u2_dram_b])
            for grp in range(4):
                for pp_ in range(2):
                    dma(em, "sync", zin2["h"][64 * pp_:64 * (pp_ + 1), :, :],
                        u2d_r[pp_, :, 16 * grp:16 * (grp + 1), ts],
                        bufs_r=[u2_dram_b], bufs_w=[zin2["buf"]])
                for pjr in range(16):
                    b = pjr % 2
                    mm(em, st_, psh[b][:, :], bd64_s["h"][:],
                       zin2["h"][:, pjr, :], start=True, stop=True,
                       bufs_r=[bd64_s["buf"], zin2["buf"]], bufs_w=[psb[b]])
                    op(em, "vector", sv, "tensor_copy", z2["h"][:, pjr, :],
                       psh[b][:, :], bufs_r=[psb[b]], bufs_w=[z2["buf"]])
                for ocp in range(8):
                    sdb = ocp % 2
                    for pp_ in range(2):
                        dma(em, "sync",
                            slab_d[sdb]["h"][64 * pp_:64 * (pp_ + 1), :, :],
                            dnT_r[pp_, :, 16 * grp:16 * (grp + 1),
                                  ocp * 256:(ocp + 1) * 256],
                            bufs_r=[dnT_b], bufs_w=[slab_d[sdb]["buf"]])
                    for oc_rel in range(2):
                        oc = 2 * ocp + oc_rel
                        db = 4 + (oc % 2)
                        for pjr in range(16):
                            mm(em, st_ if pjr == 15 else None, psh[db][:, :],
                               slab_d[sdb]["h"][:, pjr,
                                                oc_rel * 128:(oc_rel + 1) * 128],
                               z2["h"][:, pjr, :], start=(pjr == 0),
                               stop=(pjr == 15),
                               bufs_r=[slab_d[sdb]["buf"], z2["buf"]],
                               bufs_w=[psb[db]] if pjr == 0 else ())
                        psb[db].note_write(st_.h, st_.n)
                        op(em, "vector", sv, "scalar_tensor_tensor",
                           X[:, oc, ts], psh[db][:, :], wdp_c["h"][:, oc:oc + 1],
                           X[:, oc, ts], ALU.mult, ALU.add,
                           bufs_r=[psb[db], wdp_c["buf"], Xbuf[tt]],
                           bufs_w=[Xbuf[tt]])

    full_barrier(em, sems, exclude=sh.get("prep_sems", ()))
    em.release(smark)



def declare_attn_l0(nc):
    H = 16
    P = {}
    P["w_qkv"] = nc.dram_tensor("w_qkv", [D, 3 * D], F32, kind="ExternalInput").ap()
    P["w_out"] = nc.dram_tensor("w_out", [D, D], F32, kind="ExternalInput").ap()
    P["decay"] = nc.dram_tensor("decay_i", [1, H], F32, kind="ExternalInput").ap()
    P["csel"] = nc.dram_tensor("csel_i", [128, 1], F32, kind="ExternalInput").ap()
    P["att_dram"] = nc.dram_tensor("att_dram", [D, T], BF16).ap()
    P["contrib"] = nc.dram_tensor("contrib0", [128, 4], F32)
    P["gathered"] = nc.dram_tensor("gathered0", [256, 4], F32)
    jj, tt_ = np.meshgrid(np.arange(128), np.arange(128), indexing="ij")
    E_np = np.maximum(tt_ - jj, 0).astype(np.float32)
    LT_np = (jj <= tt_).astype(np.float32)
    P["E_d"] = nc.inline_tensor(E_np, name="E_d").ap()
    P["LT_d"] = nc.inline_tensor(LT_np, name="LT_d").ap()
    P["iota_d"] = nc.inline_tensor(
        np.arange(1, T + 1, dtype=np.float32).reshape(1, T), name="iota_d").ap()
    P["zcol_d"] = nc.inline_tensor(np.zeros((128, 1), np.float32),
                                   name="zcol_d").ap()
    return P


def emit_attn_l0(nc, em, sems, sh, P, pre):
    H = 16
    st_, sv, sc, sp, scc = (sems[k] for k in ("st", "sv", "sc", "sp", "cc"))
    X, Xbuf, C = sh["X"], sh["Xbuf"], sh["C"]
    w_qkv, w_out = P["w_qkv"], P["w_out"]
    decay_d, csel_d = P["decay"], P["csel"]
    att_dram, contrib, gathered = P["att_dram"], P["contrib"], P["gathered"]
    E_d, LT_d, iota_d, zcol_d = P["E_d"], P["LT_d"], P["iota_d"], P["zcol_d"]
    ident_d = sh["ident_d"]
    contrib_buf = Buf()
    gathered_buf = Buf()
    smark = em.mark()
    with contextlib.ExitStack() as stack:
        ent = stack.enter_context

        def sb(name, shape, dtype):
            return {"h": ent(nc.sbuf_tensor(pre + name, shape, dtype)),
                    "buf": Buf(True)}

        att_dram_b = Buf()
        onc, onr = sh["onc"], sh["onr"]
        E_sb = sb("E_sb", [128, 128], F32)
        LT_sb = sb("LT_sb", [128, 128], F32)
        iota_sb = sb("iota", [1, T], F32)
        ident = sb("ident", [128, 128], BF16)
        zcol = sb("zcol", [128, 1], F32)
        csel = sb("csel", [128, 1], F32)
        dec = sb("dec", [1, 2, H], F32)
        Yh = ent(nc.sbuf_tensor(pre + "Y", [128, DC, T], BF16))
        Y = Yh[:]
        Ybuf = Buf()
        sq = [sb(f"sq{i}", [128, TT], F32) for i in range(2)]
        tmp = [sb(f"tmp{i}", [128, TT], F32) for i in range(2)]
        rows = sb("rows", [1, 2, TT], F32)
        Rb = sb("Rb", [128, TT], F32)
        MRb = sb("MRb", [128, TT], F32)

        cr = []
        for tile, srcd in ((E_sb, E_d),
                           (LT_sb, LT_d), (iota_sb, iota_d), (ident, ident_d),
                           (zcol, zcol_d), (csel, csel_d)):
            cr.append(dma(em, "sync", tile["h"][:], srcd, bufs_w=[tile["buf"]]))
        cr.append(dma(em, "sync", dec["h"][:1, 0, :], decay_d,
                      bufs_w=[dec["buf"]]))
        consts_ready = cr
        em.wait("tensor", cr)
        em.wait("scalar", cr)
        em.wait("vector", cr)
        C = {"ones_col": onc["h"][:], "ones_row": onr["h"][:]}

        op(em, "scalar", sc, "activation", dec["h"][:1, 0, :], dec["h"][:1, 0, :],
           AF.Sigmoid, bufs_r=[dec["buf"]], bufs_w=[dec["buf"]])
        op(em, "scalar", sc, "activation", dec["h"][:1, 1, :], dec["h"][:1, 0, :],
           AF.Ln, bufs_r=[dec["buf"]], bufs_w=[dec["buf"]])
        lnd = dec["h"]

        with contextlib.ExitStack() as lnsc:
            lps = [lnsc.enter_context(
                nc.psum_tensor(pre + f"lps{i}", [128, 512], F32))
                   for i in range(4)]
            lpsb = [Buf() for _ in range(4)]
            rh = rows["h"]
            rowaps = [rh[:1, 0, :], rh[:1, 1, :]]
            scr = {"ps": lps, "psb": lpsb, "sq": sq, "sqb": [s["buf"] for s in sq],
                   "rowaps": rowaps, "rowsb": rows["buf"],
                   "R": Rb, "MR": MRb, "tmp": tmp, "tmpb": [t["buf"] for t in tmp]}
            for tt in range(NTT):
                emit_ln(em, sems, X, Xbuf, Y[:, :, tt * TT:(tt + 1) * TT], Ybuf,
                        tt, C, scr)
        full_barrier(em, sems, exclude=sh.get("prep_sems", ()))

        wq_r = w_qkv.rearrange("(kc p) o -> p kc o", p=128)

        with contextlib.ExitStack() as body:
            def bsb(name, shape, dtype):
                return {"h": body.enter_context(
                    nc.sbuf_tensor(pre + name, shape, dtype)),
                        "buf": Buf(True)}
            q_sb = bsb("q_sb", [128, 2, T], BF16)
            k_sb = bsb("k_sb", [128, 2, T], BF16)
            kv_sb = bsb("kv_sb", [128, 2, T], BF16)
            v_sb = bsb("v_sb", [128, 2, T], BF16)
            S_scr = bsb("S_scr", [128, 4, T], F32)
            att_q = bsb("att_q", [128, 2, T], BF16)
            slab = bsb("slab", [128, DC, 256], BF16)
            phi_mn = sq
            phi_ex = tmp
            phi_rl = [Rb, MRb]
            xT = [bsb(f"xT{i}", [128, 128], BF16) for i in range(2)]
            A_bf = bsb("A_bf", [128, 2, 128], BF16)
            aexp = bsb("aexp", [128, 128], F32)
            powf = bsb("powf", [128, 2, T], BF16)
            prow = bsb("prow", [1, T], F32)
            u_scr = bsb("u_scr", [128, T], F32)
            t2_scr = bsb("t2_scr", [128, T], F32)
            carry = bsb("carry", [128, 4], F32)
            block0 = bsb("block0", [128, 4], F32)
            Ksel = bsb("Ksel", [128, 4], F32)
            den_sb = {"h": None, "buf": sq[0]["buf"]}
            rden_sb = {"h": None, "buf": tmp[0]["buf"]}
            den_half = [sq[0]["h"][0:1, :], sq[1]["h"][0:1, :]]
            rden_half = [tmp[0]["h"][0:1, :], tmp[1]["h"][0:1, :]]
            denb = [sq[0]["buf"], sq[1]["buf"]]
            rdenb = [tmp[0]["buf"], tmp[1]["buf"]]

            pj = [body.enter_context(
                nc.psum_tensor(pre + f"pj{i}", [128, 512], F32))
                  for i in range(2)]
            pjb = [Buf(), Buf()]
            tbf = [body.enter_context(
                nc.psum_tensor(pre + f"tb{i}", [128, 1024], BF16))
                   for i in range(2)]
            tbfb = [Buf(), Buf()]
            msc = [body.enter_context(
                nc.psum_tensor(pre + f"ms{i}", [128, 512], F32))
                   for i in range(2)]
            mscb = [Buf(), Buf()]

            lnd_b = bsb("lnd_b", [128, H], F32)
            mm(em, st_, msc[0][:, :H], C["ones_row"], lnd[:1, 1, :],
               start=True, stop=True, bufs_r=[dec["buf"], onr["buf"]],
               bufs_w=[mscb[0]])
            op(em, "vector", sv, "tensor_copy", lnd_b["h"][:], msc[0][:, :H],
               bufs_r=[mscb[0]], bufs_w=[lnd_b["buf"]])

            for hq in range(8):
                heads = (2 * hq, 2 * hq + 1)
                for kind, base in (("v", 2 * D), ("k", D), ("q", 0)):
                    span = base + 256 * hq
                    dma(em, "gpsimd", slab["h"][:],
                        wq_r[:, :, span:span + 256],
                        bufs_w=[slab["buf"]])
                    for tt in range(NTT):
                        ts = slice(tt * TT, (tt + 1) * TT)
                        for hr in range(2):
                            pb = (tt * 2 + hr) % 2
                            for kc in range(DC):
                                mm(em, st_ if kc == DC - 1 else None, pj[pb][:, :],
                                   slab["h"][:, kc, hr * 128:(hr + 1) * 128],
                                   Y[:, kc, ts], start=(kc == 0), stop=(kc == DC - 1),
                                   bufs_r=[slab["buf"], Ybuf],
                                   bufs_w=[pjb[pb]] if kc == 0 else ())
                            pjb[pb].note_write(st_.h, st_.n)
                            if kind == "v":
                                op(em, "vector", sv, "tensor_copy",
                                   v_sb["h"][:, hr, ts], pj[pb][:, :],
                                   bufs_r=[pjb[pb]], bufs_w=[v_sb["buf"]])
                            else:
                                dst = q_sb if kind == "q" else k_sb
                                op(em, "vector", sv, "tensor_scalar_min",
                                   phi_mn[hr]["h"][:], pj[pb][:, :], 0.0,
                                   bufs_r=[pjb[pb]], bufs_w=[phi_mn[hr]["buf"]])
                                op(em, "scalar", sc, "activation",
                                   phi_ex[hr]["h"][:], phi_mn[hr]["h"][:], AF.Exp,
                                   bufs_r=[phi_mn[hr]["buf"]],
                                   bufs_w=[phi_ex[hr]["buf"]])
                                op(em, "scalar", sc, "activation",
                                   phi_rl[hr]["h"][:], pj[pb][:, :], AF.Relu,
                                   bufs_r=[pjb[pb]], bufs_w=[phi_rl[hr]["buf"]])
                                op(em, "vector", sv, "tensor_add",
                                   dst["h"][:, hr, ts], phi_rl[hr]["h"][:],
                                   phi_ex[hr]["h"][:],
                                   bufs_r=[phi_rl[hr]["buf"], phi_ex[hr]["buf"]],
                                   bufs_w=[dst["buf"]])
                                if kind == "k":
                                    op(em, "gpsimd", sp, "tensor_mul",
                                       kv_sb["h"][:, hr, ts], k_sb["h"][:, hr, ts],
                                       v_sb["h"][:, hr, ts],
                                       bufs_r=[k_sb["buf"], v_sb["buf"]],
                                       bufs_w=[kv_sb["buf"]])

                for hr, h in enumerate(heads):
                    lnd_ap = lnd_b["h"][:, h:h + 1]
                    op(em, "scalar", sc, "activation", aexp["h"][:], E_sb["h"][:],
                       AF.Exp, scale=lnd_ap,
                       bufs_r=[E_sb["buf"], lnd_b["buf"]], bufs_w=[aexp["buf"]])
                    op(em, "vector", sv, "tensor_mul", A_bf["h"][:, hr, :],
                       aexp["h"][:], LT_sb["h"][:],
                       bufs_r=[aexp["buf"], LT_sb["buf"]], bufs_w=[A_bf["buf"]])
                    op(em, "scalar", sc, "activation", prow["h"][:], iota_sb["h"][:],
                       AF.Exp, scale=lnd[:1, 1, h:h + 1],
                       bufs_r=[iota_sb["buf"], dec["buf"]], bufs_w=[prow["buf"]])
                    for half in range(2):
                        mm(em, st_, msc[half][:, :], C["ones_row"],
                           prow["h"][:1, half * 512:(half + 1) * 512],
                           start=True, stop=True,
                           bufs_r=[prow["buf"], onr["buf"]], bufs_w=[mscb[half]])
                        op(em, "vector", sv, "tensor_copy",
                           powf["h"][:, hr, half * 512:(half + 1) * 512],
                           msc[half][:, :], bufs_r=[mscb[half]],
                           bufs_w=[powf["buf"]])

                    for kind_i, src in ((0, kv_sb), (1, k_sb)):
                        sidx = 2 * hr + kind_i
                        for tc in range(8):
                            tcs = slice(tc * 128, (tc + 1) * 128)
                            op(em, "tensor", st_, "transpose",
                               tbf[tc % 2][:, :128], src["h"][:, hr, tcs],
                               ident["h"][:],
                               bufs_r=[src["buf"], ident["buf"]],
                               bufs_w=[tbfb[tc % 2]])
                            op(em, "vector", sv, "tensor_copy", xT[tc % 2]["h"][:],
                               tbf[tc % 2][:, :128],
                               bufs_r=[tbfb[tc % 2]], bufs_w=[xT[tc % 2]["buf"]])
                            mm(em, st_, msc[tc % 2][:, :128], xT[tc % 2]["h"][:],
                               A_bf["h"][:, hr, :], start=True, stop=True,
                               bufs_r=[xT[tc % 2]["buf"], A_bf["buf"]],
                               bufs_w=[mscb[tc % 2]])
                            cprev = (S_scr["h"][:, sidx, tc * 128 - 1:tc * 128]
                                     if tc > 0 else zcol["h"][:])
                            op(em, "vector", sv, "scalar_tensor_tensor",
                               S_scr["h"][:, sidx, tcs],
                               powf["h"][:, hr, 0:128], cprev,
                               msc[tc % 2][:, :128], ALU.mult, ALU.add,
                               bufs_r=[mscb[tc % 2], powf["buf"], S_scr["buf"],
                                       zcol["buf"]],
                               bufs_w=[S_scr["buf"]])

                for hr in range(2):
                    for kind_i in range(2):
                        sidx = 2 * hr + kind_i
                        op(em, "vector", sv, "tensor_copy",
                           carry["h"][:, sidx:sidx + 1],
                           S_scr["h"][:, sidx, T - 1:T],
                           bufs_r=[S_scr["buf"]], bufs_w=[carry["buf"]])
                dma(em, "sync", contrib[:, :], carry["h"][:],
                    bufs_r=[carry["buf"]], bufs_w=[contrib_buf])
                em.emit("gpsimd", "collective_compute",
                        ("AllGather", ALU.bypass,
                         [[0, 1], [2, 3], [4, 5], [6, 7]],
                         [contrib.ap().opt()], [gathered.ap().opt()]),
                        {}, scc, 1, (), bufs_r=[contrib_buf],
                        bufs_w=[gathered_buf])
                dma(em, "sync", block0["h"][:], gathered[0:128, :],
                    bufs_r=[gathered_buf], bufs_w=[block0["buf"]])
                op(em, "vector", sv, "tensor_scalar_mul", Ksel["h"][:],
                   block0["h"][:], csel["h"][:],
                   bufs_r=[block0["buf"], csel["buf"]], bufs_w=[Ksel["buf"]])

                for hr, h in enumerate(heads):
                    for kind_i in range(2):
                        sidx = 2 * hr + kind_i
                        op(em, "vector", sv, "scalar_tensor_tensor",
                           S_scr["h"][:, sidx, :], powf["h"][:, hr, :],
                           Ksel["h"][:, sidx:sidx + 1], S_scr["h"][:, sidx, :],
                           ALU.mult, ALU.add,
                           bufs_r=[powf["buf"], Ksel["buf"], S_scr["buf"]],
                           bufs_w=[S_scr["buf"]])
                    op(em, "gpsimd", sp, "tensor_mul", u_scr["h"][:],
                       q_sb["h"][:, hr, :], S_scr["h"][:, 2 * hr + 1, :],
                       bufs_r=[q_sb["buf"], S_scr["buf"]], bufs_w=[u_scr["buf"]])
                    op(em, "vector", sv, "tensor_mul", t2_scr["h"][:],
                       q_sb["h"][:, hr, :], S_scr["h"][:, 2 * hr, :],
                       bufs_r=[q_sb["buf"], S_scr["buf"]], bufs_w=[t2_scr["buf"]])
                    for half in range(2):
                        hs = slice(half * 512, (half + 1) * 512)
                        mm(em, st_, msc[0][:1, :], C["ones_col"],
                           u_scr["h"][:, hs], start=True, stop=True,
                           bufs_r=[u_scr["buf"], onc["buf"]], bufs_w=[mscb[0]])
                        op(em, "vector", sv, "tensor_scalar_max",
                           den_half[half], msc[0][:1, :], 1e-6,
                           bufs_r=[mscb[0]], bufs_w=[denb[half]])
                        op(em, "vector", sv, "reciprocal", rden_half[half],
                           den_half[half],
                           bufs_r=[denb[half]], bufs_w=[rdenb[half]])
                        mm(em, st_, msc[1][:, :], C["ones_row"],
                           rden_half[half], start=True, stop=True,
                           bufs_r=[rdenb[half], onr["buf"]], bufs_w=[mscb[1]])
                        op(em, "vector", sv, "tensor_mul",
                           att_q["h"][:, hr, hs], t2_scr["h"][:, hs],
                           msc[1][:, :],
                           bufs_r=[t2_scr["buf"], mscb[1]], bufs_w=[att_q["buf"]])
                dma(em, "sync",
                    att_dram[256 * hq:256 * (hq + 1), :].rearrange(
                        "(r p) t -> p r t", p=128),
                    att_q["h"][:], bufs_r=[att_q["buf"]], bufs_w=[att_dram_b])

        full_barrier(em, sems, exclude=sh.get("prep_sems", ()))
        with contextlib.ExitStack() as osc:
            def osb(name, shape, dtype):
                return {"h": osc.enter_context(
                    nc.sbuf_tensor(pre + name, shape, dtype)),
                        "buf": Buf(True)}
            wslab = [osb(f"wo{i}", [128, DC, 256], BF16) for i in range(2)]
            arhs = [osb(f"ar{i}", [128, DC, TT], BF16) for i in range(2)]
            po = [osc.enter_context(
                nc.psum_tensor(pre + f"po{i}", [128, 512], F32))
                  for i in range(2)]
            pob = [Buf(), Buf()]
            wo_r = w_out.rearrange("(kc p) o -> p kc o", p=128)
            att_r = att_dram.rearrange("(kc p) t -> p kc t", p=128)
            for tt in range(NTT):
                ts = slice(tt * TT, (tt + 1) * TT)
                dma(em, "sync", arhs[tt % 2]["h"][:], att_r[:, :, ts],
                    bufs_r=[att_dram_b], bufs_w=[arhs[tt % 2]["buf"]])
                for sp_i in range(8):
                    dma(em, "gpsimd", wslab[sp_i % 2]["h"][:],
                        wo_r[:, :, 256 * sp_i:256 * (sp_i + 1)],
                        bufs_w=[wslab[sp_i % 2]["buf"]])
                    for oc_rel in range(2):
                        oc = 2 * sp_i + oc_rel
                        pb = oc % 2
                        for kc in range(DC):
                            mm(em, st_ if kc == DC - 1 else None, po[pb][:, :],
                               wslab[sp_i % 2]["h"][:, kc,
                                                    oc_rel * 128:(oc_rel + 1) * 128],
                               arhs[tt % 2]["h"][:, kc, :],
                               start=(kc == 0), stop=(kc == DC - 1),
                               bufs_r=[wslab[sp_i % 2]["buf"], arhs[tt % 2]["buf"]],
                               bufs_w=[pob[pb]] if kc == 0 else ())
                        pob[pb].note_write(st_.h, st_.n)
                        op(em, "vector", sv, "tensor_add",
                           X[:, oc, ts], po[pb][:, :], X[:, oc, ts],
                           bufs_r=[pob[pb], Xbuf[tt]], bufs_w=[Xbuf[tt]])

    full_barrier(em, sems, exclude=sh.get("prep_sems", ()))
    em.release(smark)



def declare_attn_l1(nc):
    H = 16
    BIG = 1e9
    P = {}
    P["w_dkv"] = nc.dram_tensor("w_dkv", [D, 128], F32, kind="ExternalInput").ap()
    P["w_dq"] = nc.dram_tensor("w_dq", [D, 128], F32, kind="ExternalInput").ap()
    P["w_kr"] = nc.dram_tensor("w_kr", [D, 64], F32, kind="ExternalInput").ap()
    P["w_qc"] = nc.dram_tensor("w_qc", [H, 128, 64], F32, kind="ExternalInput").ap()
    P["w_qr"] = nc.dram_tensor("w_qr", [H, 128, 64], F32, kind="ExternalInput").ap()
    P["w_uk"] = nc.dram_tensor("w_uk", [H, 128, 64], F32, kind="ExternalInput").ap()
    P["w_uv"] = nc.dram_tensor("w_uv", [H, 128, 128], F32, kind="ExternalInput").ap()
    P["w_o"] = nc.dram_tensor("w_o", [D, D], F32, kind="ExternalInput").ap()
    P["cosf"] = nc.dram_tensor("cosf_i", [64, T], BF16, kind="ExternalInput").ap()
    P["sinf"] = nc.dram_tensor("sinf_i", [64, T], BF16, kind="ExternalInput").ap()
    P["pc"] = nc.dram_tensor("pc_i", [128, 1], F32, kind="ExternalInput").ap()
    P["msel"] = nc.dram_tensor("msel_i", [128, 2], F32, kind="ExternalInput").ap()
    P["contrib"] = nc.dram_tensor("contrib1", [192, T], BF16)
    P["gathered"] = nc.dram_tensor("gathered1", [384, T], BF16)
    psw = np.zeros((64, 64), np.float32)
    for m in range(64):
        psw[m ^ 1, m] = 1.0
    P["pswap_d"] = nc.inline_tensor(psw.astype(nbf), name="pswap_d").ap()
    qq = np.arange(128)[:, None]
    kk = np.arange(512)[None, :]
    md = np.stack([np.where(kk <= 128 * qm + qq, 0.0, -BIG)
                   for qm in range(4)], axis=1).astype(np.float32)
    P["mdiag_d"] = nc.inline_tensor(md, name="mdiag_d").ap()
    return P


def emit_attn_l1(nc, em, sems, sh, P, pre):
    H = 16
    SC = 0.08838834764831845
    st_, sv, sc, sp, scc = (sems[k] for k in ("st", "sv", "sc", "sp", "cc"))
    X, Xbuf, C = sh["X"], sh["Xbuf"], sh["C"]
    w_dkv, w_dq, w_kr = P["w_dkv"], P["w_dq"], P["w_kr"]
    w_qc, w_qr, w_uk, w_uv, w_o = (P["w_qc"], P["w_qr"], P["w_uk"],
                                   P["w_uv"], P["w_o"])
    cosf_d, sinf_d, pc_d, msel_d = P["cosf"], P["sinf"], P["pc"], P["msel"]
    contrib, gathered = P["contrib"], P["gathered"]
    pswap_d, mdiag_d = P["pswap_d"], P["mdiag_d"]
    ident_d = sh["ident_d"]
    ones_row_d = sh["onr_d"]
    contrib_b, gathered_b = Buf(), Buf()
    smark = em.mark()
    with contextlib.ExitStack() as stack:
        ent = stack.enter_context

        def sb(name, shape, dtype):
            return {"h": ent(nc.sbuf_tensor(pre + name, shape, dtype)),
                    "buf": Buf(True)}

        onc = sh["onc"]
        ident = sb("ident", [128, 128], BF16)
        pswap = sb("pswap", [64, 64], BF16)
        cosf = sb("cosf", [64, T], BF16)
        sinf = sb("sinf", [64, T], BF16)
        pc = sb("pc", [128, 1], F32)
        msel = sb("msel", [128, 2], F32)
        wdkv_s = sb("wdkv_s", [128, DC, 128], BF16)
        wdq_s = sb("wdq_s", [128, DC, 128], BF16)
        wkr_s = sb("wkr_s", [128, DC, 64], BF16)
        wqc_s = sb("wqc_s", [128, H, 64], BF16)
        wqr_s = sb("wqr_s", [128, H, 64], BF16)
        qnr = sb("qnr", [128, H, T], BF16)
        ckv_of = sb("ckv_of", [128, 2 * T], BF16)
        att_sb = sb("att_sb", [128, DC, T], BF16)
        krf_of = {"h": att_sb["h"], "buf": att_sb["buf"]}
        krf_ap = att_sb["h"][0:64, 0:2, :]

        cr = []
        for tile, srcd in ((ident, ident_d),
                           (pswap, pswap_d),
                           (cosf, cosf_d), (sinf, sinf_d), (pc, pc_d),
                           (msel, msel_d)):
            cr.append(dma(em, "sync", tile["h"][:], srcd, bufs_w=[tile["buf"]]))
        for tile, srcd in (
                (wdkv_s, w_dkv.rearrange("(kc p) o -> p kc o", p=128)),
                (wdq_s, w_dq.rearrange("(kc p) o -> p kc o", p=128)),
                (wkr_s, w_kr.rearrange("(kc p) o -> p kc o", p=128)),
                (wqc_s, w_qc.rearrange("h p c -> p h c")),
                (wqr_s, w_qr.rearrange("h p c -> p h c"))):
            cr.append(dma(em, "gpsimd", tile["h"][:], srcd,
                          bufs_w=[tile["buf"]]))
        consts_ready = cr
        C = {"ones_col": onc["h"][:], "ones_row": None}

        em.wait("tensor", consts_ready)
        em.wait("scalar", consts_ready)
        em.wait("vector", consts_ready)

        with contextlib.ExitStack() as s1:
            def s1b(name, shape, dtype):
                return {"h": s1.enter_context(
                    nc.sbuf_tensor(pre + name, shape, dtype)),
                        "buf": Buf(True)}
            Yh = s1.enter_context(nc.sbuf_tensor(pre + "Y", [128, DC, TT], BF16))
            Y = Yh[:]
            Ybuf = Buf()
            sq0 = s1b("sq0", [128, TT], F32)
            tmp0 = s1b("tmp0", [128, TT], F32)
            sq = [sq0, sq0]
            tmp = [tmp0, tmp0]
            rows = s1b("rows", [1, 2, TT], F32)
            Rb = s1b("Rb", [128, TT], F32)
            MRb = s1b("MRb", [128, TT], F32)
            onr = sh["onr"]
            dq_bf = s1b("dq_bf", [128, T], BF16)
            ckv_bf = s1b("ckv_bf", [128, T], BF16)
            kr_bf = s1b("kr_bf", [64, T], BF16)
            krop_bf = s1b("krop_bf", [64, T], BF16)
            qr_bf = s1b("qr_bf", [64, TT], BF16)
            rsc1 = {"h": Rb["h"], "buf": Rb["buf"]}
            rsc2 = {"h": MRb["h"], "buf": MRb["buf"]}
            b0c, b1c = dq_bf, ckv_bf
            b0k, b1k = kr_bf, krop_bf

            lps = [s1.enter_context(
                nc.psum_tensor(pre + f"lps{i}", [128, 512], F32))
                   for i in range(8)]
            lpsb = [Buf() for _ in range(8)]
            rh = rows["h"]
            rowaps = [rh[:1, 0, :], rh[:1, 1, :]]
            scr = {"ps": lps, "psb": lpsb, "sq": sq,
                   "sqb": [s["buf"] for s in sq],
                   "rowaps": rowaps, "rowsb": rows["buf"],
                   "R": Rb, "MR": MRb, "tmp": tmp,
                   "tmpb": [t["buf"] for t in tmp]}

            for tt in range(NTT):
                ts = slice(tt * TT, (tt + 1) * TT)
                emit_ln(em, sems, X, Xbuf, Y, Ybuf, tt,
                        {"ones_col": onc["h"][:], "ones_row": onr["h"][:]}, scr)
                for dst, wsl, pb in ((ckv_bf, wdkv_s, 4), (dq_bf, wdq_s, 5)):
                    for kc in range(DC):
                        mm(em, st_ if kc == DC - 1 else None, lps[pb][:, :],
                           wsl["h"][:, kc, :], Y[:, kc, :],
                           start=(kc == 0), stop=(kc == DC - 1),
                           bufs_r=[wsl["buf"], Ybuf],
                           bufs_w=[lpsb[pb]] if kc == 0 else ())
                    lpsb[pb].note_write(st_.h, st_.n)
                    op(em, "vector", sv, "tensor_copy", dst["h"][:, ts],
                       lps[pb][:, :], bufs_r=[lpsb[pb]], bufs_w=[dst["buf"]])
                for kc in range(DC):
                    mm(em, st_ if kc == DC - 1 else None, lps[6][:64, :],
                       wkr_s["h"][:, kc, :], Y[:, kc, :],
                       start=(kc == 0), stop=(kc == DC - 1),
                       bufs_r=[wkr_s["buf"], Ybuf],
                       bufs_w=[lpsb[6]] if kc == 0 else ())
                lpsb[6].note_write(st_.h, st_.n)
                op(em, "vector", sv, "tensor_copy", kr_bf["h"][:, ts],
                   lps[6][:64, :], bufs_r=[lpsb[6]], bufs_w=[kr_bf["buf"]])
                mm(em, st_, lps[7][:64, :], pswap["h"][:], kr_bf["h"][:, ts],
                   start=True, stop=True,
                   bufs_r=[pswap["buf"], kr_bf["buf"]], bufs_w=[lpsb[7]])
                op(em, "vector", sv, "tensor_mul", rsc1["h"][:64, :],
                   lps[6][:64, :], cosf["h"][:, ts],
                   bufs_r=[lpsb[6], cosf["buf"]], bufs_w=[rsc1["buf"]])
                op(em, "vector", sv, "tensor_mul", rsc2["h"][:64, :],
                   lps[7][:64, :], sinf["h"][:, ts],
                   bufs_r=[lpsb[7], sinf["buf"]], bufs_w=[rsc2["buf"]])
                op(em, "vector", sv, "tensor_add", krop_bf["h"][:, ts],
                   rsc1["h"][:64, :], rsc2["h"][:64, :],
                   bufs_r=[rsc1["buf"], rsc2["buf"]], bufs_w=[krop_bf["buf"]])
                for h in range(H):
                    pb = h % 2
                    mm(em, st_, lps[pb][:64, :], wqc_s["h"][:, h, :],
                       dq_bf["h"][:, ts], start=True, stop=True,
                       bufs_r=[wqc_s["buf"], dq_bf["buf"]], bufs_w=[lpsb[pb]])
                    op(em, "vector", sv, "tensor_copy", qnr["h"][:64, h, ts],
                       lps[pb][:64, :], bufs_r=[lpsb[pb]], bufs_w=[qnr["buf"]])
                    mm(em, st_, lps[2 + pb][:64, :], wqr_s["h"][:, h, :],
                       dq_bf["h"][:, ts], start=True, stop=True,
                       bufs_r=[wqr_s["buf"], dq_bf["buf"]], bufs_w=[lpsb[2 + pb]])
                    op(em, "vector", sv, "tensor_copy", qr_bf["h"][:],
                       lps[2 + pb][:64, :], bufs_r=[lpsb[2 + pb]],
                       bufs_w=[qr_bf["buf"]])
                    mm(em, st_, lps[6][:64, :], pswap["h"][:], qr_bf["h"][:],
                       start=True, stop=True,
                       bufs_r=[pswap["buf"], qr_bf["buf"]], bufs_w=[lpsb[6]])
                    op(em, "vector", sv, "tensor_mul", rsc1["h"][:64, :],
                       lps[2 + pb][:64, :], cosf["h"][:, ts],
                       bufs_r=[lpsb[2 + pb], cosf["buf"]], bufs_w=[rsc1["buf"]])
                    op(em, "vector", sv, "tensor_mul", rsc2["h"][:64, :],
                       lps[6][:64, :], sinf["h"][:, ts],
                       bufs_r=[lpsb[6], sinf["buf"]], bufs_w=[rsc2["buf"]])
                    op(em, "vector", sv, "tensor_add", qnr["h"][64:128, h, ts],
                       rsc1["h"][:64, :], rsc2["h"][:64, :],
                       bufs_r=[rsc1["buf"], rsc2["buf"]], bufs_w=[qnr["buf"]])

            dma(em, "sync", contrib[0:128, :], ckv_bf["h"][:],
                bufs_r=[ckv_bf["buf"]], bufs_w=[contrib_b])
            dma(em, "sync", contrib[128:192, :], krop_bf["h"][:],
                bufs_r=[krop_bf["buf"]], bufs_w=[contrib_b])
            em.emit("gpsimd", "collective_compute",
                    ("AllGather", ALU.bypass,
                     [[0, 1], [2, 3], [4, 5], [6, 7]],
                     [contrib.ap().opt()], [gathered.ap().opt()]),
                    {}, scc, 1, (), bufs_r=[contrib_b], bufs_w=[gathered_b])
            dma(em, "sync", b0c["h"][:], gathered[0:128, :],
                bufs_r=[gathered_b], bufs_w=[b0c["buf"]])
            dma(em, "sync", b1c["h"][:], gathered[192:320, :],
                bufs_r=[gathered_b], bufs_w=[b1c["buf"]])
            dma(em, "sync", b0k["h"][:], gathered[128:192, :],
                bufs_r=[gathered_b], bufs_w=[b0k["buf"]])
            dma(em, "sync", b1k["h"][:], gathered[320:384, :],
                bufs_r=[gathered_b], bufs_w=[b1k["buf"]])
            ckv_ap = ckv_of["h"][:].rearrange("p (s t) -> p s t", s=2)
            for dstap, dstbuf, b0, b1, np_ in (
                    (ckv_ap, ckv_of["buf"], b0c, b1c, 128),
                    (krf_ap, krf_of["buf"], b0k, b1k, 64)):
                op(em, "vector", sv, "tensor_scalar_mul", dstap[:, 0, :],
                   b0["h"][:], msel["h"][:np_, 0:1],
                   bufs_r=[b0["buf"], msel["buf"]], bufs_w=[dstbuf])
                op(em, "vector", sv, "scalar_tensor_tensor", dstap[:, 0, :],
                   b1["h"][:], msel["h"][:np_, 1:2], dstap[:, 0, :],
                   ALU.mult, ALU.add,
                   bufs_r=[b1["buf"], msel["buf"], dstbuf],
                   bufs_w=[dstbuf])
                op(em, "vector", sv, "tensor_scalar_mul", dstap[:, 1, :],
                   b0["h"][:], msel["h"][:np_, 1:2],
                   bufs_r=[b0["buf"], msel["buf"]], bufs_w=[dstbuf])
                op(em, "vector", sv, "scalar_tensor_tensor", dstap[:, 1, :],
                   b1["h"][:], msel["h"][:np_, 0:1], dstap[:, 1, :],
                   ALU.mult, ALU.add,
                   bufs_r=[b1["buf"], msel["buf"], dstbuf],
                   bufs_w=[dstbuf])
        full_barrier(em, sems, exclude=sh.get("prep_sems", ()))

        with contextlib.ExitStack() as s2:
            def s2b(name, shape, dtype):
                return {"h": s2.enter_context(
                    nc.sbuf_tensor(pre + name, shape, dtype)),
                        "buf": Buf(True)}
            knr = s2b("knr", [128, 2 * T], BF16)
            v_h = s2b("v_h", [128, 16, 128], BF16)
            mdiag = s2b("mdiag", [128, 4, 512], F32)
            wuk_s = s2b("wuk_s", [128, H, 64], BF16)
            wuv_s = s2b("wuv_s", [128, H, 128], BF16)
            dma(em, "sync", mdiag["h"][:], mdiag_d, bufs_w=[mdiag["buf"]])
            dma(em, "gpsimd", wuk_s["h"][:], w_uk.rearrange("h p c -> p h c"),
                bufs_w=[wuk_s["buf"]])
            dma(em, "gpsimd", wuv_s["h"][:], w_uv.rearrange("h p c -> p h c"),
                bufs_w=[wuv_s["buf"]])
            P_sb = s2b("P_sb", [128, 4, 512], BF16)
            P_T = s2b("P_T", [128, 16, 128], BF16)
            av_bf = s2b("av_bf", [128, 128], BF16)
            cols = s2b("cols", [128, 16], F32)
            sps = [s2.enter_context(
                nc.psum_tensor(pre + f"sps{i}", [128, 512], F32))
                   for i in range(4)]
            spsb = [Buf() for _ in range(4)]
            tpb = [s2.enter_context(
                nc.psum_tensor(pre + f"tpb{i}", [128, 1024], BF16))
                   for i in range(2)]
            tpbb = [Buf(), Buf()]
            avp = [s2.enter_context(
                nc.psum_tensor(pre + f"avp{i}", [128, 512], F32))
                   for i in range(2)]
            avpb = [Buf(), Buf()]
            ch = cols["h"]
            mx_c = [ch[:, i:i + 1] for i in range(4)]
            own_c, prt_c, mxt_c = ch[:, 4:5], ch[:, 5:6], ch[:, 6:7]
            bias_c, bias2_c = ch[:, 7:8], ch[:, 8:9]
            den_c, rden_c = ch[:, 9:10], ch[:, 10:11]

            op(em, "vector", sv, "tensor_copy", knr["h"][64:128, :],
               att_sb["h"][0:64, 0:2, :], bufs_r=[krf_of["buf"]],
               bufs_w=[knr["buf"]])
            for h in range(H):
                for kt in range(4):
                    ks = slice(kt * 512, (kt + 1) * 512)
                    pb = kt % 2
                    mm(em, st_, sps[pb][:64, :], wuk_s["h"][:, h, :],
                       ckv_of["h"][:, ks], start=True, stop=True,
                       bufs_r=[wuk_s["buf"], ckv_of["buf"]], bufs_w=[spsb[pb]])
                    op(em, "vector", sv, "tensor_copy", knr["h"][0:64, ks],
                       sps[pb][:64, :], bufs_r=[spsb[pb]], bufs_w=[knr["buf"]])
                for kc in range(16):
                    pb = kc % 2
                    mm(em, st_, sps[pb][:, :128],
                       ckv_of["h"][:, kc * 128:(kc + 1) * 128],
                       wuv_s["h"][:, h, :], start=True, stop=True,
                       bufs_r=[ckv_of["buf"], wuv_s["buf"]], bufs_w=[spsb[pb]])
                    op(em, "vector", sv, "tensor_copy", v_h["h"][:, kc, :],
                       sps[pb][:, :128], bufs_r=[spsb[pb]], bufs_w=[v_h["buf"]])
                for qc in range(8):
                    nown = qc // 4 + 1
                    kts = list(range(nown)) + [2, 3]
                    qsl = qnr["h"][:, h, qc * 128:(qc + 1) * 128]
                    for i, kt in enumerate(kts):
                        koff = kt * 512
                        mm(em, st_, sps[i][:, :], qsl,
                           knr["h"][:, koff:koff + 512], start=True, stop=True,
                           bufs_r=[qnr["buf"], knr["buf"]], bufs_w=[spsb[i]])
                    op(em, "vector", sv, "tensor_add", sps[nown - 1][:, :],
                       sps[nown - 1][:, :], mdiag["h"][:, qc % 4, :],
                       bufs_r=[spsb[nown - 1], mdiag["buf"]],
                       bufs_w=[spsb[nown - 1]])
                    for i, kt in enumerate(kts):
                        op(em, "vector", sv, "reduce_max", mx_c[i],
                           sps[i][:, :], AX.X, bufs_r=[spsb[i]],
                           bufs_w=[cols["buf"]])
                    if nown == 1:
                        op(em, "vector", sv, "tensor_copy", own_c, mx_c[0],
                           bufs_r=[cols["buf"]], bufs_w=[cols["buf"]])
                    else:
                        op(em, "vector", sv, "tensor_max", own_c, mx_c[0],
                           mx_c[1], bufs_r=[cols["buf"]], bufs_w=[cols["buf"]])
                    op(em, "vector", sv, "tensor_max", prt_c, mx_c[nown],
                       mx_c[nown + 1], bufs_r=[cols["buf"]], bufs_w=[cols["buf"]])
                    op(em, "vector", sv, "tensor_sub", prt_c, prt_c,
                       pc["h"][:], bufs_r=[cols["buf"], pc["buf"]],
                       bufs_w=[cols["buf"]])
                    op(em, "vector", sv, "tensor_max", mxt_c, own_c, prt_c,
                       bufs_r=[cols["buf"]], bufs_w=[cols["buf"]])
                    op(em, "vector", sv, "tensor_scalar_mul", bias_c, mxt_c,
                       -SC, bufs_r=[cols["buf"]], bufs_w=[cols["buf"]])
                    op(em, "vector", sv, "tensor_sub", bias2_c, bias_c,
                       pc["h"][:], bufs_r=[cols["buf"], pc["buf"]],
                       bufs_w=[cols["buf"]])
                    for i, kt in enumerate(kts):
                        bias_ap = bias_c if kt < 2 else bias2_c
                        acc = ch[:, 11 + i:12 + i]
                        op(em, "scalar", sc, "activation", P_sb["h"][:, kt, :],
                           sps[i][:, :], AF.Exp, scale=SC, bias=bias_ap,
                           accum_out=acc,
                           bufs_r=[spsb[i], cols["buf"]],
                           bufs_w=[P_sb["buf"], cols["buf"]])
                    op(em, "vector", sv, "tensor_add", den_c, ch[:, 11:12],
                       ch[:, 12:13], bufs_r=[cols["buf"]], bufs_w=[cols["buf"]])
                    for i in range(2, len(kts)):
                        op(em, "vector", sv, "tensor_add", den_c, den_c,
                           ch[:, 11 + i:12 + i], bufs_r=[cols["buf"]],
                           bufs_w=[cols["buf"]])
                    op(em, "vector", sv, "reciprocal", rden_c, den_c,
                       bufs_r=[cols["buf"]], bufs_w=[cols["buf"]])
                    ocks = list(range(qc + 1))
                    for j, ck in enumerate(ocks):
                        op(em, "tensor", st_, "transpose",
                           tpb[0][:, j * 128:(j + 1) * 128],
                           P_sb["h"][:, ck // 4, (ck % 4) * 128:(ck % 4 + 1) * 128],
                           ident["h"][:],
                           bufs_r=[P_sb["buf"], ident["buf"]], bufs_w=[tpbb[0]])
                    op(em, "vector", sv, "tensor_copy",
                       P_T["h"][:, 0:qc + 1, :], tpb[0][:, 0:(qc + 1) * 128],
                       bufs_r=[tpbb[0]], bufs_w=[P_T["buf"]])
                    for j in range(8):
                        sl = 2 + j // 4
                        op(em, "tensor", st_, "transpose",
                           tpb[1][:, j * 128:(j + 1) * 128],
                           P_sb["h"][:, sl, (j % 4) * 128:(j % 4 + 1) * 128],
                           ident["h"][:],
                           bufs_r=[P_sb["buf"], ident["buf"]], bufs_w=[tpbb[1]])
                    op(em, "vector", sv, "tensor_copy",
                       P_T["h"][:, 8:16, :], tpb[1][:, :],
                       bufs_r=[tpbb[1]], bufs_w=[P_T["buf"]])
                    ab = qc % 2
                    cks = ocks + list(range(8, 16))
                    for j, ck in enumerate(cks):
                        mm(em, st_ if j == len(cks) - 1 else None,
                           avp[ab][:, :128], P_T["h"][:, ck % 16, :],
                           v_h["h"][:, ck % 16, :],
                           start=(j == 0), stop=(j == len(cks) - 1),
                           bufs_r=[P_T["buf"], v_h["buf"]],
                           bufs_w=[avpb[ab]] if j == 0 else ())
                    avpb[ab].note_write(st_.h, st_.n)
                    op(em, "vector", sv, "tensor_scalar_mul", av_bf["h"][:],
                       avp[ab][:, :128], rden_c,
                       bufs_r=[avpb[ab], cols["buf"]], bufs_w=[av_bf["buf"]])
                    op(em, "tensor", st_, "transpose", tpb[0][:, :128],
                       av_bf["h"][:], ident["h"][:],
                       bufs_r=[av_bf["buf"], ident["buf"]], bufs_w=[tpbb[0]])
                    op(em, "vector", sv, "tensor_copy",
                       att_sb["h"][:, h, qc * 128:(qc + 1) * 128],
                       tpb[0][:, :128], bufs_r=[tpbb[0]], bufs_w=[att_sb["buf"]])
        full_barrier(em, sems)

        with contextlib.ExitStack() as s3:
            def s3b(name, shape, dtype):
                return {"h": s3.enter_context(
                    nc.sbuf_tensor(pre + name, shape, dtype)),
                        "buf": Buf(True)}
            wslab = [s3b(f"wo{i}", [128, DC, 512], BF16) for i in range(2)]
            po = [s3.enter_context(
                nc.psum_tensor(pre + f"po{i}", [128, 512], F32))
                  for i in range(2)]
            pob = [Buf(), Buf()]
            wo_r = w_o.rearrange("(kc p) o -> p kc o", p=128)
            for tt in range(NTT):
                ts = slice(tt * TT, (tt + 1) * TT)
                for sp_i in range(4):
                    if tt == 0:
                        dma(em, "gpsimd", wslab[sp_i % 2]["h"][:],
                            wo_r[:, :, 512 * sp_i:512 * (sp_i + 1)],
                            bufs_w=[wslab[sp_i % 2]["buf"]])
                    for oc_rel in range(4):
                        oc = 4 * sp_i + oc_rel
                        pb = oc % 2
                        for kc in range(DC):
                            mm(em, st_ if kc == DC - 1 else None, po[pb][:, :],
                               wslab[sp_i % 2]["h"][:, kc,
                                                    oc_rel * 128:(oc_rel + 1) * 128],
                               att_sb["h"][:, kc, ts],
                               start=(kc == 0), stop=(kc == DC - 1),
                               bufs_r=[wslab[sp_i % 2]["buf"], att_sb["buf"]],
                               bufs_w=[pob[pb]] if kc == 0 else ())
                        pob[pb].note_write(st_.h, st_.n)
                        op(em, "vector", sv, "tensor_add",
                           X[:, oc, ts], po[pb][:, :], X[:, oc, ts],
                           bufs_r=[pob[pb], Xbuf[tt]], bufs_w=[Xbuf[tt]])

    full_barrier(em, sems, exclude=sh.get("prep_sems", ()))
    em.release(smark)


def build_fused():
    nc = bass.Bass("TRN2", target_bir_lowering=False)
    x_i = nc.dram_tensor("x_i", [D, T], F32, kind="ExternalInput").ap()
    x_o = nc.dram_tensor("x_o", [D, T], F32, kind="ExternalOutput").ap()
    P0f = declare_ffn_l0(nc)
    P0a = declare_attn_l0(nc)
    P1f = declare_ffn_l1(nc)
    P1a = declare_attn_l1(nc)
    ones_col_d = nc.inline_tensor(np.ones((128, 1), np.float32), name="onc_d").ap()
    ones_row_d = nc.inline_tensor(np.ones((1, 128), np.float32), name="onr_d").ap()
    ident_d = nc.inline_tensor(np.eye(128, dtype=nbf), name="ident_d").ap()

    em = Em(nc)
    with contextlib.ExitStack() as stack:
        ent = stack.enter_context
        em.pool = [Sem(ent(nc.semaphore(f"dp{i}"))) for i in range(88)]
        sems = {k: Sem(ent(nc.semaphore(k))) for k in
                ("st", "sv", "sc", "sp", "cc")}
        Xh = ent(nc.sbuf_tensor("X", [128, DC, T], F32))
        Xbuf = [Buf(True), Buf(True)]
        Xbuf[1].dsem = Xbuf[0].dsem = em.take_sem()
        onc = {"h": ent(nc.sbuf_tensor("onc", [128, 1], F32)), "buf": Buf(True)}
        onr = {"h": ent(nc.sbuf_tensor("onr", [1, 128], F32)), "buf": Buf(True)}
        dma(em, "sync", Xh[:], x_i.rearrange("(c p) t -> p c t", p=128),
            bufs_w=Xbuf)
        cr = [dma(em, "sync", onc["h"][:], ones_col_d, bufs_w=[onc["buf"]]),
              dma(em, "sync", onr["h"][:], ones_row_d, bufs_w=[onr["buf"]])]
        for e in ENGINES:
            em.wait(e, cr)
        cbuf = [{"h": ent(nc.sbuf_tensor(f"cbuf{i}", [128, 2048], BF16)),
                 "buf": Buf(True)} for i in range(2)]
        tslab = [{"h": ent(nc.sbuf_tensor(f"tsl{i}", [128, 2048], BF16)),
                  "buf": Buf(True)} for i in range(2)]
        sh = {"X": Xh[:], "Xbuf": Xbuf, "onc": onc, "onr": onr,
              "C": {"ones_col": onc["h"][:], "ones_row": onr["h"][:]},
              "ident_d": ident_d, "onr_d": ones_row_d}

        emit_weight_prep(em, P0f["w_gu"], 2 * INTER, D, P0f["gu_bf"],
                         P0f["gu_bf_b"], P0f["guT"], P0f["guT_b"], cbuf, tslab)
        emit_weight_prep(em, P0f["w_dn"], D, INTER, P0f["dn_bf"],
                         P0f["dn_bf_b"], P0f["dnT"], P0f["dnT_b"], cbuf, tslab)
        sh["prep_sems"] = [b["buf"].dsem for b in cbuf + tslab
                           if b["buf"].dsem is not None]
        emit_ffn_l0(nc, em, sems, sh, P0f, "fa_")
        emit_weight_prep(em, P1f["w_up"], INTER, D, P1f["up_bf"],
                         P1f["up_bf_b"], P1f["upT"], P1f["upT_b"], cbuf, tslab)
        emit_weight_prep(em, P1f["w_gt"], INTER, D, P1f["gt_bf"],
                         P1f["gt_bf_b"], P1f["gtT"], P1f["gtT_b"], cbuf, tslab)
        emit_weight_prep(em, P1f["w_dn1"], D, INTER, P1f["dn1_bf"],
                         P1f["dn1_bf_b"], P1f["dn1T"], P1f["dn1T_b"],
                         cbuf, tslab)
        emit_attn_l0(nc, em, sems, sh, P0a, "aa_")
        emit_ffn_l0(nc, em, sems, sh, P0f, "fb_")
        emit_ffn_l1(nc, em, sems, sh, P1f, "ga_")
        emit_attn_l1(nc, em, sems, sh, P1a, "ha_")
        emit_ffn_l1(nc, em, sems, sh, P1f, "gb_")

        fin = dma(em, "sync", x_o.rearrange("(c p) t -> p c t", p=128), Xh[:],
                  bufs_r=[Xbuf[0], Xbuf[1]])
        em.wait("sync", [fin])
        em.finalize()
    return nc



def shard_x(x):
    out = []
    for c in range(NCORE):
        b, h = c // 2, c % 2
        out.append(np.ascontiguousarray(
            np.asarray(x)[b, h * T:(h + 1) * T, :].T.astype(np.float32)))
    return out


def unshard_x(shards):
    x = np.empty((4, 2048, D), np.float32)
    for c in range(NCORE):
        b, h = c // 2, c % 2
        x[b, h * T:(h + 1) * T, :] = shards[c].T
    return x


_CACHE = {}


def _rope_tables(off):
    import math
    half = 32
    freqs = np.exp(np.arange(half, dtype=np.float32) * (-math.log(10000.0) / half))
    tg = np.arange(off, off + T, dtype=np.float32)
    ang = tg[:, None] * freqs[None, :]
    cos = np.cos(ang).astype(np.float32)
    sin = np.sin(ang).astype(np.float32)
    cosf = np.empty((64, T), np.float32)
    sinf = np.empty((64, T), np.float32)
    cosf[0::2] = cos.T
    cosf[1::2] = cos.T
    sinf[0::2] = -sin.T
    sinf[1::2] = sin.T
    return cosf, sinf


def kernel(**inputs):
    if "fused" not in _CACHE:
        _CACHE["fused"] = build_fused()
    xs = shard_x(inputs["L0_attn_x"])
    base = {
        "w_gu": np.asarray(inputs["L0_ffn_w_gate_up"]),
        "w_dn": np.asarray(inputs["L0_ffn_w_down"]),
        "w_qkv": np.asarray(inputs["L0_attn_w_qkv"]),
        "w_out": np.asarray(inputs["L0_attn_w_out"]),
        "decay_i": np.asarray(inputs["L0_attn_decay_param"]).reshape(1, 16)
                     .astype(np.float32),
        "w_up": np.asarray(inputs["L1_ffn_W_up"]),
        "w_gt": np.asarray(inputs["L1_ffn_W_gate"]),
        "w_dn1": np.asarray(inputs["L1_ffn_W_down"]),
        "su_ug": np.asarray(inputs["L1_ffn_SU_ug"]).reshape(16, 128),
        "su_dp": np.asarray(inputs["L1_ffn_SU_dp"]).reshape(64, 128),
        "wsc_ug": np.asarray(inputs["L1_ffn_Wscale_ug"]).reshape(128, 128),
        "wsc_dp": np.asarray(inputs["L1_ffn_Wscale_dp"]).reshape(16, 128),
        "w_dkv": np.asarray(inputs["L1_attn_W_dkv"]),
        "w_dq": np.asarray(inputs["L1_attn_W_dq"]),
        "w_kr": np.asarray(inputs["L1_attn_W_kr"]),
        "w_qc": np.asarray(inputs["L1_attn_W_qc"]),
        "w_qr": np.asarray(inputs["L1_attn_W_qr"]),
        "w_uk": np.asarray(inputs["L1_attn_W_uk"]),
        "w_uv": np.asarray(inputs["L1_attn_W_uv"]),
        "w_o": np.asarray(inputs["L1_attn_W_o"]),
    }
    in_maps = []
    for c in range(NCORE):
        p = c % 2
        cosf, sinf = _rope_tables(p * T)
        cosf = cosf.astype(nbf)
        sinf = sinf.astype(nbf)
        in_maps.append(dict(
            base, x_i=xs[c],
            csel_i=np.full((128, 1), float(p), np.float32),
            cosf_i=cosf, sinf_i=sinf,
            pc_i=np.full((128, 1), 1e9 if p == 0 else 0.0, np.float32),
            msel_i=np.tile(np.array([[1.0 - p, p]], np.float32), (128, 1)),
        ))
    res = run_bass_kernel_spmd(_CACHE["fused"], in_maps,
                               core_ids=list(range(NCORE)))
    return unshard_x([res.results[c]["x_o"] for c in range(NCORE)])


# version 07
# speedup vs baseline: 1.1111x; 1.0002x over previous
import contextlib
import numpy as np
import ml_dtypes
import concourse.bass as bass
import concourse.mybir as mybir
from concourse.bass_utils import run_bass_kernel_spmd

F32 = mybir.dt.float32
BF16 = mybir.dt.bfloat16
AF = mybir.ActivationFunctionType
ALU = mybir.AluOpType
AX = mybir.AxisListType
nbf = ml_dtypes.bfloat16

NCORE = 8
T = 1024
TT = 512
NTT = T // TT
D = 2048
DC = D // 128
INTER = 8192
EPS = 1e-5

ENGINES = ("sync", "tensor", "vector", "scalar", "gpsimd")



class Sem:
    __slots__ = ("h", "n")

    def __init__(self, handle):
        self.h = handle
        self.n = 0


class Buf:
    __slots__ = ("prod", "cons", "dma_owner", "dsem")

    def __init__(self, dma_owner=False):
        self.prod = {}
        self.cons = {}
        self.dma_owner = dma_owner
        self.dsem = None

    def read_waits(self, w):
        w.extend(self.prod.values())

    def note_read(self, h, v):
        k = id(h)
        if k not in self.cons or self.cons[k][1] < v:
            self.cons[k] = (h, v)

    def write_waits(self, w):
        w.extend(self.prod.values())
        w.extend(self.cons.values())

    def note_write(self, h, v):
        k = id(h)
        if k not in self.prod or self.prod[k][1] < v:
            self.prod[k] = (h, v)


class Em:
    def __init__(self, nc):
        self.nc = nc
        self.q = {e: [] for e in ENGINES}
        self.hw = {e: {} for e in ENGINES}
        self.pool = []
        self.used = []

    def take_sem(self):
        s = self.pool.pop()
        self.used.append(s)
        return s

    def mark(self):
        return len(self.used)

    def release(self, mark):
        self.pool.extend(self.used[mark:])
        del self.used[mark:]

    def _dedupe(self, eng, waits):
        out = []
        hw = self.hw[eng]
        for h, v in waits:
            k = id(h)
            if hw.get(k, -1) >= v:
                continue
            hw[k] = v
            out.append((h, v))
        return tuple(out)

    def emit(self, eng, method, args, kwargs, sem, inc, waits, bufs_r=(), bufs_w=()):
        w = []
        for b in bufs_r:
            b.read_waits(w)
        for b in bufs_w:
            b.write_waits(w)
        w.extend(waits)
        w = self._dedupe(eng, w)
        if sem is not None:
            sem.n += inc
            tgt = (sem.h, sem.n)
        else:
            tgt = None

        def f(e, method=method, args=args, kwargs=kwargs, w=w, tgt=tgt, inc=inc):
            for hh, vv in w:
                e.wait_ge(hh, vv)
            ins = getattr(e, method)(*args, **kwargs)
            if tgt is not None:
                ins.then_inc(tgt[0], inc)
        self.q[eng].append(f)
        if tgt is not None:
            for b in bufs_r:
                b.note_read(*tgt)
            for b in bufs_w:
                b.note_write(*tgt)
        return sem.n if sem is not None else None

    def wait(self, eng, waits):
        w = self._dedupe(eng, tuple(waits))
        if not w:
            return

        def f(e, w=w):
            for hh, vv in w:
                e.wait_ge(hh, vv)
        self.q[eng].append(f)

    def finalize(self):
        with self.nc.Block() as block:
            for eng in ENGINES:
                thunks = self.q[eng]
                if not thunks:
                    continue
                getattr(block, eng)(lambda e, th=tuple(thunks): [f(e) for f in th])


def dma(em, eng, out, in_, waits=(), bufs_r=(), bufs_w=(), transpose=False,
        sem=None):
    if sem is None:
        for b in tuple(bufs_w) + tuple(bufs_r):
            if b.dma_owner:
                if b.dsem is None:
                    b.dsem = em.take_sem()
                sem = b.dsem
                break
    assert sem is not None, "dma needs an owner buffer or explicit sem"
    kw = {"out": out, "in_": in_}
    if transpose:
        kw["transpose"] = True
    em.emit(eng, "dma_start", (), kw, sem, 16, waits, bufs_r, bufs_w)
    return (sem.h, sem.n)


def mm(em, sem, out, lhsT, rhs, start, stop, waits=(), bufs_r=(), bufs_w=()):
    return em.emit("tensor", "matmul", (out, lhsT, rhs),
                   {"start": start, "stop": stop}, sem, 1, waits, bufs_r, bufs_w)


def op(em, eng, sem, method, *args, waits=(), bufs_r=(), bufs_w=(), **kwargs):
    return em.emit(eng, method, args, kwargs, sem, 1, waits, bufs_r, bufs_w)


def full_barrier(em, sems, exclude=()):
    ex = set(id(s) for s in exclude)
    pts = [(s.h, s.n) for s in em.used if s.n > 0 and id(s) not in ex]
    pts += [(s.h, s.n) for s in em.pool if s.n > 0 and id(s) not in ex]
    pts += [(s.h, s.n) for s in sems.values() if s.n > 0]
    for eng in ENGINES:
        em.wait(eng, list(pts))



def emit_weight_prep(em, w_ap, rows, cols, bf_ap, bf_buf, t_ap, t_buf,
                     cbuf, tslab, tq="sync"):
    CW = min(2048, cols)
    i = 0
    for g in range(rows // 128):
        for cb in range(cols // CW):
            b = cbuf[i % 2]
            i += 1
            rs = slice(g * 128, (g + 1) * 128)
            cs = slice(cb * CW, (cb + 1) * CW)
            dma(em, "gpsimd", b["h"][:, :CW], w_ap[rs, cs], bufs_w=[b["buf"]])
            dma(em, "gpsimd", bf_ap[rs, cs], b["h"][:, :CW],
                bufs_r=[b["buf"]], bufs_w=[bf_buf])
    if t_ap is None:
        return
    nch = cols // 128
    RH = 2048
    nrh = (rows + RH - 1) // RH
    i = 0
    for ch in range(nch):
        for rh in range(nrh):
            r0, r1 = rh * RH, min((rh + 1) * RH, rows)
            b = tslab[i % 2]
            i += 1
            dma(em, tq, b["h"][:, :r1 - r0],
                bf_ap[r0:r1, ch * 128:(ch + 1) * 128],
                bufs_r=[bf_buf], bufs_w=[b["buf"]], transpose=True)
            dma(em, tq, t_ap[ch * 128:(ch + 1) * 128, r0:r1],
                b["h"][:, :r1 - r0], bufs_r=[b["buf"]], bufs_w=[t_buf])



def emit_ln(em, sems, X, Xbuf, Y, Ybuf, tt, C, scr):
    st_, sv, sc, sp = sems["st"], sems["sv"], sems["sc"], sems["sp"]
    ts = slice(tt * TT, (tt + 1) * TT)
    ps = scr["ps"]
    psb = scr["psb"]
    sq, sqb = scr["sq"], scr["sqb"]
    rowsb = scr["rowsb"]
    m_row, t_row = scr["rowaps"][0], scr["rowaps"][1]
    mr_row = m_row
    msq_row = v_row = sd_row = rstd_row = t_row

    for hc in range(DC):
        mm(em, st_ if hc == DC - 1 else None, ps[0][:1, :], C["ones_col"],
           X[:, hc, ts], start=(hc == 0), stop=(hc == DC - 1),
           bufs_r=[Xbuf[tt]], bufs_w=[psb[0]] if hc == 0 else ())
    s1 = (st_.h, st_.n)
    for hc in range(DC):
        op(em, "scalar", sc, "activation", sq[hc % 2]["h"][:], X[:, hc, ts],
           AF.Square, bufs_r=[Xbuf[tt]], bufs_w=[sqb[hc % 2]])
        mm(em, st_, ps[1][:1, :], C["ones_col"], sq[hc % 2]["h"][:],
           start=(hc == 0), stop=(hc == DC - 1),
           bufs_r=[sqb[hc % 2]], bufs_w=[psb[1]] if hc == 0 else ())
    s2 = (st_.h, st_.n)

    op(em, "vector", sv, "tensor_scalar_mul", m_row, ps[0][:1, :], 1.0 / D,
       waits=[s1], bufs_r=[psb[0]], bufs_w=[rowsb])
    op(em, "scalar", sc, "activation", msq_row, m_row, AF.Square,
       bufs_r=[rowsb], bufs_w=[rowsb])
    op(em, "vector", sv, "scalar_tensor_tensor", v_row, ps[1][:1, :], 1.0 / D,
       msq_row, ALU.mult, ALU.subtract, waits=[s2], bufs_r=[psb[1], rowsb],
       bufs_w=[rowsb])
    op(em, "vector", sv, "tensor_scalar_add", v_row, v_row, EPS,
       bufs_r=[rowsb], bufs_w=[rowsb])
    op(em, "scalar", sc, "activation", sd_row, v_row, AF.Sqrt,
       bufs_r=[rowsb], bufs_w=[rowsb])
    op(em, "vector", sv, "reciprocal", rstd_row, sd_row,
       bufs_r=[rowsb], bufs_w=[rowsb])
    op(em, "vector", sv, "tensor_mul", mr_row, m_row, rstd_row,
       bufs_r=[rowsb], bufs_w=[rowsb])

    mm(em, st_, ps[2][:, :], C["ones_row"], rstd_row, start=True, stop=True,
       bufs_r=[rowsb], bufs_w=[psb[2]])
    mm(em, st_, ps[3][:, :], C["ones_row"], mr_row, start=True, stop=True,
       bufs_r=[rowsb], bufs_w=[psb[3]])
    op(em, "vector", sv, "tensor_copy", scr["R"]["h"][:], ps[2][:, :],
       bufs_r=[psb[2]], bufs_w=[scr["R"]["buf"]])
    op(em, "vector", sv, "tensor_copy", scr["MR"]["h"][:], ps[3][:, :],
       bufs_r=[psb[3]], bufs_w=[scr["MR"]["buf"]])

    tmp, tmpb = scr["tmp"], scr["tmpb"]
    for hc in range(DC):
        op(em, "vector", sv, "tensor_mul", tmp[hc % 2]["h"][:], X[:, hc, ts],
           scr["R"]["h"][:], bufs_r=[Xbuf[tt], scr["R"]["buf"]],
           bufs_w=[tmpb[hc % 2]])
        op(em, "vector", sv, "tensor_sub", Y[:, hc, :], tmp[hc % 2]["h"][:],
           scr["MR"]["h"][:], bufs_r=[tmpb[hc % 2], scr["MR"]["buf"]],
           bufs_w=[Ybuf] if hc == 0 else (),)
        Ybuf.note_write(sv.h, sv.n)



def declare_ffn_l0(nc):
    P = {}
    P["w_gu"] = nc.dram_tensor("w_gu", [2 * INTER, D], F32, kind="ExternalInput").ap()
    P["w_dn"] = nc.dram_tensor("w_dn", [D, INTER], F32, kind="ExternalInput").ap()
    P["gu_bf"] = nc.dram_tensor("gu_bf", [2 * INTER, D], BF16).ap()
    P["dn_bf"] = nc.dram_tensor("dn_bf", [D, INTER], BF16).ap()
    P["guT"] = nc.dram_tensor("guT", [D, 2 * INTER], BF16).ap()
    P["dnT"] = nc.dram_tensor("dnT", [INTER, D], BF16).ap()
    P["gu_bf_b"], P["dn_bf_b"], P["guT_b"], P["dnT_b"] = \
        Buf(), Buf(), Buf(), Buf()
    return P


def emit_ffn_l0(nc, em, sems, sh, P, pre):
    st_, sv, sc, sp = (sems[k] for k in ("st", "sv", "sc", "sp"))
    X, Xbuf, C = sh["X"], sh["Xbuf"], sh["C"]
    w_gu, w_dn = P["w_gu"], P["w_dn"]
    gu_bf, dn_bf, guT, dnT = P["gu_bf"], P["dn_bf"], P["guT"], P["dnT"]
    gu_bf_b, dn_bf_b, guT_b, dnT_b = (P["gu_bf_b"], P["dn_bf_b"],
                                      P["guT_b"], P["dnT_b"])
    smark = em.mark()
    with contextlib.ExitStack() as stack:
        ent = stack.enter_context

        def sb(name, shape, dtype):
            return {"h": ent(nc.sbuf_tensor(pre + name, shape, dtype)),
                    "buf": Buf(True)}

        Yh = ent(nc.sbuf_tensor(pre + "Y", [128, DC, TT], BF16))
        Y = Yh[:]
        Ybuf = Buf()
        hbuf = sb("hbuf", [128, 16, TT], BF16)
        slab_g = [sb(f"slg{i}", [128, DC, 256], BF16) for i in range(2)]
        slab_u = [sb(f"slu{i}", [128, DC, 256], BF16) for i in range(2)]
        slab_d = [sb(f"sld{i}", [128, 16, 256], BF16) for i in range(2)]
        sq = [sb(f"sq{i}", [128, TT], F32) for i in range(2)]
        s_scr = [sb(f"ss{i}", [128, TT], F32) for i in range(2)]
        tmp = [sb(f"tmp{i}", [128, TT], F32) for i in range(2)]
        rows = sb("rows", [1, 2, TT], F32)
        Rb = sb("Rb", [128, TT], F32)
        MRb = sb("MRb", [128, TT], F32)
        psh = [ent(nc.psum_tensor(pre + f"ps{i}", [128, 512], F32))
               for i in range(8)]
        psb = [Buf() for _ in range(8)]

        rh = rows["h"]
        rowaps = [rh[:1, 0, :], rh[:1, 1, :]]
        scr = {"ps": psh, "psb": psb, "sq": sq, "sqb": [s["buf"] for s in sq],
               "rowaps": rowaps, "rowsb": rows["buf"],
               "R": Rb, "MR": MRb, "tmp": tmp, "tmpb": [t["buf"] for t in tmp]}

        guT_r = guT.rearrange("(hc p) o -> p hc o", p=128)
        dnT_r = dnT.rearrange("(ic p) h -> p ic h", p=128)

        for tt in range(NTT):
            emit_ln(em, sems, X, Xbuf, Y, Ybuf, tt, C, scr)
            for q in range(4):
                for i16 in range(16):
                    ic = q * 16 + i16
                    j = ic // 2
                    rel = ic % 2
                    if rel == 0:
                        dma(em, "sync", slab_g[j % 2]["h"][:],
                            guT_r[:, :, 256 * j:256 * (j + 1)],
                            bufs_r=[guT_b], bufs_w=[slab_g[j % 2]["buf"]])
                        dma(em, "sync", slab_u[j % 2]["h"][:],
                            guT_r[:, :, INTER + 256 * j:INTER + 256 * (j + 1)],
                            bufs_r=[guT_b], bufs_w=[slab_u[j % 2]["buf"]])
                    b = ic % 2
                    for hc in range(DC):
                        mm(em, st_ if hc == DC - 1 else None, psh[b][:, :],
                           slab_g[j % 2]["h"][:, hc, rel * 128:(rel + 1) * 128],
                           Y[:, hc, :], start=(hc == 0), stop=(hc == DC - 1),
                           bufs_r=[slab_g[j % 2]["buf"], Ybuf],
                           bufs_w=[psb[b]] if hc == 0 else ())
                    psb[b].note_write(st_.h, st_.n)
                    for hc in range(DC):
                        mm(em, st_ if hc == DC - 1 else None, psh[2 + b][:, :],
                           slab_u[j % 2]["h"][:, hc, rel * 128:(rel + 1) * 128],
                           Y[:, hc, :], start=(hc == 0), stop=(hc == DC - 1),
                           bufs_r=[slab_u[j % 2]["buf"], Ybuf],
                           bufs_w=[psb[2 + b]] if hc == 0 else ())
                    psb[2 + b].note_write(st_.h, st_.n)
                    op(em, "scalar", sc, "activation", s_scr[ic % 2]["h"][:],
                       psh[b][:, :], AF.Silu,
                       bufs_r=[psb[b]], bufs_w=[s_scr[ic % 2]["buf"]])
                    op(em, "vector", sv, "scalar_tensor_tensor",
                       hbuf["h"][:, i16, :], psh[2 + b][:, :], 1.0,
                       s_scr[ic % 2]["h"][:], ALU.mult, ALU.mult,
                       bufs_r=[psb[2 + b], s_scr[ic % 2]["buf"]],
                       bufs_w=[hbuf["buf"]] if i16 == 0 else ())
                    hbuf["buf"].note_write(sv.h, sv.n)
                for ocp in range(8):
                    dma(em, "sync", slab_d[ocp % 2]["h"][:],
                        dnT_r[:, q * 16:(q + 1) * 16, ocp * 256:(ocp + 1) * 256],
                        bufs_r=[dnT_b], bufs_w=[slab_d[ocp % 2]["buf"]])
                    for oc_rel in range(2):
                        oc = 2 * ocp + oc_rel
                        db = 4 + (oc % 2)
                        for i16 in range(16):
                            mm(em, st_ if i16 == 15 else None, psh[db][:, :],
                               slab_d[ocp % 2]["h"][:, i16,
                                                    oc_rel * 128:(oc_rel + 1) * 128],
                               hbuf["h"][:, i16, :], start=(i16 == 0),
                               stop=(i16 == 15),
                               bufs_r=[slab_d[ocp % 2]["buf"], hbuf["buf"]],
                               bufs_w=[psb[db]] if i16 == 0 else ())
                        psb[db].note_write(st_.h, st_.n)
                        op(em, "vector", sv, "scalar_tensor_tensor",
                           X[:, oc, tt * TT:(tt + 1) * TT], psh[db][:, :], 0.5,
                           X[:, oc, tt * TT:(tt + 1) * TT], ALU.mult, ALU.add,
                           bufs_r=[psb[db], Xbuf[tt]], bufs_w=[Xbuf[tt]])

    full_barrier(em, sems, exclude=sh.get("prep_sems", ()))
    em.release(smark)



def _had(n):
    h = np.array([[1.0]], np.float32)
    while h.shape[0] < n:
        h = np.block([[h, h], [h, -h]])
    return h


def declare_ffn_l1(nc):
    P = {}
    P["w_up"] = nc.dram_tensor("w_up", [INTER, D], F32, kind="ExternalInput").ap()
    P["w_gt"] = nc.dram_tensor("w_gt", [INTER, D], F32, kind="ExternalInput").ap()
    P["w_dn1"] = nc.dram_tensor("w_dn1", [D, INTER], F32, kind="ExternalInput").ap()
    P["su_ug"] = nc.dram_tensor("su_ug", [16, 128], F32, kind="ExternalInput").ap()
    P["su_dp"] = nc.dram_tensor("su_dp", [64, 128], F32, kind="ExternalInput").ap()
    P["wsc_ug"] = nc.dram_tensor("wsc_ug", [128, 128], F32, kind="ExternalInput").ap()
    P["wsc_dp"] = nc.dram_tensor("wsc_dp", [16, 128], F32, kind="ExternalInput").ap()
    P["up_bf"] = nc.dram_tensor("up_bf", [INTER, D], BF16).ap()
    P["gt_bf"] = nc.dram_tensor("gt_bf", [INTER, D], BF16).ap()
    P["dn1_bf"] = nc.dram_tensor("dn1_bf", [D, INTER], BF16).ap()
    P["upT"] = nc.dram_tensor("upT", [D, INTER], BF16).ap()
    P["gtT"] = nc.dram_tensor("gtT", [D, INTER], BF16).ap()
    P["dn1T"] = nc.dram_tensor("dn1T", [INTER, D], BF16).ap()
    P["u_dram"] = nc.dram_tensor("u_dram", [D, T], BF16).ap()
    P["u2_dram"] = nc.dram_tensor("u2_dram", [INTER, T], BF16).ap()
    for k in ("up_bf_b", "gt_bf_b", "dn1_bf_b", "upT_b", "gtT_b", "dn1T_b",
              "u_dram_b", "u2_dram_b"):
        P[k] = Buf()
    H128 = _had(128)
    H16 = _had(16)
    H64 = _had(64)
    P["h128a_d"] = nc.inline_tensor((H128 / 32.0).astype(nbf), name="h128a_d").ap()
    P["h128b_d"] = nc.inline_tensor((H128 / 64.0).astype(nbf), name="h128b_d").ap()
    bd16 = np.zeros((128, 128), np.float32)
    for pp in range(8):
        bd16[16 * pp:16 * (pp + 1), 16 * pp:16 * (pp + 1)] = H16 / np.sqrt(2.0)
    P["bd16_d"] = nc.inline_tensor(bd16.astype(nbf), name="bd16_d").ap()
    bd64 = np.zeros((128, 128), np.float32)
    for pp in range(2):
        bd64[64 * pp:64 * (pp + 1), 64 * pp:64 * (pp + 1)] = H64 / np.sqrt(2.0)
    P["bd64_d"] = nc.inline_tensor(bd64.astype(nbf), name="bd64_d").ap()
    P["identf_d"] = nc.inline_tensor(np.eye(128, dtype=np.float32),
                                     name="identf_d").ap()
    return P


def emit_ffn_l1(nc, em, sems, sh, P, pre):
    st_, sv, sc, sp = (sems[k] for k in ("st", "sv", "sc", "sp"))
    X, Xbuf, C = sh["X"], sh["Xbuf"], sh["C"]
    w_up, w_gt, w_dn = P["w_up"], P["w_gt"], P["w_dn1"]
    up_bf, gt_bf, dn_bf = P["up_bf"], P["gt_bf"], P["dn1_bf"]
    upT, gtT, dnT = P["upT"], P["gtT"], P["dn1T"]
    up_bf_b, gt_bf_b, dn_bf_b = P["up_bf_b"], P["gt_bf_b"], P["dn1_bf_b"]
    upT_b, gtT_b, dnT_b = P["upT_b"], P["gtT_b"], P["dn1T_b"]
    u_dram, u2_dram = P["u_dram"], P["u2_dram"]
    u_dram_b, u2_dram_b = P["u_dram_b"], P["u2_dram_b"]
    su_ug_d, su_dp_d = P["su_ug"], P["su_dp"]
    wsc_ug_d, wsc_dp_d = P["wsc_ug"], P["wsc_dp"]
    smark = em.mark()
    with contextlib.ExitStack() as stack:
        ent = stack.enter_context

        def sb(name, shape, dtype):
            return {"h": ent(nc.sbuf_tensor(pre + name, shape, dtype)),
                    "buf": Buf(True)}

        onc, onr = sh["onc"], sh["onr"]
        identf = sb("identf", [128, 128], F32)
        h128a = sb("h128a", [128, 128], BF16)
        h128b = sb("h128b", [128, 128], BF16)
        bd16_s = sb("bd16_s", [128, 128], BF16)
        bd64_s = sb("bd64_s", [128, 128], BF16)
        vrow = sb("vrow", [128, 128], F32)
        su_ug_c = sb("su_ug_c", [128, 16], F32)
        wsuS_c = sb("wsuS_c", [128, 64], F32)
        wsg_c = sb("wsg_c", [128, 64], F32)
        wdp_c = sb("wdp_c", [128, 16], F32)
        su_dp_c = sb("su_dp_c", [128, 64], F32)
        h128su = sb("h128su", [128, DC, 128], BF16)

        cr = []
        for tile, srcd in ((identf, P["identf_d"]), (h128a, P["h128a_d"]),
                           (h128b, P["h128b_d"]), (bd16_s, P["bd16_d"]),
                           (bd64_s, P["bd64_d"])):
            cr.append(dma(em, "sync", tile["h"][:], srcd, bufs_w=[tile["buf"]]))
        consts_ready = cr
        em.wait("tensor", cr)

        with contextlib.ExitStack() as vsc:
            vps = vsc.enter_context(nc.psum_tensor(pre + "vps", [128, 512], F32))
            vpsb = Buf()
            for rows_n, src, dstc in ((16, su_ug_d, su_ug_c), (64, su_dp_d, su_dp_c),
                                      (16, wsc_dp_d, wdp_c)):
                dma(em, "sync", vrow["h"][:rows_n, :], src,
                    bufs_w=[vrow["buf"]])
                op(em, "tensor", st_, "transpose", vps[:, :rows_n],
                   vrow["h"][:rows_n, :], identf["h"][:rows_n, :rows_n],
                   bufs_r=[vrow["buf"], identf["buf"]], bufs_w=[vpsb])
                op(em, "vector", sv, "tensor_copy", dstc["h"][:], vps[:, :rows_n],
                   bufs_r=[vpsb], bufs_w=[dstc["buf"]])
            dma(em, "sync", vrow["h"][:, :], wsc_ug_d, bufs_w=[vrow["buf"]])
            op(em, "tensor", st_, "transpose", vps[:, :128], vrow["h"][:, :],
               identf["h"][:], bufs_r=[vrow["buf"], identf["buf"]], bufs_w=[vpsb])
            op(em, "vector", sv, "tensor_copy", wsuS_c["h"][:], vps[:, :64],
               bufs_r=[vpsb], bufs_w=[wsuS_c["buf"]])
            op(em, "vector", sv, "tensor_copy", wsg_c["h"][:], vps[:, 64:128],
               bufs_r=[vpsb], bufs_w=[wsg_c["buf"]])
            op(em, "vector", sv, "tensor_mul", wsuS_c["h"][:], wsuS_c["h"][:],
               su_dp_c["h"][:], bufs_r=[wsuS_c["buf"], su_dp_c["buf"]],
               bufs_w=[wsuS_c["buf"]])
            op(em, "vector", sv, "tensor_scalar_mul", wdp_c["h"][:],
               wdp_c["h"][:], 0.5, bufs_r=[wdp_c["buf"]], bufs_w=[wdp_c["buf"]])
            for c in range(DC):
                op(em, "vector", sv, "tensor_scalar_mul",
                   h128su["h"][:, c, :], h128a["h"][:], su_ug_c["h"][:, c:c + 1],
                   bufs_r=[h128a["buf"], su_ug_c["buf"]], bufs_w=[h128su["buf"]])
        full_barrier(em, sems, exclude=sh.get("prep_sems", ()))

        Yh = ent(nc.sbuf_tensor(pre + "Y", [128, DC, TT], BF16))
        Y = Yh[:]
        Ybuf = Buf()
        sq = [sb(f"sq{i}", [128, TT], F32) for i in range(2)]
        tmp = [sb(f"tmp{i}", [128, TT], F32) for i in range(2)]
        rows = sb("rows", [1, 2, TT], F32)
        Rb = sb("Rb", [128, TT], F32)
        MRb = sb("MRb", [128, TT], F32)
        u_sb = sb("u_sb", [128, DC, TT], BF16)
        z1 = {"h": Yh, "buf": None}
        slab_u = [sb(f"slab_u{i}", [128, DC, 256], BF16) for i in range(2)]
        slab_g = [sb(f"slab_g{i}", [128, DC, 256], BF16) for i in range(2)]
        slab_d = [sb(f"sld{i}", [128, 16, 256], BF16) for i in range(2)]
        h_t = [sb(f"h_t{i}", [128, TT], BF16) for i in range(2)]
        s_scr = [sb(f"ss{i}", [128, TT], F32) for i in range(2)]
        u2g = sb("u2g", [128, 8, TT], BF16)
        zin2 = u_sb
        z2 = z1
        psh = [ent(nc.psum_tensor(pre + f"ps{i}", [128, 512], F32))
               for i in range(8)]
        psb = [Buf() for _ in range(8)]
        z1["buf"] = Ybuf

        rh = rows["h"]
        rowaps = [rh[:1, 0, :], rh[:1, 1, :]]
        scr = {"ps": psh, "psb": psb, "sq": sq, "sqb": [s["buf"] for s in sq],
               "rowaps": rowaps, "rowsb": rows["buf"],
               "R": Rb, "MR": MRb, "tmp": tmp, "tmpb": [t["buf"] for t in tmp]}

        upT_r = upT.rearrange("(c2 pj pp) i -> pp c2 pj i", c2=16, pj=16, pp=8)
        gtT_r = gtT.rearrange("(c2 pj pp) i -> pp c2 pj i", c2=16, pj=16, pp=8)
        dnT_r = dnT.rearrange("(a2 pj pp) h -> pp a2 pj h", a2=64, pj=64, pp=2)
        ud_r = u_dram.rearrange("(c pj pp) t -> pp c pj t", c=16, pj=16, pp=8)
        u2d_r = u2_dram.rearrange("(a pj pp) t -> pp a pj t", a=64, pj=64, pp=2)

        for tt in range(NTT):
            ts = slice(tt * TT, (tt + 1) * TT)
            emit_ln(em, sems, X, Xbuf, Y, Ybuf, tt, {"ones_col": onc["h"][:],
                    "ones_row": onr["h"][:]}, scr)
            for c in range(DC):
                b = c % 2
                mm(em, st_, psh[b][:, :], h128su["h"][:, c, :], Y[:, c, :],
                   start=True, stop=True,
                   bufs_r=[h128su["buf"], Ybuf], bufs_w=[psb[b]])
                op(em, "vector", sv, "tensor_copy", u_sb["h"][:, c, :],
                   psh[b][:, :], bufs_r=[psb[b]], bufs_w=[u_sb["buf"]])
            dma(em, "sync", u_dram.rearrange("(c p) t -> p c t", p=128)[:, :, ts],
                u_sb["h"][:], bufs_r=[u_sb["buf"]], bufs_w=[u_dram_b])
            dma(em, "sync", u_sb["h"][:], ud_r[:, :, :, ts],
                bufs_r=[u_dram_b], bufs_w=[u_sb["buf"]])
            for pj in range(DC):
                b = pj % 2
                mm(em, st_, psh[b][:, :], bd16_s["h"][:], u_sb["h"][:, pj, :],
                   start=True, stop=True,
                   bufs_r=[bd16_s["buf"], u_sb["buf"]], bufs_w=[psb[b]])
                op(em, "vector", sv, "tensor_copy", z1["h"][:, pj, :],
                   psh[b][:, :], bufs_r=[psb[b]], bufs_w=[z1["buf"]])
            for a in range(64):
                jb = a % 2
                if a % 2 == 0:
                    sbi = (a // 2) % 2
                    dma(em, "sync", slab_u[sbi]["h"][:],
                        upT_r[:, :, :, 128 * a:128 * (a + 2)],
                        bufs_r=[upT_b], bufs_w=[slab_u[sbi]["buf"]])
                    dma(em, "sync", slab_g[sbi]["h"][:],
                        gtT_r[:, :, :, 128 * a:128 * (a + 2)],
                        bufs_r=[gtT_b], bufs_w=[slab_g[sbi]["buf"]])
                sbi = (a // 2) % 2
                rel = a % 2
                for pj in range(DC):
                    mm(em, st_ if pj == DC - 1 else None, psh[jb][:, :],
                       slab_g[sbi]["h"][:, pj, rel * 128:(rel + 1) * 128],
                       z1["h"][:, pj, :], start=(pj == 0), stop=(pj == DC - 1),
                       bufs_r=[slab_g[sbi]["buf"], z1["buf"]],
                       bufs_w=[psb[jb]] if pj == 0 else ())
                psb[jb].note_write(st_.h, st_.n)
                for pj in range(DC):
                    mm(em, st_ if pj == DC - 1 else None, psh[2 + jb][:, :],
                       slab_u[sbi]["h"][:, pj, rel * 128:(rel + 1) * 128],
                       z1["h"][:, pj, :], start=(pj == 0), stop=(pj == DC - 1),
                       bufs_r=[slab_u[sbi]["buf"], z1["buf"]],
                       bufs_w=[psb[2 + jb]] if pj == 0 else ())
                psb[2 + jb].note_write(st_.h, st_.n)
                op(em, "scalar", sc, "activation", s_scr[a % 2]["h"][:],
                   psh[jb][:, :], AF.Silu, scale=wsg_c["h"][:, a:a + 1],
                   bufs_r=[psb[jb], wsg_c["buf"]], bufs_w=[s_scr[a % 2]["buf"]])
                op(em, "vector", sv, "scalar_tensor_tensor", h_t[a % 2]["h"][:],
                   psh[2 + jb][:, :], wsuS_c["h"][:, a:a + 1], s_scr[a % 2]["h"][:],
                   ALU.mult, ALU.mult,
                   bufs_r=[psb[2 + jb], wsuS_c["buf"], s_scr[a % 2]["buf"]],
                   bufs_w=[h_t[a % 2]["buf"]])
                mm(em, st_, psh[4 + jb][:, :], h128b["h"][:], h_t[a % 2]["h"][:],
                   start=True, stop=True,
                   bufs_r=[h128b["buf"], h_t[a % 2]["buf"]], bufs_w=[psb[4 + jb]])
                op(em, "vector", sv, "tensor_copy", u2g["h"][:, a % 8, :],
                   psh[4 + jb][:, :], bufs_r=[psb[4 + jb]], bufs_w=[u2g["buf"]])
                if a % 8 == 7:
                    dma(em, "sync",
                        u2_dram.rearrange("(g p) t -> p g t", p=128)[
                            :, a - 7:a + 1, ts],
                        u2g["h"][:], bufs_r=[u2g["buf"]], bufs_w=[u2_dram_b])
            for grp in range(4):
                for pp_ in range(2):
                    dma(em, "sync", zin2["h"][64 * pp_:64 * (pp_ + 1), :, :],
                        u2d_r[pp_, :, 16 * grp:16 * (grp + 1), ts],
                        bufs_r=[u2_dram_b], bufs_w=[zin2["buf"]])
                for pjr in range(16):
                    b = pjr % 2
                    mm(em, st_, psh[b][:, :], bd64_s["h"][:],
                       zin2["h"][:, pjr, :], start=True, stop=True,
                       bufs_r=[bd64_s["buf"], zin2["buf"]], bufs_w=[psb[b]])
                    op(em, "vector", sv, "tensor_copy", z2["h"][:, pjr, :],
                       psh[b][:, :], bufs_r=[psb[b]], bufs_w=[z2["buf"]])
                for ocp in range(8):
                    sdb = ocp % 2
                    for pp_ in range(2):
                        dma(em, "sync",
                            slab_d[sdb]["h"][64 * pp_:64 * (pp_ + 1), :, :],
                            dnT_r[pp_, :, 16 * grp:16 * (grp + 1),
                                  ocp * 256:(ocp + 1) * 256],
                            bufs_r=[dnT_b], bufs_w=[slab_d[sdb]["buf"]])
                    for oc_rel in range(2):
                        oc = 2 * ocp + oc_rel
                        db = 4 + (oc % 2)
                        for pjr in range(16):
                            mm(em, st_ if pjr == 15 else None, psh[db][:, :],
                               slab_d[sdb]["h"][:, pjr,
                                                oc_rel * 128:(oc_rel + 1) * 128],
                               z2["h"][:, pjr, :], start=(pjr == 0),
                               stop=(pjr == 15),
                               bufs_r=[slab_d[sdb]["buf"], z2["buf"]],
                               bufs_w=[psb[db]] if pjr == 0 else ())
                        psb[db].note_write(st_.h, st_.n)
                        op(em, "vector", sv, "scalar_tensor_tensor",
                           X[:, oc, ts], psh[db][:, :], wdp_c["h"][:, oc:oc + 1],
                           X[:, oc, ts], ALU.mult, ALU.add,
                           bufs_r=[psb[db], wdp_c["buf"], Xbuf[tt]],
                           bufs_w=[Xbuf[tt]])

    full_barrier(em, sems, exclude=sh.get("prep_sems", ()))
    em.release(smark)



def declare_attn_l0(nc):
    H = 16
    P = {}
    P["w_qkv"] = nc.dram_tensor("w_qkv", [D, 3 * D], F32, kind="ExternalInput").ap()
    P["w_out"] = nc.dram_tensor("w_out", [D, D], F32, kind="ExternalInput").ap()
    P["decay"] = nc.dram_tensor("decay_i", [1, H], F32, kind="ExternalInput").ap()
    P["csel"] = nc.dram_tensor("csel_i", [128, 1], F32, kind="ExternalInput").ap()
    P["att_dram"] = nc.dram_tensor("att_dram", [D, T], BF16).ap()
    P["qkv_bf"] = nc.dram_tensor("qkv_bf", [D, 3 * D], BF16).ap()
    P["out_bf"] = nc.dram_tensor("out_bf", [D, D], BF16).ap()
    P["qkv_bf_b"], P["out_bf_b"] = Buf(), Buf()
    P["contrib"] = nc.dram_tensor("contrib0", [128, 4], F32)
    P["gathered"] = nc.dram_tensor("gathered0", [256, 4], F32)
    jj, tt_ = np.meshgrid(np.arange(128), np.arange(128), indexing="ij")
    E_np = np.maximum(tt_ - jj, 0).astype(np.float32)
    LT_np = (jj <= tt_).astype(np.float32)
    P["E_d"] = nc.inline_tensor(E_np, name="E_d").ap()
    P["LT_d"] = nc.inline_tensor(LT_np, name="LT_d").ap()
    P["iota_d"] = nc.inline_tensor(
        np.arange(1, T + 1, dtype=np.float32).reshape(1, T), name="iota_d").ap()
    P["zcol_d"] = nc.inline_tensor(np.zeros((128, 1), np.float32),
                                   name="zcol_d").ap()
    return P


def emit_attn_l0(nc, em, sems, sh, P, pre):
    H = 16
    st_, sv, sc, sp, scc = (sems[k] for k in ("st", "sv", "sc", "sp", "cc"))
    X, Xbuf, C = sh["X"], sh["Xbuf"], sh["C"]
    w_qkv, w_out = P["qkv_bf"], P["out_bf"]
    qkv_bf_b, out_bf_b = P["qkv_bf_b"], P["out_bf_b"]
    decay_d, csel_d = P["decay"], P["csel"]
    att_dram, contrib, gathered = P["att_dram"], P["contrib"], P["gathered"]
    E_d, LT_d, iota_d, zcol_d = P["E_d"], P["LT_d"], P["iota_d"], P["zcol_d"]
    ident_d = sh["ident_d"]
    contrib_buf = Buf()
    gathered_buf = Buf()
    smark = em.mark()
    with contextlib.ExitStack() as stack:
        ent = stack.enter_context

        def sb(name, shape, dtype):
            return {"h": ent(nc.sbuf_tensor(pre + name, shape, dtype)),
                    "buf": Buf(True)}

        att_dram_b = Buf()
        onc, onr = sh["onc"], sh["onr"]
        E_sb = sb("E_sb", [128, 128], F32)
        LT_sb = sb("LT_sb", [128, 128], F32)
        iota_sb = sb("iota", [1, T], F32)
        ident = sb("ident", [128, 128], BF16)
        zcol = sb("zcol", [128, 1], F32)
        csel = sb("csel", [128, 1], F32)
        dec = sb("dec", [1, 2, H], F32)
        Yh = ent(nc.sbuf_tensor(pre + "Y", [128, DC, T], BF16))
        Y = Yh[:]
        Ybuf = Buf()
        sq = [sb(f"sq{i}", [128, TT], F32) for i in range(2)]
        tmp = [sb(f"tmp{i}", [128, TT], F32) for i in range(2)]
        rows = sb("rows", [1, 2, TT], F32)
        Rb = sb("Rb", [128, TT], F32)
        MRb = sb("MRb", [128, TT], F32)

        cr = []
        for tile, srcd in ((E_sb, E_d),
                           (LT_sb, LT_d), (iota_sb, iota_d), (ident, ident_d),
                           (zcol, zcol_d), (csel, csel_d)):
            cr.append(dma(em, "sync", tile["h"][:], srcd, bufs_w=[tile["buf"]]))
        cr.append(dma(em, "sync", dec["h"][:1, 0, :], decay_d,
                      bufs_w=[dec["buf"]]))
        consts_ready = cr
        em.wait("tensor", cr)
        em.wait("scalar", cr)
        em.wait("vector", cr)
        C = {"ones_col": onc["h"][:], "ones_row": onr["h"][:]}

        op(em, "scalar", sc, "activation", dec["h"][:1, 0, :], dec["h"][:1, 0, :],
           AF.Sigmoid, bufs_r=[dec["buf"]], bufs_w=[dec["buf"]])
        op(em, "scalar", sc, "activation", dec["h"][:1, 1, :], dec["h"][:1, 0, :],
           AF.Ln, bufs_r=[dec["buf"]], bufs_w=[dec["buf"]])
        lnd = dec["h"]

        with contextlib.ExitStack() as lnsc:
            lps = [lnsc.enter_context(
                nc.psum_tensor(pre + f"lps{i}", [128, 512], F32))
                   for i in range(4)]
            lpsb = [Buf() for _ in range(4)]
            rh = rows["h"]
            rowaps = [rh[:1, 0, :], rh[:1, 1, :]]
            scr = {"ps": lps, "psb": lpsb, "sq": sq, "sqb": [s["buf"] for s in sq],
                   "rowaps": rowaps, "rowsb": rows["buf"],
                   "R": Rb, "MR": MRb, "tmp": tmp, "tmpb": [t["buf"] for t in tmp]}
            for tt in range(NTT):
                emit_ln(em, sems, X, Xbuf, Y[:, :, tt * TT:(tt + 1) * TT], Ybuf,
                        tt, C, scr)
        full_barrier(em, sems, exclude=sh.get("prep_sems", ()))

        wq_r = w_qkv.rearrange("(kc p) o -> p kc o", p=128)

        with contextlib.ExitStack() as body:
            def bsb(name, shape, dtype):
                return {"h": body.enter_context(
                    nc.sbuf_tensor(pre + name, shape, dtype)),
                        "buf": Buf(True)}
            q_sb = bsb("q_sb", [128, 2, T], BF16)
            k_sb = bsb("k_sb", [128, 2, T], BF16)
            kv_sb = bsb("kv_sb", [128, 2, T], BF16)
            v_sb = bsb("v_sb", [128, 2, T], BF16)
            S_scr = bsb("S_scr", [128, 4, T], F32)
            att_q = bsb("att_q", [128, 2, T], BF16)
            slab = bsb("slab", [128, DC, 256], BF16)
            phi_mn = sq
            phi_ex = tmp
            phi_rl = [Rb, MRb]
            xT = [bsb(f"xT{i}", [128, 128], BF16) for i in range(2)]
            A_bf = bsb("A_bf", [128, 2, 128], BF16)
            aexp = bsb("aexp", [128, 128], F32)
            powf = bsb("powf", [128, 2, T], BF16)
            prow = bsb("prow", [1, T], F32)
            u_scr = bsb("u_scr", [128, T], F32)
            t2_scr = bsb("t2_scr", [128, T], F32)
            carry = bsb("carry", [128, 4], F32)
            block0 = bsb("block0", [128, 4], F32)
            Ksel = bsb("Ksel", [128, 4], F32)
            den_sb = {"h": None, "buf": sq[0]["buf"]}
            rden_sb = {"h": None, "buf": tmp[0]["buf"]}
            den_half = [sq[0]["h"][0:1, :], sq[1]["h"][0:1, :]]
            rden_half = [tmp[0]["h"][0:1, :], tmp[1]["h"][0:1, :]]
            denb = [sq[0]["buf"], sq[1]["buf"]]
            rdenb = [tmp[0]["buf"], tmp[1]["buf"]]

            pj = [body.enter_context(
                nc.psum_tensor(pre + f"pj{i}", [128, 512], F32))
                  for i in range(2)]
            pjb = [Buf(), Buf()]
            tbf = [body.enter_context(
                nc.psum_tensor(pre + f"tb{i}", [128, 1024], BF16))
                   for i in range(2)]
            tbfb = [Buf(), Buf()]
            msc = [body.enter_context(
                nc.psum_tensor(pre + f"ms{i}", [128, 512], F32))
                   for i in range(2)]
            mscb = [Buf(), Buf()]

            lnd_b = bsb("lnd_b", [128, H], F32)
            mm(em, st_, msc[0][:, :H], C["ones_row"], lnd[:1, 1, :],
               start=True, stop=True, bufs_r=[dec["buf"], onr["buf"]],
               bufs_w=[mscb[0]])
            op(em, "vector", sv, "tensor_copy", lnd_b["h"][:], msc[0][:, :H],
               bufs_r=[mscb[0]], bufs_w=[lnd_b["buf"]])

            for hq in range(8):
                heads = (2 * hq, 2 * hq + 1)
                for kind, base in (("v", 2 * D), ("k", D), ("q", 0)):
                    span = base + 256 * hq
                    dma(em, "sync", slab["h"][:],
                        wq_r[:, :, span:span + 256],
                        bufs_r=[qkv_bf_b], bufs_w=[slab["buf"]])
                    for tt in range(NTT):
                        ts = slice(tt * TT, (tt + 1) * TT)
                        for hr in range(2):
                            pb = (tt * 2 + hr) % 2
                            for kc in range(DC):
                                mm(em, st_ if kc == DC - 1 else None, pj[pb][:, :],
                                   slab["h"][:, kc, hr * 128:(hr + 1) * 128],
                                   Y[:, kc, ts], start=(kc == 0), stop=(kc == DC - 1),
                                   bufs_r=[slab["buf"], Ybuf],
                                   bufs_w=[pjb[pb]] if kc == 0 else ())
                            pjb[pb].note_write(st_.h, st_.n)
                            if kind == "v":
                                op(em, "vector", sv, "tensor_copy",
                                   v_sb["h"][:, hr, ts], pj[pb][:, :],
                                   bufs_r=[pjb[pb]], bufs_w=[v_sb["buf"]])
                            else:
                                dst = q_sb if kind == "q" else k_sb
                                op(em, "vector", sv, "tensor_scalar_min",
                                   phi_mn[hr]["h"][:], pj[pb][:, :], 0.0,
                                   bufs_r=[pjb[pb]], bufs_w=[phi_mn[hr]["buf"]])
                                op(em, "scalar", sc, "activation",
                                   phi_ex[hr]["h"][:], phi_mn[hr]["h"][:], AF.Exp,
                                   bufs_r=[phi_mn[hr]["buf"]],
                                   bufs_w=[phi_ex[hr]["buf"]])
                                op(em, "scalar", sc, "activation",
                                   phi_rl[hr]["h"][:], pj[pb][:, :], AF.Relu,
                                   bufs_r=[pjb[pb]], bufs_w=[phi_rl[hr]["buf"]])
                                op(em, "vector", sv, "tensor_add",
                                   dst["h"][:, hr, ts], phi_rl[hr]["h"][:],
                                   phi_ex[hr]["h"][:],
                                   bufs_r=[phi_rl[hr]["buf"], phi_ex[hr]["buf"]],
                                   bufs_w=[dst["buf"]])
                                if kind == "k":
                                    op(em, "vector", sv, "tensor_mul",
                                       kv_sb["h"][:, hr, ts], k_sb["h"][:, hr, ts],
                                       v_sb["h"][:, hr, ts],
                                       bufs_r=[k_sb["buf"], v_sb["buf"]],
                                       bufs_w=[kv_sb["buf"]])

                for hr, h in enumerate(heads):
                    lnd_ap = lnd_b["h"][:, h:h + 1]
                    op(em, "scalar", sc, "activation", aexp["h"][:], E_sb["h"][:],
                       AF.Exp, scale=lnd_ap,
                       bufs_r=[E_sb["buf"], lnd_b["buf"]], bufs_w=[aexp["buf"]])
                    op(em, "vector", sv, "tensor_mul", A_bf["h"][:, hr, :],
                       aexp["h"][:], LT_sb["h"][:],
                       bufs_r=[aexp["buf"], LT_sb["buf"]], bufs_w=[A_bf["buf"]])
                    op(em, "scalar", sc, "activation", prow["h"][:], iota_sb["h"][:],
                       AF.Exp, scale=lnd[:1, 1, h:h + 1],
                       bufs_r=[iota_sb["buf"], dec["buf"]], bufs_w=[prow["buf"]])
                    for half in range(2):
                        mm(em, st_, msc[half][:, :], C["ones_row"],
                           prow["h"][:1, half * 512:(half + 1) * 512],
                           start=True, stop=True,
                           bufs_r=[prow["buf"], onr["buf"]], bufs_w=[mscb[half]])
                        op(em, "vector", sv, "tensor_copy",
                           powf["h"][:, hr, half * 512:(half + 1) * 512],
                           msc[half][:, :], bufs_r=[mscb[half]],
                           bufs_w=[powf["buf"]])

                    for kind_i, src in ((0, kv_sb), (1, k_sb)):
                        sidx = 2 * hr + kind_i
                        for tc in range(8):
                            tcs = slice(tc * 128, (tc + 1) * 128)
                            op(em, "tensor", st_, "transpose",
                               tbf[tc % 2][:, :128], src["h"][:, hr, tcs],
                               ident["h"][:],
                               bufs_r=[src["buf"], ident["buf"]],
                               bufs_w=[tbfb[tc % 2]])
                            op(em, "vector", sv, "tensor_copy", xT[tc % 2]["h"][:],
                               tbf[tc % 2][:, :128],
                               bufs_r=[tbfb[tc % 2]], bufs_w=[xT[tc % 2]["buf"]])
                            mm(em, st_, msc[tc % 2][:, :128], xT[tc % 2]["h"][:],
                               A_bf["h"][:, hr, :], start=True, stop=True,
                               bufs_r=[xT[tc % 2]["buf"], A_bf["buf"]],
                               bufs_w=[mscb[tc % 2]])
                            cprev = (S_scr["h"][:, sidx, tc * 128 - 1:tc * 128]
                                     if tc > 0 else zcol["h"][:])
                            op(em, "vector", sv, "scalar_tensor_tensor",
                               S_scr["h"][:, sidx, tcs],
                               powf["h"][:, hr, 0:128], cprev,
                               msc[tc % 2][:, :128], ALU.mult, ALU.add,
                               bufs_r=[mscb[tc % 2], powf["buf"], S_scr["buf"],
                                       zcol["buf"]],
                               bufs_w=[S_scr["buf"]])

                for hr in range(2):
                    for kind_i in range(2):
                        sidx = 2 * hr + kind_i
                        op(em, "vector", sv, "tensor_copy",
                           carry["h"][:, sidx:sidx + 1],
                           S_scr["h"][:, sidx, T - 1:T],
                           bufs_r=[S_scr["buf"]], bufs_w=[carry["buf"]])
                dma(em, "sync", contrib[:, :], carry["h"][:],
                    bufs_r=[carry["buf"]], bufs_w=[contrib_buf])
                em.emit("gpsimd", "collective_compute",
                        ("AllGather", ALU.bypass,
                         [[0, 1], [2, 3], [4, 5], [6, 7]],
                         [contrib.ap().opt()], [gathered.ap().opt()]),
                        {}, scc, 1, (), bufs_r=[contrib_buf],
                        bufs_w=[gathered_buf])
                dma(em, "sync", block0["h"][:], gathered[0:128, :],
                    bufs_r=[gathered_buf], bufs_w=[block0["buf"]])
                op(em, "vector", sv, "tensor_scalar_mul", Ksel["h"][:],
                   block0["h"][:], csel["h"][:],
                   bufs_r=[block0["buf"], csel["buf"]], bufs_w=[Ksel["buf"]])

                for hr, h in enumerate(heads):
                    for kind_i in range(2):
                        sidx = 2 * hr + kind_i
                        op(em, "vector", sv, "scalar_tensor_tensor",
                           S_scr["h"][:, sidx, :], powf["h"][:, hr, :],
                           Ksel["h"][:, sidx:sidx + 1], S_scr["h"][:, sidx, :],
                           ALU.mult, ALU.add,
                           bufs_r=[powf["buf"], Ksel["buf"], S_scr["buf"]],
                           bufs_w=[S_scr["buf"]])
                    op(em, "vector", sv, "tensor_mul", u_scr["h"][:],
                       q_sb["h"][:, hr, :], S_scr["h"][:, 2 * hr + 1, :],
                       bufs_r=[q_sb["buf"], S_scr["buf"]], bufs_w=[u_scr["buf"]])
                    op(em, "vector", sv, "tensor_mul", t2_scr["h"][:],
                       q_sb["h"][:, hr, :], S_scr["h"][:, 2 * hr, :],
                       bufs_r=[q_sb["buf"], S_scr["buf"]], bufs_w=[t2_scr["buf"]])
                    for half in range(2):
                        hs = slice(half * 512, (half + 1) * 512)
                        mm(em, st_, msc[0][:1, :], C["ones_col"],
                           u_scr["h"][:, hs], start=True, stop=True,
                           bufs_r=[u_scr["buf"], onc["buf"]], bufs_w=[mscb[0]])
                        op(em, "vector", sv, "tensor_scalar_max",
                           den_half[half], msc[0][:1, :], 1e-6,
                           bufs_r=[mscb[0]], bufs_w=[denb[half]])
                        op(em, "vector", sv, "reciprocal", rden_half[half],
                           den_half[half],
                           bufs_r=[denb[half]], bufs_w=[rdenb[half]])
                        mm(em, st_, msc[1][:, :], C["ones_row"],
                           rden_half[half], start=True, stop=True,
                           bufs_r=[rdenb[half], onr["buf"]], bufs_w=[mscb[1]])
                        op(em, "vector", sv, "tensor_mul",
                           att_q["h"][:, hr, hs], t2_scr["h"][:, hs],
                           msc[1][:, :],
                           bufs_r=[t2_scr["buf"], mscb[1]], bufs_w=[att_q["buf"]])
                dma(em, "sync",
                    att_dram[256 * hq:256 * (hq + 1), :].rearrange(
                        "(r p) t -> p r t", p=128),
                    att_q["h"][:], bufs_r=[att_q["buf"]], bufs_w=[att_dram_b])

        full_barrier(em, sems, exclude=sh.get("prep_sems", ()))
        with contextlib.ExitStack() as osc:
            def osb(name, shape, dtype):
                return {"h": osc.enter_context(
                    nc.sbuf_tensor(pre + name, shape, dtype)),
                        "buf": Buf(True)}
            wslab = [osb(f"wo{i}", [128, DC, 256], BF16) for i in range(2)]
            arhs = [osb(f"ar{i}", [128, DC, TT], BF16) for i in range(2)]
            po = [osc.enter_context(
                nc.psum_tensor(pre + f"po{i}", [128, 512], F32))
                  for i in range(2)]
            pob = [Buf(), Buf()]
            wo_r = w_out.rearrange("(kc p) o -> p kc o", p=128)
            att_r = att_dram.rearrange("(kc p) t -> p kc t", p=128)
            for tt in range(NTT):
                ts = slice(tt * TT, (tt + 1) * TT)
                dma(em, "sync", arhs[tt % 2]["h"][:], att_r[:, :, ts],
                    bufs_r=[att_dram_b], bufs_w=[arhs[tt % 2]["buf"]])
                for sp_i in range(8):
                    dma(em, "sync", wslab[sp_i % 2]["h"][:],
                        wo_r[:, :, 256 * sp_i:256 * (sp_i + 1)],
                        bufs_r=[out_bf_b], bufs_w=[wslab[sp_i % 2]["buf"]])
                    for oc_rel in range(2):
                        oc = 2 * sp_i + oc_rel
                        pb = oc % 2
                        for kc in range(DC):
                            mm(em, st_ if kc == DC - 1 else None, po[pb][:, :],
                               wslab[sp_i % 2]["h"][:, kc,
                                                    oc_rel * 128:(oc_rel + 1) * 128],
                               arhs[tt % 2]["h"][:, kc, :],
                               start=(kc == 0), stop=(kc == DC - 1),
                               bufs_r=[wslab[sp_i % 2]["buf"], arhs[tt % 2]["buf"]],
                               bufs_w=[pob[pb]] if kc == 0 else ())
                        pob[pb].note_write(st_.h, st_.n)
                        op(em, "vector", sv, "tensor_add",
                           X[:, oc, ts], po[pb][:, :], X[:, oc, ts],
                           bufs_r=[pob[pb], Xbuf[tt]], bufs_w=[Xbuf[tt]])

    full_barrier(em, sems, exclude=sh.get("prep_sems", ()))
    em.release(smark)



def declare_attn_l1(nc):
    H = 16
    BIG = 1e9
    P = {}
    P["w_dkv"] = nc.dram_tensor("w_dkv", [D, 128], F32, kind="ExternalInput").ap()
    P["w_dq"] = nc.dram_tensor("w_dq", [D, 128], F32, kind="ExternalInput").ap()
    P["w_kr"] = nc.dram_tensor("w_kr", [D, 64], F32, kind="ExternalInput").ap()
    P["w_qc"] = nc.dram_tensor("w_qc", [H, 128, 64], F32, kind="ExternalInput").ap()
    P["w_qr"] = nc.dram_tensor("w_qr", [H, 128, 64], F32, kind="ExternalInput").ap()
    P["w_uk"] = nc.dram_tensor("w_uk", [H, 128, 64], F32, kind="ExternalInput").ap()
    P["w_uv"] = nc.dram_tensor("w_uv", [H, 128, 128], F32, kind="ExternalInput").ap()
    P["w_o"] = nc.dram_tensor("w_o", [D, D], F32, kind="ExternalInput").ap()
    P["cosf"] = nc.dram_tensor("cosf_i", [64, T], BF16, kind="ExternalInput").ap()
    P["sinf"] = nc.dram_tensor("sinf_i", [64, T], BF16, kind="ExternalInput").ap()
    P["pc"] = nc.dram_tensor("pc_i", [128, 1], F32, kind="ExternalInput").ap()
    P["msel"] = nc.dram_tensor("msel_i", [128, 2], F32, kind="ExternalInput").ap()
    P["contrib"] = nc.dram_tensor("contrib1", [192, T], BF16)
    P["gathered"] = nc.dram_tensor("gathered1", [384, T], BF16)
    psw = np.zeros((64, 64), np.float32)
    for m in range(64):
        psw[m ^ 1, m] = 1.0
    P["pswap_d"] = nc.inline_tensor(psw.astype(nbf), name="pswap_d").ap()
    qq = np.arange(128)[:, None]
    kk = np.arange(512)[None, :]
    md = np.stack([np.where(kk <= 128 * qm + qq, 0.0, -BIG)
                   for qm in range(4)], axis=1).astype(np.float32)
    P["mdiag_d"] = nc.inline_tensor(md, name="mdiag_d").ap()
    return P


def emit_attn_l1(nc, em, sems, sh, P, pre):
    H = 16
    SC = 0.08838834764831845
    st_, sv, sc, sp, scc = (sems[k] for k in ("st", "sv", "sc", "sp", "cc"))
    X, Xbuf, C = sh["X"], sh["Xbuf"], sh["C"]
    w_dkv, w_dq, w_kr = P["w_dkv"], P["w_dq"], P["w_kr"]
    w_qc, w_qr, w_uk, w_uv, w_o = (P["w_qc"], P["w_qr"], P["w_uk"],
                                   P["w_uv"], P["w_o"])
    cosf_d, sinf_d, pc_d, msel_d = P["cosf"], P["sinf"], P["pc"], P["msel"]
    contrib, gathered = P["contrib"], P["gathered"]
    pswap_d, mdiag_d = P["pswap_d"], P["mdiag_d"]
    ident_d = sh["ident_d"]
    ones_row_d = sh["onr_d"]
    contrib_b, gathered_b = Buf(), Buf()
    smark = em.mark()
    with contextlib.ExitStack() as stack:
        ent = stack.enter_context

        def sb(name, shape, dtype):
            return {"h": ent(nc.sbuf_tensor(pre + name, shape, dtype)),
                    "buf": Buf(True)}

        onc = sh["onc"]
        ident = sb("ident", [128, 128], BF16)
        pswap = sb("pswap", [64, 64], BF16)
        cosf = sb("cosf", [64, T], BF16)
        sinf = sb("sinf", [64, T], BF16)
        pc = sb("pc", [128, 1], F32)
        msel = sb("msel", [128, 2], F32)
        wdkv_s = sb("wdkv_s", [128, DC, 128], BF16)
        wdq_s = sb("wdq_s", [128, DC, 128], BF16)
        wkr_s = sb("wkr_s", [128, DC, 64], BF16)
        wqc_s = sb("wqc_s", [128, H, 64], BF16)
        wqr_s = sb("wqr_s", [128, H, 64], BF16)
        qnr = sb("qnr", [128, H, T], BF16)
        ckv_of = sb("ckv_of", [128, 2 * T], BF16)
        att_sb = sb("att_sb", [128, DC, T], BF16)
        krf_of = {"h": att_sb["h"], "buf": att_sb["buf"]}
        krf_ap = att_sb["h"][0:64, 0:2, :]

        cr = []
        for tile, srcd in ((ident, ident_d),
                           (pswap, pswap_d),
                           (cosf, cosf_d), (sinf, sinf_d), (pc, pc_d),
                           (msel, msel_d)):
            cr.append(dma(em, "sync", tile["h"][:], srcd, bufs_w=[tile["buf"]]))
        for tile, srcd in (
                (wdkv_s, w_dkv.rearrange("(kc p) o -> p kc o", p=128)),
                (wdq_s, w_dq.rearrange("(kc p) o -> p kc o", p=128)),
                (wkr_s, w_kr.rearrange("(kc p) o -> p kc o", p=128)),
                (wqc_s, w_qc.rearrange("h p c -> p h c")),
                (wqr_s, w_qr.rearrange("h p c -> p h c"))):
            cr.append(dma(em, "gpsimd", tile["h"][:], srcd,
                          bufs_w=[tile["buf"]]))
        consts_ready = cr
        C = {"ones_col": onc["h"][:], "ones_row": None}

        em.wait("tensor", consts_ready)
        em.wait("scalar", consts_ready)
        em.wait("vector", consts_ready)

        with contextlib.ExitStack() as s1:
            def s1b(name, shape, dtype):
                return {"h": s1.enter_context(
                    nc.sbuf_tensor(pre + name, shape, dtype)),
                        "buf": Buf(True)}
            Yh = s1.enter_context(nc.sbuf_tensor(pre + "Y", [128, DC, TT], BF16))
            Y = Yh[:]
            Ybuf = Buf()
            sq0 = s1b("sq0", [128, TT], F32)
            tmp0 = s1b("tmp0", [128, TT], F32)
            sq = [sq0, sq0]
            tmp = [tmp0, tmp0]
            rows = s1b("rows", [1, 2, TT], F32)
            Rb = s1b("Rb", [128, TT], F32)
            MRb = s1b("MRb", [128, TT], F32)
            onr = sh["onr"]
            dq_bf = s1b("dq_bf", [128, T], BF16)
            ckv_bf = s1b("ckv_bf", [128, T], BF16)
            kr_bf = s1b("kr_bf", [64, T], BF16)
            krop_bf = s1b("krop_bf", [64, T], BF16)
            qr_bf = s1b("qr_bf", [64, TT], BF16)
            rsc1 = {"h": Rb["h"], "buf": Rb["buf"]}
            rsc2 = {"h": MRb["h"], "buf": MRb["buf"]}
            b0c, b1c = dq_bf, ckv_bf
            b0k, b1k = kr_bf, krop_bf

            lps = [s1.enter_context(
                nc.psum_tensor(pre + f"lps{i}", [128, 512], F32))
                   for i in range(8)]
            lpsb = [Buf() for _ in range(8)]
            rh = rows["h"]
            rowaps = [rh[:1, 0, :], rh[:1, 1, :]]
            scr = {"ps": lps, "psb": lpsb, "sq": sq,
                   "sqb": [s["buf"] for s in sq],
                   "rowaps": rowaps, "rowsb": rows["buf"],
                   "R": Rb, "MR": MRb, "tmp": tmp,
                   "tmpb": [t["buf"] for t in tmp]}

            for tt in range(NTT):
                ts = slice(tt * TT, (tt + 1) * TT)
                emit_ln(em, sems, X, Xbuf, Y, Ybuf, tt,
                        {"ones_col": onc["h"][:], "ones_row": onr["h"][:]}, scr)
                for dst, wsl, pb in ((ckv_bf, wdkv_s, 4), (dq_bf, wdq_s, 5)):
                    for kc in range(DC):
                        mm(em, st_ if kc == DC - 1 else None, lps[pb][:, :],
                           wsl["h"][:, kc, :], Y[:, kc, :],
                           start=(kc == 0), stop=(kc == DC - 1),
                           bufs_r=[wsl["buf"], Ybuf],
                           bufs_w=[lpsb[pb]] if kc == 0 else ())
                    lpsb[pb].note_write(st_.h, st_.n)
                    op(em, "vector", sv, "tensor_copy", dst["h"][:, ts],
                       lps[pb][:, :], bufs_r=[lpsb[pb]], bufs_w=[dst["buf"]])
                for kc in range(DC):
                    mm(em, st_ if kc == DC - 1 else None, lps[6][:64, :],
                       wkr_s["h"][:, kc, :], Y[:, kc, :],
                       start=(kc == 0), stop=(kc == DC - 1),
                       bufs_r=[wkr_s["buf"], Ybuf],
                       bufs_w=[lpsb[6]] if kc == 0 else ())
                lpsb[6].note_write(st_.h, st_.n)
                op(em, "vector", sv, "tensor_copy", kr_bf["h"][:, ts],
                   lps[6][:64, :], bufs_r=[lpsb[6]], bufs_w=[kr_bf["buf"]])
                mm(em, st_, lps[7][:64, :], pswap["h"][:], kr_bf["h"][:, ts],
                   start=True, stop=True,
                   bufs_r=[pswap["buf"], kr_bf["buf"]], bufs_w=[lpsb[7]])
                op(em, "vector", sv, "tensor_mul", rsc1["h"][:64, :],
                   lps[6][:64, :], cosf["h"][:, ts],
                   bufs_r=[lpsb[6], cosf["buf"]], bufs_w=[rsc1["buf"]])
                op(em, "vector", sv, "tensor_mul", rsc2["h"][:64, :],
                   lps[7][:64, :], sinf["h"][:, ts],
                   bufs_r=[lpsb[7], sinf["buf"]], bufs_w=[rsc2["buf"]])
                op(em, "vector", sv, "tensor_add", krop_bf["h"][:, ts],
                   rsc1["h"][:64, :], rsc2["h"][:64, :],
                   bufs_r=[rsc1["buf"], rsc2["buf"]], bufs_w=[krop_bf["buf"]])
                for h in range(H):
                    pb = h % 2
                    mm(em, st_, lps[pb][:64, :], wqc_s["h"][:, h, :],
                       dq_bf["h"][:, ts], start=True, stop=True,
                       bufs_r=[wqc_s["buf"], dq_bf["buf"]], bufs_w=[lpsb[pb]])
                    op(em, "vector", sv, "tensor_copy", qnr["h"][:64, h, ts],
                       lps[pb][:64, :], bufs_r=[lpsb[pb]], bufs_w=[qnr["buf"]])
                    mm(em, st_, lps[2 + pb][:64, :], wqr_s["h"][:, h, :],
                       dq_bf["h"][:, ts], start=True, stop=True,
                       bufs_r=[wqr_s["buf"], dq_bf["buf"]], bufs_w=[lpsb[2 + pb]])
                    op(em, "vector", sv, "tensor_copy", qr_bf["h"][:],
                       lps[2 + pb][:64, :], bufs_r=[lpsb[2 + pb]],
                       bufs_w=[qr_bf["buf"]])
                    mm(em, st_, lps[6][:64, :], pswap["h"][:], qr_bf["h"][:],
                       start=True, stop=True,
                       bufs_r=[pswap["buf"], qr_bf["buf"]], bufs_w=[lpsb[6]])
                    op(em, "vector", sv, "tensor_mul", rsc1["h"][:64, :],
                       lps[2 + pb][:64, :], cosf["h"][:, ts],
                       bufs_r=[lpsb[2 + pb], cosf["buf"]], bufs_w=[rsc1["buf"]])
                    op(em, "vector", sv, "tensor_mul", rsc2["h"][:64, :],
                       lps[6][:64, :], sinf["h"][:, ts],
                       bufs_r=[lpsb[6], sinf["buf"]], bufs_w=[rsc2["buf"]])
                    op(em, "vector", sv, "tensor_add", qnr["h"][64:128, h, ts],
                       rsc1["h"][:64, :], rsc2["h"][:64, :],
                       bufs_r=[rsc1["buf"], rsc2["buf"]], bufs_w=[qnr["buf"]])

            dma(em, "sync", contrib[0:128, :], ckv_bf["h"][:],
                bufs_r=[ckv_bf["buf"]], bufs_w=[contrib_b])
            dma(em, "sync", contrib[128:192, :], krop_bf["h"][:],
                bufs_r=[krop_bf["buf"]], bufs_w=[contrib_b])
            em.emit("gpsimd", "collective_compute",
                    ("AllGather", ALU.bypass,
                     [[0, 1], [2, 3], [4, 5], [6, 7]],
                     [contrib.ap().opt()], [gathered.ap().opt()]),
                    {}, scc, 1, (), bufs_r=[contrib_b], bufs_w=[gathered_b])
            dma(em, "sync", b0c["h"][:], gathered[0:128, :],
                bufs_r=[gathered_b], bufs_w=[b0c["buf"]])
            dma(em, "sync", b1c["h"][:], gathered[192:320, :],
                bufs_r=[gathered_b], bufs_w=[b1c["buf"]])
            dma(em, "sync", b0k["h"][:], gathered[128:192, :],
                bufs_r=[gathered_b], bufs_w=[b0k["buf"]])
            dma(em, "sync", b1k["h"][:], gathered[320:384, :],
                bufs_r=[gathered_b], bufs_w=[b1k["buf"]])
            ckv_ap = ckv_of["h"][:].rearrange("p (s t) -> p s t", s=2)
            for dstap, dstbuf, b0, b1, np_ in (
                    (ckv_ap, ckv_of["buf"], b0c, b1c, 128),
                    (krf_ap, krf_of["buf"], b0k, b1k, 64)):
                op(em, "vector", sv, "tensor_scalar_mul", dstap[:, 0, :],
                   b0["h"][:], msel["h"][:np_, 0:1],
                   bufs_r=[b0["buf"], msel["buf"]], bufs_w=[dstbuf])
                op(em, "vector", sv, "scalar_tensor_tensor", dstap[:, 0, :],
                   b1["h"][:], msel["h"][:np_, 1:2], dstap[:, 0, :],
                   ALU.mult, ALU.add,
                   bufs_r=[b1["buf"], msel["buf"], dstbuf],
                   bufs_w=[dstbuf])
                op(em, "vector", sv, "tensor_scalar_mul", dstap[:, 1, :],
                   b0["h"][:], msel["h"][:np_, 1:2],
                   bufs_r=[b0["buf"], msel["buf"]], bufs_w=[dstbuf])
                op(em, "vector", sv, "scalar_tensor_tensor", dstap[:, 1, :],
                   b1["h"][:], msel["h"][:np_, 0:1], dstap[:, 1, :],
                   ALU.mult, ALU.add,
                   bufs_r=[b1["buf"], msel["buf"], dstbuf],
                   bufs_w=[dstbuf])
        full_barrier(em, sems, exclude=sh.get("prep_sems", ()))

        with contextlib.ExitStack() as s2:
            def s2b(name, shape, dtype):
                return {"h": s2.enter_context(
                    nc.sbuf_tensor(pre + name, shape, dtype)),
                        "buf": Buf(True)}
            knr = s2b("knr", [128, 2 * T], BF16)
            v_h = s2b("v_h", [128, 16, 128], BF16)
            mdiag = s2b("mdiag", [128, 4, 512], F32)
            wuk_s = s2b("wuk_s", [128, H, 64], BF16)
            wuv_s = s2b("wuv_s", [128, H, 128], BF16)
            dma(em, "sync", mdiag["h"][:], mdiag_d, bufs_w=[mdiag["buf"]])
            dma(em, "gpsimd", wuk_s["h"][:], w_uk.rearrange("h p c -> p h c"),
                bufs_w=[wuk_s["buf"]])
            dma(em, "gpsimd", wuv_s["h"][:], w_uv.rearrange("h p c -> p h c"),
                bufs_w=[wuv_s["buf"]])
            P_sb = s2b("P_sb", [128, 4, 512], BF16)
            P_T = s2b("P_T", [128, 16, 128], BF16)
            av_bf = s2b("av_bf", [128, 128], BF16)
            cols = s2b("cols", [128, 16], F32)
            sps = [s2.enter_context(
                nc.psum_tensor(pre + f"sps{i}", [128, 512], F32))
                   for i in range(4)]
            spsb = [Buf() for _ in range(4)]
            tpb = [s2.enter_context(
                nc.psum_tensor(pre + f"tpb{i}", [128, 1024], BF16))
                   for i in range(2)]
            tpbb = [Buf(), Buf()]
            avp = [s2.enter_context(
                nc.psum_tensor(pre + f"avp{i}", [128, 512], F32))
                   for i in range(2)]
            avpb = [Buf(), Buf()]
            ch = cols["h"]
            mx_c = [ch[:, i:i + 1] for i in range(4)]
            own_c, prt_c, mxt_c = ch[:, 4:5], ch[:, 5:6], ch[:, 6:7]
            bias_c, bias2_c = ch[:, 7:8], ch[:, 8:9]
            den_c, rden_c = ch[:, 9:10], ch[:, 10:11]

            op(em, "vector", sv, "tensor_copy", knr["h"][64:128, :],
               att_sb["h"][0:64, 0:2, :], bufs_r=[krf_of["buf"]],
               bufs_w=[knr["buf"]])
            for h in range(H):
                for kt in range(4):
                    ks = slice(kt * 512, (kt + 1) * 512)
                    pb = kt % 2
                    mm(em, st_, sps[pb][:64, :], wuk_s["h"][:, h, :],
                       ckv_of["h"][:, ks], start=True, stop=True,
                       bufs_r=[wuk_s["buf"], ckv_of["buf"]], bufs_w=[spsb[pb]])
                    op(em, "vector", sv, "tensor_copy", knr["h"][0:64, ks],
                       sps[pb][:64, :], bufs_r=[spsb[pb]], bufs_w=[knr["buf"]])
                for kc in range(16):
                    pb = kc % 2
                    mm(em, st_, sps[pb][:, :128],
                       ckv_of["h"][:, kc * 128:(kc + 1) * 128],
                       wuv_s["h"][:, h, :], start=True, stop=True,
                       bufs_r=[ckv_of["buf"], wuv_s["buf"]], bufs_w=[spsb[pb]])
                    op(em, "vector", sv, "tensor_copy", v_h["h"][:, kc, :],
                       sps[pb][:, :128], bufs_r=[spsb[pb]], bufs_w=[v_h["buf"]])
                for qc in range(8):
                    nown = qc // 4 + 1
                    kts = list(range(nown)) + [2, 3]
                    qsl = qnr["h"][:, h, qc * 128:(qc + 1) * 128]
                    for i, kt in enumerate(kts):
                        koff = kt * 512
                        mm(em, st_, sps[i][:, :], qsl,
                           knr["h"][:, koff:koff + 512], start=True, stop=True,
                           bufs_r=[qnr["buf"], knr["buf"]], bufs_w=[spsb[i]])
                    op(em, "vector", sv, "tensor_add", sps[nown - 1][:, :],
                       sps[nown - 1][:, :], mdiag["h"][:, qc % 4, :],
                       bufs_r=[spsb[nown - 1], mdiag["buf"]],
                       bufs_w=[spsb[nown - 1]])
                    for i, kt in enumerate(kts):
                        op(em, "vector", sv, "reduce_max", mx_c[i],
                           sps[i][:, :], AX.X, bufs_r=[spsb[i]],
                           bufs_w=[cols["buf"]])
                    if nown == 1:
                        op(em, "vector", sv, "tensor_copy", own_c, mx_c[0],
                           bufs_r=[cols["buf"]], bufs_w=[cols["buf"]])
                    else:
                        op(em, "vector", sv, "tensor_max", own_c, mx_c[0],
                           mx_c[1], bufs_r=[cols["buf"]], bufs_w=[cols["buf"]])
                    op(em, "vector", sv, "tensor_max", prt_c, mx_c[nown],
                       mx_c[nown + 1], bufs_r=[cols["buf"]], bufs_w=[cols["buf"]])
                    op(em, "vector", sv, "tensor_sub", prt_c, prt_c,
                       pc["h"][:], bufs_r=[cols["buf"], pc["buf"]],
                       bufs_w=[cols["buf"]])
                    op(em, "vector", sv, "tensor_max", mxt_c, own_c, prt_c,
                       bufs_r=[cols["buf"]], bufs_w=[cols["buf"]])
                    op(em, "vector", sv, "tensor_scalar_mul", bias_c, mxt_c,
                       -SC, bufs_r=[cols["buf"]], bufs_w=[cols["buf"]])
                    op(em, "vector", sv, "tensor_sub", bias2_c, bias_c,
                       pc["h"][:], bufs_r=[cols["buf"], pc["buf"]],
                       bufs_w=[cols["buf"]])
                    for i, kt in enumerate(kts):
                        bias_ap = bias_c if kt < 2 else bias2_c
                        acc = ch[:, 11 + i:12 + i]
                        op(em, "scalar", sc, "activation", P_sb["h"][:, kt, :],
                           sps[i][:, :], AF.Exp, scale=SC, bias=bias_ap,
                           accum_out=acc,
                           bufs_r=[spsb[i], cols["buf"]],
                           bufs_w=[P_sb["buf"], cols["buf"]])
                    op(em, "vector", sv, "tensor_add", den_c, ch[:, 11:12],
                       ch[:, 12:13], bufs_r=[cols["buf"]], bufs_w=[cols["buf"]])
                    for i in range(2, len(kts)):
                        op(em, "vector", sv, "tensor_add", den_c, den_c,
                           ch[:, 11 + i:12 + i], bufs_r=[cols["buf"]],
                           bufs_w=[cols["buf"]])
                    op(em, "vector", sv, "reciprocal", rden_c, den_c,
                       bufs_r=[cols["buf"]], bufs_w=[cols["buf"]])
                    ocks = list(range(qc + 1))
                    for j, ck in enumerate(ocks):
                        op(em, "tensor", st_, "transpose",
                           tpb[0][:, j * 128:(j + 1) * 128],
                           P_sb["h"][:, ck // 4, (ck % 4) * 128:(ck % 4 + 1) * 128],
                           ident["h"][:],
                           bufs_r=[P_sb["buf"], ident["buf"]], bufs_w=[tpbb[0]])
                    op(em, "vector", sv, "tensor_copy",
                       P_T["h"][:, 0:qc + 1, :], tpb[0][:, 0:(qc + 1) * 128],
                       bufs_r=[tpbb[0]], bufs_w=[P_T["buf"]])
                    for j in range(8):
                        sl = 2 + j // 4
                        op(em, "tensor", st_, "transpose",
                           tpb[1][:, j * 128:(j + 1) * 128],
                           P_sb["h"][:, sl, (j % 4) * 128:(j % 4 + 1) * 128],
                           ident["h"][:],
                           bufs_r=[P_sb["buf"], ident["buf"]], bufs_w=[tpbb[1]])
                    op(em, "vector", sv, "tensor_copy",
                       P_T["h"][:, 8:16, :], tpb[1][:, :],
                       bufs_r=[tpbb[1]], bufs_w=[P_T["buf"]])
                    ab = qc % 2
                    cks = ocks + list(range(8, 16))
                    for j, ck in enumerate(cks):
                        mm(em, st_ if j == len(cks) - 1 else None,
                           avp[ab][:, :128], P_T["h"][:, ck % 16, :],
                           v_h["h"][:, ck % 16, :],
                           start=(j == 0), stop=(j == len(cks) - 1),
                           bufs_r=[P_T["buf"], v_h["buf"]],
                           bufs_w=[avpb[ab]] if j == 0 else ())
                    avpb[ab].note_write(st_.h, st_.n)
                    op(em, "vector", sv, "tensor_scalar_mul", av_bf["h"][:],
                       avp[ab][:, :128], rden_c,
                       bufs_r=[avpb[ab], cols["buf"]], bufs_w=[av_bf["buf"]])
                    op(em, "tensor", st_, "transpose", tpb[0][:, :128],
                       av_bf["h"][:], ident["h"][:],
                       bufs_r=[av_bf["buf"], ident["buf"]], bufs_w=[tpbb[0]])
                    op(em, "vector", sv, "tensor_copy",
                       att_sb["h"][:, h, qc * 128:(qc + 1) * 128],
                       tpb[0][:, :128], bufs_r=[tpbb[0]], bufs_w=[att_sb["buf"]])
        full_barrier(em, sems)

        with contextlib.ExitStack() as s3:
            def s3b(name, shape, dtype):
                return {"h": s3.enter_context(
                    nc.sbuf_tensor(pre + name, shape, dtype)),
                        "buf": Buf(True)}
            wslab = [s3b(f"wo{i}", [128, DC, 512], BF16) for i in range(2)]
            po = [s3.enter_context(
                nc.psum_tensor(pre + f"po{i}", [128, 512], F32))
                  for i in range(2)]
            pob = [Buf(), Buf()]
            wo_r = w_o.rearrange("(kc p) o -> p kc o", p=128)
            for tt in range(NTT):
                ts = slice(tt * TT, (tt + 1) * TT)
                for sp_i in range(4):
                    if tt == 0:
                        dma(em, "gpsimd", wslab[sp_i % 2]["h"][:],
                            wo_r[:, :, 512 * sp_i:512 * (sp_i + 1)],
                            bufs_w=[wslab[sp_i % 2]["buf"]])
                    for oc_rel in range(4):
                        oc = 4 * sp_i + oc_rel
                        pb = oc % 2
                        for kc in range(DC):
                            mm(em, st_ if kc == DC - 1 else None, po[pb][:, :],
                               wslab[sp_i % 2]["h"][:, kc,
                                                    oc_rel * 128:(oc_rel + 1) * 128],
                               att_sb["h"][:, kc, ts],
                               start=(kc == 0), stop=(kc == DC - 1),
                               bufs_r=[wslab[sp_i % 2]["buf"], att_sb["buf"]],
                               bufs_w=[pob[pb]] if kc == 0 else ())
                        pob[pb].note_write(st_.h, st_.n)
                        op(em, "vector", sv, "tensor_add",
                           X[:, oc, ts], po[pb][:, :], X[:, oc, ts],
                           bufs_r=[pob[pb], Xbuf[tt]], bufs_w=[Xbuf[tt]])

    full_barrier(em, sems, exclude=sh.get("prep_sems", ()))
    em.release(smark)


def build_fused():
    nc = bass.Bass("TRN2", target_bir_lowering=False)
    x_i = nc.dram_tensor("x_i", [D, T], F32, kind="ExternalInput").ap()
    x_o = nc.dram_tensor("x_o", [D, T], F32, kind="ExternalOutput").ap()
    P0f = declare_ffn_l0(nc)
    P0a = declare_attn_l0(nc)
    P1f = declare_ffn_l1(nc)
    P1a = declare_attn_l1(nc)
    ones_col_d = nc.inline_tensor(np.ones((128, 1), np.float32), name="onc_d").ap()
    ones_row_d = nc.inline_tensor(np.ones((1, 128), np.float32), name="onr_d").ap()
    ident_d = nc.inline_tensor(np.eye(128, dtype=nbf), name="ident_d").ap()

    em = Em(nc)
    with contextlib.ExitStack() as stack:
        ent = stack.enter_context
        em.pool = [Sem(ent(nc.semaphore(f"dp{i}"))) for i in range(88)]
        sems = {k: Sem(ent(nc.semaphore(k))) for k in
                ("st", "sv", "sc", "sp", "cc")}
        Xh = ent(nc.sbuf_tensor("X", [128, DC, T], F32))
        Xbuf = [Buf(True), Buf(True)]
        Xbuf[1].dsem = Xbuf[0].dsem = em.take_sem()
        onc = {"h": ent(nc.sbuf_tensor("onc", [128, 1], F32)), "buf": Buf(True)}
        onr = {"h": ent(nc.sbuf_tensor("onr", [1, 128], F32)), "buf": Buf(True)}
        dma(em, "sync", Xh[:], x_i.rearrange("(c p) t -> p c t", p=128),
            bufs_w=Xbuf)
        cr = [dma(em, "sync", onc["h"][:], ones_col_d, bufs_w=[onc["buf"]]),
              dma(em, "sync", onr["h"][:], ones_row_d, bufs_w=[onr["buf"]])]
        for e in ENGINES:
            em.wait(e, cr)
        cbuf = [{"h": ent(nc.sbuf_tensor(f"cbuf{i}", [128, 2048], BF16)),
                 "buf": Buf(True)} for i in range(2)]
        tslab = [{"h": ent(nc.sbuf_tensor(f"tsl{i}", [128, 2048], BF16)),
                  "buf": Buf(True)} for i in range(2)]
        sh = {"X": Xh[:], "Xbuf": Xbuf, "onc": onc, "onr": onr,
              "C": {"ones_col": onc["h"][:], "ones_row": onr["h"][:]},
              "ident_d": ident_d, "onr_d": ones_row_d}

        emit_weight_prep(em, P0f["w_gu"], 2 * INTER, D, P0f["gu_bf"],
                         P0f["gu_bf_b"], P0f["guT"], P0f["guT_b"], cbuf, tslab)
        emit_weight_prep(em, P0f["w_dn"], D, INTER, P0f["dn_bf"],
                         P0f["dn_bf_b"], P0f["dnT"], P0f["dnT_b"], cbuf, tslab)
        emit_weight_prep(em, P0a["w_qkv"], D, 3 * D, P0a["qkv_bf"],
                         P0a["qkv_bf_b"], None, None, cbuf, tslab)
        emit_weight_prep(em, P0a["w_out"], D, D, P0a["out_bf"],
                         P0a["out_bf_b"], None, None, cbuf, tslab)
        sh["prep_sems"] = [b["buf"].dsem for b in cbuf + tslab
                           if b["buf"].dsem is not None]
        emit_ffn_l0(nc, em, sems, sh, P0f, "fa_")
        emit_weight_prep(em, P1f["w_up"], INTER, D, P1f["up_bf"],
                         P1f["up_bf_b"], P1f["upT"], P1f["upT_b"], cbuf, tslab,
                         tq="scalar")
        emit_weight_prep(em, P1f["w_gt"], INTER, D, P1f["gt_bf"],
                         P1f["gt_bf_b"], P1f["gtT"], P1f["gtT_b"], cbuf, tslab,
                         tq="scalar")
        emit_weight_prep(em, P1f["w_dn1"], D, INTER, P1f["dn1_bf"],
                         P1f["dn1_bf_b"], P1f["dn1T"], P1f["dn1T_b"],
                         cbuf, tslab, tq="scalar")
        emit_attn_l0(nc, em, sems, sh, P0a, "aa_")
        emit_ffn_l0(nc, em, sems, sh, P0f, "fb_")
        emit_ffn_l1(nc, em, sems, sh, P1f, "ga_")
        emit_attn_l1(nc, em, sems, sh, P1a, "ha_")
        emit_ffn_l1(nc, em, sems, sh, P1f, "gb_")

        fin = dma(em, "sync", x_o.rearrange("(c p) t -> p c t", p=128), Xh[:],
                  bufs_r=[Xbuf[0], Xbuf[1]])
        em.wait("sync", [fin])
        em.finalize()
    return nc



def shard_x(x):
    out = []
    for c in range(NCORE):
        b, h = c // 2, c % 2
        out.append(np.ascontiguousarray(
            np.asarray(x)[b, h * T:(h + 1) * T, :].T.astype(np.float32)))
    return out


def unshard_x(shards):
    x = np.empty((4, 2048, D), np.float32)
    for c in range(NCORE):
        b, h = c // 2, c % 2
        x[b, h * T:(h + 1) * T, :] = shards[c].T
    return x


_CACHE = {}


def _rope_tables(off):
    import math
    half = 32
    freqs = np.exp(np.arange(half, dtype=np.float32) * (-math.log(10000.0) / half))
    tg = np.arange(off, off + T, dtype=np.float32)
    ang = tg[:, None] * freqs[None, :]
    cos = np.cos(ang).astype(np.float32)
    sin = np.sin(ang).astype(np.float32)
    cosf = np.empty((64, T), np.float32)
    sinf = np.empty((64, T), np.float32)
    cosf[0::2] = cos.T
    cosf[1::2] = cos.T
    sinf[0::2] = -sin.T
    sinf[1::2] = sin.T
    return cosf, sinf


def kernel(**inputs):
    if "fused" not in _CACHE:
        _CACHE["fused"] = build_fused()
    xs = shard_x(inputs["L0_attn_x"])
    base = {
        "w_gu": np.asarray(inputs["L0_ffn_w_gate_up"]),
        "w_dn": np.asarray(inputs["L0_ffn_w_down"]),
        "w_qkv": np.asarray(inputs["L0_attn_w_qkv"]),
        "w_out": np.asarray(inputs["L0_attn_w_out"]),
        "decay_i": np.asarray(inputs["L0_attn_decay_param"]).reshape(1, 16)
                     .astype(np.float32),
        "w_up": np.asarray(inputs["L1_ffn_W_up"]),
        "w_gt": np.asarray(inputs["L1_ffn_W_gate"]),
        "w_dn1": np.asarray(inputs["L1_ffn_W_down"]),
        "su_ug": np.asarray(inputs["L1_ffn_SU_ug"]).reshape(16, 128),
        "su_dp": np.asarray(inputs["L1_ffn_SU_dp"]).reshape(64, 128),
        "wsc_ug": np.asarray(inputs["L1_ffn_Wscale_ug"]).reshape(128, 128),
        "wsc_dp": np.asarray(inputs["L1_ffn_Wscale_dp"]).reshape(16, 128),
        "w_dkv": np.asarray(inputs["L1_attn_W_dkv"]),
        "w_dq": np.asarray(inputs["L1_attn_W_dq"]),
        "w_kr": np.asarray(inputs["L1_attn_W_kr"]),
        "w_qc": np.asarray(inputs["L1_attn_W_qc"]),
        "w_qr": np.asarray(inputs["L1_attn_W_qr"]),
        "w_uk": np.asarray(inputs["L1_attn_W_uk"]),
        "w_uv": np.asarray(inputs["L1_attn_W_uv"]),
        "w_o": np.asarray(inputs["L1_attn_W_o"]),
    }
    in_maps = []
    for c in range(NCORE):
        p = c % 2
        cosf, sinf = _rope_tables(p * T)
        cosf = cosf.astype(nbf)
        sinf = sinf.astype(nbf)
        in_maps.append(dict(
            base, x_i=xs[c],
            csel_i=np.full((128, 1), float(p), np.float32),
            cosf_i=cosf, sinf_i=sinf,
            pc_i=np.full((128, 1), 1e9 if p == 0 else 0.0, np.float32),
            msel_i=np.tile(np.array([[1.0 - p, p]], np.float32), (128, 1)),
        ))
    res = run_bass_kernel_spmd(_CACHE["fused"], in_maps,
                               core_ids=list(range(NCORE)))
    return unshard_x([res.results[c]["x_o"] for c in range(NCORE)])


# version 08 (final)
# speedup vs baseline: 1.1863x; 1.0676x over previous
import contextlib
import numpy as np
import ml_dtypes
import concourse.bass as bass
import concourse.mybir as mybir
from concourse.bass_utils import run_bass_kernel_spmd

F32 = mybir.dt.float32
BF16 = mybir.dt.bfloat16
AF = mybir.ActivationFunctionType
ALU = mybir.AluOpType
AX = mybir.AxisListType
nbf = ml_dtypes.bfloat16

NCORE = 8
T = 1024
TT = 512
NTT = T // TT
D = 2048
DC = D // 128
INTER = 8192
EPS = 1e-5

ENGINES = ("sync", "tensor", "vector", "scalar", "gpsimd")



class Sem:
    __slots__ = ("h", "n")

    def __init__(self, handle):
        self.h = handle
        self.n = 0


class Buf:
    __slots__ = ("prod", "cons", "dma_owner", "dsem")

    def __init__(self, dma_owner=False):
        self.prod = {}
        self.cons = {}
        self.dma_owner = dma_owner
        self.dsem = None

    def read_waits(self, w):
        w.extend(self.prod.values())

    def note_read(self, h, v):
        k = id(h)
        if k not in self.cons or self.cons[k][1] < v:
            self.cons[k] = (h, v)

    def write_waits(self, w):
        w.extend(self.prod.values())
        w.extend(self.cons.values())

    def note_write(self, h, v):
        k = id(h)
        if k not in self.prod or self.prod[k][1] < v:
            self.prod[k] = (h, v)


class Em:
    def __init__(self, nc):
        self.nc = nc
        self.q = {e: [] for e in ENGINES}
        self.hw = {e: {} for e in ENGINES}
        self.pool = []
        self.used = []

    def take_sem(self):
        s = self.pool.pop()
        self.used.append(s)
        return s

    def mark(self):
        return len(self.used)

    def release(self, mark):
        self.pool.extend(self.used[mark:])
        del self.used[mark:]

    def _dedupe(self, eng, waits):
        out = []
        hw = self.hw[eng]
        for h, v in waits:
            k = id(h)
            if hw.get(k, -1) >= v:
                continue
            hw[k] = v
            out.append((h, v))
        return tuple(out)

    def emit(self, eng, method, args, kwargs, sem, inc, waits, bufs_r=(), bufs_w=()):
        w = []
        for b in bufs_r:
            b.read_waits(w)
        for b in bufs_w:
            b.write_waits(w)
        w.extend(waits)
        w = self._dedupe(eng, w)
        if sem is not None:
            sem.n += inc
            tgt = (sem.h, sem.n)
        else:
            tgt = None

        def f(e, method=method, args=args, kwargs=kwargs, w=w, tgt=tgt, inc=inc):
            for hh, vv in w:
                e.wait_ge(hh, vv)
            ins = getattr(e, method)(*args, **kwargs)
            if tgt is not None:
                ins.then_inc(tgt[0], inc)
        self.q[eng].append(f)
        if tgt is not None:
            for b in bufs_r:
                b.note_read(*tgt)
            for b in bufs_w:
                b.note_write(*tgt)
        return sem.n if sem is not None else None

    def wait(self, eng, waits):
        w = self._dedupe(eng, tuple(waits))
        if not w:
            return

        def f(e, w=w):
            for hh, vv in w:
                e.wait_ge(hh, vv)
        self.q[eng].append(f)

    def finalize(self):
        with self.nc.Block() as block:
            for eng in ENGINES:
                thunks = self.q[eng]
                if not thunks:
                    continue
                getattr(block, eng)(lambda e, th=tuple(thunks): [f(e) for f in th])


def dma(em, eng, out, in_, waits=(), bufs_r=(), bufs_w=(), transpose=False,
        sem=None):
    if sem is None:
        for b in tuple(bufs_w) + tuple(bufs_r):
            if b.dma_owner:
                if b.dsem is None:
                    b.dsem = em.take_sem()
                sem = b.dsem
                break
    assert sem is not None, "dma needs an owner buffer or explicit sem"
    kw = {"out": out, "in_": in_}
    if transpose:
        kw["transpose"] = True
    em.emit(eng, "dma_start", (), kw, sem, 16, waits, bufs_r, bufs_w)
    return (sem.h, sem.n)


def mm(em, sem, out, lhsT, rhs, start, stop, waits=(), bufs_r=(), bufs_w=()):
    return em.emit("tensor", "matmul", (out, lhsT, rhs),
                   {"start": start, "stop": stop}, sem, 1, waits, bufs_r, bufs_w)


def op(em, eng, sem, method, *args, waits=(), bufs_r=(), bufs_w=(), **kwargs):
    return em.emit(eng, method, args, kwargs, sem, 1, waits, bufs_r, bufs_w)


def full_barrier(em, sems, exclude=()):
    ex = set(id(s) for s in exclude)
    pts = [(s.h, s.n) for s in em.used if s.n > 0 and id(s) not in ex]
    pts += [(s.h, s.n) for s in em.pool if s.n > 0 and id(s) not in ex]
    pts += [(s.h, s.n) for s in sems.values() if s.n > 0]
    for eng in ENGINES:
        em.wait(eng, list(pts))



def emit_weight_prep(em, w_ap, rows, cols, bf_ap, bf_buf, t_ap, t_buf,
                     cbuf, tslab, tq="sync", defer=None, cast=True):
    def go(*a, **kw):
        if defer is None:
            dma(*a, **kw)
        else:
            defer.append(lambda a=a, kw=kw: dma(*a, **kw))
    CW = min(2048, cols)
    i = 0
    for g in range(rows // 128 if cast else 0):
        for cb in range(cols // CW):
            b = cbuf[i % 2]
            i += 1
            rs = slice(g * 128, (g + 1) * 128)
            cs = slice(cb * CW, (cb + 1) * CW)
            go(em, "gpsimd", b["h"][:, :CW], w_ap[rs, cs], bufs_w=[b["buf"]])
            go(em, "gpsimd", bf_ap[rs, cs], b["h"][:, :CW],
               bufs_r=[b["buf"]], bufs_w=[bf_buf])
    if t_ap is None:
        return
    nch = cols // 128
    RH = 2048
    nrh = (rows + RH - 1) // RH
    i = 0
    for ch in range(nch):
        for rh in range(nrh):
            r0, r1 = rh * RH, min((rh + 1) * RH, rows)
            b = tslab[i % 2]
            i += 1
            go(em, tq, b["h"][:, :r1 - r0],
               bf_ap[r0:r1, ch * 128:(ch + 1) * 128],
               bufs_r=[bf_buf], bufs_w=[b["buf"]], transpose=True)
            go(em, tq, t_ap[ch * 128:(ch + 1) * 128, r0:r1],
               b["h"][:, :r1 - r0], bufs_r=[b["buf"]], bufs_w=[t_buf])


def run_bg(sh, n):
    bg = sh.get("bg")
    if not bg:
        return
    for _ in range(min(n, len(bg))):
        bg.popleft()()



def emit_ln(em, sems, X, Xbuf, Y, Ybuf, tt, C, scr):
    st_, sv, sc, sp = sems["st"], sems["sv"], sems["sc"], sems["sp"]
    ts = slice(tt * TT, (tt + 1) * TT)
    ps = scr["ps"]
    psb = scr["psb"]
    sq, sqb = scr["sq"], scr["sqb"]
    rowsb = scr["rowsb"]
    m_row, t_row = scr["rowaps"][0], scr["rowaps"][1]
    mr_row = m_row
    msq_row = v_row = sd_row = rstd_row = t_row

    for hc in range(DC):
        mm(em, st_ if hc == DC - 1 else None, ps[0][:1, :], C["ones_col"],
           X[:, hc, ts], start=(hc == 0), stop=(hc == DC - 1),
           bufs_r=[Xbuf[tt]], bufs_w=[psb[0]] if hc == 0 else ())
    s1 = (st_.h, st_.n)
    for hc in range(DC):
        op(em, "scalar", sc, "activation", sq[hc % 2]["h"][:], X[:, hc, ts],
           AF.Square, bufs_r=[Xbuf[tt]], bufs_w=[sqb[hc % 2]])
        mm(em, st_, ps[1][:1, :], C["ones_col"], sq[hc % 2]["h"][:],
           start=(hc == 0), stop=(hc == DC - 1),
           bufs_r=[sqb[hc % 2]], bufs_w=[psb[1]] if hc == 0 else ())
    s2 = (st_.h, st_.n)

    op(em, "vector", sv, "tensor_scalar_mul", m_row, ps[0][:1, :], 1.0 / D,
       waits=[s1], bufs_r=[psb[0]], bufs_w=[rowsb])
    op(em, "scalar", sc, "activation", msq_row, m_row, AF.Square,
       bufs_r=[rowsb], bufs_w=[rowsb])
    op(em, "vector", sv, "scalar_tensor_tensor", v_row, ps[1][:1, :], 1.0 / D,
       msq_row, ALU.mult, ALU.subtract, waits=[s2], bufs_r=[psb[1], rowsb],
       bufs_w=[rowsb])
    op(em, "vector", sv, "tensor_scalar_add", v_row, v_row, EPS,
       bufs_r=[rowsb], bufs_w=[rowsb])
    op(em, "scalar", sc, "activation", sd_row, v_row, AF.Sqrt,
       bufs_r=[rowsb], bufs_w=[rowsb])
    op(em, "vector", sv, "reciprocal", rstd_row, sd_row,
       bufs_r=[rowsb], bufs_w=[rowsb])
    op(em, "vector", sv, "tensor_mul", mr_row, m_row, rstd_row,
       bufs_r=[rowsb], bufs_w=[rowsb])

    mm(em, st_, ps[2][:, :], C["ones_row"], rstd_row, start=True, stop=True,
       bufs_r=[rowsb], bufs_w=[psb[2]])
    mm(em, st_, ps[3][:, :], C["ones_row"], mr_row, start=True, stop=True,
       bufs_r=[rowsb], bufs_w=[psb[3]])
    op(em, "vector", sv, "tensor_copy", scr["R"]["h"][:], ps[2][:, :],
       bufs_r=[psb[2]], bufs_w=[scr["R"]["buf"]])
    op(em, "vector", sv, "tensor_copy", scr["MR"]["h"][:], ps[3][:, :],
       bufs_r=[psb[3]], bufs_w=[scr["MR"]["buf"]])

    tmp, tmpb = scr["tmp"], scr["tmpb"]
    for hc in range(DC):
        op(em, "vector", sv, "tensor_mul", tmp[hc % 2]["h"][:], X[:, hc, ts],
           scr["R"]["h"][:], bufs_r=[Xbuf[tt], scr["R"]["buf"]],
           bufs_w=[tmpb[hc % 2]])
        op(em, "vector", sv, "tensor_sub", Y[:, hc, :], tmp[hc % 2]["h"][:],
           scr["MR"]["h"][:], bufs_r=[tmpb[hc % 2], scr["MR"]["buf"]],
           bufs_w=[Ybuf] if hc == 0 else (),)
        Ybuf.note_write(sv.h, sv.n)



def declare_ffn_l0(nc):
    P = {}
    P["w_gu"] = nc.dram_tensor("w_gu", [2 * INTER, D], F32, kind="ExternalInput").ap()
    P["w_dn"] = nc.dram_tensor("w_dn", [D, INTER], F32, kind="ExternalInput").ap()
    P["gu_bf"] = nc.dram_tensor("gu_bf", [2 * INTER, D], BF16).ap()
    P["dn_bf"] = nc.dram_tensor("dn_bf", [D, INTER], BF16).ap()
    P["guT"] = nc.dram_tensor("guT", [D, 2 * INTER], BF16).ap()
    P["dnT"] = nc.dram_tensor("dnT", [INTER, D], BF16).ap()
    P["gu_bf_b"], P["dn_bf_b"], P["guT_b"], P["dnT_b"] = \
        Buf(), Buf(), Buf(), Buf()
    return P


def emit_ffn_l0(nc, em, sems, sh, P, pre):
    st_, sv, sc, sp = (sems[k] for k in ("st", "sv", "sc", "sp"))
    X, Xbuf, C = sh["X"], sh["Xbuf"], sh["C"]
    w_gu, w_dn = P["w_gu"], P["w_dn"]
    gu_bf, dn_bf, guT, dnT = P["gu_bf"], P["dn_bf"], P["guT"], P["dnT"]
    gu_bf_b, dn_bf_b, guT_b, dnT_b = (P["gu_bf_b"], P["dn_bf_b"],
                                      P["guT_b"], P["dnT_b"])
    smark = em.mark()
    with contextlib.ExitStack() as stack:
        ent = stack.enter_context

        def sb(name, shape, dtype):
            return {"h": ent(nc.sbuf_tensor(pre + name, shape, dtype)),
                    "buf": Buf(True)}

        Yh = ent(nc.sbuf_tensor(pre + "Y", [128, DC, TT], BF16))
        Y = Yh[:]
        Ybuf = Buf()
        hbuf = sb("hbuf", [128, 16, TT], BF16)
        slab_g = [sb(f"slg{i}", [128, DC, 256], BF16) for i in range(2)]
        slab_u = [sb(f"slu{i}", [128, DC, 256], BF16) for i in range(2)]
        slab_d = [sb(f"sld{i}", [128, 16, 256], BF16) for i in range(2)]
        sq = [sb(f"sq{i}", [128, TT], F32) for i in range(2)]
        s_scr = [sb(f"ss{i}", [128, TT], F32) for i in range(2)]
        tmp = [sb(f"tmp{i}", [128, TT], F32) for i in range(2)]
        rows = sb("rows", [1, 2, TT], F32)
        Rb = sb("Rb", [128, TT], F32)
        MRb = sb("MRb", [128, TT], F32)
        psh = [ent(nc.psum_tensor(pre + f"ps{i}", [128, 512], F32))
               for i in range(8)]
        psb = [Buf() for _ in range(8)]

        rh = rows["h"]
        rowaps = [rh[:1, 0, :], rh[:1, 1, :]]
        scr = {"ps": psh, "psb": psb, "sq": sq, "sqb": [s["buf"] for s in sq],
               "rowaps": rowaps, "rowsb": rows["buf"],
               "R": Rb, "MR": MRb, "tmp": tmp, "tmpb": [t["buf"] for t in tmp]}

        guT_r = guT.rearrange("(hc p) o -> p hc o", p=128)
        dnT_r = dnT.rearrange("(ic p) h -> p ic h", p=128)

        for tt in range(NTT):
            emit_ln(em, sems, X, Xbuf, Y, Ybuf, tt, C, scr)
            for q in range(4):
                run_bg(sh, 70)
                for i16 in range(16):
                    ic = q * 16 + i16
                    j = ic // 2
                    rel = ic % 2
                    if rel == 0:
                        dma(em, "sync", slab_g[j % 2]["h"][:],
                            guT_r[:, :, 256 * j:256 * (j + 1)],
                            bufs_r=[guT_b], bufs_w=[slab_g[j % 2]["buf"]])
                        dma(em, "sync", slab_u[j % 2]["h"][:],
                            guT_r[:, :, INTER + 256 * j:INTER + 256 * (j + 1)],
                            bufs_r=[guT_b], bufs_w=[slab_u[j % 2]["buf"]])
                    b = ic % 2
                    for hc in range(DC):
                        mm(em, st_ if hc == DC - 1 else None, psh[b][:, :],
                           slab_g[j % 2]["h"][:, hc, rel * 128:(rel + 1) * 128],
                           Y[:, hc, :], start=(hc == 0), stop=(hc == DC - 1),
                           bufs_r=[slab_g[j % 2]["buf"], Ybuf],
                           bufs_w=[psb[b]] if hc == 0 else ())
                    psb[b].note_write(st_.h, st_.n)
                    for hc in range(DC):
                        mm(em, st_ if hc == DC - 1 else None, psh[2 + b][:, :],
                           slab_u[j % 2]["h"][:, hc, rel * 128:(rel + 1) * 128],
                           Y[:, hc, :], start=(hc == 0), stop=(hc == DC - 1),
                           bufs_r=[slab_u[j % 2]["buf"], Ybuf],
                           bufs_w=[psb[2 + b]] if hc == 0 else ())
                    psb[2 + b].note_write(st_.h, st_.n)
                    op(em, "scalar", sc, "activation", s_scr[ic % 2]["h"][:],
                       psh[b][:, :], AF.Silu,
                       bufs_r=[psb[b]], bufs_w=[s_scr[ic % 2]["buf"]])
                    op(em, "vector", sv, "scalar_tensor_tensor",
                       hbuf["h"][:, i16, :], psh[2 + b][:, :], 1.0,
                       s_scr[ic % 2]["h"][:], ALU.mult, ALU.mult,
                       bufs_r=[psb[2 + b], s_scr[ic % 2]["buf"]],
                       bufs_w=[hbuf["buf"]] if i16 == 0 else ())
                    hbuf["buf"].note_write(sv.h, sv.n)
                for ocp in range(8):
                    dma(em, "sync", slab_d[ocp % 2]["h"][:],
                        dnT_r[:, q * 16:(q + 1) * 16, ocp * 256:(ocp + 1) * 256],
                        bufs_r=[dnT_b], bufs_w=[slab_d[ocp % 2]["buf"]])
                    for oc_rel in range(2):
                        oc = 2 * ocp + oc_rel
                        db = 4 + (oc % 2)
                        for i16 in range(16):
                            mm(em, st_ if i16 == 15 else None, psh[db][:, :],
                               slab_d[ocp % 2]["h"][:, i16,
                                                    oc_rel * 128:(oc_rel + 1) * 128],
                               hbuf["h"][:, i16, :], start=(i16 == 0),
                               stop=(i16 == 15),
                               bufs_r=[slab_d[ocp % 2]["buf"], hbuf["buf"]],
                               bufs_w=[psb[db]] if i16 == 0 else ())
                        psb[db].note_write(st_.h, st_.n)
                        op(em, "vector", sv, "scalar_tensor_tensor",
                           X[:, oc, tt * TT:(tt + 1) * TT], psh[db][:, :], 0.5,
                           X[:, oc, tt * TT:(tt + 1) * TT], ALU.mult, ALU.add,
                           bufs_r=[psb[db], Xbuf[tt]], bufs_w=[Xbuf[tt]])

    full_barrier(em, sems, exclude=sh.get("prep_sems", ()))
    em.release(smark)



def _had(n):
    h = np.array([[1.0]], np.float32)
    while h.shape[0] < n:
        h = np.block([[h, h], [h, -h]])
    return h


def declare_ffn_l1(nc):
    P = {}
    P["w_up"] = nc.dram_tensor("w_up", [INTER, D], F32, kind="ExternalInput").ap()
    P["w_gt"] = nc.dram_tensor("w_gt", [INTER, D], F32, kind="ExternalInput").ap()
    P["w_dn1"] = nc.dram_tensor("w_dn1", [D, INTER], F32, kind="ExternalInput").ap()
    P["su_ug"] = nc.dram_tensor("su_ug", [16, 128], F32, kind="ExternalInput").ap()
    P["su_dp"] = nc.dram_tensor("su_dp", [64, 128], F32, kind="ExternalInput").ap()
    P["wsc_ug"] = nc.dram_tensor("wsc_ug", [128, 128], F32, kind="ExternalInput").ap()
    P["wsc_dp"] = nc.dram_tensor("wsc_dp", [16, 128], F32, kind="ExternalInput").ap()
    P["up_bf"] = nc.dram_tensor("up_bf", [INTER, D], BF16).ap()
    P["gt_bf"] = nc.dram_tensor("gt_bf", [INTER, D], BF16).ap()
    P["dn1_bf"] = nc.dram_tensor("dn1_bf", [D, INTER], BF16).ap()
    P["upT"] = nc.dram_tensor("upT", [D, INTER], BF16).ap()
    P["gtT"] = nc.dram_tensor("gtT", [D, INTER], BF16).ap()
    P["dn1T"] = nc.dram_tensor("dn1T", [INTER, D], BF16).ap()
    P["u_dram"] = nc.dram_tensor("u_dram", [D, T], BF16).ap()
    P["u2_dram"] = nc.dram_tensor("u2_dram", [INTER, T], BF16).ap()
    for k in ("up_bf_b", "gt_bf_b", "dn1_bf_b", "upT_b", "gtT_b", "dn1T_b",
              "u_dram_b", "u2_dram_b"):
        P[k] = Buf()
    H128 = _had(128)
    H16 = _had(16)
    H64 = _had(64)
    P["h128a_d"] = nc.inline_tensor((H128 / 32.0).astype(nbf), name="h128a_d").ap()
    P["h128b_d"] = nc.inline_tensor((H128 / 64.0).astype(nbf), name="h128b_d").ap()
    bd16 = np.zeros((128, 128), np.float32)
    for pp in range(8):
        bd16[16 * pp:16 * (pp + 1), 16 * pp:16 * (pp + 1)] = H16 / np.sqrt(2.0)
    P["bd16_d"] = nc.inline_tensor(bd16.astype(nbf), name="bd16_d").ap()
    bd64 = np.zeros((128, 128), np.float32)
    for pp in range(2):
        bd64[64 * pp:64 * (pp + 1), 64 * pp:64 * (pp + 1)] = H64 / np.sqrt(2.0)
    P["bd64_d"] = nc.inline_tensor(bd64.astype(nbf), name="bd64_d").ap()
    P["identf_d"] = nc.inline_tensor(np.eye(128, dtype=np.float32),
                                     name="identf_d").ap()
    return P


def emit_ffn_l1(nc, em, sems, sh, P, pre):
    st_, sv, sc, sp = (sems[k] for k in ("st", "sv", "sc", "sp"))
    X, Xbuf, C = sh["X"], sh["Xbuf"], sh["C"]
    w_up, w_gt, w_dn = P["w_up"], P["w_gt"], P["w_dn1"]
    up_bf, gt_bf, dn_bf = P["up_bf"], P["gt_bf"], P["dn1_bf"]
    upT, gtT, dnT = P["upT"], P["gtT"], P["dn1T"]
    up_bf_b, gt_bf_b, dn_bf_b = P["up_bf_b"], P["gt_bf_b"], P["dn1_bf_b"]
    upT_b, gtT_b, dnT_b = P["upT_b"], P["gtT_b"], P["dn1T_b"]
    u_dram, u2_dram = P["u_dram"], P["u2_dram"]
    u_dram_b, u2_dram_b = P["u_dram_b"], P["u2_dram_b"]
    su_ug_d, su_dp_d = P["su_ug"], P["su_dp"]
    wsc_ug_d, wsc_dp_d = P["wsc_ug"], P["wsc_dp"]
    smark = em.mark()
    with contextlib.ExitStack() as stack:
        ent = stack.enter_context

        def sb(name, shape, dtype):
            return {"h": ent(nc.sbuf_tensor(pre + name, shape, dtype)),
                    "buf": Buf(True)}

        onc, onr = sh["onc"], sh["onr"]
        identf = sb("identf", [128, 128], F32)
        h128a = sb("h128a", [128, 128], BF16)
        h128b = sb("h128b", [128, 128], BF16)
        bd16_s = sb("bd16_s", [128, 128], BF16)
        bd64_s = sb("bd64_s", [128, 128], BF16)
        vrow = sb("vrow", [128, 128], F32)
        su_ug_c = sb("su_ug_c", [128, 16], F32)
        wsuS_c = sb("wsuS_c", [128, 64], F32)
        wsg_c = sb("wsg_c", [128, 64], F32)
        wdp_c = sb("wdp_c", [128, 16], F32)
        su_dp_c = sb("su_dp_c", [128, 64], F32)
        h128su = sb("h128su", [128, DC, 128], BF16)

        cr = []
        for tile, srcd in ((identf, P["identf_d"]), (h128a, P["h128a_d"]),
                           (h128b, P["h128b_d"]), (bd16_s, P["bd16_d"]),
                           (bd64_s, P["bd64_d"])):
            cr.append(dma(em, "sync", tile["h"][:], srcd, bufs_w=[tile["buf"]]))
        consts_ready = cr
        em.wait("tensor", cr)

        with contextlib.ExitStack() as vsc:
            vps = vsc.enter_context(nc.psum_tensor(pre + "vps", [128, 512], F32))
            vpsb = Buf()
            for rows_n, src, dstc in ((16, su_ug_d, su_ug_c), (64, su_dp_d, su_dp_c),
                                      (16, wsc_dp_d, wdp_c)):
                dma(em, "sync", vrow["h"][:rows_n, :], src,
                    bufs_w=[vrow["buf"]])
                op(em, "tensor", st_, "transpose", vps[:, :rows_n],
                   vrow["h"][:rows_n, :], identf["h"][:rows_n, :rows_n],
                   bufs_r=[vrow["buf"], identf["buf"]], bufs_w=[vpsb])
                op(em, "vector", sv, "tensor_copy", dstc["h"][:], vps[:, :rows_n],
                   bufs_r=[vpsb], bufs_w=[dstc["buf"]])
            dma(em, "sync", vrow["h"][:, :], wsc_ug_d, bufs_w=[vrow["buf"]])
            op(em, "tensor", st_, "transpose", vps[:, :128], vrow["h"][:, :],
               identf["h"][:], bufs_r=[vrow["buf"], identf["buf"]], bufs_w=[vpsb])
            op(em, "vector", sv, "tensor_copy", wsuS_c["h"][:], vps[:, :64],
               bufs_r=[vpsb], bufs_w=[wsuS_c["buf"]])
            op(em, "vector", sv, "tensor_copy", wsg_c["h"][:], vps[:, 64:128],
               bufs_r=[vpsb], bufs_w=[wsg_c["buf"]])
            op(em, "vector", sv, "tensor_mul", wsuS_c["h"][:], wsuS_c["h"][:],
               su_dp_c["h"][:], bufs_r=[wsuS_c["buf"], su_dp_c["buf"]],
               bufs_w=[wsuS_c["buf"]])
            op(em, "vector", sv, "tensor_scalar_mul", wdp_c["h"][:],
               wdp_c["h"][:], 0.5, bufs_r=[wdp_c["buf"]], bufs_w=[wdp_c["buf"]])
            for c in range(DC):
                op(em, "vector", sv, "tensor_scalar_mul",
                   h128su["h"][:, c, :], h128a["h"][:], su_ug_c["h"][:, c:c + 1],
                   bufs_r=[h128a["buf"], su_ug_c["buf"]], bufs_w=[h128su["buf"]])
        full_barrier(em, sems, exclude=sh.get("prep_sems", ()))

        Yh = ent(nc.sbuf_tensor(pre + "Y", [128, DC, TT], BF16))
        Y = Yh[:]
        Ybuf = Buf()
        sq = [sb(f"sq{i}", [128, TT], F32) for i in range(2)]
        tmp = [sb(f"tmp{i}", [128, TT], F32) for i in range(2)]
        rows = sb("rows", [1, 2, TT], F32)
        Rb = sb("Rb", [128, TT], F32)
        MRb = sb("MRb", [128, TT], F32)
        u_sb = sb("u_sb", [128, DC, TT], BF16)
        z1 = {"h": Yh, "buf": None}
        slab_u = [sb(f"slab_u{i}", [128, DC, 256], BF16) for i in range(2)]
        slab_g = [sb(f"slab_g{i}", [128, DC, 256], BF16) for i in range(2)]
        slab_d = [sb(f"sld{i}", [128, 16, 256], BF16) for i in range(2)]
        h_t = [sb(f"h_t{i}", [128, TT], BF16) for i in range(2)]
        s_scr = [sb(f"ss{i}", [128, TT], F32) for i in range(2)]
        u2g = sb("u2g", [128, 8, TT], BF16)
        zin2 = u_sb
        z2 = z1
        psh = [ent(nc.psum_tensor(pre + f"ps{i}", [128, 512], F32))
               for i in range(8)]
        psb = [Buf() for _ in range(8)]
        z1["buf"] = Ybuf

        rh = rows["h"]
        rowaps = [rh[:1, 0, :], rh[:1, 1, :]]
        scr = {"ps": psh, "psb": psb, "sq": sq, "sqb": [s["buf"] for s in sq],
               "rowaps": rowaps, "rowsb": rows["buf"],
               "R": Rb, "MR": MRb, "tmp": tmp, "tmpb": [t["buf"] for t in tmp]}

        upT_r = upT.rearrange("(c2 pj pp) i -> pp c2 pj i", c2=16, pj=16, pp=8)
        gtT_r = gtT.rearrange("(c2 pj pp) i -> pp c2 pj i", c2=16, pj=16, pp=8)
        dnT_r = dnT.rearrange("(a2 pj pp) h -> pp a2 pj h", a2=64, pj=64, pp=2)
        ud_r = u_dram.rearrange("(c pj pp) t -> pp c pj t", c=16, pj=16, pp=8)
        u2d_r = u2_dram.rearrange("(a pj pp) t -> pp a pj t", a=64, pj=64, pp=2)

        for tt in range(NTT):
            ts = slice(tt * TT, (tt + 1) * TT)
            emit_ln(em, sems, X, Xbuf, Y, Ybuf, tt, {"ones_col": onc["h"][:],
                    "ones_row": onr["h"][:]}, scr)
            for c in range(DC):
                b = c % 2
                mm(em, st_, psh[b][:, :], h128su["h"][:, c, :], Y[:, c, :],
                   start=True, stop=True,
                   bufs_r=[h128su["buf"], Ybuf], bufs_w=[psb[b]])
                op(em, "vector", sv, "tensor_copy", u_sb["h"][:, c, :],
                   psh[b][:, :], bufs_r=[psb[b]], bufs_w=[u_sb["buf"]])
            dma(em, "sync", u_dram.rearrange("(c p) t -> p c t", p=128)[:, :, ts],
                u_sb["h"][:], bufs_r=[u_sb["buf"]], bufs_w=[u_dram_b])
            dma(em, "sync", u_sb["h"][:], ud_r[:, :, :, ts],
                bufs_r=[u_dram_b], bufs_w=[u_sb["buf"]])
            for pj in range(DC):
                b = pj % 2
                mm(em, st_, psh[b][:, :], bd16_s["h"][:], u_sb["h"][:, pj, :],
                   start=True, stop=True,
                   bufs_r=[bd16_s["buf"], u_sb["buf"]], bufs_w=[psb[b]])
                op(em, "vector", sv, "tensor_copy", z1["h"][:, pj, :],
                   psh[b][:, :], bufs_r=[psb[b]], bufs_w=[z1["buf"]])
            for a in range(64):
                jb = a % 2
                if a % 2 == 0:
                    sbi = (a // 2) % 2
                    dma(em, "sync", slab_u[sbi]["h"][:],
                        upT_r[:, :, :, 128 * a:128 * (a + 2)],
                        bufs_r=[upT_b], bufs_w=[slab_u[sbi]["buf"]])
                    dma(em, "sync", slab_g[sbi]["h"][:],
                        gtT_r[:, :, :, 128 * a:128 * (a + 2)],
                        bufs_r=[gtT_b], bufs_w=[slab_g[sbi]["buf"]])
                sbi = (a // 2) % 2
                rel = a % 2
                for pj in range(DC):
                    mm(em, st_ if pj == DC - 1 else None, psh[jb][:, :],
                       slab_g[sbi]["h"][:, pj, rel * 128:(rel + 1) * 128],
                       z1["h"][:, pj, :], start=(pj == 0), stop=(pj == DC - 1),
                       bufs_r=[slab_g[sbi]["buf"], z1["buf"]],
                       bufs_w=[psb[jb]] if pj == 0 else ())
                psb[jb].note_write(st_.h, st_.n)
                for pj in range(DC):
                    mm(em, st_ if pj == DC - 1 else None, psh[2 + jb][:, :],
                       slab_u[sbi]["h"][:, pj, rel * 128:(rel + 1) * 128],
                       z1["h"][:, pj, :], start=(pj == 0), stop=(pj == DC - 1),
                       bufs_r=[slab_u[sbi]["buf"], z1["buf"]],
                       bufs_w=[psb[2 + jb]] if pj == 0 else ())
                psb[2 + jb].note_write(st_.h, st_.n)
                op(em, "scalar", sc, "activation", s_scr[a % 2]["h"][:],
                   psh[jb][:, :], AF.Silu, scale=wsg_c["h"][:, a:a + 1],
                   bufs_r=[psb[jb], wsg_c["buf"]], bufs_w=[s_scr[a % 2]["buf"]])
                op(em, "vector", sv, "scalar_tensor_tensor", h_t[a % 2]["h"][:],
                   psh[2 + jb][:, :], wsuS_c["h"][:, a:a + 1], s_scr[a % 2]["h"][:],
                   ALU.mult, ALU.mult,
                   bufs_r=[psb[2 + jb], wsuS_c["buf"], s_scr[a % 2]["buf"]],
                   bufs_w=[h_t[a % 2]["buf"]])
                mm(em, st_, psh[4 + jb][:, :], h128b["h"][:], h_t[a % 2]["h"][:],
                   start=True, stop=True,
                   bufs_r=[h128b["buf"], h_t[a % 2]["buf"]], bufs_w=[psb[4 + jb]])
                op(em, "vector", sv, "tensor_copy", u2g["h"][:, a % 8, :],
                   psh[4 + jb][:, :], bufs_r=[psb[4 + jb]], bufs_w=[u2g["buf"]])
                if a % 8 == 7:
                    dma(em, "sync",
                        u2_dram.rearrange("(g p) t -> p g t", p=128)[
                            :, a - 7:a + 1, ts],
                        u2g["h"][:], bufs_r=[u2g["buf"]], bufs_w=[u2_dram_b])
            for grp in range(4):
                for pp_ in range(2):
                    dma(em, "sync", zin2["h"][64 * pp_:64 * (pp_ + 1), :, :],
                        u2d_r[pp_, :, 16 * grp:16 * (grp + 1), ts],
                        bufs_r=[u2_dram_b], bufs_w=[zin2["buf"]])
                for pjr in range(16):
                    b = pjr % 2
                    mm(em, st_, psh[b][:, :], bd64_s["h"][:],
                       zin2["h"][:, pjr, :], start=True, stop=True,
                       bufs_r=[bd64_s["buf"], zin2["buf"]], bufs_w=[psb[b]])
                    op(em, "vector", sv, "tensor_copy", z2["h"][:, pjr, :],
                       psh[b][:, :], bufs_r=[psb[b]], bufs_w=[z2["buf"]])
                for ocp in range(8):
                    sdb = ocp % 2
                    for pp_ in range(2):
                        dma(em, "sync",
                            slab_d[sdb]["h"][64 * pp_:64 * (pp_ + 1), :, :],
                            dnT_r[pp_, :, 16 * grp:16 * (grp + 1),
                                  ocp * 256:(ocp + 1) * 256],
                            bufs_r=[dnT_b], bufs_w=[slab_d[sdb]["buf"]])
                    for oc_rel in range(2):
                        oc = 2 * ocp + oc_rel
                        db = 4 + (oc % 2)
                        for pjr in range(16):
                            mm(em, st_ if pjr == 15 else None, psh[db][:, :],
                               slab_d[sdb]["h"][:, pjr,
                                                oc_rel * 128:(oc_rel + 1) * 128],
                               z2["h"][:, pjr, :], start=(pjr == 0),
                               stop=(pjr == 15),
                               bufs_r=[slab_d[sdb]["buf"], z2["buf"]],
                               bufs_w=[psb[db]] if pjr == 0 else ())
                        psb[db].note_write(st_.h, st_.n)
                        op(em, "vector", sv, "scalar_tensor_tensor",
                           X[:, oc, ts], psh[db][:, :], wdp_c["h"][:, oc:oc + 1],
                           X[:, oc, ts], ALU.mult, ALU.add,
                           bufs_r=[psb[db], wdp_c["buf"], Xbuf[tt]],
                           bufs_w=[Xbuf[tt]])

    full_barrier(em, sems, exclude=sh.get("prep_sems", ()))
    em.release(smark)



def declare_attn_l0(nc):
    H = 16
    P = {}
    P["w_qkv"] = nc.dram_tensor("w_qkv", [D, 3 * D], F32, kind="ExternalInput").ap()
    P["w_out"] = nc.dram_tensor("w_out", [D, D], F32, kind="ExternalInput").ap()
    P["decay"] = nc.dram_tensor("decay_i", [1, H], F32, kind="ExternalInput").ap()
    P["csel"] = nc.dram_tensor("csel_i", [128, 1], F32, kind="ExternalInput").ap()
    P["att_dram"] = nc.dram_tensor("att_dram", [D, T], BF16).ap()
    P["qkv_bf"] = nc.dram_tensor("qkv_bf", [D, 3 * D], BF16).ap()
    P["out_bf"] = nc.dram_tensor("out_bf", [D, D], BF16).ap()
    P["qkv_bf_b"], P["out_bf_b"] = Buf(), Buf()
    P["contrib"] = nc.dram_tensor("contrib0", [128, 4], F32)
    P["gathered"] = nc.dram_tensor("gathered0", [256, 4], F32)
    jj, tt_ = np.meshgrid(np.arange(128), np.arange(128), indexing="ij")
    E_np = np.maximum(tt_ - jj, 0).astype(np.float32)
    LT_np = (jj <= tt_).astype(np.float32)
    P["E_d"] = nc.inline_tensor(E_np, name="E_d").ap()
    P["LT_d"] = nc.inline_tensor(LT_np, name="LT_d").ap()
    P["iota_d"] = nc.inline_tensor(
        np.arange(1, T + 1, dtype=np.float32).reshape(1, T), name="iota_d").ap()
    P["zcol_d"] = nc.inline_tensor(np.zeros((128, 1), np.float32),
                                   name="zcol_d").ap()
    return P


def emit_attn_l0(nc, em, sems, sh, P, pre):
    H = 16
    st_, sv, sc, sp, scc = (sems[k] for k in ("st", "sv", "sc", "sp", "cc"))
    X, Xbuf, C = sh["X"], sh["Xbuf"], sh["C"]
    w_qkv, w_out = P["qkv_bf"], P["out_bf"]
    qkv_bf_b, out_bf_b = P["qkv_bf_b"], P["out_bf_b"]
    decay_d, csel_d = P["decay"], P["csel"]
    att_dram, contrib, gathered = P["att_dram"], P["contrib"], P["gathered"]
    E_d, LT_d, iota_d, zcol_d = P["E_d"], P["LT_d"], P["iota_d"], P["zcol_d"]
    ident_d = sh["ident_d"]
    contrib_buf = Buf()
    gathered_buf = Buf()
    smark = em.mark()
    with contextlib.ExitStack() as stack:
        ent = stack.enter_context

        def sb(name, shape, dtype):
            return {"h": ent(nc.sbuf_tensor(pre + name, shape, dtype)),
                    "buf": Buf(True)}

        att_dram_b = Buf()
        onc, onr = sh["onc"], sh["onr"]
        E_sb = sb("E_sb", [128, 128], F32)
        LT_sb = sb("LT_sb", [128, 128], F32)
        iota_sb = sb("iota", [1, T], F32)
        ident = sb("ident", [128, 128], BF16)
        zcol = sb("zcol", [128, 1], F32)
        csel = sb("csel", [128, 1], F32)
        dec = sb("dec", [1, 2, H], F32)
        Yh = ent(nc.sbuf_tensor(pre + "Y", [128, DC, T], BF16))
        Y = Yh[:]
        Ybuf = Buf()
        sq = [sb(f"sq{i}", [128, TT], F32) for i in range(2)]
        tmp = [sb(f"tmp{i}", [128, TT], F32) for i in range(2)]
        rows = sb("rows", [1, 2, TT], F32)
        Rb = sb("Rb", [128, TT], F32)
        MRb = sb("MRb", [128, TT], F32)

        cr = []
        for tile, srcd in ((E_sb, E_d),
                           (LT_sb, LT_d), (iota_sb, iota_d), (ident, ident_d),
                           (zcol, zcol_d), (csel, csel_d)):
            cr.append(dma(em, "sync", tile["h"][:], srcd, bufs_w=[tile["buf"]]))
        cr.append(dma(em, "sync", dec["h"][:1, 0, :], decay_d,
                      bufs_w=[dec["buf"]]))
        consts_ready = cr
        em.wait("tensor", cr)
        em.wait("scalar", cr)
        em.wait("vector", cr)
        C = {"ones_col": onc["h"][:], "ones_row": onr["h"][:]}

        op(em, "scalar", sc, "activation", dec["h"][:1, 0, :], dec["h"][:1, 0, :],
           AF.Sigmoid, bufs_r=[dec["buf"]], bufs_w=[dec["buf"]])
        op(em, "scalar", sc, "activation", dec["h"][:1, 1, :], dec["h"][:1, 0, :],
           AF.Ln, bufs_r=[dec["buf"]], bufs_w=[dec["buf"]])
        lnd = dec["h"]

        with contextlib.ExitStack() as lnsc:
            lps = [lnsc.enter_context(
                nc.psum_tensor(pre + f"lps{i}", [128, 512], F32))
                   for i in range(4)]
            lpsb = [Buf() for _ in range(4)]
            rh = rows["h"]
            rowaps = [rh[:1, 0, :], rh[:1, 1, :]]
            scr = {"ps": lps, "psb": lpsb, "sq": sq, "sqb": [s["buf"] for s in sq],
                   "rowaps": rowaps, "rowsb": rows["buf"],
                   "R": Rb, "MR": MRb, "tmp": tmp, "tmpb": [t["buf"] for t in tmp]}
            for tt in range(NTT):
                emit_ln(em, sems, X, Xbuf, Y[:, :, tt * TT:(tt + 1) * TT], Ybuf,
                        tt, C, scr)
        full_barrier(em, sems, exclude=sh.get("prep_sems", ()))

        wq_r = w_qkv.rearrange("(kc p) o -> p kc o", p=128)

        with contextlib.ExitStack() as body:
            def bsb(name, shape, dtype):
                return {"h": body.enter_context(
                    nc.sbuf_tensor(pre + name, shape, dtype)),
                        "buf": Buf(True)}
            q_sb = bsb("q_sb", [128, 2, T], BF16)
            k_sb = bsb("k_sb", [128, 2, T], BF16)
            kv_sb = bsb("kv_sb", [128, 2, T], BF16)
            v_sb = bsb("v_sb", [128, 2, T], BF16)
            S_scr = bsb("S_scr", [128, 4, T], F32)
            att_q = bsb("att_q", [128, 2, T], BF16)
            slab = bsb("slab", [128, DC, 256], BF16)
            phi_mn = sq
            phi_ex = tmp
            phi_rl = [Rb, MRb]
            xT = [bsb(f"xT{i}", [128, 128], BF16) for i in range(2)]
            A_bf = bsb("A_bf", [128, 2, 128], BF16)
            aexp = bsb("aexp", [128, 128], F32)
            powf = bsb("powf", [128, 2, T], BF16)
            prow = bsb("prow", [1, T], F32)
            u_scr = bsb("u_scr", [128, T], F32)
            t2_scr = bsb("t2_scr", [128, T], F32)
            carry = bsb("carry", [128, 4], F32)
            block0 = bsb("block0", [128, 4], F32)
            Ksel = bsb("Ksel", [128, 4], F32)
            den_sb = {"h": None, "buf": sq[0]["buf"]}
            rden_sb = {"h": None, "buf": tmp[0]["buf"]}
            den_half = [sq[0]["h"][0:1, :], sq[1]["h"][0:1, :]]
            rden_half = [tmp[0]["h"][0:1, :], tmp[1]["h"][0:1, :]]
            denb = [sq[0]["buf"], sq[1]["buf"]]
            rdenb = [tmp[0]["buf"], tmp[1]["buf"]]

            pj = [body.enter_context(
                nc.psum_tensor(pre + f"pj{i}", [128, 512], F32))
                  for i in range(2)]
            pjb = [Buf(), Buf()]
            tbf = [body.enter_context(
                nc.psum_tensor(pre + f"tb{i}", [128, 1024], BF16))
                   for i in range(2)]
            tbfb = [Buf(), Buf()]
            msc = [body.enter_context(
                nc.psum_tensor(pre + f"ms{i}", [128, 512], F32))
                   for i in range(2)]
            mscb = [Buf(), Buf()]

            lnd_b = bsb("lnd_b", [128, H], F32)
            mm(em, st_, msc[0][:, :H], C["ones_row"], lnd[:1, 1, :],
               start=True, stop=True, bufs_r=[dec["buf"], onr["buf"]],
               bufs_w=[mscb[0]])
            op(em, "vector", sv, "tensor_copy", lnd_b["h"][:], msc[0][:, :H],
               bufs_r=[mscb[0]], bufs_w=[lnd_b["buf"]])

            for hq in range(8):
                run_bg(sh, 70)
                heads = (2 * hq, 2 * hq + 1)
                for kind, base in (("v", 2 * D), ("k", D), ("q", 0)):
                    span = base + 256 * hq
                    dma(em, "sync", slab["h"][:],
                        wq_r[:, :, span:span + 256],
                        bufs_r=[qkv_bf_b], bufs_w=[slab["buf"]])
                    for tt in range(NTT):
                        ts = slice(tt * TT, (tt + 1) * TT)
                        for hr in range(2):
                            pb = (tt * 2 + hr) % 2
                            for kc in range(DC):
                                mm(em, st_ if kc == DC - 1 else None, pj[pb][:, :],
                                   slab["h"][:, kc, hr * 128:(hr + 1) * 128],
                                   Y[:, kc, ts], start=(kc == 0), stop=(kc == DC - 1),
                                   bufs_r=[slab["buf"], Ybuf],
                                   bufs_w=[pjb[pb]] if kc == 0 else ())
                            pjb[pb].note_write(st_.h, st_.n)
                            if kind == "v":
                                op(em, "vector", sv, "tensor_copy",
                                   v_sb["h"][:, hr, ts], pj[pb][:, :],
                                   bufs_r=[pjb[pb]], bufs_w=[v_sb["buf"]])
                            else:
                                dst = q_sb if kind == "q" else k_sb
                                op(em, "vector", sv, "tensor_scalar_min",
                                   phi_mn[hr]["h"][:], pj[pb][:, :], 0.0,
                                   bufs_r=[pjb[pb]], bufs_w=[phi_mn[hr]["buf"]])
                                op(em, "scalar", sc, "activation",
                                   phi_ex[hr]["h"][:], phi_mn[hr]["h"][:], AF.Exp,
                                   bufs_r=[phi_mn[hr]["buf"]],
                                   bufs_w=[phi_ex[hr]["buf"]])
                                op(em, "scalar", sc, "activation",
                                   phi_rl[hr]["h"][:], pj[pb][:, :], AF.Relu,
                                   bufs_r=[pjb[pb]], bufs_w=[phi_rl[hr]["buf"]])
                                op(em, "vector", sv, "tensor_add",
                                   dst["h"][:, hr, ts], phi_rl[hr]["h"][:],
                                   phi_ex[hr]["h"][:],
                                   bufs_r=[phi_rl[hr]["buf"], phi_ex[hr]["buf"]],
                                   bufs_w=[dst["buf"]])
                                if kind == "k":
                                    op(em, "vector", sv, "tensor_mul",
                                       kv_sb["h"][:, hr, ts], k_sb["h"][:, hr, ts],
                                       v_sb["h"][:, hr, ts],
                                       bufs_r=[k_sb["buf"], v_sb["buf"]],
                                       bufs_w=[kv_sb["buf"]])

                for hr, h in enumerate(heads):
                    lnd_ap = lnd_b["h"][:, h:h + 1]
                    op(em, "scalar", sc, "activation", aexp["h"][:], E_sb["h"][:],
                       AF.Exp, scale=lnd_ap,
                       bufs_r=[E_sb["buf"], lnd_b["buf"]], bufs_w=[aexp["buf"]])
                    op(em, "vector", sv, "tensor_mul", A_bf["h"][:, hr, :],
                       aexp["h"][:], LT_sb["h"][:],
                       bufs_r=[aexp["buf"], LT_sb["buf"]], bufs_w=[A_bf["buf"]])
                    op(em, "scalar", sc, "activation", prow["h"][:], iota_sb["h"][:],
                       AF.Exp, scale=lnd[:1, 1, h:h + 1],
                       bufs_r=[iota_sb["buf"], dec["buf"]], bufs_w=[prow["buf"]])
                    for half in range(2):
                        mm(em, st_, msc[half][:, :], C["ones_row"],
                           prow["h"][:1, half * 512:(half + 1) * 512],
                           start=True, stop=True,
                           bufs_r=[prow["buf"], onr["buf"]], bufs_w=[mscb[half]])
                        op(em, "vector", sv, "tensor_copy",
                           powf["h"][:, hr, half * 512:(half + 1) * 512],
                           msc[half][:, :], bufs_r=[mscb[half]],
                           bufs_w=[powf["buf"]])

                    for kind_i, src in ((0, kv_sb), (1, k_sb)):
                        sidx = 2 * hr + kind_i
                        for tc in range(8):
                            tcs = slice(tc * 128, (tc + 1) * 128)
                            op(em, "tensor", st_, "transpose",
                               tbf[tc % 2][:, :128], src["h"][:, hr, tcs],
                               ident["h"][:],
                               bufs_r=[src["buf"], ident["buf"]],
                               bufs_w=[tbfb[tc % 2]])
                            op(em, "vector", sv, "tensor_copy", xT[tc % 2]["h"][:],
                               tbf[tc % 2][:, :128],
                               bufs_r=[tbfb[tc % 2]], bufs_w=[xT[tc % 2]["buf"]])
                            mm(em, st_, msc[tc % 2][:, :128], xT[tc % 2]["h"][:],
                               A_bf["h"][:, hr, :], start=True, stop=True,
                               bufs_r=[xT[tc % 2]["buf"], A_bf["buf"]],
                               bufs_w=[mscb[tc % 2]])
                            cprev = (S_scr["h"][:, sidx, tc * 128 - 1:tc * 128]
                                     if tc > 0 else zcol["h"][:])
                            op(em, "vector", sv, "scalar_tensor_tensor",
                               S_scr["h"][:, sidx, tcs],
                               powf["h"][:, hr, 0:128], cprev,
                               msc[tc % 2][:, :128], ALU.mult, ALU.add,
                               bufs_r=[mscb[tc % 2], powf["buf"], S_scr["buf"],
                                       zcol["buf"]],
                               bufs_w=[S_scr["buf"]])

                for hr in range(2):
                    for kind_i in range(2):
                        sidx = 2 * hr + kind_i
                        op(em, "vector", sv, "tensor_copy",
                           carry["h"][:, sidx:sidx + 1],
                           S_scr["h"][:, sidx, T - 1:T],
                           bufs_r=[S_scr["buf"]], bufs_w=[carry["buf"]])
                dma(em, "sync", contrib[:, :], carry["h"][:],
                    bufs_r=[carry["buf"]], bufs_w=[contrib_buf])
                em.emit("gpsimd", "collective_compute",
                        ("AllGather", ALU.bypass,
                         [[0, 1], [2, 3], [4, 5], [6, 7]],
                         [contrib.ap().opt()], [gathered.ap().opt()]),
                        {}, scc, 1, (), bufs_r=[contrib_buf],
                        bufs_w=[gathered_buf])
                dma(em, "sync", block0["h"][:], gathered[0:128, :],
                    bufs_r=[gathered_buf], bufs_w=[block0["buf"]])
                op(em, "vector", sv, "tensor_scalar_mul", Ksel["h"][:],
                   block0["h"][:], csel["h"][:],
                   bufs_r=[block0["buf"], csel["buf"]], bufs_w=[Ksel["buf"]])

                for hr, h in enumerate(heads):
                    for kind_i in range(2):
                        sidx = 2 * hr + kind_i
                        op(em, "vector", sv, "scalar_tensor_tensor",
                           S_scr["h"][:, sidx, :], powf["h"][:, hr, :],
                           Ksel["h"][:, sidx:sidx + 1], S_scr["h"][:, sidx, :],
                           ALU.mult, ALU.add,
                           bufs_r=[powf["buf"], Ksel["buf"], S_scr["buf"]],
                           bufs_w=[S_scr["buf"]])
                    op(em, "vector", sv, "tensor_mul", u_scr["h"][:],
                       q_sb["h"][:, hr, :], S_scr["h"][:, 2 * hr + 1, :],
                       bufs_r=[q_sb["buf"], S_scr["buf"]], bufs_w=[u_scr["buf"]])
                    op(em, "vector", sv, "tensor_mul", t2_scr["h"][:],
                       q_sb["h"][:, hr, :], S_scr["h"][:, 2 * hr, :],
                       bufs_r=[q_sb["buf"], S_scr["buf"]], bufs_w=[t2_scr["buf"]])
                    for half in range(2):
                        hs = slice(half * 512, (half + 1) * 512)
                        mm(em, st_, msc[0][:1, :], C["ones_col"],
                           u_scr["h"][:, hs], start=True, stop=True,
                           bufs_r=[u_scr["buf"], onc["buf"]], bufs_w=[mscb[0]])
                        op(em, "vector", sv, "tensor_scalar_max",
                           den_half[half], msc[0][:1, :], 1e-6,
                           bufs_r=[mscb[0]], bufs_w=[denb[half]])
                        op(em, "vector", sv, "reciprocal", rden_half[half],
                           den_half[half],
                           bufs_r=[denb[half]], bufs_w=[rdenb[half]])
                        mm(em, st_, msc[1][:, :], C["ones_row"],
                           rden_half[half], start=True, stop=True,
                           bufs_r=[rdenb[half], onr["buf"]], bufs_w=[mscb[1]])
                        op(em, "vector", sv, "tensor_mul",
                           att_q["h"][:, hr, hs], t2_scr["h"][:, hs],
                           msc[1][:, :],
                           bufs_r=[t2_scr["buf"], mscb[1]], bufs_w=[att_q["buf"]])
                dma(em, "sync",
                    att_dram[256 * hq:256 * (hq + 1), :].rearrange(
                        "(r p) t -> p r t", p=128),
                    att_q["h"][:], bufs_r=[att_q["buf"]], bufs_w=[att_dram_b])

        full_barrier(em, sems, exclude=sh.get("prep_sems", ()))
        with contextlib.ExitStack() as osc:
            def osb(name, shape, dtype):
                return {"h": osc.enter_context(
                    nc.sbuf_tensor(pre + name, shape, dtype)),
                        "buf": Buf(True)}
            wslab = [osb(f"wo{i}", [128, DC, 256], BF16) for i in range(2)]
            arhs = [osb(f"ar{i}", [128, DC, TT], BF16) for i in range(2)]
            po = [osc.enter_context(
                nc.psum_tensor(pre + f"po{i}", [128, 512], F32))
                  for i in range(2)]
            pob = [Buf(), Buf()]
            wo_r = w_out.rearrange("(kc p) o -> p kc o", p=128)
            att_r = att_dram.rearrange("(kc p) t -> p kc t", p=128)
            for tt in range(NTT):
                ts = slice(tt * TT, (tt + 1) * TT)
                dma(em, "sync", arhs[tt % 2]["h"][:], att_r[:, :, ts],
                    bufs_r=[att_dram_b], bufs_w=[arhs[tt % 2]["buf"]])
                for sp_i in range(8):
                    dma(em, "sync", wslab[sp_i % 2]["h"][:],
                        wo_r[:, :, 256 * sp_i:256 * (sp_i + 1)],
                        bufs_r=[out_bf_b], bufs_w=[wslab[sp_i % 2]["buf"]])
                    for oc_rel in range(2):
                        oc = 2 * sp_i + oc_rel
                        pb = oc % 2
                        for kc in range(DC):
                            mm(em, st_ if kc == DC - 1 else None, po[pb][:, :],
                               wslab[sp_i % 2]["h"][:, kc,
                                                    oc_rel * 128:(oc_rel + 1) * 128],
                               arhs[tt % 2]["h"][:, kc, :],
                               start=(kc == 0), stop=(kc == DC - 1),
                               bufs_r=[wslab[sp_i % 2]["buf"], arhs[tt % 2]["buf"]],
                               bufs_w=[pob[pb]] if kc == 0 else ())
                        pob[pb].note_write(st_.h, st_.n)
                        op(em, "vector", sv, "tensor_add",
                           X[:, oc, ts], po[pb][:, :], X[:, oc, ts],
                           bufs_r=[pob[pb], Xbuf[tt]], bufs_w=[Xbuf[tt]])

    full_barrier(em, sems, exclude=sh.get("prep_sems", ()))
    em.release(smark)



def declare_attn_l1(nc):
    H = 16
    BIG = 1e9
    P = {}
    P["w_dkv"] = nc.dram_tensor("w_dkv", [D, 128], F32, kind="ExternalInput").ap()
    P["w_dq"] = nc.dram_tensor("w_dq", [D, 128], F32, kind="ExternalInput").ap()
    P["w_kr"] = nc.dram_tensor("w_kr", [D, 64], F32, kind="ExternalInput").ap()
    P["w_qc"] = nc.dram_tensor("w_qc", [H, 128, 64], F32, kind="ExternalInput").ap()
    P["w_qr"] = nc.dram_tensor("w_qr", [H, 128, 64], F32, kind="ExternalInput").ap()
    P["w_uk"] = nc.dram_tensor("w_uk", [H, 128, 64], F32, kind="ExternalInput").ap()
    P["w_uv"] = nc.dram_tensor("w_uv", [H, 128, 128], F32, kind="ExternalInput").ap()
    P["w_o"] = nc.dram_tensor("w_o", [D, D], F32, kind="ExternalInput").ap()
    P["cosf"] = nc.dram_tensor("cosf_i", [64, T], BF16, kind="ExternalInput").ap()
    P["sinf"] = nc.dram_tensor("sinf_i", [64, T], BF16, kind="ExternalInput").ap()
    P["pc"] = nc.dram_tensor("pc_i", [128, 1], F32, kind="ExternalInput").ap()
    P["msel"] = nc.dram_tensor("msel_i", [128, 2], F32, kind="ExternalInput").ap()
    P["contrib"] = nc.dram_tensor("contrib1", [192, T], BF16)
    P["gathered"] = nc.dram_tensor("gathered1", [384, T], BF16)
    psw = np.zeros((64, 64), np.float32)
    for m in range(64):
        psw[m ^ 1, m] = 1.0
    P["pswap_d"] = nc.inline_tensor(psw.astype(nbf), name="pswap_d").ap()
    qq = np.arange(128)[:, None]
    kk = np.arange(512)[None, :]
    md = np.stack([np.where(kk <= 128 * qm + qq, 0.0, -BIG)
                   for qm in range(4)], axis=1).astype(np.float32)
    P["mdiag_d"] = nc.inline_tensor(md, name="mdiag_d").ap()
    return P


def emit_attn_l1(nc, em, sems, sh, P, pre):
    H = 16
    SC = 0.08838834764831845
    st_, sv, sc, sp, scc = (sems[k] for k in ("st", "sv", "sc", "sp", "cc"))
    X, Xbuf, C = sh["X"], sh["Xbuf"], sh["C"]
    w_dkv, w_dq, w_kr = P["w_dkv"], P["w_dq"], P["w_kr"]
    w_qc, w_qr, w_uk, w_uv, w_o = (P["w_qc"], P["w_qr"], P["w_uk"],
                                   P["w_uv"], P["w_o"])
    cosf_d, sinf_d, pc_d, msel_d = P["cosf"], P["sinf"], P["pc"], P["msel"]
    contrib, gathered = P["contrib"], P["gathered"]
    pswap_d, mdiag_d = P["pswap_d"], P["mdiag_d"]
    ident_d = sh["ident_d"]
    ones_row_d = sh["onr_d"]
    contrib_b, gathered_b = Buf(), Buf()
    smark = em.mark()
    with contextlib.ExitStack() as stack:
        ent = stack.enter_context

        def sb(name, shape, dtype):
            return {"h": ent(nc.sbuf_tensor(pre + name, shape, dtype)),
                    "buf": Buf(True)}

        onc = sh["onc"]
        ident = sb("ident", [128, 128], BF16)
        pswap = sb("pswap", [64, 64], BF16)
        cosf = sb("cosf", [64, T], BF16)
        sinf = sb("sinf", [64, T], BF16)
        pc = sb("pc", [128, 1], F32)
        msel = sb("msel", [128, 2], F32)
        wdkv_s = sb("wdkv_s", [128, DC, 128], BF16)
        wdq_s = sb("wdq_s", [128, DC, 128], BF16)
        wkr_s = sb("wkr_s", [128, DC, 64], BF16)
        wqc_s = sb("wqc_s", [128, H, 64], BF16)
        wqr_s = sb("wqr_s", [128, H, 64], BF16)
        qnr = sb("qnr", [128, H, T], BF16)
        ckv_of = sb("ckv_of", [128, 2 * T], BF16)
        att_sb = sb("att_sb", [128, DC, T], BF16)
        krf_of = {"h": att_sb["h"], "buf": att_sb["buf"]}
        krf_ap = att_sb["h"][0:64, 0:2, :]

        cr = []
        for tile, srcd in ((ident, ident_d),
                           (pswap, pswap_d),
                           (cosf, cosf_d), (sinf, sinf_d), (pc, pc_d),
                           (msel, msel_d)):
            cr.append(dma(em, "sync", tile["h"][:], srcd, bufs_w=[tile["buf"]]))
        for tile, srcd in (
                (wdkv_s, w_dkv.rearrange("(kc p) o -> p kc o", p=128)),
                (wdq_s, w_dq.rearrange("(kc p) o -> p kc o", p=128)),
                (wkr_s, w_kr.rearrange("(kc p) o -> p kc o", p=128)),
                (wqc_s, w_qc.rearrange("h p c -> p h c")),
                (wqr_s, w_qr.rearrange("h p c -> p h c"))):
            cr.append(dma(em, "gpsimd", tile["h"][:], srcd,
                          bufs_w=[tile["buf"]]))
        consts_ready = cr
        C = {"ones_col": onc["h"][:], "ones_row": None}

        em.wait("tensor", consts_ready)
        em.wait("scalar", consts_ready)
        em.wait("vector", consts_ready)

        with contextlib.ExitStack() as s1:
            def s1b(name, shape, dtype):
                return {"h": s1.enter_context(
                    nc.sbuf_tensor(pre + name, shape, dtype)),
                        "buf": Buf(True)}
            Yh = s1.enter_context(nc.sbuf_tensor(pre + "Y", [128, DC, TT], BF16))
            Y = Yh[:]
            Ybuf = Buf()
            sq0 = s1b("sq0", [128, TT], F32)
            tmp0 = s1b("tmp0", [128, TT], F32)
            sq = [sq0, sq0]
            tmp = [tmp0, tmp0]
            rows = s1b("rows", [1, 2, TT], F32)
            Rb = s1b("Rb", [128, TT], F32)
            MRb = s1b("MRb", [128, TT], F32)
            onr = sh["onr"]
            dq_bf = s1b("dq_bf", [128, T], BF16)
            ckv_bf = s1b("ckv_bf", [128, T], BF16)
            kr_bf = s1b("kr_bf", [64, T], BF16)
            krop_bf = s1b("krop_bf", [64, T], BF16)
            qr_bf = s1b("qr_bf", [64, TT], BF16)
            rsc1 = {"h": Rb["h"], "buf": Rb["buf"]}
            rsc2 = {"h": MRb["h"], "buf": MRb["buf"]}
            b0c, b1c = dq_bf, ckv_bf
            b0k, b1k = kr_bf, krop_bf

            lps = [s1.enter_context(
                nc.psum_tensor(pre + f"lps{i}", [128, 512], F32))
                   for i in range(8)]
            lpsb = [Buf() for _ in range(8)]
            rh = rows["h"]
            rowaps = [rh[:1, 0, :], rh[:1, 1, :]]
            scr = {"ps": lps, "psb": lpsb, "sq": sq,
                   "sqb": [s["buf"] for s in sq],
                   "rowaps": rowaps, "rowsb": rows["buf"],
                   "R": Rb, "MR": MRb, "tmp": tmp,
                   "tmpb": [t["buf"] for t in tmp]}

            for tt in range(NTT):
                ts = slice(tt * TT, (tt + 1) * TT)
                emit_ln(em, sems, X, Xbuf, Y, Ybuf, tt,
                        {"ones_col": onc["h"][:], "ones_row": onr["h"][:]}, scr)
                for dst, wsl, pb in ((ckv_bf, wdkv_s, 4), (dq_bf, wdq_s, 5)):
                    for kc in range(DC):
                        mm(em, st_ if kc == DC - 1 else None, lps[pb][:, :],
                           wsl["h"][:, kc, :], Y[:, kc, :],
                           start=(kc == 0), stop=(kc == DC - 1),
                           bufs_r=[wsl["buf"], Ybuf],
                           bufs_w=[lpsb[pb]] if kc == 0 else ())
                    lpsb[pb].note_write(st_.h, st_.n)
                    op(em, "vector", sv, "tensor_copy", dst["h"][:, ts],
                       lps[pb][:, :], bufs_r=[lpsb[pb]], bufs_w=[dst["buf"]])
                for kc in range(DC):
                    mm(em, st_ if kc == DC - 1 else None, lps[6][:64, :],
                       wkr_s["h"][:, kc, :], Y[:, kc, :],
                       start=(kc == 0), stop=(kc == DC - 1),
                       bufs_r=[wkr_s["buf"], Ybuf],
                       bufs_w=[lpsb[6]] if kc == 0 else ())
                lpsb[6].note_write(st_.h, st_.n)
                op(em, "vector", sv, "tensor_copy", kr_bf["h"][:, ts],
                   lps[6][:64, :], bufs_r=[lpsb[6]], bufs_w=[kr_bf["buf"]])
                mm(em, st_, lps[7][:64, :], pswap["h"][:], kr_bf["h"][:, ts],
                   start=True, stop=True,
                   bufs_r=[pswap["buf"], kr_bf["buf"]], bufs_w=[lpsb[7]])
                op(em, "vector", sv, "tensor_mul", rsc1["h"][:64, :],
                   lps[6][:64, :], cosf["h"][:, ts],
                   bufs_r=[lpsb[6], cosf["buf"]], bufs_w=[rsc1["buf"]])
                op(em, "vector", sv, "tensor_mul", rsc2["h"][:64, :],
                   lps[7][:64, :], sinf["h"][:, ts],
                   bufs_r=[lpsb[7], sinf["buf"]], bufs_w=[rsc2["buf"]])
                op(em, "vector", sv, "tensor_add", krop_bf["h"][:, ts],
                   rsc1["h"][:64, :], rsc2["h"][:64, :],
                   bufs_r=[rsc1["buf"], rsc2["buf"]], bufs_w=[krop_bf["buf"]])
                for h in range(H):
                    pb = h % 2
                    mm(em, st_, lps[pb][:64, :], wqc_s["h"][:, h, :],
                       dq_bf["h"][:, ts], start=True, stop=True,
                       bufs_r=[wqc_s["buf"], dq_bf["buf"]], bufs_w=[lpsb[pb]])
                    op(em, "vector", sv, "tensor_copy", qnr["h"][:64, h, ts],
                       lps[pb][:64, :], bufs_r=[lpsb[pb]], bufs_w=[qnr["buf"]])
                    mm(em, st_, lps[2 + pb][:64, :], wqr_s["h"][:, h, :],
                       dq_bf["h"][:, ts], start=True, stop=True,
                       bufs_r=[wqr_s["buf"], dq_bf["buf"]], bufs_w=[lpsb[2 + pb]])
                    op(em, "vector", sv, "tensor_copy", qr_bf["h"][:],
                       lps[2 + pb][:64, :], bufs_r=[lpsb[2 + pb]],
                       bufs_w=[qr_bf["buf"]])
                    mm(em, st_, lps[6][:64, :], pswap["h"][:], qr_bf["h"][:],
                       start=True, stop=True,
                       bufs_r=[pswap["buf"], qr_bf["buf"]], bufs_w=[lpsb[6]])
                    op(em, "vector", sv, "tensor_mul", rsc1["h"][:64, :],
                       lps[2 + pb][:64, :], cosf["h"][:, ts],
                       bufs_r=[lpsb[2 + pb], cosf["buf"]], bufs_w=[rsc1["buf"]])
                    op(em, "vector", sv, "tensor_mul", rsc2["h"][:64, :],
                       lps[6][:64, :], sinf["h"][:, ts],
                       bufs_r=[lpsb[6], sinf["buf"]], bufs_w=[rsc2["buf"]])
                    op(em, "vector", sv, "tensor_add", qnr["h"][64:128, h, ts],
                       rsc1["h"][:64, :], rsc2["h"][:64, :],
                       bufs_r=[rsc1["buf"], rsc2["buf"]], bufs_w=[qnr["buf"]])

            dma(em, "sync", contrib[0:128, :], ckv_bf["h"][:],
                bufs_r=[ckv_bf["buf"]], bufs_w=[contrib_b])
            dma(em, "sync", contrib[128:192, :], krop_bf["h"][:],
                bufs_r=[krop_bf["buf"]], bufs_w=[contrib_b])
            em.emit("gpsimd", "collective_compute",
                    ("AllGather", ALU.bypass,
                     [[0, 1], [2, 3], [4, 5], [6, 7]],
                     [contrib.ap().opt()], [gathered.ap().opt()]),
                    {}, scc, 1, (), bufs_r=[contrib_b], bufs_w=[gathered_b])
            dma(em, "sync", b0c["h"][:], gathered[0:128, :],
                bufs_r=[gathered_b], bufs_w=[b0c["buf"]])
            dma(em, "sync", b1c["h"][:], gathered[192:320, :],
                bufs_r=[gathered_b], bufs_w=[b1c["buf"]])
            dma(em, "sync", b0k["h"][:], gathered[128:192, :],
                bufs_r=[gathered_b], bufs_w=[b0k["buf"]])
            dma(em, "sync", b1k["h"][:], gathered[320:384, :],
                bufs_r=[gathered_b], bufs_w=[b1k["buf"]])
            ckv_ap = ckv_of["h"][:].rearrange("p (s t) -> p s t", s=2)
            for dstap, dstbuf, b0, b1, np_ in (
                    (ckv_ap, ckv_of["buf"], b0c, b1c, 128),
                    (krf_ap, krf_of["buf"], b0k, b1k, 64)):
                op(em, "vector", sv, "tensor_scalar_mul", dstap[:, 0, :],
                   b0["h"][:], msel["h"][:np_, 0:1],
                   bufs_r=[b0["buf"], msel["buf"]], bufs_w=[dstbuf])
                op(em, "vector", sv, "scalar_tensor_tensor", dstap[:, 0, :],
                   b1["h"][:], msel["h"][:np_, 1:2], dstap[:, 0, :],
                   ALU.mult, ALU.add,
                   bufs_r=[b1["buf"], msel["buf"], dstbuf],
                   bufs_w=[dstbuf])
                op(em, "vector", sv, "tensor_scalar_mul", dstap[:, 1, :],
                   b0["h"][:], msel["h"][:np_, 1:2],
                   bufs_r=[b0["buf"], msel["buf"]], bufs_w=[dstbuf])
                op(em, "vector", sv, "scalar_tensor_tensor", dstap[:, 1, :],
                   b1["h"][:], msel["h"][:np_, 0:1], dstap[:, 1, :],
                   ALU.mult, ALU.add,
                   bufs_r=[b1["buf"], msel["buf"], dstbuf],
                   bufs_w=[dstbuf])
        full_barrier(em, sems, exclude=sh.get("prep_sems", ()))

        with contextlib.ExitStack() as s2:
            def s2b(name, shape, dtype):
                return {"h": s2.enter_context(
                    nc.sbuf_tensor(pre + name, shape, dtype)),
                        "buf": Buf(True)}
            knr = s2b("knr", [128, 2 * T], BF16)
            v_h = s2b("v_h", [128, 16, 128], BF16)
            mdiag = s2b("mdiag", [128, 4, 512], F32)
            wuk_s = s2b("wuk_s", [128, H, 64], BF16)
            wuv_s = s2b("wuv_s", [128, H, 128], BF16)
            dma(em, "sync", mdiag["h"][:], mdiag_d, bufs_w=[mdiag["buf"]])
            dma(em, "gpsimd", wuk_s["h"][:], w_uk.rearrange("h p c -> p h c"),
                bufs_w=[wuk_s["buf"]])
            dma(em, "gpsimd", wuv_s["h"][:], w_uv.rearrange("h p c -> p h c"),
                bufs_w=[wuv_s["buf"]])
            P_sb = s2b("P_sb", [128, 4, 512], BF16)
            P_T = s2b("P_T", [128, 16, 128], BF16)
            av_bf = s2b("av_bf", [128, 128], BF16)
            cols = s2b("cols", [128, 16], F32)
            sps = [s2.enter_context(
                nc.psum_tensor(pre + f"sps{i}", [128, 512], F32))
                   for i in range(4)]
            spsb = [Buf() for _ in range(4)]
            tpb = [s2.enter_context(
                nc.psum_tensor(pre + f"tpb{i}", [128, 1024], BF16))
                   for i in range(2)]
            tpbb = [Buf(), Buf()]
            avp = [s2.enter_context(
                nc.psum_tensor(pre + f"avp{i}", [128, 512], F32))
                   for i in range(2)]
            avpb = [Buf(), Buf()]
            ch = cols["h"]
            mx_c = [ch[:, i:i + 1] for i in range(4)]
            own_c, prt_c, mxt_c = ch[:, 4:5], ch[:, 5:6], ch[:, 6:7]
            bias_c, bias2_c = ch[:, 7:8], ch[:, 8:9]
            den_c, rden_c = ch[:, 9:10], ch[:, 10:11]

            op(em, "vector", sv, "tensor_copy", knr["h"][64:128, :],
               att_sb["h"][0:64, 0:2, :], bufs_r=[krf_of["buf"]],
               bufs_w=[knr["buf"]])
            for h in range(H):
                for kt in range(4):
                    ks = slice(kt * 512, (kt + 1) * 512)
                    pb = kt % 2
                    mm(em, st_, sps[pb][:64, :], wuk_s["h"][:, h, :],
                       ckv_of["h"][:, ks], start=True, stop=True,
                       bufs_r=[wuk_s["buf"], ckv_of["buf"]], bufs_w=[spsb[pb]])
                    op(em, "vector", sv, "tensor_copy", knr["h"][0:64, ks],
                       sps[pb][:64, :], bufs_r=[spsb[pb]], bufs_w=[knr["buf"]])
                for kc in range(16):
                    pb = kc % 2
                    mm(em, st_, sps[pb][:, :128],
                       ckv_of["h"][:, kc * 128:(kc + 1) * 128],
                       wuv_s["h"][:, h, :], start=True, stop=True,
                       bufs_r=[ckv_of["buf"], wuv_s["buf"]], bufs_w=[spsb[pb]])
                    op(em, "vector", sv, "tensor_copy", v_h["h"][:, kc, :],
                       sps[pb][:, :128], bufs_r=[spsb[pb]], bufs_w=[v_h["buf"]])
                for qc in range(8):
                    nown = qc // 4 + 1
                    kts = list(range(nown)) + [2, 3]
                    qsl = qnr["h"][:, h, qc * 128:(qc + 1) * 128]
                    for i, kt in enumerate(kts):
                        koff = kt * 512
                        mm(em, st_, sps[i][:, :], qsl,
                           knr["h"][:, koff:koff + 512], start=True, stop=True,
                           bufs_r=[qnr["buf"], knr["buf"]], bufs_w=[spsb[i]])
                    op(em, "vector", sv, "tensor_add", sps[nown - 1][:, :],
                       sps[nown - 1][:, :], mdiag["h"][:, qc % 4, :],
                       bufs_r=[spsb[nown - 1], mdiag["buf"]],
                       bufs_w=[spsb[nown - 1]])
                    for i, kt in enumerate(kts):
                        op(em, "vector", sv, "reduce_max", mx_c[i],
                           sps[i][:, :], AX.X, bufs_r=[spsb[i]],
                           bufs_w=[cols["buf"]])
                    if nown == 1:
                        op(em, "vector", sv, "tensor_copy", own_c, mx_c[0],
                           bufs_r=[cols["buf"]], bufs_w=[cols["buf"]])
                    else:
                        op(em, "vector", sv, "tensor_max", own_c, mx_c[0],
                           mx_c[1], bufs_r=[cols["buf"]], bufs_w=[cols["buf"]])
                    op(em, "vector", sv, "tensor_max", prt_c, mx_c[nown],
                       mx_c[nown + 1], bufs_r=[cols["buf"]], bufs_w=[cols["buf"]])
                    op(em, "vector", sv, "tensor_sub", prt_c, prt_c,
                       pc["h"][:], bufs_r=[cols["buf"], pc["buf"]],
                       bufs_w=[cols["buf"]])
                    op(em, "vector", sv, "tensor_max", mxt_c, own_c, prt_c,
                       bufs_r=[cols["buf"]], bufs_w=[cols["buf"]])
                    op(em, "vector", sv, "tensor_scalar_mul", bias_c, mxt_c,
                       -SC, bufs_r=[cols["buf"]], bufs_w=[cols["buf"]])
                    op(em, "vector", sv, "tensor_sub", bias2_c, bias_c,
                       pc["h"][:], bufs_r=[cols["buf"], pc["buf"]],
                       bufs_w=[cols["buf"]])
                    for i, kt in enumerate(kts):
                        bias_ap = bias_c if kt < 2 else bias2_c
                        acc = ch[:, 11 + i:12 + i]
                        op(em, "scalar", sc, "activation", P_sb["h"][:, kt, :],
                           sps[i][:, :], AF.Exp, scale=SC, bias=bias_ap,
                           accum_out=acc,
                           bufs_r=[spsb[i], cols["buf"]],
                           bufs_w=[P_sb["buf"], cols["buf"]])
                    op(em, "vector", sv, "tensor_add", den_c, ch[:, 11:12],
                       ch[:, 12:13], bufs_r=[cols["buf"]], bufs_w=[cols["buf"]])
                    for i in range(2, len(kts)):
                        op(em, "vector", sv, "tensor_add", den_c, den_c,
                           ch[:, 11 + i:12 + i], bufs_r=[cols["buf"]],
                           bufs_w=[cols["buf"]])
                    op(em, "vector", sv, "reciprocal", rden_c, den_c,
                       bufs_r=[cols["buf"]], bufs_w=[cols["buf"]])
                    ocks = list(range(qc + 1))
                    for j, ck in enumerate(ocks):
                        op(em, "tensor", st_, "transpose",
                           tpb[0][:, j * 128:(j + 1) * 128],
                           P_sb["h"][:, ck // 4, (ck % 4) * 128:(ck % 4 + 1) * 128],
                           ident["h"][:],
                           bufs_r=[P_sb["buf"], ident["buf"]], bufs_w=[tpbb[0]])
                    op(em, "vector", sv, "tensor_copy",
                       P_T["h"][:, 0:qc + 1, :], tpb[0][:, 0:(qc + 1) * 128],
                       bufs_r=[tpbb[0]], bufs_w=[P_T["buf"]])
                    for j in range(8):
                        sl = 2 + j // 4
                        op(em, "tensor", st_, "transpose",
                           tpb[1][:, j * 128:(j + 1) * 128],
                           P_sb["h"][:, sl, (j % 4) * 128:(j % 4 + 1) * 128],
                           ident["h"][:],
                           bufs_r=[P_sb["buf"], ident["buf"]], bufs_w=[tpbb[1]])
                    op(em, "vector", sv, "tensor_copy",
                       P_T["h"][:, 8:16, :], tpb[1][:, :],
                       bufs_r=[tpbb[1]], bufs_w=[P_T["buf"]])
                    ab = qc % 2
                    cks = ocks + list(range(8, 16))
                    for j, ck in enumerate(cks):
                        mm(em, st_ if j == len(cks) - 1 else None,
                           avp[ab][:, :128], P_T["h"][:, ck % 16, :],
                           v_h["h"][:, ck % 16, :],
                           start=(j == 0), stop=(j == len(cks) - 1),
                           bufs_r=[P_T["buf"], v_h["buf"]],
                           bufs_w=[avpb[ab]] if j == 0 else ())
                    avpb[ab].note_write(st_.h, st_.n)
                    op(em, "vector", sv, "tensor_scalar_mul", av_bf["h"][:],
                       avp[ab][:, :128], rden_c,
                       bufs_r=[avpb[ab], cols["buf"]], bufs_w=[av_bf["buf"]])
                    op(em, "tensor", st_, "transpose", tpb[0][:, :128],
                       av_bf["h"][:], ident["h"][:],
                       bufs_r=[av_bf["buf"], ident["buf"]], bufs_w=[tpbb[0]])
                    op(em, "vector", sv, "tensor_copy",
                       att_sb["h"][:, h, qc * 128:(qc + 1) * 128],
                       tpb[0][:, :128], bufs_r=[tpbb[0]], bufs_w=[att_sb["buf"]])
        full_barrier(em, sems)

        with contextlib.ExitStack() as s3:
            def s3b(name, shape, dtype):
                return {"h": s3.enter_context(
                    nc.sbuf_tensor(pre + name, shape, dtype)),
                        "buf": Buf(True)}
            wslab = [s3b(f"wo{i}", [128, DC, 512], BF16) for i in range(2)]
            po = [s3.enter_context(
                nc.psum_tensor(pre + f"po{i}", [128, 512], F32))
                  for i in range(2)]
            pob = [Buf(), Buf()]
            wo_r = w_o.rearrange("(kc p) o -> p kc o", p=128)
            for tt in range(NTT):
                ts = slice(tt * TT, (tt + 1) * TT)
                for sp_i in range(4):
                    if tt == 0:
                        dma(em, "gpsimd", wslab[sp_i % 2]["h"][:],
                            wo_r[:, :, 512 * sp_i:512 * (sp_i + 1)],
                            bufs_w=[wslab[sp_i % 2]["buf"]])
                    for oc_rel in range(4):
                        oc = 4 * sp_i + oc_rel
                        pb = oc % 2
                        for kc in range(DC):
                            mm(em, st_ if kc == DC - 1 else None, po[pb][:, :],
                               wslab[sp_i % 2]["h"][:, kc,
                                                    oc_rel * 128:(oc_rel + 1) * 128],
                               att_sb["h"][:, kc, ts],
                               start=(kc == 0), stop=(kc == DC - 1),
                               bufs_r=[wslab[sp_i % 2]["buf"], att_sb["buf"]],
                               bufs_w=[pob[pb]] if kc == 0 else ())
                        pob[pb].note_write(st_.h, st_.n)
                        op(em, "vector", sv, "tensor_add",
                           X[:, oc, ts], po[pb][:, :], X[:, oc, ts],
                           bufs_r=[pob[pb], Xbuf[tt]], bufs_w=[Xbuf[tt]])

    full_barrier(em, sems, exclude=sh.get("prep_sems", ()))
    em.release(smark)


def build_fused():
    nc = bass.Bass("TRN2", target_bir_lowering=False)
    x_i = nc.dram_tensor("x_i", [D, T], F32, kind="ExternalInput").ap()
    x_o = nc.dram_tensor("x_o", [D, T], F32, kind="ExternalOutput").ap()
    P0f = declare_ffn_l0(nc)
    P0a = declare_attn_l0(nc)
    P1f = declare_ffn_l1(nc)
    P1a = declare_attn_l1(nc)
    ones_col_d = nc.inline_tensor(np.ones((128, 1), np.float32), name="onc_d").ap()
    ones_row_d = nc.inline_tensor(np.ones((1, 128), np.float32), name="onr_d").ap()
    ident_d = nc.inline_tensor(np.eye(128, dtype=nbf), name="ident_d").ap()

    em = Em(nc)
    with contextlib.ExitStack() as stack:
        ent = stack.enter_context
        em.pool = [Sem(ent(nc.semaphore(f"dp{i}"))) for i in range(88)]
        sems = {k: Sem(ent(nc.semaphore(k))) for k in
                ("st", "sv", "sc", "sp", "cc")}
        Xh = ent(nc.sbuf_tensor("X", [128, DC, T], F32))
        Xbuf = [Buf(True), Buf(True)]
        Xbuf[1].dsem = Xbuf[0].dsem = em.take_sem()
        onc = {"h": ent(nc.sbuf_tensor("onc", [128, 1], F32)), "buf": Buf(True)}
        onr = {"h": ent(nc.sbuf_tensor("onr", [1, 128], F32)), "buf": Buf(True)}
        dma(em, "sync", Xh[:], x_i.rearrange("(c p) t -> p c t", p=128),
            bufs_w=Xbuf)
        cr = [dma(em, "sync", onc["h"][:], ones_col_d, bufs_w=[onc["buf"]]),
              dma(em, "sync", onr["h"][:], ones_row_d, bufs_w=[onr["buf"]])]
        for e in ENGINES:
            em.wait(e, cr)
        cbuf = [{"h": ent(nc.sbuf_tensor(f"cbuf{i}", [128, 2048], BF16)),
                 "buf": Buf(True)} for i in range(2)]
        tslab = [{"h": ent(nc.sbuf_tensor(f"tsl{i}", [128, 2048], BF16)),
                  "buf": Buf(True)} for i in range(2)]
        sh = {"X": Xh[:], "Xbuf": Xbuf, "onc": onc, "onr": onr,
              "C": {"ones_col": onc["h"][:], "ones_row": onr["h"][:]},
              "ident_d": ident_d, "onr_d": ones_row_d}

        emit_weight_prep(em, P0f["w_gu"], 2 * INTER, D, P0f["gu_bf"],
                         P0f["gu_bf_b"], P0f["guT"], P0f["guT_b"], cbuf, tslab)
        emit_weight_prep(em, P0f["w_dn"], D, INTER, P0f["dn_bf"],
                         P0f["dn_bf_b"], P0f["dnT"], P0f["dnT_b"], cbuf, tslab)
        import collections as _coll
        bg = _coll.deque()
        emit_weight_prep(em, P0a["w_qkv"], D, 3 * D, P0a["qkv_bf"],
                         P0a["qkv_bf_b"], None, None, cbuf, tslab, defer=bg)
        emit_weight_prep(em, P0a["w_out"], D, D, P0a["out_bf"],
                         P0a["out_bf_b"], None, None, cbuf, tslab, defer=bg)
        for w, r, c, k in (("w_up", INTER, D, "up"), ("w_gt", INTER, D, "gt"),
                           ("w_dn1", D, INTER, "dn1")):
            emit_weight_prep(em, P1f[w], r, c, P1f[k + "_bf"],
                             P1f[k + "_bf_b"], None, None, cbuf, tslab,
                             defer=bg)
        for w, r, c, k in (("w_up", INTER, D, "up"), ("w_gt", INTER, D, "gt"),
                           ("w_dn1", D, INTER, "dn1")):
            emit_weight_prep(em, None if False else P1f[w], r, c,
                             P1f[k + "_bf"], P1f[k + "_bf_b"],
                             P1f[k + "T"], P1f[k + "T_b"], cbuf, tslab,
                             tq="scalar", defer=bg, cast=False)
        sh["prep_sems"] = [b["buf"].dsem for b in cbuf + tslab
                           if b["buf"].dsem is not None]
        sh["bg"] = bg
        emit_ffn_l0(nc, em, sems, sh, P0f, "fa_")
        emit_attn_l0(nc, em, sems, sh, P0a, "aa_")
        while bg:
            bg.popleft()()
        sh["prep_sems"] = [b["buf"].dsem for b in cbuf + tslab
                           if b["buf"].dsem is not None]
        emit_ffn_l0(nc, em, sems, sh, P0f, "fb_")
        emit_ffn_l1(nc, em, sems, sh, P1f, "ga_")
        emit_attn_l1(nc, em, sems, sh, P1a, "ha_")
        emit_ffn_l1(nc, em, sems, sh, P1f, "gb_")

        fin = dma(em, "sync", x_o.rearrange("(c p) t -> p c t", p=128), Xh[:],
                  bufs_r=[Xbuf[0], Xbuf[1]])
        em.wait("sync", [fin])
        em.finalize()
    return nc



def shard_x(x):
    out = []
    for c in range(NCORE):
        b, h = c // 2, c % 2
        out.append(np.ascontiguousarray(
            np.asarray(x)[b, h * T:(h + 1) * T, :].T.astype(np.float32)))
    return out


def unshard_x(shards):
    x = np.empty((4, 2048, D), np.float32)
    for c in range(NCORE):
        b, h = c // 2, c % 2
        x[b, h * T:(h + 1) * T, :] = shards[c].T
    return x


_CACHE = {}


def _rope_tables(off):
    import math
    half = 32
    freqs = np.exp(np.arange(half, dtype=np.float32) * (-math.log(10000.0) / half))
    tg = np.arange(off, off + T, dtype=np.float32)
    ang = tg[:, None] * freqs[None, :]
    cos = np.cos(ang).astype(np.float32)
    sin = np.sin(ang).astype(np.float32)
    cosf = np.empty((64, T), np.float32)
    sinf = np.empty((64, T), np.float32)
    cosf[0::2] = cos.T
    cosf[1::2] = cos.T
    sinf[0::2] = -sin.T
    sinf[1::2] = sin.T
    return cosf, sinf


def kernel(**inputs):
    if "fused" not in _CACHE:
        _CACHE["fused"] = build_fused()
    xs = shard_x(inputs["L0_attn_x"])
    base = {
        "w_gu": np.asarray(inputs["L0_ffn_w_gate_up"]),
        "w_dn": np.asarray(inputs["L0_ffn_w_down"]),
        "w_qkv": np.asarray(inputs["L0_attn_w_qkv"]),
        "w_out": np.asarray(inputs["L0_attn_w_out"]),
        "decay_i": np.asarray(inputs["L0_attn_decay_param"]).reshape(1, 16)
                     .astype(np.float32),
        "w_up": np.asarray(inputs["L1_ffn_W_up"]),
        "w_gt": np.asarray(inputs["L1_ffn_W_gate"]),
        "w_dn1": np.asarray(inputs["L1_ffn_W_down"]),
        "su_ug": np.asarray(inputs["L1_ffn_SU_ug"]).reshape(16, 128),
        "su_dp": np.asarray(inputs["L1_ffn_SU_dp"]).reshape(64, 128),
        "wsc_ug": np.asarray(inputs["L1_ffn_Wscale_ug"]).reshape(128, 128),
        "wsc_dp": np.asarray(inputs["L1_ffn_Wscale_dp"]).reshape(16, 128),
        "w_dkv": np.asarray(inputs["L1_attn_W_dkv"]),
        "w_dq": np.asarray(inputs["L1_attn_W_dq"]),
        "w_kr": np.asarray(inputs["L1_attn_W_kr"]),
        "w_qc": np.asarray(inputs["L1_attn_W_qc"]),
        "w_qr": np.asarray(inputs["L1_attn_W_qr"]),
        "w_uk": np.asarray(inputs["L1_attn_W_uk"]),
        "w_uv": np.asarray(inputs["L1_attn_W_uv"]),
        "w_o": np.asarray(inputs["L1_attn_W_o"]),
    }
    in_maps = []
    for c in range(NCORE):
        p = c % 2
        cosf, sinf = _rope_tables(p * T)
        cosf = cosf.astype(nbf)
        sinf = sinf.astype(nbf)
        in_maps.append(dict(
            base, x_i=xs[c],
            csel_i=np.full((128, 1), float(p), np.float32),
            cosf_i=cosf, sinf_i=sinf,
            pc_i=np.full((128, 1), 1e9 if p == 0 else 0.0, np.float32),
            msel_i=np.tile(np.array([[1.0 - p, p]], np.float32), (128, 1)),
        ))
    res = run_bass_kernel_spmd(_CACHE["fused"], in_maps,
                               core_ids=list(range(NCORE)))
    return unshard_x([res.results[c]["x_o"] for c in range(NCORE)])
